# version 01
# baseline (speedup 1.0000x reference)
import numpy as np
import ml_dtypes
from contextlib import ExitStack
import concourse.bass as bass
import concourse.mybir as mybir
from concourse.bass_utils import run_bass_kernel_spmd

F32 = mybir.dt.float32
BF16 = mybir.dt.bfloat16
AF = mybir.ActivationFunctionType
ALU = mybir.AluOpType
AX = mybir.AxisListType

ENGINES = ("te", "ve", "sc", "gp", "sy")
RG = [[0, 1, 2, 3], [4, 5, 6, 7]]

D = 2048
T = 2048
TB = 512
NBLK = 4
KT = 16
SEQ = 8192
EPS_LN = 1e-5
EPS_S2 = 1e-6
EPS_MOE = 1e-6
ISQL = float(1.0 / np.sqrt(128.0))

FUSED = False



class Prog:
    def __init__(self):
        self.nc = bass.Bass("TRN2", target_bir_lowering=False,
                            disable_frame_to_traceback=True)
        self.ops = {e: [] for e in ENGINES}
        self.sems = {}
        self.sem_order = []
        self._h = {}

    def _bump(self, name, n):
        if name not in self.sems:
            self.sems[name] = 0
            self.sem_order.append(name)
        self.sems[name] += n
        return (name, self.sems[name])

    def tok(self, name):
        return (name, self.sems.get(name, 0))

    def wait(self, eng, tok):
        if tok is None:
            return
        name, val = tok
        if val <= 0:
            return
        assert val <= self.sems.get(name, 0), f"wait beyond incs: {name} {val}"
        self.ops[eng].append(lambda e, n=name, v=val: e.wait_ge(self._h[n], v))

    def dma(self, q, out, in_, transpose=False, sem=None):
        s = sem or (q + "dma")
        tok = self._bump(s, 16)
        def fn(e, out=out, in_=in_, t=transpose, n=s):
            e.dma_start(out=out, in_=in_, transpose=t).then_inc(self._h[n], 16)
        self.ops[q].append(fn)
        return tok

    def op(self, eng, fn, sig=False):
        if not sig:
            self.ops[eng].append(fn)
            return None
        s = eng + "sig"
        tok = self._bump(s, 1)
        def wrapped(e, n=s):
            fn(e).then_inc(self._h[n], 1)
        self.ops[eng].append(wrapped)
        return tok

    def cc(self, kind, ins, outs, name, op=None):
        tok = self._bump(name, 1)
        aop = op if op is not None else ALU.bypass
        def fn(e, n=name):
            e.collective_compute(kind, aop, replica_groups=RG,
                                 ins=ins, outs=outs).then_inc(self._h[n])
        self.ops["gp"].append(fn)
        return tok

    def finalize(self):
        nc = self.nc
        stack = ExitStack()
        for n in self.sem_order:
            self._h[n] = stack.enter_context(nc.semaphore(n))
        block = stack.enter_context(nc.Block())
        decos = {"te": block.tensor, "ve": block.vector, "sc": block.scalar,
                 "gp": block.gpsimd, "sy": block.sync}
        for name in ENGINES:
            def mk(oplist):
                def body(e):
                    for fn in oplist:
                        fn(e)
                return body
            decos[name](mk(self.ops[name]))
        stack.close()
        return nc


class Rot:
    def __init__(self, p, banks):
        self.p, self.banks = p, banks
        self.free = [None] * len(banks)
        self.i = 0
    def get(self):
        j = self.i % len(self.banks)
        self.i += 1
        return j, self.banks[j], self.free[j]
    def done(self, j, tok):
        self.free[j] = tok


def stream_slabs(p, n, make_load, body, q="sy"):
    loads = [None] * n
    free = {}
    def issue(i):
        if i - 2 >= 0:
            p.wait(q, free[i - 2])
        loads[i] = make_load(i, i % 2)
    p.wait(q, p.tok("tesig"))
    if n > 0:
        issue(0)
    if n > 1:
        issue(1)
    for i in range(n):
        free[i] = body(i, i % 2, loads[i])
        assert free[i] is not None
        if i + 2 < n:
            issue(i + 2)


class Arenas:
    def __init__(self, nc, stack):
        E = stack.enter_context
        self.A1 = E(nc.sbuf_tensor("A1", [128, NBLK, D], F32))
        self.A2 = E(nc.sbuf_tensor("A2", [128, NBLK, D], BF16))
        self.A3 = E(nc.sbuf_tensor("A3", [128, KT, TB], BF16))
        self.A4 = E(nc.sbuf_tensor("A4", [128, 32768], BF16))
        self.A5 = E(nc.sbuf_tensor("A5", [128, 2, KT, TB], BF16))
        self.A6 = E(nc.sbuf_tensor("A6", [128, 6144], F32))
        self.A7 = E(nc.sbuf_tensor("A7", [128, 128], F32))
        self.BG = E(nc.sbuf_tensor("BG", [128, 64], F32))
        self.RMSW = E(nc.sbuf_tensor("RMSW", [128, 4], F32))
        self.BD = E(nc.sbuf_tensor("BD", [128, 4], F32))
        self.BL = E(nc.sbuf_tensor("BL", [128, 1], F32))
        self.ID = E(nc.sbuf_tensor("ID", [128, 128], BF16))
        self.ONES = E(nc.sbuf_tensor("ONES", [128, 128], BF16))
        self.MSK = E(nc.sbuf_tensor("MSK", [128, 1024], BF16))
        self.WB = E(nc.sbuf_tensor("WB", [128, 2304], BF16))
        self.PS = [E(nc.psum_tensor(f"PS{i}", [128, 512], F32)) for i in range(6)]
        self.PB = [E(nc.psum_tensor(f"PB{i}", [128, 1024], BF16)) for i in range(2)]

    def v4(self, off, a_, b_):
        return self.A4[:, off:off + a_ * b_].rearrange("p (a b) -> p a b", a=a_)

    def f4(self, off, n):
        return self.A4[:, off:off + n]

    def r4(self, off, n):
        return self.A4[0:1, off:off + n]

    def v6(self, off, n):
        return self.A6[:, off:off + n]

    def v6b(self, off, a_, b_):
        return self.A6[:, off:off + a_ * b_].rearrange("p (a b) -> p a b", a=a_)


def rr(ap, p=128):
    return ap.rearrange("(a p) c -> p a c", p=p)



def emit_wt_construct(p, a, w_hbm, wb, dst, O, K, perm=None):
    nob, nkb = O // 128, K // 128
    perm = perm or list(range(nob))
    p.wait("gp", p.tok("wts0"))
    p.wait("gp", p.tok("wts1"))
    p.wait("sy", p.tok("wts0"))
    p.wait("sy", p.tok("wts1"))
    buf = [a.A5[:, 0].rearrange("p a b -> p (a b)"),
           a.A5[:, 1].rearrange("p a b -> p (a b)")]
    last_use = [None, None]
    for i in range(nob):
        par = i % 2
        p.wait("gp", last_use[par])
        ld = p.dma("gp", buf[par][:, 0:K],
                   w_hbm[perm[i] * 128:(perm[i] + 1) * 128, :],
                   sem=f"wcl{par}")
        p.wait("sy", ld)
        last_use[par] = p.dma("sy", wb[i * 128:(i + 1) * 128, :],
                              buf[par][:, 0:K], sem=f"wcs{par}")
    p.wait("sy", last_use[0])
    p.wait("sy", last_use[1])
    tbuf = [a.f4(0, O), a.f4(16384, O)]
    last_st = [None, None]
    for k in range(nkb):
        par = k % 2
        p.wait("sy", last_st[par])
        tl = p.dma("sy", tbuf[par], wb[:, k * 128:(k + 1) * 128],
                   transpose=True, sem=f"wtt{par}")
        p.wait("gp", tl)
        last_st[par] = p.dma("gp", dst[k * 128:(k + 1) * 128, :], tbuf[par],
                             sem=f"wts{par}")
    p.wait("gp", last_st[0])
    p.wait("gp", last_st[1])



def emit_ln_block(p, a, src_x, blk, lnt=None, to_fm=True, ln_rows=None,
                  prior=None):
    p.wait("gp", prior)
    xt = p.dma("gp", a.A1[:], rr(src_x[blk * TB:(blk + 1) * TB, :]))
    SB = a.v6b(5632, 16, 8)
    dummy = a.v6(0, 2048)
    p.wait("sc", prior)
    p.wait("ve", prior)
    atoks = []
    for tt in range(4):
        p.wait("ve", xt)
        p.wait("sc", xt)
        t0 = p.op("ve", lambda e, tt=tt: e.reduce_sum(
            out=SB[:, tt, 0:1], in_=a.A1[:, tt, :], axis=AX.X), sig=True)
        s_ssq = p.op("sc", lambda e, tt=tt: e.activation(
            out=dummy, in_=a.A1[:, tt, :], func=AF.Square,
            accum_out=SB[:, tt, 1:2]), sig=True)
        p.wait("ve", s_ssq)
        p.wait("ve", t0)
        t1 = p.op("ve", lambda e, tt=tt: e.tensor_scalar(
            out=SB[:, tt, 2:3], in0=SB[:, tt, 0:1], scalar1=1.0 / D,
            scalar2=None, op0=ALU.mult), sig=True)
        p.wait("ve", t1)
        t2 = p.op("ve", lambda e, tt=tt: e.tensor_tensor(
            out=SB[:, tt, 3:4], in0=SB[:, tt, 2:3], in1=SB[:, tt, 2:3],
            op=ALU.mult), sig=True)
        t3 = p.op("ve", lambda e, tt=tt: e.tensor_scalar(
            out=SB[:, tt, 4:5], in0=SB[:, tt, 1:2], scalar1=1.0 / D,
            scalar2=EPS_LN, op0=ALU.mult, op1=ALU.add), sig=True)
        p.wait("ve", t2)
        p.wait("ve", t3)
        vtok = p.op("ve", lambda e, tt=tt: e.tensor_tensor(
            out=SB[:, tt, 5:6], in0=SB[:, tt, 4:5], in1=SB[:, tt, 3:4],
            op=ALU.subtract), sig=True)
        p.wait("sc", vtok)
        stok = p.op("sc", lambda e, tt=tt: e.activation(
            out=SB[:, tt, 6:7], in_=SB[:, tt, 5:6], func=AF.Sqrt), sig=True)
        p.wait("ve", stok)
        t4 = p.op("ve", lambda e, tt=tt: e.reciprocal(
            out=SB[:, tt, 7:8], in_=SB[:, tt, 6:7]), sig=True)
        p.wait("ve", t4)
        atok = p.op("ve", lambda e, tt=tt: e.tensor_scalar(
            out=a.A2[:, tt, :], in0=a.A1[:, tt, :], scalar1=SB[:, tt, 2:3],
            scalar2=SB[:, tt, 7:8], op0=ALU.subtract, op1=ALU.mult), sig=True)
        atoks.append(atok)
    p.wait("gp", atoks[-1])
    if ln_rows is not None:
        dst, off = ln_rows
        lntok = p.dma("gp", rr(dst[off + blk * TB: off + (blk + 1) * TB, :]),
                      a.A2[:])
        return xt, lntok
    lntok = p.dma("gp", rr(lnt[blk * TB:(blk + 1) * TB, :]), a.A2[:])
    fm_tok = None
    if to_fm:
        p.wait("sy", lntok)
        p.wait("sy", p.tok("tesig"))
        for kt in range(KT):
            fm_tok = p.dma("sy", a.A3[:, kt, :],
                           lnt[blk * TB:(blk + 1) * TB, kt * 128:(kt + 1) * 128],
                           transpose=True, sem="fmt")
    return xt, fm_tok



def emit_ffn0(p, a, src_x, dst_x, lnt, gup_T, wdn_T):
    prior = None
    for blk in range(NBLK):
        xt, fm = emit_ln_block(p, a, src_x, blk, lnt=lnt, prior=prior)
        HM = a.v4(0, 64, 512)
        SG = a.v6(4096, 512)
        ve_all = p.tok("vesig")
        sc_all = p.tok("scsig")
        rot = Rot(p, [(0, 1), (2, 3)])
        hm_last = [None]

        def load_gu(i, par):
            return p.dma("sy", a.A5[:, par],
                         rr(gup_T)[:, :, i * 512:(i + 1) * 512],
                         sem=f"sl{par}")

        def body_gu(i, par, ltok):
            last_te = None
            for half in range(2):
                op = 2 * i + half
                j, pair, ftok = rot.get()
                pg, pu = a.PS[pair[0]], a.PS[pair[1]]
                p.wait("te", ftok)
                p.wait("te", ltok)
                if op == 0:
                    p.wait("te", fm)
                    p.wait("te", ve_all)
                    p.wait("te", sc_all)
                for kt in range(KT):
                    p.op("te", lambda e, kt=kt, par=par, half=half, pg=pg: e.matmul(
                        out=pg[:], lhsT=a.A5[:, par, kt, half * 256:half * 256 + 128],
                        rhs=a.A3[:, kt, :], start=(kt == 0), stop=(kt == KT - 1)))
                for kt in range(KT):
                    tok = p.op("te", lambda e, kt=kt, par=par, half=half, pu=pu: e.matmul(
                        out=pu[:], lhsT=a.A5[:, par, kt, half * 256 + 128:half * 256 + 256],
                        rhs=a.A3[:, kt, :], start=(kt == 0), stop=(kt == KT - 1)),
                        sig=(kt == KT - 1))
                    if kt == KT - 1:
                        last_te = tok
                p.wait("sc", last_te)
                sg = p.op("sc", lambda e, pg=pg: e.activation(
                    out=SG, in_=pg[:], func=AF.Silu), sig=True)
                p.wait("ve", sg)
                ev = p.op("ve", lambda e, op=op, pu=pu: e.tensor_tensor(
                    out=HM[:, op, :], in0=SG, in1=pu[:], op=ALU.mult), sig=True)
                rot.done(j, ev)
                hm_last[0] = ev
            return last_te

        stream_slabs(p, 32, load_gu, body_gu)
        hm_done = hm_last[0]

        STG = a.v6b(0, 4, 512)
        for fh in range(4):
            prev_ev = p.tok("vesig")

            def load_dn(g, par, fh=fh):
                return p.dma("sy", a.A5[:, par, 0:8, :],
                             rr(wdn_T)[:, g * 8:(g + 1) * 8,
                                       fh * 512:(fh + 1) * 512],
                             sem=f"sl{par}")

            def body_dn(g, par, ltok, fh=fh):
                p.wait("te", ltok)
                if g == 0:
                    p.wait("te", hm_done)
                    p.wait("te", prev_ev)
                tok = None
                for kk in range(8):
                    for tt in range(4):
                        sig = (kk == 7 and tt == 3)
                        tok2 = p.op("te", lambda e, g=g, kk=kk, tt=tt, par=par: e.matmul(
                            out=a.PS[tt][:],
                            lhsT=HM[:, g * 8 + kk, tt * 128:(tt + 1) * 128],
                            rhs=a.A5[:, par, kk, :],
                            start=(g == 0 and kk == 0),
                            stop=(g == 7 and kk == 7)), sig=sig)
                        if sig:
                            tok = tok2
                return tok

            mm_done = [None]

            def body_dn2(g, par, ltok, fh=fh):
                t = body_dn(g, par, ltok, fh)
                mm_done[0] = t
                return t

            stream_slabs(p, 8, load_dn, body_dn2)
            p.wait("ve", mm_done[0])
            p.wait("ve", prior)
            ev = None
            for tt in range(4):
                ev = p.op("ve", lambda e, tt=tt, fh=fh: e.scalar_tensor_tensor(
                    out=STG[:, tt, :], in0=a.PS[tt][:], scalar=0.5,
                    in1=a.A1[:, tt, fh * 512:(fh + 1) * 512],
                    op0=ALU.mult, op1=ALU.add), sig=(tt == 3))
            p.wait("gp", ev)
            prior = p.dma("gp",
                          rr(dst_x[blk * TB:(blk + 1) * TB,
                                   fh * 512:(fh + 1) * 512]), STG[:],
                          sem="xst")
    return prior



def emit_l0attn(p, a, src_x, dst_x, lnt, win_T, wgt_T, wout_T,
                cp_scr, cl_scr, o_scr, agin, agout):
    ZER = a.v6(3584, 512)
    p.op("ve", lambda e: e.memset(ZER, 0.0))
    CPC = a.A7[:, 64:80]
    CLC = a.A7[:, 80:96]
    CIN = a.A7[:, 96:112]
    SCM = a.A7[:, 112:115]
    p.op("ve", lambda e: e.memset(CPC, 1.0))
    p.op("ve", lambda e: e.memset(CLC, 0.0))

    XPF = a.v4(0, 16, 512)
    CPS = a.v4(8192, 16, 512)
    CLS = a.v4(16384, 16, 512)
    OS = a.v4(24576, 16, 512)
    T1, T2, T3, T4 = (a.v6(512, 512), a.v6(1024, 512), a.v6(1536, 512),
                      a.v6(2048, 512))

    prior = None
    for blk in range(NBLK):
        xt, fm = emit_ln_block(p, a, src_x, blk, lnt=lnt, prior=prior)
        ve_all = p.tok("vesig")
        sc_all = p.tok("scsig")
        rot = Rot(p, [0, 1, 2, 3])

        def load_xp(i, par):
            return p.dma("sy", a.A5[:, par],
                         rr(win_T)[:, :, i * 512:(i + 1) * 512], sem=f"sl{par}")

        def body_xp(i, par, ltok):
            last = None
            for q in range(4):
                ht = i * 4 + q
                j, bank, ftok = rot.get()
                p.wait("te", ftok)
                p.wait("te", ltok)
                if ht == 0:
                    p.wait("te", fm)
                    p.wait("te", ve_all)
                    p.wait("te", sc_all)
                for kt in range(KT):
                    last = p.op("te", lambda e, kt=kt, par=par, q=q, bank=bank: e.matmul(
                        out=a.PS[bank][:],
                        lhsT=a.A5[:, par, kt, q * 128:(q + 1) * 128],
                        rhs=a.A3[:, kt, :], start=(kt == 0), stop=(kt == KT - 1)),
                        sig=(kt == KT - 1))
                p.wait("sc", last)
                ev = p.op("sc", lambda e, ht=ht, bank=bank: e.copy(
                    out=XPF[:, ht, :], in_=a.PS[bank][:]), sig=True)
                rot.done(j, ev)
            return last

        stream_slabs(p, 4, load_xp, body_xp)
        xp_done = p.tok("scsig")

        def load_g(i, par):
            return p.dma("sy", a.A5[:, par],
                         rr(wgt_T)[:, :, i * 512:(i + 1) * 512], sem=f"sl{par}")

        def body_g(i, par, ltok):
            head, jj = i // 4, i % 4
            ct = i
            gt = head * 16
            p.wait("te", ltok)
            if i == 0:
                p.wait("te", xp_done)
            else:
                p.wait("te", p.tok("scsig"))
            tok = None
            for ty in range(4):
                for kt in range(KT):
                    sig = (ty == 3 and kt == KT - 1)
                    tok2 = p.op("te", lambda e, ty=ty, kt=kt, par=par: e.matmul(
                        out=a.PS[ty][:],
                        lhsT=a.A5[:, par, kt, ty * 128:(ty + 1) * 128],
                        rhs=XPF[:, kt, :], start=(kt == 0), stop=(kt == KT - 1)),
                        sig=sig)
                    if sig:
                        tok = tok2
            p.wait("sc", tok)
            c_i, c_f, c_o, c_c = gt + jj, gt + 4 + jj, gt + 8 + jj, gt + 12 + jj
            p.wait("sc", p.tok("vesig"))
            p.op("sc", lambda e, c=c_i: e.activation(
                out=T1, in_=a.PS[0][:], func=AF.Exp, bias=a.BG[:, c:c + 1]))
            p.op("sc", lambda e, c=c_c: e.activation(
                out=T2, in_=a.PS[3][:], func=AF.Tanh, bias=a.BG[:, c:c + 1]))
            p.op("sc", lambda e, c=c_f: e.activation(
                out=T4, in_=a.PS[1][:], func=AF.Sigmoid, bias=a.BG[:, c:c + 1]))
            sce = p.op("sc", lambda e, c=c_o, ct=ct: e.activation(
                out=OS[:, ct, :], in_=a.PS[2][:], func=AF.Sigmoid,
                bias=a.BG[:, c:c + 1]), sig=True)
            p.wait("ve", sce)
            v1 = p.op("ve", lambda e: e.tensor_tensor(out=T3, in0=T1, in1=T2,
                                                 op=ALU.mult), sig=True)
            v2 = p.op("ve", lambda e, ct=ct: e.tensor_tensor_scan(
                out=CPS[:, ct, :], data0=T4, data1=ZER,
                initial=CPC[:, ct:ct + 1], op0=ALU.mult, op1=ALU.add), sig=True)
            p.wait("ve", v2)
            v3 = p.op("ve", lambda e, ct=ct: e.tensor_copy(
                out=CPC[:, ct:ct + 1], in_=CPS[:, ct, 511:512]), sig=True)
            p.wait("ve", v1)
            v4 = p.op("ve", lambda e, ct=ct: e.tensor_tensor_scan(
                out=CLS[:, ct, :], data0=T4, data1=T3,
                initial=CLC[:, ct:ct + 1], op0=ALU.mult, op1=ALU.add), sig=True)
            p.wait("ve", v4)
            p.op("ve", lambda e, ct=ct: e.tensor_copy(
                out=CLC[:, ct:ct + 1], in_=CLS[:, ct, 511:512]), sig=True)
            return tok

        stream_slabs(p, 16, load_g, body_g)
        scans_done = p.tok("vesig")
        p.wait("gp", scans_done)
        p.dma("gp", rr(cp_scr)[:, :, blk * TB:(blk + 1) * TB], CPS[:])
        p.dma("gp", rr(cl_scr)[:, :, blk * TB:(blk + 1) * TB], CLS[:])
        prior = p.dma("gp", rr(o_scr)[:, :, blk * TB:(blk + 1) * TB], OS[:])

    agt = p.dma("gp", agin[:], a.A7[:, 64:96])
    p.wait("gp", agt)
    cct = p.cc("AllGather", [agin[:]], [agout[:]], "ccscan")
    p.wait("gp", cct)
    G = []
    gt_tok = None
    for r in range(4):
        Gr = a.A6[:, 4096 + r * 32: 4096 + (r + 1) * 32]
        G.append(Gr)
        gt_tok = p.dma("gp", Gr, agout[r * 128:(r + 1) * 128, :])
    p.wait("ve", gt_tok)
    TA = a.A6[:, 4224:4240]
    TBv = a.A6[:, 4240:4256]
    tk = p.op("ve", lambda e: e.memset(CIN, 0.0), sig=True)
    for r in range(3):
        p.wait("ve", tk)
        tk = p.op("ve", lambda e, r=r: e.tensor_scalar(
            out=TA, in0=G[r][:, 0:16], scalar1=1.0, scalar2=SCM[:, r:r + 1],
            op0=ALU.subtract, op1=ALU.mult), sig=True)
        p.wait("ve", tk)
        tk = p.op("ve", lambda e: e.tensor_scalar(
            out=TA, in0=TA, scalar1=1.0, scalar2=None, op0=ALU.add), sig=True)
        p.wait("ve", tk)
        tk = p.op("ve", lambda e: e.tensor_tensor(out=CIN, in0=CIN, in1=TA,
                                             op=ALU.mult), sig=True)
        p.wait("ve", tk)
        tk = p.op("ve", lambda e, r=r: e.tensor_scalar(
            out=TBv, in0=G[r][:, 16:32], scalar1=SCM[:, r:r + 1], scalar2=None,
            op0=ALU.mult), sig=True)
        p.wait("ve", tk)
        tk = p.op("ve", lambda e: e.tensor_tensor(out=CIN, in0=CIN, in1=TBv,
                                             op=ALU.add), sig=True)
    p.wait("ve", tk)
    cin_done = p.op("ve", lambda e: e.tensor_copy(out=TA, in_=CIN), sig=True)

    HFM = a.v4(0, 16, 512)
    Cr = a.v6b(0, 4, 512)
    CN = a.v6(2048, 512)
    RrowF = a.A6[0:1, 2560:3072]
    Rrow16 = a.r4(30720, 512)
    prior2 = p.tok("gpdma")
    prev_ps4 = None
    prev_ps5 = None
    for blk in range(NBLK):
        p.wait("gp", prior2)
        xt = p.dma("gp", a.A1[:], rr(src_x[blk * TB:(blk + 1) * TB, :]))
        p.dma("gp", CPS[:], rr(cp_scr)[:, :, blk * TB:(blk + 1) * TB])
        p.dma("gp", CLS[:], rr(cl_scr)[:, :, blk * TB:(blk + 1) * TB])
        l3 = p.dma("gp", OS[:], rr(o_scr)[:, :, blk * TB:(blk + 1) * TB])
        p.wait("ve", prior2)
        p.wait("ve", l3)
        p.wait("ve", cin_done)
        for head in range(4):
            ctok = None
            for jj in range(4):
                ct = head * 4 + jj
                ctok = p.op("ve", lambda e, ct=ct, jj=jj: e.scalar_tensor_tensor(
                    out=Cr[:, jj, :], in0=CPS[:, ct, :],
                    scalar=CIN[:, ct:ct + 1], in1=CLS[:, ct, :],
                    op0=ALU.mult, op1=ALU.add), sig=(jj == 3))
            p.wait("sc", ctok)
            sq_last = None
            for jj in range(4):
                ct = head * 4 + jj
                sq_last = p.op("sc", lambda e, ct=ct, jj=jj: e.activation(
                    out=HFM[:, ct, :], in_=Cr[:, jj, :], func=AF.Square),
                    sig=(jj == 3))
            p.wait("te", sq_last)
            p.wait("te", prev_ps4)
            mt = None
            for jj in range(4):
                ct = head * 4 + jj
                mt = p.op("te", lambda e, ct=ct, jj=jj: e.matmul(
                    out=a.PS[4][0:1, :], lhsT=a.ONES[:, 0:1],
                    rhs=HFM[:, ct, :], start=(jj == 0), stop=(jj == 3)),
                    sig=(jj == 3))
            p.wait("ve", mt)
            vt = p.op("ve", lambda e: e.tensor_scalar(
                out=RrowF, in0=a.PS[4][0:1, :], scalar1=1.0 / 512.0,
                scalar2=EPS_S2, op0=ALU.mult, op1=ALU.add), sig=True)
            prev_ps4 = vt
            p.wait("sc", vt)
            st2 = p.op("sc", lambda e: e.activation(
                out=RrowF, in_=RrowF, func=AF.Sqrt), sig=True)
            p.wait("ve", st2)
            t5 = p.op("ve", lambda e: e.reciprocal(out=RrowF, in_=RrowF),
                      sig=True)
            p.wait("ve", t5)
            rv = p.op("ve", lambda e: e.tensor_copy(out=Rrow16, in_=RrowF),
                      sig=True)
            p.wait("te", rv)
            p.wait("te", prev_ps5)
            bt = p.op("te", lambda e: e.matmul(
                out=a.PS[5][:], lhsT=a.ONES[0:1, :], rhs=Rrow16,
                start=True, stop=True), sig=True)
            p.wait("ve", bt)
            for jj in range(4):
                ct = head * 4 + jj
                cn = p.op("ve", lambda e, jj=jj: e.tensor_tensor(
                    out=CN, in0=Cr[:, jj, :], in1=a.PS[5][:], op=ALU.mult),
                    sig=True)
                if jj == 3:
                    prev_ps5 = cn
                p.wait("sc", cn)
                th = p.op("sc", lambda e, ct=ct, jj=jj: e.activation(
                    out=HFM[:, ct, :], in_=CN, func=AF.Tanh,
                    scale=a.RMSW[:, jj:jj + 1]), sig=True)
                p.wait("ve", th)
                p.op("ve", lambda e, ct=ct: e.tensor_tensor(
                    out=HFM[:, ct, :], in0=HFM[:, ct, :], in1=OS[:, ct, :],
                    op=ALU.mult), sig=True)
        h_done = p.tok("vesig")

        rot = Rot(p, [0, 1, 2, 3])
        STG = a.v6b(0, 4, 512)
        stores = [None]

        def load_wo(i, par):
            return p.dma("sy", a.A5[:, par],
                         rr(wout_T)[:, :, i * 512:(i + 1) * 512], sem=f"sl{par}")

        def body_wo(cc, par, ltok):
            p.wait("te", ltok)
            if cc == 0:
                p.wait("te", h_done)
            p.wait("ve", stores[0])
            last = None
            ev = None
            for tt in range(4):
                j, bank, ftok = rot.get()
                p.wait("te", ftok)
                for ctc in range(KT):
                    last = p.op("te", lambda e, ctc=ctc, tt=tt, par=par, bank=bank: e.matmul(
                        out=a.PS[bank][:],
                        lhsT=HFM[:, ctc, tt * 128:(tt + 1) * 128],
                        rhs=a.A5[:, par, ctc, :],
                        start=(ctc == 0), stop=(ctc == KT - 1)),
                        sig=(ctc == KT - 1))
                p.wait("ve", last)
                p.wait("ve", xt)
                ev = p.op("ve", lambda e, tt=tt, cc=cc, bank=bank: e.tensor_tensor(
                    out=STG[:, tt, :], in0=a.PS[bank][:],
                    in1=a.A1[:, tt, cc * 512:(cc + 1) * 512], op=ALU.add),
                    sig=True)
                rot.done(j, ev)
            p.wait("gp", ev)
            stores[0] = p.dma(
                "gp", rr(dst_x[blk * TB:(blk + 1) * TB,
                               cc * 512:(cc + 1) * 512]), STG[:], sem="xst")
            return last

        stream_slabs(p, 4, load_wo, body_wo)
        prior2 = stores[0]
    return prior2



def emit_moe(p, a, src_x, dst_x, lnt, acc_scr, w, tag):
    NW = a.f4(8192, 2048)
    PRJB = a.r4(10240, 2048)
    NWROW = a.r4(12288, 2048)
    EB = a.r4(14336, 16384)
    RTRB = a.r4(30720, 8)
    SH = [a.f4(30728, 512), a.f4(31240, 512)]
    ACC = a.v4(0, 4, 2048)
    RW = a.WB[:, 2048:2176].rearrange("p (a b) -> p a b", a=16)

    p.dma("gp", PRJB, w["proj_b"])
    p.dma("gp", RTRB, w["router_b"])
    p.dma("gp", NWROW, w["norm_w"])
    p.dma("gp", EB, w["expert_b"])
    cw = p.dma("gp", RW, rr(w["router_w"]))
    p.wait("te", cw)
    nwt = None
    for cc in range(4):
        p.wait("te", nwt)
        bt = p.op("te", lambda e, cc=cc: e.matmul(
            out=a.PS[4][:], lhsT=a.ONES[0:1, 0:128],
            rhs=NWROW[0:1, cc * 512:(cc + 1) * 512], start=True, stop=True),
            sig=True)
        p.wait("sc", bt)
        nwt = p.op("sc", lambda e, cc=cc: e.copy(
            out=NW[:, cc * 512:(cc + 1) * 512], in_=a.PS[4][:]), sig=True)
    p.wait("te", nwt)

    LG = a.v6b(4096, 4, 8)
    W8 = a.v6b(4224, 4, 8)
    TOPW = a.v6b(4352, 4, 16)
    Y = a.v6b(0, 2, 2048)
    XN = a.v6(4096, 2048)
    q2 = a.A7[:, 0:1]
    rq = a.A7[:, 1:2]

    prior = None
    for blk in range(NBLK):
        xt, fm = emit_ln_block(p, a, src_x, blk, lnt=lnt, prior=prior)
        ve_all = p.tok("vesig")
        sc_all = p.tok("scsig")
        p.wait("te", fm)
        p.wait("te", ve_all)
        p.wait("te", sc_all)
        rt = None
        for tt in range(4):
            p.op("te", lambda e, tt=tt: e.matmul(
                out=a.PS[4][:, tt * 8:(tt + 1) * 8], lhsT=a.ONES[0:1, 0:128],
                rhs=RTRB, start=True, stop=False))
            for kt in range(KT):
                rt = p.op("te", lambda e, tt=tt, kt=kt: e.matmul(
                    out=a.PS[4][:, tt * 8:(tt + 1) * 8],
                    lhsT=a.A3[:, kt, tt * 128:(tt + 1) * 128], rhs=RW[:, kt, :],
                    start=False, stop=(kt == KT - 1)), sig=(kt == KT - 1 and tt == 3))
        p.wait("ve", rt)
        tk = p.op("ve", lambda e: e.tensor_copy(out=LG.rearrange("p a b -> p (a b)"),
                                                in_=a.PS[4][:, 0:32]), sig=True)
        for tt in range(4):
            m1 = TOPW[:, tt, 0:1]
            s1 = TOPW[:, tt, 1:2]
            m2 = TOPW[:, tt, 2:3]
            s2 = TOPW[:, tt, 3:4]
            dd = TOPW[:, tt, 4:5]
            e2 = TOPW[:, tt, 5:6]
            w1 = TOPW[:, tt, 6:7]
            w2 = TOPW[:, tt, 7:8]
            EQ1 = TOPW[:, tt, 8:16]
            LM = W8[:, tt, :]
            def ch(fn, tk):
                p.wait("ve", tk)
                return p.op("ve", fn, sig=True)
            tk = ch(lambda e, tt=tt, m1=m1: e.reduce_max(
                out=m1, in_=LG[:, tt, :], axis=AX.X), tk)
            tk = ch(lambda e, tt=tt, m1=m1, EQ1=EQ1: e.tensor_scalar(
                out=EQ1, in0=LG[:, tt, :], scalar1=m1, scalar2=None,
                op0=ALU.is_equal), tk)
            tk = ch(lambda e, s1=s1, EQ1=EQ1: e.reduce_sum(
                out=s1, in_=EQ1, axis=AX.X), tk)
            tk = ch(lambda e, s1=s1: e.reciprocal(out=s1, in_=s1), tk)
            tk = ch(lambda e, tt=tt, EQ1=EQ1, LM=LM: e.scalar_tensor_tensor(
                out=LM, in0=EQ1, scalar=-1e9, in1=LG[:, tt, :],
                op0=ALU.mult, op1=ALU.add), tk)
            tk = ch(lambda e, m2=m2, LM=LM: e.reduce_max(
                out=m2, in_=LM, axis=AX.X), tk)
            tk = ch(lambda e, dd=dd, m2=m2, m1=m1: e.tensor_tensor(
                out=dd, in0=m2, in1=m1, op=ALU.subtract), tk)
            p.wait("sc", tk)
            etok = p.op("sc", lambda e, e2=e2, dd=dd: e.activation(
                out=e2, in_=dd, func=AF.Exp), sig=True)
            p.wait("ve", etok)
            tk = p.op("ve", lambda e, w1=w1, e2=e2: e.tensor_scalar(
                out=w1, in0=e2, scalar1=1.0, scalar2=None, op0=ALU.add), sig=True)
            tk = ch(lambda e, w1=w1: e.reciprocal(out=w1, in_=w1), tk)
            tk = ch(lambda e, w2=w2, e2=e2, w1=w1: e.tensor_tensor(
                out=w2, in0=e2, in1=w1, op=ALU.mult), tk)
            tk = ch(lambda e, m2=m2, LM=LM: e.tensor_scalar(
                out=LM, in0=LM, scalar1=m2, scalar2=None, op0=ALU.is_equal), tk)
            tk = ch(lambda e, s2=s2, LM=LM: e.reduce_sum(
                out=s2, in_=LM, axis=AX.X), tk)
            tk = ch(lambda e, s2=s2: e.reciprocal(out=s2, in_=s2), tk)
            tk = ch(lambda e, w2=w2, s2=s2: e.tensor_tensor(
                out=w2, in0=w2, in1=s2, op=ALU.mult), tk)
            tk = ch(lambda e, w1=w1, s1=s1: e.tensor_tensor(
                out=w1, in0=w1, in1=s1, op=ALU.mult), tk)
            tk = ch(lambda e, tt=tt, EQ1=EQ1, w1=w1: e.tensor_scalar(
                out=EQ1, in0=EQ1, scalar1=w1, scalar2=None, op0=ALU.mult), tk)
            tk = ch(lambda e, tt=tt, LM=LM, w2=w2, EQ1=EQ1: e.scalar_tensor_tensor(
                out=W8[:, tt, :], in0=LM, scalar=w2, in1=EQ1,
                op0=ALU.mult, op1=ALU.add), tk)
        w8_done = p.tok("vesig")

        rot = Rot(p, [0, 1, 2, 3])
        shp = [0]

        def load_ex(i, par):
            e_, fc = i // 4, i % 4
            return p.dma("gp", a.A5[:, par],
                         rr(w["expert_w"][e_][:, fc * 512:(fc + 1) * 512]),
                         sem=f"exl{par}")

        def body_ex(i, par, ltok):
            e_, fc = i // 4, i % 4
            p.wait("te", ltok)
            if i == 0:
                p.wait("te", w8_done)
            last = None
            for tt in range(4):
                j, bank, ftok = rot.get()
                p.wait("te", ftok)
                p.op("te", lambda e, e_=e_, fc=fc, tt=tt, bank=bank: e.matmul(
                    out=a.PS[bank][:], lhsT=a.ONES[0:1, 0:128],
                    rhs=EB[0:1, e_ * 2048 + fc * 512: e_ * 2048 + (fc + 1) * 512],
                    start=True, stop=False))
                for kt in range(KT):
                    last = p.op("te", lambda e, kt=kt, tt=tt, par=par, bank=bank: e.matmul(
                        out=a.PS[bank][:], lhsT=a.A3[:, kt, tt * 128:(tt + 1) * 128],
                        rhs=a.A5[:, par, kt, :], start=False,
                        stop=(kt == KT - 1)), sig=(kt == KT - 1))
                p.wait("sc", last)
                sp = shp[0] % 2
                shp[0] += 1
                p.wait("sc", p.tok("vesig") if shp[0] > 2 else None)
                sg = p.op("sc", lambda e, bank=bank, sp=sp: e.activation(
                    out=SH[sp], in_=a.PS[bank][:], func=AF.Silu), sig=True)
                p.wait("ve", sg)
                if e_ == 0:
                    ev = p.op("ve", lambda e, tt=tt, fc=fc, sp=sp: e.tensor_scalar(
                        out=ACC[:, tt, fc * 512:(fc + 1) * 512], in0=SH[sp],
                        scalar1=W8[:, tt, 0:1], scalar2=None, op0=ALU.mult),
                        sig=True)
                else:
                    ev = p.op("ve", lambda e, tt=tt, fc=fc, sp=sp, e_=e_: e.scalar_tensor_tensor(
                        out=ACC[:, tt, fc * 512:(fc + 1) * 512], in0=SH[sp],
                        scalar=W8[:, tt, e_:e_ + 1],
                        in1=ACC[:, tt, fc * 512:(fc + 1) * 512],
                        op0=ALU.mult, op1=ALU.add), sig=True)
                rot.done(j, ev)
            return last

        stream_slabs(p, 32, load_ex, body_ex, q="gp")
        acc_done = p.tok("vesig")

        p.wait("gp", acc_done)
        accst = p.dma("gp", rr(acc_scr[blk * TB:(blk + 1) * TB, :]), ACC[:])
        p.wait("sy", accst)
        p.wait("sy", p.tok("tesig"))
        fmt = None
        for kt in range(KT):
            fmt = p.dma("sy", a.A3[:, kt, :],
                        acc_scr[blk * TB:(blk + 1) * TB, kt * 128:(kt + 1) * 128],
                        transpose=True, sem="fmt")
        last_store = None
        for tth in range(2):
            rot2 = Rot(p, [0, 1, 2, 3])
            yev = None

            def load_pj(cc, par, tth=tth):
                return p.dma("gp", a.A5[:, par],
                             rr(w["proj_w"][:, cc * 512:(cc + 1) * 512]),
                             sem=f"exl{par}")

            def body_pj(cc, par, ltok, tth=tth):
                nonlocal yev
                p.wait("te", ltok)
                if cc == 0:
                    p.wait("te", fmt)
                    p.wait("te", p.tok("vesig"))
                last = None
                for tt2 in range(2):
                    tt = tth * 2 + tt2
                    j, bank, ftok = rot2.get()
                    p.wait("te", ftok)
                    p.op("te", lambda e, cc=cc, bank=bank: e.matmul(
                        out=a.PS[bank][:], lhsT=a.ONES[0:1, 0:128],
                        rhs=PRJB[0:1, cc * 512:(cc + 1) * 512],
                        start=True, stop=False))
                    for ft in range(KT):
                        last = p.op("te", lambda e, ft=ft, tt=tt, par=par, bank=bank: e.matmul(
                            out=a.PS[bank][:],
                            lhsT=a.A3[:, ft, tt * 128:(tt + 1) * 128],
                            rhs=a.A5[:, par, ft, :], start=False,
                            stop=(ft == KT - 1)), sig=(ft == KT - 1))
                    p.wait("ve", last)
                    p.wait("ve", prior)
                    yev = p.op("ve", lambda e, tt2=tt2, tt=tt, cc=cc, bank=bank: e.tensor_tensor(
                        out=Y[:, tt2, cc * 512:(cc + 1) * 512], in0=a.PS[bank][:],
                        in1=a.A2[:, tt, cc * 512:(cc + 1) * 512], op=ALU.add),
                        sig=True)
                    rot2.done(j, yev)
                return last

            stream_slabs(p, 4, load_pj, body_pj, q="gp")
            for tt2 in range(2):
                tt = tth * 2 + tt2
                Yt = Y[:, tt2, :]
                p.wait("ve", last_store)
                p.wait("ve", prior)
                p.wait("ve", yev)
                tk2 = p.op("ve", lambda e, Yt=Yt: e.scalar_tensor_tensor(
                    out=XN, in0=Yt, scalar=1.0, in1=Yt,
                    op0=ALU.mult, op1=ALU.mult, accum_out=q2), sig=True)
                p.wait("ve", tk2)
                tk2 = p.op("ve", lambda e: e.tensor_scalar(
                    out=rq, in0=q2, scalar1=1.0 / D, scalar2=EPS_MOE,
                    op0=ALU.mult, op1=ALU.add), sig=True)
                p.wait("sc", tk2)
                sq = p.op("sc", lambda e: e.activation(
                    out=rq, in_=rq, func=AF.Sqrt), sig=True)
                p.wait("ve", sq)
                tk2 = p.op("ve", lambda e: e.reciprocal(out=rq, in_=rq), sig=True)
                p.wait("ve", tk2)
                tk2 = p.op("ve", lambda e: e.tensor_scalar(
                    out=rq, in0=rq, scalar1=0.5, scalar2=None, op0=ALU.mult),
                    sig=True)
                p.wait("ve", tk2)
                tk2 = p.op("ve", lambda e, Yt=Yt: e.scalar_tensor_tensor(
                    out=XN, in0=Yt, scalar=rq, in1=NW,
                    op0=ALU.mult, op1=ALU.mult), sig=True)
                p.wait("ve", tk2)
                xntok = p.op("ve", lambda e, tt=tt: e.tensor_tensor(
                    out=XN, in0=XN, in1=a.A1[:, tt, :], op=ALU.add), sig=True)
                p.wait("gp", xntok)
                last_store = p.dma(
                    "gp", dst_x[blk * TB + tt * 128: blk * TB + (tt + 1) * 128, :],
                    XN, sem="xst")
        prior = last_store
    return prior



def emit_l1attn(p, a, src_x, dst_x, agin_ln, agout_ln, proj_part, rs_out, w):
    KTf = a.f4(0, 8192)
    KTM = a.v4(8192, 64, 128)
    QH = a.v4(16384, 4, 512)
    E2 = [a.f4(18432, 512), a.f4(18944, 512)]
    RBF = a.f4(19456, 512)
    YS4 = a.v4(19968, 4, 512)
    RB16 = a.r4(22016, 512)
    WPJ = a.A4[:, 23040:23040 + 8192].rearrange(
        "p (a c d) -> p a c d", a=4, c=4)
    RrowF = a.A6[0:1, 2560:3072]
    WL = a.WB[:, 0:2048].rearrange("p (a b) -> p a b", a=16)

    prior = None
    for blk in range(NBLK):
        xt, lntok = emit_ln_block(p, a, src_x, blk, ln_rows=(agin_ln, 0),
                                  prior=prior)
        prior = lntok
    p.wait("gp", prior)
    cct = p.cc("AllGather", [agin_ln[:]], [agout_ln[:]], "ccln")

    cw1 = p.dma("gp", WL, rr(w["W_lat"]))
    cw2 = p.dma("gp", a.A5[:, 0], rr(w["W_d_slice"]))
    cw3 = p.dma("gp", WPJ,
                rr(w["W_proj_slice"]).rearrange("p a (c d) -> p a c d", d=512))
    p.wait("sy", cct)

    STG = a.v6b(0, 4, 512)
    prev_q_evac = None
    prev_dn_tok = {}
    prev_ys = None
    prev_rc = None
    prev_head_exps = [None, None]
    prev_stg_store = None
    ktm_ev = [None, None]
    for sc in range(16):
        p.wait("sy", p.tok("tesig"))
        fmt = None
        for kt in range(KT):
            fmt = p.dma("sy", a.A3[:, kt, :],
                        agout_ln[sc * 512:(sc + 1) * 512,
                                 kt * 128:(kt + 1) * 128],
                        transpose=True, sem="fmt")
        p.wait("te", fmt)
        p.wait("te", prev_q_evac)
        p.wait("te", cw1)
        if sc == 0:
            p.wait("te", cw2)
            p.wait("te", cw3)
        km = None
        for kt in range(KT):
            km = p.op("te", lambda e, kt=kt, sc=sc: e.matmul(
                out=a.PS[0][:], lhsT=WL[:, kt, :], rhs=a.A3[:, kt, :],
                start=(kt == 0), stop=(kt == KT - 1)), sig=(kt == KT - 1))
        p.wait("sc", km)
        kcopy = p.op("sc", lambda e, sc=sc: e.activation(
            out=KTf[:, sc * 512:(sc + 1) * 512], in_=a.PS[0][:], func=AF.Identity,
            bias=a.BL[:, 0:1]), sig=True)
        par = sc % 2
        p.wait("te", kcopy)
        p.wait("te", ktm_ev[par])
        tpt = None
        for ss in range(4):
            tpt = p.op("te", lambda e, sc=sc, ss=ss, par=par: e.transpose(
                out=a.PB[par][:, ss * 128:(ss + 1) * 128],
                in_=KTf[:, sc * 512 + ss * 128: sc * 512 + (ss + 1) * 128],
                identity=a.ID[:]), sig=(ss == 3))
        p.wait("ve", tpt)
        ktm_ev[par] = p.op("ve", lambda e, sc=sc, par=par: e.tensor_copy(
            out=KTM[:, 4 * sc:4 * sc + 4, :].rearrange("p a b -> p (a b)"),
            in_=a.PB[par][:, 0:512]), sig=True)
        qe = None
        for h in range(4):
            p.wait("te", prev_q_evac)
            qm = None
            for kt in range(KT):
                qm = p.op("te", lambda e, kt=kt, h=h: e.matmul(
                    out=a.PS[1][:], lhsT=a.A5[:, 0, kt, h * 128:(h + 1) * 128],
                    rhs=a.A3[:, kt, :], start=(kt == 0), stop=(kt == KT - 1)),
                    sig=(kt == KT - 1))
            p.wait("sc", qm)
            qe = p.op("sc", lambda e, h=h: e.activation(
                out=QH[:, h, :], in_=a.PS[1][:], func=AF.Identity,
                bias=a.BD[:, h:h + 1]), sig=True)
            prev_q_evac = qe
        nsb = 4 * (sc + 1)
        p.wait("te", ktm_ev[par])
        for h in range(4):
            if h == 0:
                p.wait("te", qe)
            eready = {}
            for sb in range(nsb):
                bpar = sb % 2
                bank = a.PS[2 + bpar]
                p.wait("te", eready.get(sb - 2))
                if sb < 2:
                    p.wait("te", prev_head_exps[sb])
                st = p.op("te", lambda e, sb=sb, h=h, bank=bank: e.matmul(
                    out=bank[:], lhsT=KTf[:, sb * 128:(sb + 1) * 128],
                    rhs=QH[:, h, :], start=True, stop=True), sig=True)
                p.wait("sc", st)
                if sb >= 2:
                    p.wait("sc", prev_dn_tok.get(sb - 2))
                else:
                    p.wait("sc", prev_dn_tok.get(nsb - 2 + sb))
                ex = p.op("sc", lambda e, bpar=bpar, bank=bank: e.activation(
                    out=E2[bpar], in_=bank[:], func=AF.Exp, scale=ISQL),
                    sig=True)
                v = sb - 4 * sc
                if v >= 0:
                    p.wait("ve", ex)
                    mk = p.op("ve", lambda e, bpar=bpar, v=v: e.tensor_tensor(
                        out=E2[bpar], in0=E2[bpar],
                        in1=a.MSK[:, 512 - v * 128:1024 - v * 128],
                        op=ALU.mult), sig=True)
                    eready[sb] = mk
                else:
                    eready[sb] = ex
                p.wait("te", eready[sb])
                if sb == 0:
                    p.wait("te", prev_ys)
                    p.wait("te", prev_rc)
                p.op("te", lambda e, sb=sb, bpar=bpar, nsb=nsb: e.matmul(
                    out=a.PS[4][:], lhsT=KTM[:, sb, :], rhs=E2[bpar],
                    start=(sb == 0), stop=(sb == nsb - 1)))
                dn = p.op("te", lambda e, sb=sb, bpar=bpar, nsb=nsb: e.matmul(
                    out=a.PS[5][0:1, :], lhsT=a.ONES[:, 0:1], rhs=E2[bpar],
                    start=(sb == 0), stop=(sb == nsb - 1)), sig=True)
                prev_dn_tok[sb] = dn
            prev_head_exps = [eready.get(nsb - 2), eready.get(nsb - 1)]
            p.wait("ve", prev_dn_tok[nsb - 1])
            t6 = p.op("ve", lambda e: e.reciprocal(
                out=RrowF, in_=a.PS[5][0:1, :]), sig=True)
            prev_rc = t6
            p.wait("ve", t6)
            rv = p.op("ve", lambda e: e.tensor_copy(out=RB16, in_=RrowF),
                      sig=True)
            p.wait("te", rv)
            p.wait("te", prev_q_evac)
            bc = p.op("te", lambda e: e.matmul(
                out=a.PS[1][:], lhsT=a.ONES[0:1, :], rhs=RB16,
                start=True, stop=True), sig=True)
            p.wait("sc", bc)
            rbf = p.op("sc", lambda e: e.copy(out=RBF, in_=a.PS[1][:]), sig=True)
            prev_q_evac = rbf
            p.wait("ve", rbf)
            ys = p.op("ve", lambda e, h=h: e.tensor_tensor(
                out=YS4[:, h, :], in0=a.PS[4][:], in1=RBF, op=ALU.mult),
                sig=True)
            prev_ys = ys
        ys_done = prev_ys
        p.wait("te", ys_done)
        for cc_ in range(4):
            p.wait("ve", prev_stg_store)
            last = None
            evs = None
            for tt in range(4):
                bank = cc_ % 2
                p.wait("te", evs)
                for h in range(4):
                    last = p.op("te", lambda e, h=h, tt=tt, cc_=cc_, bank=bank: e.matmul(
                        out=a.PS[bank][:],
                        lhsT=YS4[:, h, tt * 128:(tt + 1) * 128],
                        rhs=WPJ[:, h, cc_, :],
                        start=(h == 0), stop=(h == 3)), sig=(h == 3))
                p.wait("ve", last)
                evs = p.op("ve", lambda e, tt=tt, bank=bank: e.tensor_copy(
                    out=STG[:, tt, :], in_=a.PS[bank][:]), sig=True)
            p.wait("gp", evs)
            prev_stg_store = p.dma(
                "gp", rr(proj_part[sc * 512:(sc + 1) * 512,
                                   cc_ * 512:(cc_ + 1) * 512]), STG[:],
                sem="ppst")
        prev_q_evac = p.tok("vesig")

    p.wait("gp", prev_stg_store)
    ccr = p.cc("ReduceScatter", [proj_part[:]], [rs_out[:]], "ccrs", op=ALU.add)
    p.wait("gp", ccr)

    bp2 = p.dma("gp", a.r4(0, 2048), w["b_proj"], sem="bpl")
    p.wait("te", bp2)
    p.wait("te", p.tok("vesig"))
    p.wait("te", p.tok("scsig"))
    bft = None
    for cc_ in range(4):
        p.wait("te", bft)
        bt = p.op("te", lambda e, cc_=cc_: e.matmul(
            out=a.PS[1][:], lhsT=a.ONES[0:1, :],
            rhs=a.r4(cc_ * 512, 512), start=True, stop=True), sig=True)
        p.wait("sc", bt)
        bft = p.op("sc", lambda e, cc_=cc_: e.copy(
            out=a.A2[:, 0, cc_ * 512:(cc_ + 1) * 512], in_=a.PS[1][:]),
            sig=True)

    RS6 = a.v6(0, 2048)
    XN6 = a.v6(2048, 2048)
    prior = None
    for blk in range(NBLK):
        p.wait("gp", prior)
        xt = p.dma("gp", a.A1[:], rr(src_x[blk * TB:(blk + 1) * TB, :]))
        for tt in range(4):
            r0 = blk * TB + tt * 128
            lt = p.dma("gp", RS6, rs_out[r0:r0 + 128, :], sem="rsl")
            p.wait("ve", lt)
            p.wait("ve", xt)
            p.wait("ve", bft)
            p.wait("ve", prior)
            t7 = p.op("ve", lambda e: e.tensor_tensor(
                out=XN6, in0=RS6, in1=a.A2[:, 0, :], op=ALU.add), sig=True)
            p.wait("ve", t7)
            t8 = p.op("ve", lambda e, tt=tt: e.tensor_tensor(
                out=XN6, in0=XN6, in1=a.A1[:, tt, :], op=ALU.add), sig=True)
            p.wait("gp", t8)
            prior = p.dma("gp", dst_x[r0:r0 + 128, :], XN6, sem="xst")
    return prior



WT_SPECS = {
    "gup": ("L0_ffn_w_gate_up", 16384, 2048),
    "wdn": ("L0_ffn_w_down", 2048, 8192),
    "win": ("L0_attn_W_in", 2048, 2048),
    "wgt": ("L0_attn_W_gate", 8192, 2048),
    "wout": ("L0_attn_W_out", 2048, 2048),
}

def wt_perm(name):
    if name == "gup":
        return [(i // 2) + 64 * (i % 2) for i in range(128)]
    if name == "wgt":
        perm = []
        for i in range(64):
            head, jj, ty = i // 16, (i // 4) % 4, i % 4
            perm.append(head * 16 + ty * 4 + jj)
        return perm
    return None


def build_program(secs, inputs_present):
    p = Prog()
    nc = p.nc
    stack = ExitStack()
    a = Arenas(nc, stack)

    def din(name, shape):
        return nc.dram_tensor(name, shape, F32, kind="ExternalInput").ap()

    xin = din("xin", [T, D])
    xout = nc.dram_tensor("xout", [T, D], F32, kind="ExternalOutput").ap()

    idt = nc.dram_tensor("c_ident", [128, 128], F32, kind="ExternalInput").ap()
    onest = nc.dram_tensor("c_ones", [128, 128], F32, kind="ExternalInput").ap()
    mskt = nc.dram_tensor("c_mask", [128, 1024], F32, kind="ExternalInput").ap()
    ld1 = p.dma("gp", a.ID[:], idt[:])
    ld2 = p.dma("gp", a.ONES[:], onest[:])
    ld3 = p.dma("gp", a.MSK[:], mskt[:])

    need = set()
    for s in secs:
        if s in (1, 3):
            need |= {"gup", "wdn"}
        if s == 2:
            need |= {"win", "wgt", "wout"}

    lnt = nc.dram_tensor("lnt", [T, D], BF16).ap()
    wt = {}
    for nm in sorted(need):
        src_name, O, K = WT_SPECS[nm]
        w_hbm = din(src_name, [O, K])
        wb = nc.dram_tensor(f"wb_{nm}", [O, K], BF16).ap()
        dst = nc.dram_tensor(f"wt_{nm}", [K, O], BF16).ap()
        emit_wt_construct(p, a, w_hbm, wb, dst, O, K, perm=wt_perm(nm))
        wt[nm] = dst
    for eng in ENGINES:
        p.wait(eng, p.tok("wts0"))
        p.wait(eng, p.tok("wts1"))

    n = len(secs)
    if n == 1:
        hops = [(xin, xout)]
    else:
        xs = [nc.dram_tensor(f"xs{i}", [T, D], F32).ap() for i in range(n - 1)]
        hops = []
        cur = xin
        for i in range(n - 1):
            hops.append((cur, xs[i]))
            cur = xs[i]
        hops.append((cur, xout))

    for (s, (src, dst)) in zip(secs, hops):
        if s in (1, 3):
            emit_ffn0(p, a, src, dst, lnt, wt["gup"], wt["wdn"])
        elif s == 2:
            for nm2 in ("b_gate", "rms_w"):
                pass
            bgt = din("L0_attn_b_gate_r", [128, 64])
            rwt = din("L0_attn_rms_w_r", [128, 4])
            b1 = p.dma("gp", a.BG[:], bgt[:])
            b2 = p.dma("gp", a.RMSW[:], rwt[:])
            p.wait("sc", b2)
            cp_scr = nc.dram_tensor("cp_scr", [T, D], BF16).ap()
            cl_scr = nc.dram_tensor("cl_scr", [T, D], BF16).ap()
            o_scr = nc.dram_tensor("o_scr", [T, D], BF16).ap()
            agin = nc.dram_tensor("ag_scan_in", [128, 32], F32).ap()
            agout = nc.dram_tensor("ag_scan_out", [512, 32], F32).ap()
            scm = din("scanmask", [128, 3])
            b3 = p.dma("gp", a.A7[:, 112:115], scm[:])
            p.wait("ve", b3)
            emit_l0attn(p, a, src, dst, lnt, wt["win"], wt["wgt"], wt["wout"],
                        cp_scr, cl_scr, o_scr, agin, agout)
        elif s in (4, 6):
            w = {
                "router_w": din("L1_ffn_router_w", [D, 8]),
                "router_b": din("L1_ffn_router_b_r", [1, 8]),
                "expert_w": din("L1_ffn_expert_w", [8, D, D]),
                "expert_b": din("L1_ffn_expert_b_r", [1, 16384]),
                "proj_w": din("L1_ffn_proj_w", [D, D]),
                "proj_b": din("L1_ffn_proj_b_r", [1, D]),
                "norm_w": din("L1_ffn_norm_w_r", [1, D]),
            } if s == 4 or 4 not in secs else build_program._moe_w
            if s == 4 or 4 not in secs:
                build_program._moe_w = w
            acc_scr = nc.dram_tensor(f"acc_scr{s}", [T, D], BF16).ap()
            emit_moe(p, a, src, dst, lnt, acc_scr, w, tag=f"m{s}")
        elif s == 5:
            w5 = {
                "W_lat": din("L1_attn_W_lat", [D, 128]),
                "W_d_slice": din("W_d_slice", [D, 512]),
                "W_proj_slice": din("W_proj_slice", [512, D]),
                "b_proj": din("L1_attn_b_proj_r", [1, D]),
            }
            blt = din("b_lat_r", [128, 1])
            bdt = din("b_d_slice_r", [128, 4])
            p.dma("gp", a.BL[:], blt[:])
            b4 = p.dma("gp", a.BD[:], bdt[:])
            p.wait("sc", b4)
            agin_ln = nc.dram_tensor("agin_ln", [T, D], BF16).ap()
            agout_ln = nc.dram_tensor("agout_ln", [SEQ, D], BF16).ap()
            proj_part = nc.dram_tensor("proj_part", [SEQ, D], F32).ap()
            rs_out = nc.dram_tensor("rs_out", [T, D], F32).ap()
            emit_l1attn(p, a, src, dst, agin_ln, agout_ln, proj_part, rs_out, w5)

    p.wait("gp", p.tok("xst"))
    p.wait("gp", p.tok("gpdma"))
    nc_final = p.finalize()
    stack.close()
    return nc_final



def _consts():
    ident = np.eye(128, dtype=np.float32)
    ones = np.ones((128, 128), dtype=np.float32)
    msk = np.zeros((128, 1024), dtype=np.float32)
    pp, qq = np.meshgrid(np.arange(128), np.arange(1024), indexing="ij")
    msk[pp <= qq - 512] = 1.0
    return ident, ones, msk


def _prep_core_inputs(inputs, core):
    b, me = core // 4, core % 4
    ident, ones, msk = _consts()
    d = {
        "c_ident": ident, "c_ones": ones, "c_mask": msk,
        "L0_ffn_w_gate_up": inputs["L0_ffn_w_gate_up"],
        "L0_ffn_w_down": inputs["L0_ffn_w_down"],
        "L0_attn_W_in": inputs["L0_attn_W_in"],
        "L0_attn_W_gate": inputs["L0_attn_W_gate"],
        "L0_attn_W_out": inputs["L0_attn_W_out"],
        "L0_attn_b_gate_r": np.ascontiguousarray(
            inputs["L0_attn_b_gate"].reshape(64, 128).T),
        "L0_attn_rms_w_r": np.ascontiguousarray(
            inputs["L0_attn_rms_w"].reshape(4, 128).T),
        "scanmask": np.broadcast_to(
            (np.arange(3) < me).astype(np.float32), (128, 3)).copy(),
        "L1_ffn_router_w": inputs["L1_ffn_router_w"],
        "L1_ffn_router_b_r": inputs["L1_ffn_router_b"].reshape(1, 8),
        "L1_ffn_expert_w": inputs["L1_ffn_expert_w"],
        "L1_ffn_expert_b_r": inputs["L1_ffn_expert_b"].reshape(1, 16384),
        "L1_ffn_proj_w": inputs["L1_ffn_proj_w"],
        "L1_ffn_proj_b_r": inputs["L1_ffn_proj_b"].reshape(1, 2048),
        "L1_ffn_norm_w_r": inputs["L1_ffn_norm_w"].reshape(1, 2048),
        "L1_attn_W_lat": inputs["L1_attn_W_lat"],
        "W_d_slice": np.ascontiguousarray(
            inputs["L1_attn_W_d"][:, me * 512:(me + 1) * 512]),
        "W_proj_slice": np.ascontiguousarray(
            inputs["L1_attn_W_proj"][me * 512:(me + 1) * 512, :]),
        "L1_attn_b_proj_r": inputs["L1_attn_b_proj"].reshape(1, 2048),
        "b_lat_r": inputs["L1_attn_b_lat"].reshape(128, 1),
        "b_d_slice_r": np.ascontiguousarray(
            inputs["L1_attn_b_d"][me * 512:(me + 1) * 512].reshape(4, 128).T),
    }
    return {k: np.ascontiguousarray(np.asarray(v, dtype=np.float32))
            for k, v in d.items()}


_IN_BY_SEC = {
    1: ["L0_ffn_w_gate_up", "L0_ffn_w_down"],
    2: ["L0_attn_W_in", "L0_attn_W_gate", "L0_attn_W_out",
        "L0_attn_b_gate_r", "L0_attn_rms_w_r", "scanmask"],
    3: ["L0_ffn_w_gate_up", "L0_ffn_w_down"],
    4: ["L1_ffn_router_w", "L1_ffn_router_b_r", "L1_ffn_expert_w",
        "L1_ffn_expert_b_r", "L1_ffn_proj_w", "L1_ffn_proj_b_r",
        "L1_ffn_norm_w_r"],
    5: ["L1_attn_W_lat", "W_d_slice", "W_proj_slice", "L1_attn_b_proj_r",
        "b_lat_r", "b_d_slice_r"],
    6: ["L1_ffn_router_w", "L1_ffn_router_b_r", "L1_ffn_expert_w",
        "L1_ffn_expert_b_r", "L1_ffn_proj_w", "L1_ffn_proj_b_r",
        "L1_ffn_norm_w_r"],
}


def run_chain(inputs, groups):
    x = np.asarray(inputs["L0_attn_x"], dtype=np.float32)
    shards = [np.ascontiguousarray(x[c // 4, (c % 4) * T:(c % 4 + 1) * T, :])
              for c in range(8)]
    per_core = [_prep_core_inputs(inputs, c) for c in range(8)]
    for gi, secs in enumerate(groups):
        if hasattr(build_program, "_moe_w"):
            del build_program._moe_w
        nc = build_program(secs, None)
        names = set()
        for s in secs:
            names.update(_IN_BY_SEC[s])
        names |= {"c_ident", "c_ones", "c_mask"}
        in_maps = []
        for c in range(8):
            m = {k: per_core[c][k] for k in names if k in per_core[c]}
            m["xin"] = shards[c]
            in_maps.append(m)
        res = run_bass_kernel_spmd(nc, in_maps, core_ids=list(range(8)))
        shards = [np.asarray(res.results[c]["xout"], dtype=np.float32)
                  for c in range(8)]
    out = np.empty((2, SEQ, D), dtype=np.float32)
    for c in range(8):
        out[c // 4, (c % 4) * T:(c % 4 + 1) * T, :] = shards[c]
    return out


def kernel(**inputs):
    if FUSED:
        return run_chain(inputs, [[1, 2, 3, 4, 5, 6]])
    return run_chain(inputs, [[1], [2], [3], [4], [5], [6]])


# version 02
# speedup vs baseline: 1.1371x; 1.1371x over previous
import numpy as np
import ml_dtypes
from contextlib import ExitStack
import concourse.bass as bass
import concourse.mybir as mybir
from concourse.bass_utils import run_bass_kernel_spmd

F32 = mybir.dt.float32
BF16 = mybir.dt.bfloat16
AF = mybir.ActivationFunctionType
ALU = mybir.AluOpType
AX = mybir.AxisListType

ENGINES = ("te", "ve", "sc", "gp", "sy")
RG = [[0, 1, 2, 3], [4, 5, 6, 7]]

D = 2048
T = 2048
TB = 512
NBLK = 4
KT = 16
SEQ = 8192
EPS_LN = 1e-5
EPS_S2 = 1e-6
EPS_MOE = 1e-6
ISQL = float(1.0 / np.sqrt(128.0))

FUSED = True



class Prog:
    def __init__(self):
        self.nc = bass.Bass("TRN2", target_bir_lowering=False,
                            disable_frame_to_traceback=True)
        self.ops = {e: [] for e in ENGINES}
        self.sems = {}
        self.sem_order = []
        self._h = {}

    def _bump(self, name, n):
        if name not in self.sems:
            self.sems[name] = 0
            self.sem_order.append(name)
        self.sems[name] += n
        return (name, self.sems[name])

    def tok(self, name):
        return (name, self.sems.get(name, 0))

    def wait(self, eng, tok):
        if tok is None:
            return
        name, val = tok
        if val <= 0:
            return
        assert val <= self.sems.get(name, 0), f"wait beyond incs: {name} {val}"
        self.ops[eng].append(lambda e, n=name, v=val: e.wait_ge(self._h[n], v))

    def dma(self, q, out, in_, transpose=False, sem=None):
        s = sem or (q + "dma")
        tok = self._bump(s, 16)
        def fn(e, out=out, in_=in_, t=transpose, n=s):
            e.dma_start(out=out, in_=in_, transpose=t).then_inc(self._h[n], 16)
        self.ops[q].append(fn)
        return tok

    def op(self, eng, fn, sig=False):
        if not sig:
            self.ops[eng].append(fn)
            return None
        s = eng + "sig"
        tok = self._bump(s, 1)
        def wrapped(e, n=s):
            fn(e).then_inc(self._h[n], 1)
        self.ops[eng].append(wrapped)
        return tok

    def cc(self, kind, ins, outs, name, op=None):
        tok = self._bump(name, 1)
        aop = op if op is not None else ALU.bypass
        def fn(e, n=name):
            e.collective_compute(kind, aop, replica_groups=RG,
                                 ins=ins, outs=outs).then_inc(self._h[n])
        self.ops["gp"].append(fn)
        return tok

    def finalize(self):
        nc = self.nc
        stack = ExitStack()
        for n in self.sem_order:
            self._h[n] = stack.enter_context(nc.semaphore(n))
        block = stack.enter_context(nc.Block())
        decos = {"te": block.tensor, "ve": block.vector, "sc": block.scalar,
                 "gp": block.gpsimd, "sy": block.sync}
        for name in ENGINES:
            def mk(oplist):
                def body(e):
                    for fn in oplist:
                        fn(e)
                return body
            decos[name](mk(self.ops[name]))
        stack.close()
        return nc


class Rot:
    def __init__(self, p, banks):
        self.p, self.banks = p, banks
        self.free = [None] * len(banks)
        self.i = 0
    def get(self):
        j = self.i % len(self.banks)
        self.i += 1
        return j, self.banks[j], self.free[j]
    def done(self, j, tok):
        self.free[j] = tok


def stream_slabs(p, n, make_load, body, q="sy"):
    loads = [None] * n
    free = {}
    def issue(i):
        if i - 2 >= 0:
            p.wait(q, free[i - 2])
        loads[i] = make_load(i, i % 2)
    p.wait(q, p.tok("tesig"))
    if n > 0:
        issue(0)
    if n > 1:
        issue(1)
    for i in range(n):
        free[i] = body(i, i % 2, loads[i])
        assert free[i] is not None
        if i + 2 < n:
            issue(i + 2)


class Arenas:
    def __init__(self, nc, stack):
        E = stack.enter_context
        self.A1 = E(nc.sbuf_tensor("A1", [128, NBLK, D], F32))
        self.A2 = E(nc.sbuf_tensor("A2", [128, NBLK, D], BF16))
        self.A3 = E(nc.sbuf_tensor("A3", [128, KT, TB], BF16))
        self.A4 = E(nc.sbuf_tensor("A4", [128, 32768], BF16))
        self.A5 = E(nc.sbuf_tensor("A5", [128, 2, KT, TB], BF16))
        self.A6 = E(nc.sbuf_tensor("A6", [128, 6144], F32))
        self.A7 = E(nc.sbuf_tensor("A7", [128, 128], F32))
        self.BG = E(nc.sbuf_tensor("BG", [128, 64], F32))
        self.RMSW = E(nc.sbuf_tensor("RMSW", [128, 4], F32))
        self.BD = E(nc.sbuf_tensor("BD", [128, 4], F32))
        self.BL = E(nc.sbuf_tensor("BL", [128, 1], F32))
        self.ID = E(nc.sbuf_tensor("ID", [128, 128], BF16))
        self.ONES = E(nc.sbuf_tensor("ONES", [128, 128], BF16))
        self.MSK = E(nc.sbuf_tensor("MSK", [128, 1024], BF16))
        self.WB = E(nc.sbuf_tensor("WB", [128, 2304], BF16))
        self.PS = [E(nc.psum_tensor(f"PS{i}", [128, 512], F32)) for i in range(6)]
        self.PB = [E(nc.psum_tensor(f"PB{i}", [128, 1024], BF16)) for i in range(2)]

    def v4(self, off, a_, b_):
        return self.A4[:, off:off + a_ * b_].rearrange("p (a b) -> p a b", a=a_)

    def f4(self, off, n):
        return self.A4[:, off:off + n]

    def r4(self, off, n):
        return self.A4[0:1, off:off + n]

    def v6(self, off, n):
        return self.A6[:, off:off + n]

    def v6b(self, off, a_, b_):
        return self.A6[:, off:off + a_ * b_].rearrange("p (a b) -> p a b", a=a_)


def rr(ap, p=128):
    return ap.rearrange("(a p) c -> p a c", p=p)



def emit_wt_construct(p, a, w_hbm, wb, dst, O, K, perm=None):
    nob, nkb = O // 128, K // 128
    perm = perm or list(range(nob))
    p.wait("gp", p.tok("wts0"))
    p.wait("gp", p.tok("wts1"))
    p.wait("sy", p.tok("wts0"))
    p.wait("sy", p.tok("wts1"))
    buf = [a.A5[:, 0].rearrange("p a b -> p (a b)"),
           a.A5[:, 1].rearrange("p a b -> p (a b)")]
    last_use = [None, None]
    for i in range(nob):
        par = i % 2
        p.wait("gp", last_use[par])
        ld = p.dma("gp", buf[par][:, 0:K],
                   w_hbm[perm[i] * 128:(perm[i] + 1) * 128, :],
                   sem=f"wcl{par}")
        p.wait("sy", ld)
        last_use[par] = p.dma("sy", wb[i * 128:(i + 1) * 128, :],
                              buf[par][:, 0:K], sem=f"wcs{par}")
    p.wait("sy", last_use[0])
    p.wait("sy", last_use[1])
    tbuf = [a.f4(0, O), a.f4(16384, O)]
    last_st = [None, None]
    for k in range(nkb):
        par = k % 2
        p.wait("sy", last_st[par])
        tl = p.dma("sy", tbuf[par], wb[:, k * 128:(k + 1) * 128],
                   transpose=True, sem=f"wtt{par}")
        p.wait("gp", tl)
        last_st[par] = p.dma("gp", dst[k * 128:(k + 1) * 128, :], tbuf[par],
                             sem=f"wts{par}")
    p.wait("gp", last_st[0])
    p.wait("gp", last_st[1])



def emit_ln_block(p, a, src_x, blk, lnt=None, to_fm=True, ln_rows=None,
                  prior=None):
    p.wait("gp", prior)
    xt = p.dma("gp", a.A1[:], rr(src_x[blk * TB:(blk + 1) * TB, :]))
    SB = a.v6b(5632, 16, 8)
    dummy = a.v6(0, 2048)
    p.wait("sc", prior)
    p.wait("ve", prior)
    atoks = []
    for tt in range(4):
        p.wait("ve", xt)
        p.wait("sc", xt)
        t0 = p.op("ve", lambda e, tt=tt: e.reduce_sum(
            out=SB[:, tt, 0:1], in_=a.A1[:, tt, :], axis=AX.X), sig=True)
        s_ssq = p.op("sc", lambda e, tt=tt: e.activation(
            out=dummy, in_=a.A1[:, tt, :], func=AF.Square,
            accum_out=SB[:, tt, 1:2]), sig=True)
        p.wait("ve", s_ssq)
        p.wait("ve", t0)
        t1 = p.op("ve", lambda e, tt=tt: e.tensor_scalar(
            out=SB[:, tt, 2:3], in0=SB[:, tt, 0:1], scalar1=1.0 / D,
            scalar2=None, op0=ALU.mult), sig=True)
        p.wait("ve", t1)
        t2 = p.op("ve", lambda e, tt=tt: e.tensor_tensor(
            out=SB[:, tt, 3:4], in0=SB[:, tt, 2:3], in1=SB[:, tt, 2:3],
            op=ALU.mult), sig=True)
        t3 = p.op("ve", lambda e, tt=tt: e.tensor_scalar(
            out=SB[:, tt, 4:5], in0=SB[:, tt, 1:2], scalar1=1.0 / D,
            scalar2=EPS_LN, op0=ALU.mult, op1=ALU.add), sig=True)
        p.wait("ve", t2)
        p.wait("ve", t3)
        vtok = p.op("ve", lambda e, tt=tt: e.tensor_tensor(
            out=SB[:, tt, 5:6], in0=SB[:, tt, 4:5], in1=SB[:, tt, 3:4],
            op=ALU.subtract), sig=True)
        p.wait("sc", vtok)
        stok = p.op("sc", lambda e, tt=tt: e.activation(
            out=SB[:, tt, 6:7], in_=SB[:, tt, 5:6], func=AF.Sqrt), sig=True)
        p.wait("ve", stok)
        t4 = p.op("ve", lambda e, tt=tt: e.reciprocal(
            out=SB[:, tt, 7:8], in_=SB[:, tt, 6:7]), sig=True)
        p.wait("ve", t4)
        atok = p.op("ve", lambda e, tt=tt: e.tensor_scalar(
            out=a.A2[:, tt, :], in0=a.A1[:, tt, :], scalar1=SB[:, tt, 2:3],
            scalar2=SB[:, tt, 7:8], op0=ALU.subtract, op1=ALU.mult), sig=True)
        atoks.append(atok)
    p.wait("gp", atoks[-1])
    if ln_rows is not None:
        dst, off = ln_rows
        lntok = p.dma("gp", rr(dst[off + blk * TB: off + (blk + 1) * TB, :]),
                      a.A2[:])
        return xt, lntok
    lntok = p.dma("gp", rr(lnt[blk * TB:(blk + 1) * TB, :]), a.A2[:])
    fm_tok = None
    if to_fm:
        p.wait("sy", lntok)
        p.wait("sy", p.tok("tesig"))
        for kt in range(KT):
            fm_tok = p.dma("sy", a.A3[:, kt, :],
                           lnt[blk * TB:(blk + 1) * TB, kt * 128:(kt + 1) * 128],
                           transpose=True, sem="fmt")
    return xt, fm_tok



def emit_ffn0(p, a, src_x, dst_x, lnt, gup_T, wdn_T):
    prior = None
    for blk in range(NBLK):
        xt, fm = emit_ln_block(p, a, src_x, blk, lnt=lnt, prior=prior)
        HM = a.v4(0, 64, 512)
        SG = a.v6(4096, 512)
        ve_all = p.tok("vesig")
        sc_all = p.tok("scsig")
        rot = Rot(p, [(0, 1), (2, 3)])
        hm_last = [None]

        def load_gu(i, par):
            return p.dma("sy", a.A5[:, par],
                         rr(gup_T)[:, :, i * 512:(i + 1) * 512],
                         sem=f"sl{par}")

        def body_gu(i, par, ltok):
            last_te = None
            for half in range(2):
                op = 2 * i + half
                j, pair, ftok = rot.get()
                pg, pu = a.PS[pair[0]], a.PS[pair[1]]
                p.wait("te", ftok)
                p.wait("te", ltok)
                if op == 0:
                    p.wait("te", fm)
                    p.wait("te", ve_all)
                    p.wait("te", sc_all)
                for kt in range(KT):
                    p.op("te", lambda e, kt=kt, par=par, half=half, pg=pg: e.matmul(
                        out=pg[:], lhsT=a.A5[:, par, kt, half * 256:half * 256 + 128],
                        rhs=a.A3[:, kt, :], start=(kt == 0), stop=(kt == KT - 1)))
                for kt in range(KT):
                    tok = p.op("te", lambda e, kt=kt, par=par, half=half, pu=pu: e.matmul(
                        out=pu[:], lhsT=a.A5[:, par, kt, half * 256 + 128:half * 256 + 256],
                        rhs=a.A3[:, kt, :], start=(kt == 0), stop=(kt == KT - 1)),
                        sig=(kt == KT - 1))
                    if kt == KT - 1:
                        last_te = tok
                p.wait("sc", last_te)
                sg = p.op("sc", lambda e, pg=pg: e.activation(
                    out=SG, in_=pg[:], func=AF.Silu), sig=True)
                p.wait("ve", sg)
                ev = p.op("ve", lambda e, op=op, pu=pu: e.tensor_tensor(
                    out=HM[:, op, :], in0=SG, in1=pu[:], op=ALU.mult), sig=True)
                rot.done(j, ev)
                hm_last[0] = ev
            return last_te

        stream_slabs(p, 32, load_gu, body_gu)
        hm_done = hm_last[0]

        STG = a.v6b(0, 4, 512)
        for fh in range(4):
            prev_ev = p.tok("vesig")

            def load_dn(g, par, fh=fh):
                return p.dma("sy", a.A5[:, par, 0:8, :],
                             rr(wdn_T)[:, g * 8:(g + 1) * 8,
                                       fh * 512:(fh + 1) * 512],
                             sem=f"sl{par}")

            def body_dn(g, par, ltok, fh=fh):
                p.wait("te", ltok)
                if g == 0:
                    p.wait("te", hm_done)
                    p.wait("te", prev_ev)
                tok = None
                for kk in range(8):
                    for tt in range(4):
                        sig = (kk == 7 and tt == 3)
                        tok2 = p.op("te", lambda e, g=g, kk=kk, tt=tt, par=par: e.matmul(
                            out=a.PS[tt][:],
                            lhsT=HM[:, g * 8 + kk, tt * 128:(tt + 1) * 128],
                            rhs=a.A5[:, par, kk, :],
                            start=(g == 0 and kk == 0),
                            stop=(g == 7 and kk == 7)), sig=sig)
                        if sig:
                            tok = tok2
                return tok

            mm_done = [None]

            def body_dn2(g, par, ltok, fh=fh):
                t = body_dn(g, par, ltok, fh)
                mm_done[0] = t
                return t

            stream_slabs(p, 8, load_dn, body_dn2)
            p.wait("ve", mm_done[0])
            p.wait("ve", prior)
            ev = None
            for tt in range(4):
                ev = p.op("ve", lambda e, tt=tt, fh=fh: e.scalar_tensor_tensor(
                    out=STG[:, tt, :], in0=a.PS[tt][:], scalar=0.5,
                    in1=a.A1[:, tt, fh * 512:(fh + 1) * 512],
                    op0=ALU.mult, op1=ALU.add), sig=(tt == 3))
            p.wait("gp", ev)
            prior = p.dma("gp",
                          rr(dst_x[blk * TB:(blk + 1) * TB,
                                   fh * 512:(fh + 1) * 512]), STG[:],
                          sem="xst")
    return prior



def emit_l0attn(p, a, src_x, dst_x, lnt, win_T, wgt_T, wout_T,
                cp_scr, cl_scr, o_scr, agin, agout):
    ZER = a.v6(3584, 512)
    p.op("ve", lambda e: e.memset(ZER, 0.0))
    CPC = a.A7[:, 64:80]
    CLC = a.A7[:, 80:96]
    CIN = a.A7[:, 96:112]
    SCM = a.A7[:, 112:115]
    p.op("ve", lambda e: e.memset(CPC, 1.0))
    p.op("ve", lambda e: e.memset(CLC, 0.0))

    XPF = a.v4(0, 16, 512)
    CPS = a.v4(8192, 16, 512)
    CLS = a.v4(16384, 16, 512)
    OS = a.v4(24576, 16, 512)
    T1, T2, T3, T4 = (a.v6(512, 512), a.v6(1024, 512), a.v6(1536, 512),
                      a.v6(2048, 512))

    prior = None
    for blk in range(NBLK):
        xt, fm = emit_ln_block(p, a, src_x, blk, lnt=lnt, prior=prior)
        ve_all = p.tok("vesig")
        sc_all = p.tok("scsig")
        rot = Rot(p, [0, 1, 2, 3])

        def load_xp(i, par):
            return p.dma("sy", a.A5[:, par],
                         rr(win_T)[:, :, i * 512:(i + 1) * 512], sem=f"sl{par}")

        def body_xp(i, par, ltok):
            last = None
            for q in range(4):
                ht = i * 4 + q
                j, bank, ftok = rot.get()
                p.wait("te", ftok)
                p.wait("te", ltok)
                if ht == 0:
                    p.wait("te", fm)
                    p.wait("te", ve_all)
                    p.wait("te", sc_all)
                for kt in range(KT):
                    last = p.op("te", lambda e, kt=kt, par=par, q=q, bank=bank: e.matmul(
                        out=a.PS[bank][:],
                        lhsT=a.A5[:, par, kt, q * 128:(q + 1) * 128],
                        rhs=a.A3[:, kt, :], start=(kt == 0), stop=(kt == KT - 1)),
                        sig=(kt == KT - 1))
                p.wait("sc", last)
                ev = p.op("sc", lambda e, ht=ht, bank=bank: e.copy(
                    out=XPF[:, ht, :], in_=a.PS[bank][:]), sig=True)
                rot.done(j, ev)
            return last

        stream_slabs(p, 4, load_xp, body_xp)
        xp_done = p.tok("scsig")

        def load_g(i, par):
            return p.dma("sy", a.A5[:, par],
                         rr(wgt_T)[:, :, i * 512:(i + 1) * 512], sem=f"sl{par}")

        def body_g(i, par, ltok):
            head, jj = i // 4, i % 4
            ct = i
            gt = head * 16
            p.wait("te", ltok)
            if i == 0:
                p.wait("te", xp_done)
            else:
                p.wait("te", p.tok("scsig"))
            tok = None
            for ty in range(4):
                for kt in range(KT):
                    sig = (ty == 3 and kt == KT - 1)
                    tok2 = p.op("te", lambda e, ty=ty, kt=kt, par=par: e.matmul(
                        out=a.PS[ty][:],
                        lhsT=a.A5[:, par, kt, ty * 128:(ty + 1) * 128],
                        rhs=XPF[:, kt, :], start=(kt == 0), stop=(kt == KT - 1)),
                        sig=sig)
                    if sig:
                        tok = tok2
            p.wait("sc", tok)
            c_i, c_f, c_o, c_c = gt + jj, gt + 4 + jj, gt + 8 + jj, gt + 12 + jj
            p.wait("sc", p.tok("vesig"))
            p.op("sc", lambda e, c=c_i: e.activation(
                out=T1, in_=a.PS[0][:], func=AF.Exp, bias=a.BG[:, c:c + 1]))
            p.op("sc", lambda e, c=c_c: e.activation(
                out=T2, in_=a.PS[3][:], func=AF.Tanh, bias=a.BG[:, c:c + 1]))
            p.op("sc", lambda e, c=c_f: e.activation(
                out=T4, in_=a.PS[1][:], func=AF.Sigmoid, bias=a.BG[:, c:c + 1]))
            sce = p.op("sc", lambda e, c=c_o, ct=ct: e.activation(
                out=OS[:, ct, :], in_=a.PS[2][:], func=AF.Sigmoid,
                bias=a.BG[:, c:c + 1]), sig=True)
            p.wait("ve", sce)
            v1 = p.op("ve", lambda e: e.tensor_tensor(out=T3, in0=T1, in1=T2,
                                                 op=ALU.mult), sig=True)
            v2 = p.op("ve", lambda e, ct=ct: e.tensor_tensor_scan(
                out=CPS[:, ct, :], data0=T4, data1=ZER,
                initial=CPC[:, ct:ct + 1], op0=ALU.mult, op1=ALU.add), sig=True)
            p.wait("ve", v2)
            v3 = p.op("ve", lambda e, ct=ct: e.tensor_copy(
                out=CPC[:, ct:ct + 1], in_=CPS[:, ct, 511:512]), sig=True)
            p.wait("ve", v1)
            v4 = p.op("ve", lambda e, ct=ct: e.tensor_tensor_scan(
                out=CLS[:, ct, :], data0=T4, data1=T3,
                initial=CLC[:, ct:ct + 1], op0=ALU.mult, op1=ALU.add), sig=True)
            p.wait("ve", v4)
            p.op("ve", lambda e, ct=ct: e.tensor_copy(
                out=CLC[:, ct:ct + 1], in_=CLS[:, ct, 511:512]), sig=True)
            return tok

        stream_slabs(p, 16, load_g, body_g)
        scans_done = p.tok("vesig")
        p.wait("gp", scans_done)
        p.dma("gp", rr(cp_scr)[:, :, blk * TB:(blk + 1) * TB], CPS[:])
        p.dma("gp", rr(cl_scr)[:, :, blk * TB:(blk + 1) * TB], CLS[:])
        prior = p.dma("gp", rr(o_scr)[:, :, blk * TB:(blk + 1) * TB], OS[:])

    agt = p.dma("gp", agin[:], a.A7[:, 64:96])
    p.wait("gp", agt)
    cct = p.cc("AllGather", [agin[:]], [agout[:]], "ccscan")
    p.wait("gp", cct)
    G = []
    gt_tok = None
    for r in range(4):
        Gr = a.A6[:, 4096 + r * 32: 4096 + (r + 1) * 32]
        G.append(Gr)
        gt_tok = p.dma("gp", Gr, agout[r * 128:(r + 1) * 128, :])
    p.wait("ve", gt_tok)
    TA = a.A6[:, 4224:4240]
    TBv = a.A6[:, 4240:4256]
    tk = p.op("ve", lambda e: e.memset(CIN, 0.0), sig=True)
    for r in range(3):
        p.wait("ve", tk)
        tk = p.op("ve", lambda e, r=r: e.tensor_scalar(
            out=TA, in0=G[r][:, 0:16], scalar1=1.0, scalar2=SCM[:, r:r + 1],
            op0=ALU.subtract, op1=ALU.mult), sig=True)
        p.wait("ve", tk)
        tk = p.op("ve", lambda e: e.tensor_scalar(
            out=TA, in0=TA, scalar1=1.0, scalar2=None, op0=ALU.add), sig=True)
        p.wait("ve", tk)
        tk = p.op("ve", lambda e: e.tensor_tensor(out=CIN, in0=CIN, in1=TA,
                                             op=ALU.mult), sig=True)
        p.wait("ve", tk)
        tk = p.op("ve", lambda e, r=r: e.tensor_scalar(
            out=TBv, in0=G[r][:, 16:32], scalar1=SCM[:, r:r + 1], scalar2=None,
            op0=ALU.mult), sig=True)
        p.wait("ve", tk)
        tk = p.op("ve", lambda e: e.tensor_tensor(out=CIN, in0=CIN, in1=TBv,
                                             op=ALU.add), sig=True)
    p.wait("ve", tk)
    cin_done = p.op("ve", lambda e: e.tensor_copy(out=TA, in_=CIN), sig=True)

    HFM = a.v4(0, 16, 512)
    Cr = a.v6b(0, 4, 512)
    CN = a.v6(2048, 512)
    RrowF = a.A6[0:1, 2560:3072]
    Rrow16 = a.r4(30720, 512)
    prior2 = p.tok("gpdma")
    prev_ps4 = None
    prev_ps5 = None
    for blk in range(NBLK):
        p.wait("gp", prior2)
        xt = p.dma("gp", a.A1[:], rr(src_x[blk * TB:(blk + 1) * TB, :]))
        p.dma("gp", CPS[:], rr(cp_scr)[:, :, blk * TB:(blk + 1) * TB])
        p.dma("gp", CLS[:], rr(cl_scr)[:, :, blk * TB:(blk + 1) * TB])
        l3 = p.dma("gp", OS[:], rr(o_scr)[:, :, blk * TB:(blk + 1) * TB])
        p.wait("ve", prior2)
        p.wait("ve", l3)
        p.wait("ve", cin_done)
        for head in range(4):
            ctok = None
            for jj in range(4):
                ct = head * 4 + jj
                ctok = p.op("ve", lambda e, ct=ct, jj=jj: e.scalar_tensor_tensor(
                    out=Cr[:, jj, :], in0=CPS[:, ct, :],
                    scalar=CIN[:, ct:ct + 1], in1=CLS[:, ct, :],
                    op0=ALU.mult, op1=ALU.add), sig=(jj == 3))
            p.wait("sc", ctok)
            sq_last = None
            for jj in range(4):
                ct = head * 4 + jj
                sq_last = p.op("sc", lambda e, ct=ct, jj=jj: e.activation(
                    out=HFM[:, ct, :], in_=Cr[:, jj, :], func=AF.Square),
                    sig=(jj == 3))
            p.wait("te", sq_last)
            p.wait("te", prev_ps4)
            mt = None
            for jj in range(4):
                ct = head * 4 + jj
                mt = p.op("te", lambda e, ct=ct, jj=jj: e.matmul(
                    out=a.PS[4][0:1, :], lhsT=a.ONES[:, 0:1],
                    rhs=HFM[:, ct, :], start=(jj == 0), stop=(jj == 3)),
                    sig=(jj == 3))
            p.wait("ve", mt)
            vt = p.op("ve", lambda e: e.tensor_scalar(
                out=RrowF, in0=a.PS[4][0:1, :], scalar1=1.0 / 512.0,
                scalar2=EPS_S2, op0=ALU.mult, op1=ALU.add), sig=True)
            prev_ps4 = vt
            p.wait("sc", vt)
            st2 = p.op("sc", lambda e: e.activation(
                out=RrowF, in_=RrowF, func=AF.Sqrt), sig=True)
            p.wait("ve", st2)
            t5 = p.op("ve", lambda e: e.reciprocal(out=RrowF, in_=RrowF),
                      sig=True)
            p.wait("ve", t5)
            rv = p.op("ve", lambda e: e.tensor_copy(out=Rrow16, in_=RrowF),
                      sig=True)
            p.wait("te", rv)
            p.wait("te", prev_ps5)
            bt = p.op("te", lambda e: e.matmul(
                out=a.PS[5][:], lhsT=a.ONES[0:1, :], rhs=Rrow16,
                start=True, stop=True), sig=True)
            p.wait("ve", bt)
            for jj in range(4):
                ct = head * 4 + jj
                cn = p.op("ve", lambda e, jj=jj: e.tensor_tensor(
                    out=CN, in0=Cr[:, jj, :], in1=a.PS[5][:], op=ALU.mult),
                    sig=True)
                if jj == 3:
                    prev_ps5 = cn
                p.wait("sc", cn)
                th = p.op("sc", lambda e, ct=ct, jj=jj: e.activation(
                    out=HFM[:, ct, :], in_=CN, func=AF.Tanh,
                    scale=a.RMSW[:, jj:jj + 1]), sig=True)
                p.wait("ve", th)
                p.op("ve", lambda e, ct=ct: e.tensor_tensor(
                    out=HFM[:, ct, :], in0=HFM[:, ct, :], in1=OS[:, ct, :],
                    op=ALU.mult), sig=True)
        h_done = p.tok("vesig")

        rot = Rot(p, [0, 1, 2, 3])
        STG = a.v6b(0, 4, 512)
        stores = [None]

        def load_wo(i, par):
            return p.dma("sy", a.A5[:, par],
                         rr(wout_T)[:, :, i * 512:(i + 1) * 512], sem=f"sl{par}")

        def body_wo(cc, par, ltok):
            p.wait("te", ltok)
            if cc == 0:
                p.wait("te", h_done)
            p.wait("ve", stores[0])
            last = None
            ev = None
            for tt in range(4):
                j, bank, ftok = rot.get()
                p.wait("te", ftok)
                for ctc in range(KT):
                    last = p.op("te", lambda e, ctc=ctc, tt=tt, par=par, bank=bank: e.matmul(
                        out=a.PS[bank][:],
                        lhsT=HFM[:, ctc, tt * 128:(tt + 1) * 128],
                        rhs=a.A5[:, par, ctc, :],
                        start=(ctc == 0), stop=(ctc == KT - 1)),
                        sig=(ctc == KT - 1))
                p.wait("ve", last)
                p.wait("ve", xt)
                ev = p.op("ve", lambda e, tt=tt, cc=cc, bank=bank: e.tensor_tensor(
                    out=STG[:, tt, :], in0=a.PS[bank][:],
                    in1=a.A1[:, tt, cc * 512:(cc + 1) * 512], op=ALU.add),
                    sig=True)
                rot.done(j, ev)
            p.wait("gp", ev)
            stores[0] = p.dma(
                "gp", rr(dst_x[blk * TB:(blk + 1) * TB,
                               cc * 512:(cc + 1) * 512]), STG[:], sem="xst")
            return last

        stream_slabs(p, 4, load_wo, body_wo)
        prior2 = stores[0]
    return prior2



def emit_moe(p, a, src_x, dst_x, lnt, acc_scr, w, tag):
    p.wait("gp", p.tok("tesig"))
    p.wait("gp", p.tok("vesig"))
    p.wait("gp", p.tok("scsig"))
    NW = a.f4(8192, 2048)
    PRJB = a.r4(10240, 2048)
    NWROW = a.r4(12288, 2048)
    EB = a.r4(14336, 16384)
    RTRB = a.r4(30720, 8)
    SH = [a.f4(30728, 512), a.f4(31240, 512)]
    ACC = a.v4(0, 4, 2048)
    RW = a.WB[:, 2048:2176].rearrange("p (a b) -> p a b", a=16)

    p.dma("gp", PRJB, w["proj_b"])
    p.dma("gp", RTRB, w["router_b"])
    p.dma("gp", NWROW, w["norm_w"])
    p.dma("gp", EB, w["expert_b"])
    cw = p.dma("gp", RW, rr(w["router_w"]))
    p.wait("te", cw)
    nwt = None
    for cc in range(4):
        p.wait("te", nwt)
        bt = p.op("te", lambda e, cc=cc: e.matmul(
            out=a.PS[4][:], lhsT=a.ONES[0:1, 0:128],
            rhs=NWROW[0:1, cc * 512:(cc + 1) * 512], start=True, stop=True),
            sig=True)
        p.wait("sc", bt)
        nwt = p.op("sc", lambda e, cc=cc: e.copy(
            out=NW[:, cc * 512:(cc + 1) * 512], in_=a.PS[4][:]), sig=True)
    p.wait("te", nwt)

    LG = a.v6b(4096, 4, 8)
    W8 = a.v6b(4224, 4, 8)
    TOPW = a.v6b(4352, 4, 16)
    Y = a.v6b(0, 2, 2048)
    XN = a.v6(4096, 2048)
    q2 = a.A7[:, 0:1]
    rq = a.A7[:, 1:2]

    prior = None
    for blk in range(NBLK):
        xt, fm = emit_ln_block(p, a, src_x, blk, lnt=lnt, prior=prior)
        ve_all = p.tok("vesig")
        sc_all = p.tok("scsig")
        p.wait("te", fm)
        p.wait("te", ve_all)
        p.wait("te", sc_all)
        rt = None
        for tt in range(4):
            p.op("te", lambda e, tt=tt: e.matmul(
                out=a.PS[4][:, tt * 8:(tt + 1) * 8], lhsT=a.ONES[0:1, 0:128],
                rhs=RTRB, start=True, stop=False))
            for kt in range(KT):
                rt = p.op("te", lambda e, tt=tt, kt=kt: e.matmul(
                    out=a.PS[4][:, tt * 8:(tt + 1) * 8],
                    lhsT=a.A3[:, kt, tt * 128:(tt + 1) * 128], rhs=RW[:, kt, :],
                    start=False, stop=(kt == KT - 1)), sig=(kt == KT - 1 and tt == 3))
        p.wait("ve", rt)
        tk = p.op("ve", lambda e: e.tensor_copy(out=LG.rearrange("p a b -> p (a b)"),
                                                in_=a.PS[4][:, 0:32]), sig=True)
        for tt in range(4):
            m1 = TOPW[:, tt, 0:1]
            s1 = TOPW[:, tt, 1:2]
            m2 = TOPW[:, tt, 2:3]
            s2 = TOPW[:, tt, 3:4]
            dd = TOPW[:, tt, 4:5]
            e2 = TOPW[:, tt, 5:6]
            w1 = TOPW[:, tt, 6:7]
            w2 = TOPW[:, tt, 7:8]
            EQ1 = TOPW[:, tt, 8:16]
            LM = W8[:, tt, :]
            def ch(fn, tk):
                p.wait("ve", tk)
                return p.op("ve", fn, sig=True)
            tk = ch(lambda e, tt=tt, m1=m1: e.reduce_max(
                out=m1, in_=LG[:, tt, :], axis=AX.X), tk)
            tk = ch(lambda e, tt=tt, m1=m1, EQ1=EQ1: e.tensor_scalar(
                out=EQ1, in0=LG[:, tt, :], scalar1=m1, scalar2=None,
                op0=ALU.is_equal), tk)
            tk = ch(lambda e, s1=s1, EQ1=EQ1: e.reduce_sum(
                out=s1, in_=EQ1, axis=AX.X), tk)
            tk = ch(lambda e, s1=s1: e.reciprocal(out=s1, in_=s1), tk)
            tk = ch(lambda e, tt=tt, EQ1=EQ1, LM=LM: e.scalar_tensor_tensor(
                out=LM, in0=EQ1, scalar=-1e9, in1=LG[:, tt, :],
                op0=ALU.mult, op1=ALU.add), tk)
            tk = ch(lambda e, m2=m2, LM=LM: e.reduce_max(
                out=m2, in_=LM, axis=AX.X), tk)
            tk = ch(lambda e, dd=dd, m2=m2, m1=m1: e.tensor_tensor(
                out=dd, in0=m2, in1=m1, op=ALU.subtract), tk)
            p.wait("sc", tk)
            etok = p.op("sc", lambda e, e2=e2, dd=dd: e.activation(
                out=e2, in_=dd, func=AF.Exp), sig=True)
            p.wait("ve", etok)
            tk = p.op("ve", lambda e, w1=w1, e2=e2: e.tensor_scalar(
                out=w1, in0=e2, scalar1=1.0, scalar2=None, op0=ALU.add), sig=True)
            tk = ch(lambda e, w1=w1: e.reciprocal(out=w1, in_=w1), tk)
            tk = ch(lambda e, w2=w2, e2=e2, w1=w1: e.tensor_tensor(
                out=w2, in0=e2, in1=w1, op=ALU.mult), tk)
            tk = ch(lambda e, m2=m2, LM=LM: e.tensor_scalar(
                out=LM, in0=LM, scalar1=m2, scalar2=None, op0=ALU.is_equal), tk)
            tk = ch(lambda e, s2=s2, LM=LM: e.reduce_sum(
                out=s2, in_=LM, axis=AX.X), tk)
            tk = ch(lambda e, s2=s2: e.reciprocal(out=s2, in_=s2), tk)
            tk = ch(lambda e, w2=w2, s2=s2: e.tensor_tensor(
                out=w2, in0=w2, in1=s2, op=ALU.mult), tk)
            tk = ch(lambda e, w1=w1, s1=s1: e.tensor_tensor(
                out=w1, in0=w1, in1=s1, op=ALU.mult), tk)
            tk = ch(lambda e, tt=tt, EQ1=EQ1, w1=w1: e.tensor_scalar(
                out=EQ1, in0=EQ1, scalar1=w1, scalar2=None, op0=ALU.mult), tk)
            tk = ch(lambda e, tt=tt, LM=LM, w2=w2, EQ1=EQ1: e.scalar_tensor_tensor(
                out=W8[:, tt, :], in0=LM, scalar=w2, in1=EQ1,
                op0=ALU.mult, op1=ALU.add), tk)
        w8_done = p.tok("vesig")

        rot = Rot(p, [0, 1, 2, 3])
        shp = [0]

        def load_ex(i, par):
            e_, fc = i // 4, i % 4
            return p.dma("gp", a.A5[:, par],
                         rr(w["expert_w"][e_][:, fc * 512:(fc + 1) * 512]),
                         sem=f"exl{par}")

        def body_ex(i, par, ltok):
            e_, fc = i // 4, i % 4
            p.wait("te", ltok)
            if i == 0:
                p.wait("te", w8_done)
            last = None
            for tt in range(4):
                j, bank, ftok = rot.get()
                p.wait("te", ftok)
                p.op("te", lambda e, e_=e_, fc=fc, tt=tt, bank=bank: e.matmul(
                    out=a.PS[bank][:], lhsT=a.ONES[0:1, 0:128],
                    rhs=EB[0:1, e_ * 2048 + fc * 512: e_ * 2048 + (fc + 1) * 512],
                    start=True, stop=False))
                for kt in range(KT):
                    last = p.op("te", lambda e, kt=kt, tt=tt, par=par, bank=bank: e.matmul(
                        out=a.PS[bank][:], lhsT=a.A3[:, kt, tt * 128:(tt + 1) * 128],
                        rhs=a.A5[:, par, kt, :], start=False,
                        stop=(kt == KT - 1)), sig=(kt == KT - 1))
                p.wait("sc", last)
                sp = shp[0] % 2
                shp[0] += 1
                p.wait("sc", p.tok("vesig") if shp[0] > 2 else None)
                sg = p.op("sc", lambda e, bank=bank, sp=sp: e.activation(
                    out=SH[sp], in_=a.PS[bank][:], func=AF.Silu), sig=True)
                p.wait("ve", sg)
                if e_ == 0:
                    ev = p.op("ve", lambda e, tt=tt, fc=fc, sp=sp: e.tensor_scalar(
                        out=ACC[:, tt, fc * 512:(fc + 1) * 512], in0=SH[sp],
                        scalar1=W8[:, tt, 0:1], scalar2=None, op0=ALU.mult),
                        sig=True)
                else:
                    ev = p.op("ve", lambda e, tt=tt, fc=fc, sp=sp, e_=e_: e.scalar_tensor_tensor(
                        out=ACC[:, tt, fc * 512:(fc + 1) * 512], in0=SH[sp],
                        scalar=W8[:, tt, e_:e_ + 1],
                        in1=ACC[:, tt, fc * 512:(fc + 1) * 512],
                        op0=ALU.mult, op1=ALU.add), sig=True)
                rot.done(j, ev)
            return last

        stream_slabs(p, 32, load_ex, body_ex, q="gp")
        acc_done = p.tok("vesig")

        p.wait("gp", acc_done)
        accst = p.dma("gp", rr(acc_scr[blk * TB:(blk + 1) * TB, :]), ACC[:])
        p.wait("sy", accst)
        p.wait("sy", p.tok("tesig"))
        fmt = None
        for kt in range(KT):
            fmt = p.dma("sy", a.A3[:, kt, :],
                        acc_scr[blk * TB:(blk + 1) * TB, kt * 128:(kt + 1) * 128],
                        transpose=True, sem="fmt")
        last_store = None
        for tth in range(2):
            rot2 = Rot(p, [0, 1, 2, 3])
            yev = None

            def load_pj(cc, par, tth=tth):
                return p.dma("gp", a.A5[:, par],
                             rr(w["proj_w"][:, cc * 512:(cc + 1) * 512]),
                             sem=f"exl{par}")

            def body_pj(cc, par, ltok, tth=tth):
                nonlocal yev
                p.wait("te", ltok)
                if cc == 0:
                    p.wait("te", fmt)
                    p.wait("te", p.tok("vesig"))
                last = None
                for tt2 in range(2):
                    tt = tth * 2 + tt2
                    j, bank, ftok = rot2.get()
                    p.wait("te", ftok)
                    p.op("te", lambda e, cc=cc, bank=bank: e.matmul(
                        out=a.PS[bank][:], lhsT=a.ONES[0:1, 0:128],
                        rhs=PRJB[0:1, cc * 512:(cc + 1) * 512],
                        start=True, stop=False))
                    for ft in range(KT):
                        last = p.op("te", lambda e, ft=ft, tt=tt, par=par, bank=bank: e.matmul(
                            out=a.PS[bank][:],
                            lhsT=a.A3[:, ft, tt * 128:(tt + 1) * 128],
                            rhs=a.A5[:, par, ft, :], start=False,
                            stop=(ft == KT - 1)), sig=(ft == KT - 1))
                    p.wait("ve", last)
                    p.wait("ve", prior)
                    yev = p.op("ve", lambda e, tt2=tt2, tt=tt, cc=cc, bank=bank: e.tensor_tensor(
                        out=Y[:, tt2, cc * 512:(cc + 1) * 512], in0=a.PS[bank][:],
                        in1=a.A2[:, tt, cc * 512:(cc + 1) * 512], op=ALU.add),
                        sig=True)
                    rot2.done(j, yev)
                return last

            stream_slabs(p, 4, load_pj, body_pj, q="gp")
            for tt2 in range(2):
                tt = tth * 2 + tt2
                Yt = Y[:, tt2, :]
                p.wait("ve", last_store)
                p.wait("ve", prior)
                p.wait("ve", yev)
                tk2 = p.op("ve", lambda e, Yt=Yt: e.scalar_tensor_tensor(
                    out=XN, in0=Yt, scalar=1.0, in1=Yt,
                    op0=ALU.mult, op1=ALU.mult, accum_out=q2), sig=True)
                p.wait("ve", tk2)
                tk2 = p.op("ve", lambda e: e.tensor_scalar(
                    out=rq, in0=q2, scalar1=1.0 / D, scalar2=EPS_MOE,
                    op0=ALU.mult, op1=ALU.add), sig=True)
                p.wait("sc", tk2)
                sq = p.op("sc", lambda e: e.activation(
                    out=rq, in_=rq, func=AF.Sqrt), sig=True)
                p.wait("ve", sq)
                tk2 = p.op("ve", lambda e: e.reciprocal(out=rq, in_=rq), sig=True)
                p.wait("ve", tk2)
                tk2 = p.op("ve", lambda e: e.tensor_scalar(
                    out=rq, in0=rq, scalar1=0.5, scalar2=None, op0=ALU.mult),
                    sig=True)
                p.wait("ve", tk2)
                tk2 = p.op("ve", lambda e, Yt=Yt: e.scalar_tensor_tensor(
                    out=XN, in0=Yt, scalar=rq, in1=NW,
                    op0=ALU.mult, op1=ALU.mult), sig=True)
                p.wait("ve", tk2)
                xntok = p.op("ve", lambda e, tt=tt: e.tensor_tensor(
                    out=XN, in0=XN, in1=a.A1[:, tt, :], op=ALU.add), sig=True)
                p.wait("gp", xntok)
                last_store = p.dma(
                    "gp", dst_x[blk * TB + tt * 128: blk * TB + (tt + 1) * 128, :],
                    XN, sem="xst")
        prior = last_store
    return prior



def emit_l1attn(p, a, src_x, dst_x, agin_ln, agout_ln, proj_part, rs_out, w):
    KTf = a.f4(0, 8192)
    KTM = a.v4(8192, 64, 128)
    QH = a.v4(16384, 4, 512)
    E2 = [a.f4(18432, 512), a.f4(18944, 512)]
    RBF = a.f4(19456, 512)
    YS4 = a.v4(19968, 4, 512)
    RB16 = a.r4(22016, 512)
    WPJ = a.A4[:, 23040:23040 + 8192].rearrange(
        "p (a c d) -> p a c d", a=4, c=4)
    RrowF = a.A6[0:1, 2560:3072]
    WL = a.WB[:, 0:2048].rearrange("p (a b) -> p a b", a=16)

    prior = None
    for blk in range(NBLK):
        xt, lntok = emit_ln_block(p, a, src_x, blk, ln_rows=(agin_ln, 0),
                                  prior=prior)
        prior = lntok
    p.wait("gp", prior)
    cct = p.cc("AllGather", [agin_ln[:]], [agout_ln[:]], "ccln")

    p.wait("gp", p.tok("tesig"))
    p.wait("gp", p.tok("vesig"))
    p.wait("gp", p.tok("scsig"))
    cw1 = p.dma("gp", WL, rr(w["W_lat"]))
    cw2 = p.dma("gp", a.A5[:, 0], rr(w["W_d_slice"]))
    cw3 = p.dma("gp", WPJ,
                rr(w["W_proj_slice"]).rearrange("p a (c d) -> p a c d", d=512))
    p.wait("sy", cct)

    STG = a.v6b(0, 4, 512)
    prev_q_evac = None
    prev_dn_tok = {}
    prev_ys = None
    prev_rc = None
    prev_head_exps = [None, None]
    prev_stg_store = None
    ktm_ev = [None, None]
    for sc in range(16):
        p.wait("sy", p.tok("tesig"))
        fmt = None
        for kt in range(KT):
            fmt = p.dma("sy", a.A3[:, kt, :],
                        agout_ln[sc * 512:(sc + 1) * 512,
                                 kt * 128:(kt + 1) * 128],
                        transpose=True, sem="fmt")
        p.wait("te", fmt)
        p.wait("te", prev_q_evac)
        p.wait("te", cw1)
        if sc == 0:
            p.wait("te", cw2)
            p.wait("te", cw3)
        km = None
        for kt in range(KT):
            km = p.op("te", lambda e, kt=kt, sc=sc: e.matmul(
                out=a.PS[0][:], lhsT=WL[:, kt, :], rhs=a.A3[:, kt, :],
                start=(kt == 0), stop=(kt == KT - 1)), sig=(kt == KT - 1))
        p.wait("sc", km)
        kcopy = p.op("sc", lambda e, sc=sc: e.activation(
            out=KTf[:, sc * 512:(sc + 1) * 512], in_=a.PS[0][:], func=AF.Identity,
            bias=a.BL[:, 0:1]), sig=True)
        par = sc % 2
        p.wait("te", kcopy)
        p.wait("te", ktm_ev[par])
        tpt = None
        for ss in range(4):
            tpt = p.op("te", lambda e, sc=sc, ss=ss, par=par: e.transpose(
                out=a.PB[par][:, ss * 128:(ss + 1) * 128],
                in_=KTf[:, sc * 512 + ss * 128: sc * 512 + (ss + 1) * 128],
                identity=a.ID[:]), sig=(ss == 3))
        p.wait("ve", tpt)
        ktm_ev[par] = p.op("ve", lambda e, sc=sc, par=par: e.tensor_copy(
            out=KTM[:, 4 * sc:4 * sc + 4, :].rearrange("p a b -> p (a b)"),
            in_=a.PB[par][:, 0:512]), sig=True)
        qe = None
        for h in range(4):
            p.wait("te", prev_q_evac)
            qm = None
            for kt in range(KT):
                qm = p.op("te", lambda e, kt=kt, h=h: e.matmul(
                    out=a.PS[1][:], lhsT=a.A5[:, 0, kt, h * 128:(h + 1) * 128],
                    rhs=a.A3[:, kt, :], start=(kt == 0), stop=(kt == KT - 1)),
                    sig=(kt == KT - 1))
            p.wait("sc", qm)
            qe = p.op("sc", lambda e, h=h: e.activation(
                out=QH[:, h, :], in_=a.PS[1][:], func=AF.Identity,
                bias=a.BD[:, h:h + 1]), sig=True)
            prev_q_evac = qe
        nsb = 4 * (sc + 1)
        p.wait("te", ktm_ev[par])
        for h in range(4):
            if h == 0:
                p.wait("te", qe)
            eready = {}
            for sb in range(nsb):
                bpar = sb % 2
                bank = a.PS[2 + bpar]
                p.wait("te", eready.get(sb - 2))
                if sb < 2:
                    p.wait("te", prev_head_exps[sb])
                st = p.op("te", lambda e, sb=sb, h=h, bank=bank: e.matmul(
                    out=bank[:], lhsT=KTf[:, sb * 128:(sb + 1) * 128],
                    rhs=QH[:, h, :], start=True, stop=True), sig=True)
                p.wait("sc", st)
                if sb >= 2:
                    p.wait("sc", prev_dn_tok.get(sb - 2))
                else:
                    p.wait("sc", prev_dn_tok.get(nsb - 2 + sb))
                ex = p.op("sc", lambda e, bpar=bpar, bank=bank: e.activation(
                    out=E2[bpar], in_=bank[:], func=AF.Exp, scale=ISQL),
                    sig=True)
                v = sb - 4 * sc
                if v >= 0:
                    p.wait("ve", ex)
                    mk = p.op("ve", lambda e, bpar=bpar, v=v: e.tensor_tensor(
                        out=E2[bpar], in0=E2[bpar],
                        in1=a.MSK[:, 512 - v * 128:1024 - v * 128],
                        op=ALU.mult), sig=True)
                    eready[sb] = mk
                else:
                    eready[sb] = ex
                p.wait("te", eready[sb])
                if sb == 0:
                    p.wait("te", prev_ys)
                    p.wait("te", prev_rc)
                p.op("te", lambda e, sb=sb, bpar=bpar, nsb=nsb: e.matmul(
                    out=a.PS[4][:], lhsT=KTM[:, sb, :], rhs=E2[bpar],
                    start=(sb == 0), stop=(sb == nsb - 1)))
                dn = p.op("te", lambda e, sb=sb, bpar=bpar, nsb=nsb: e.matmul(
                    out=a.PS[5][0:1, :], lhsT=a.ONES[:, 0:1], rhs=E2[bpar],
                    start=(sb == 0), stop=(sb == nsb - 1)), sig=True)
                prev_dn_tok[sb] = dn
            prev_head_exps = [eready.get(nsb - 2), eready.get(nsb - 1)]
            p.wait("ve", prev_dn_tok[nsb - 1])
            t6 = p.op("ve", lambda e: e.reciprocal(
                out=RrowF, in_=a.PS[5][0:1, :]), sig=True)
            prev_rc = t6
            p.wait("ve", t6)
            rv = p.op("ve", lambda e: e.tensor_copy(out=RB16, in_=RrowF),
                      sig=True)
            p.wait("te", rv)
            p.wait("te", prev_q_evac)
            bc = p.op("te", lambda e: e.matmul(
                out=a.PS[1][:], lhsT=a.ONES[0:1, :], rhs=RB16,
                start=True, stop=True), sig=True)
            p.wait("sc", bc)
            rbf = p.op("sc", lambda e: e.copy(out=RBF, in_=a.PS[1][:]), sig=True)
            prev_q_evac = rbf
            p.wait("ve", rbf)
            ys = p.op("ve", lambda e, h=h: e.tensor_tensor(
                out=YS4[:, h, :], in0=a.PS[4][:], in1=RBF, op=ALU.mult),
                sig=True)
            prev_ys = ys
        ys_done = prev_ys
        p.wait("te", ys_done)
        for cc_ in range(4):
            p.wait("ve", prev_stg_store)
            last = None
            evs = None
            for tt in range(4):
                bank = cc_ % 2
                p.wait("te", evs)
                for h in range(4):
                    last = p.op("te", lambda e, h=h, tt=tt, cc_=cc_, bank=bank: e.matmul(
                        out=a.PS[bank][:],
                        lhsT=YS4[:, h, tt * 128:(tt + 1) * 128],
                        rhs=WPJ[:, h, cc_, :],
                        start=(h == 0), stop=(h == 3)), sig=(h == 3))
                p.wait("ve", last)
                evs = p.op("ve", lambda e, tt=tt, bank=bank: e.tensor_copy(
                    out=STG[:, tt, :], in_=a.PS[bank][:]), sig=True)
            p.wait("gp", evs)
            prev_stg_store = p.dma(
                "gp", rr(proj_part[sc * 512:(sc + 1) * 512,
                                   cc_ * 512:(cc_ + 1) * 512]), STG[:],
                sem="ppst")
        prev_q_evac = p.tok("vesig")

    p.wait("gp", prev_stg_store)
    ccr = p.cc("ReduceScatter", [proj_part[:]], [rs_out[:]], "ccrs", op=ALU.add)
    p.wait("gp", ccr)

    bp2 = p.dma("gp", a.r4(0, 2048), w["b_proj"], sem="bpl")
    p.wait("te", bp2)
    p.wait("te", p.tok("vesig"))
    p.wait("te", p.tok("scsig"))
    bft = None
    for cc_ in range(4):
        p.wait("te", bft)
        bt = p.op("te", lambda e, cc_=cc_: e.matmul(
            out=a.PS[1][:], lhsT=a.ONES[0:1, :],
            rhs=a.r4(cc_ * 512, 512), start=True, stop=True), sig=True)
        p.wait("sc", bt)
        bft = p.op("sc", lambda e, cc_=cc_: e.copy(
            out=a.A2[:, 0, cc_ * 512:(cc_ + 1) * 512], in_=a.PS[1][:]),
            sig=True)

    RS6 = a.v6(0, 2048)
    XN6 = a.v6(2048, 2048)
    prior = None
    for blk in range(NBLK):
        p.wait("gp", prior)
        xt = p.dma("gp", a.A1[:], rr(src_x[blk * TB:(blk + 1) * TB, :]))
        for tt in range(4):
            r0 = blk * TB + tt * 128
            lt = p.dma("gp", RS6, rs_out[r0:r0 + 128, :], sem="rsl")
            p.wait("ve", lt)
            p.wait("ve", xt)
            p.wait("ve", bft)
            p.wait("ve", prior)
            t7 = p.op("ve", lambda e: e.tensor_tensor(
                out=XN6, in0=RS6, in1=a.A2[:, 0, :], op=ALU.add), sig=True)
            p.wait("ve", t7)
            t8 = p.op("ve", lambda e, tt=tt: e.tensor_tensor(
                out=XN6, in0=XN6, in1=a.A1[:, tt, :], op=ALU.add), sig=True)
            p.wait("gp", t8)
            prior = p.dma("gp", dst_x[r0:r0 + 128, :], XN6, sem="xst")
    return prior



WT_SPECS = {
    "gup": ("L0_ffn_w_gate_up", 16384, 2048),
    "wdn": ("L0_ffn_w_down", 2048, 8192),
    "win": ("L0_attn_W_in", 2048, 2048),
    "wgt": ("L0_attn_W_gate", 8192, 2048),
    "wout": ("L0_attn_W_out", 2048, 2048),
}

def wt_perm(name):
    if name == "gup":
        return [(i // 2) + 64 * (i % 2) for i in range(128)]
    if name == "wgt":
        perm = []
        for i in range(64):
            head, jj, ty = i // 16, (i // 4) % 4, i % 4
            perm.append(head * 16 + ty * 4 + jj)
        return perm
    return None


def build_program(secs, inputs_present):
    p = Prog()
    nc = p.nc
    stack = ExitStack()
    a = Arenas(nc, stack)

    def din(name, shape):
        return nc.dram_tensor(name, shape, F32, kind="ExternalInput").ap()

    xin = din("xin", [T, D])
    xout = nc.dram_tensor("xout", [T, D], F32, kind="ExternalOutput").ap()

    idt = nc.dram_tensor("c_ident", [128, 128], F32, kind="ExternalInput").ap()
    onest = nc.dram_tensor("c_ones", [128, 128], F32, kind="ExternalInput").ap()
    mskt = nc.dram_tensor("c_mask", [128, 1024], F32, kind="ExternalInput").ap()
    ld1 = p.dma("gp", a.ID[:], idt[:])
    ld2 = p.dma("gp", a.ONES[:], onest[:])
    ld3 = p.dma("gp", a.MSK[:], mskt[:])

    need = set()
    for s in secs:
        if s in (1, 3):
            need |= {"gup", "wdn"}
        if s == 2:
            need |= {"win", "wgt", "wout"}

    lnt = nc.dram_tensor("lnt", [T, D], BF16).ap()
    wt = {}
    for nm in sorted(need):
        src_name, O, K = WT_SPECS[nm]
        w_hbm = din(src_name, [O, K])
        wb = nc.dram_tensor(f"wb_{nm}", [O, K], BF16).ap()
        dst = nc.dram_tensor(f"wt_{nm}", [K, O], BF16).ap()
        emit_wt_construct(p, a, w_hbm, wb, dst, O, K, perm=wt_perm(nm))
        wt[nm] = dst
    for eng in ENGINES:
        p.wait(eng, p.tok("wts0"))
        p.wait(eng, p.tok("wts1"))

    n = len(secs)
    if n == 1:
        hops = [(xin, xout)]
    else:
        xs = [nc.dram_tensor(f"xs{i}", [T, D], F32).ap() for i in range(n - 1)]
        hops = []
        cur = xin
        for i in range(n - 1):
            hops.append((cur, xs[i]))
            cur = xs[i]
        hops.append((cur, xout))

    for (s, (src, dst)) in zip(secs, hops):
        if s in (1, 3):
            emit_ffn0(p, a, src, dst, lnt, wt["gup"], wt["wdn"])
        elif s == 2:
            for nm2 in ("b_gate", "rms_w"):
                pass
            bgt = din("L0_attn_b_gate_r", [128, 64])
            rwt = din("L0_attn_rms_w_r", [128, 4])
            b1 = p.dma("gp", a.BG[:], bgt[:])
            b2 = p.dma("gp", a.RMSW[:], rwt[:])
            p.wait("sc", b2)
            cp_scr = nc.dram_tensor("cp_scr", [T, D], BF16).ap()
            cl_scr = nc.dram_tensor("cl_scr", [T, D], BF16).ap()
            o_scr = nc.dram_tensor("o_scr", [T, D], BF16).ap()
            agin = nc.dram_tensor("ag_scan_in", [128, 32], F32).ap()
            agout = nc.dram_tensor("ag_scan_out", [512, 32], F32).ap()
            scm = din("scanmask", [128, 3])
            b3 = p.dma("gp", a.A7[:, 112:115], scm[:])
            p.wait("ve", b3)
            emit_l0attn(p, a, src, dst, lnt, wt["win"], wt["wgt"], wt["wout"],
                        cp_scr, cl_scr, o_scr, agin, agout)
        elif s in (4, 6):
            w = {
                "router_w": din("L1_ffn_router_w", [D, 8]),
                "router_b": din("L1_ffn_router_b_r", [1, 8]),
                "expert_w": din("L1_ffn_expert_w", [8, D, D]),
                "expert_b": din("L1_ffn_expert_b_r", [1, 16384]),
                "proj_w": din("L1_ffn_proj_w", [D, D]),
                "proj_b": din("L1_ffn_proj_b_r", [1, D]),
                "norm_w": din("L1_ffn_norm_w_r", [1, D]),
            } if s == 4 or 4 not in secs else build_program._moe_w
            if s == 4 or 4 not in secs:
                build_program._moe_w = w
            acc_scr = nc.dram_tensor(f"acc_scr{s}", [T, D], BF16).ap()
            emit_moe(p, a, src, dst, lnt, acc_scr, w, tag=f"m{s}")
        elif s == 5:
            w5 = {
                "W_lat": din("L1_attn_W_lat", [D, 128]),
                "W_d_slice": din("W_d_slice", [D, 512]),
                "W_proj_slice": din("W_proj_slice", [512, D]),
                "b_proj": din("L1_attn_b_proj_r", [1, D]),
            }
            blt = din("b_lat_r", [128, 1])
            bdt = din("b_d_slice_r", [128, 4])
            p.dma("gp", a.BL[:], blt[:])
            b4 = p.dma("gp", a.BD[:], bdt[:])
            p.wait("sc", b4)
            agin_ln = nc.dram_tensor("agin_ln", [T, D], BF16).ap()
            agout_ln = nc.dram_tensor("agout_ln", [SEQ, D], BF16).ap()
            proj_part = nc.dram_tensor("proj_part", [SEQ, D], F32).ap()
            rs_out = nc.dram_tensor("rs_out", [T, D], F32).ap()
            emit_l1attn(p, a, src, dst, agin_ln, agout_ln, proj_part, rs_out, w5)

    p.wait("gp", p.tok("xst"))
    p.wait("gp", p.tok("gpdma"))
    nc_final = p.finalize()
    stack.close()
    return nc_final



def _consts():
    ident = np.eye(128, dtype=np.float32)
    ones = np.ones((128, 128), dtype=np.float32)
    msk = np.zeros((128, 1024), dtype=np.float32)
    pp, qq = np.meshgrid(np.arange(128), np.arange(1024), indexing="ij")
    msk[pp <= qq - 512] = 1.0
    return ident, ones, msk


def _prep_core_inputs(inputs, core):
    b, me = core // 4, core % 4
    ident, ones, msk = _consts()
    d = {
        "c_ident": ident, "c_ones": ones, "c_mask": msk,
        "L0_ffn_w_gate_up": inputs["L0_ffn_w_gate_up"],
        "L0_ffn_w_down": inputs["L0_ffn_w_down"],
        "L0_attn_W_in": inputs["L0_attn_W_in"],
        "L0_attn_W_gate": inputs["L0_attn_W_gate"],
        "L0_attn_W_out": inputs["L0_attn_W_out"],
        "L0_attn_b_gate_r": np.ascontiguousarray(
            inputs["L0_attn_b_gate"].reshape(64, 128).T),
        "L0_attn_rms_w_r": np.ascontiguousarray(
            inputs["L0_attn_rms_w"].reshape(4, 128).T),
        "scanmask": np.broadcast_to(
            (np.arange(3) < me).astype(np.float32), (128, 3)).copy(),
        "L1_ffn_router_w": inputs["L1_ffn_router_w"],
        "L1_ffn_router_b_r": inputs["L1_ffn_router_b"].reshape(1, 8),
        "L1_ffn_expert_w": inputs["L1_ffn_expert_w"],
        "L1_ffn_expert_b_r": inputs["L1_ffn_expert_b"].reshape(1, 16384),
        "L1_ffn_proj_w": inputs["L1_ffn_proj_w"],
        "L1_ffn_proj_b_r": inputs["L1_ffn_proj_b"].reshape(1, 2048),
        "L1_ffn_norm_w_r": inputs["L1_ffn_norm_w"].reshape(1, 2048),
        "L1_attn_W_lat": inputs["L1_attn_W_lat"],
        "W_d_slice": np.ascontiguousarray(
            inputs["L1_attn_W_d"][:, me * 512:(me + 1) * 512]),
        "W_proj_slice": np.ascontiguousarray(
            inputs["L1_attn_W_proj"][me * 512:(me + 1) * 512, :]),
        "L1_attn_b_proj_r": inputs["L1_attn_b_proj"].reshape(1, 2048),
        "b_lat_r": inputs["L1_attn_b_lat"].reshape(128, 1),
        "b_d_slice_r": np.ascontiguousarray(
            inputs["L1_attn_b_d"][me * 512:(me + 1) * 512].reshape(4, 128).T),
    }
    return {k: np.ascontiguousarray(np.asarray(v, dtype=np.float32))
            for k, v in d.items()}


_IN_BY_SEC = {
    1: ["L0_ffn_w_gate_up", "L0_ffn_w_down"],
    2: ["L0_attn_W_in", "L0_attn_W_gate", "L0_attn_W_out",
        "L0_attn_b_gate_r", "L0_attn_rms_w_r", "scanmask"],
    3: ["L0_ffn_w_gate_up", "L0_ffn_w_down"],
    4: ["L1_ffn_router_w", "L1_ffn_router_b_r", "L1_ffn_expert_w",
        "L1_ffn_expert_b_r", "L1_ffn_proj_w", "L1_ffn_proj_b_r",
        "L1_ffn_norm_w_r"],
    5: ["L1_attn_W_lat", "W_d_slice", "W_proj_slice", "L1_attn_b_proj_r",
        "b_lat_r", "b_d_slice_r"],
    6: ["L1_ffn_router_w", "L1_ffn_router_b_r", "L1_ffn_expert_w",
        "L1_ffn_expert_b_r", "L1_ffn_proj_w", "L1_ffn_proj_b_r",
        "L1_ffn_norm_w_r"],
}


def run_chain(inputs, groups):
    x = np.asarray(inputs["L0_attn_x"], dtype=np.float32)
    shards = [np.ascontiguousarray(x[c // 4, (c % 4) * T:(c % 4 + 1) * T, :])
              for c in range(8)]
    per_core = [_prep_core_inputs(inputs, c) for c in range(8)]
    for gi, secs in enumerate(groups):
        if hasattr(build_program, "_moe_w"):
            del build_program._moe_w
        nc = build_program(secs, None)
        names = set()
        for s in secs:
            names.update(_IN_BY_SEC[s])
        names |= {"c_ident", "c_ones", "c_mask"}
        in_maps = []
        for c in range(8):
            m = {k: per_core[c][k] for k in names if k in per_core[c]}
            m["xin"] = shards[c]
            in_maps.append(m)
        res = run_bass_kernel_spmd(nc, in_maps, core_ids=list(range(8)))
        shards = [np.asarray(res.results[c]["xout"], dtype=np.float32)
                  for c in range(8)]
    out = np.empty((2, SEQ, D), dtype=np.float32)
    for c in range(8):
        out[c // 4, (c % 4) * T:(c % 4 + 1) * T, :] = shards[c]
    return out


def kernel(**inputs):
    if FUSED:
        return run_chain(inputs, [[1, 2, 3, 4, 5, 6]])
    return run_chain(inputs, [[1], [2], [3], [4], [5], [6]])


# version 03
# speedup vs baseline: 1.3269x; 1.1669x over previous
import numpy as np
import ml_dtypes
from contextlib import ExitStack
import concourse.bass as bass
import concourse.mybir as mybir
from concourse.bass_utils import run_bass_kernel_spmd

F32 = mybir.dt.float32
BF16 = mybir.dt.bfloat16
AF = mybir.ActivationFunctionType
ALU = mybir.AluOpType
AX = mybir.AxisListType

ENGINES = ("te", "ve", "sc", "gp", "sy")
RG = [[0, 1, 2, 3], [4, 5, 6, 7]]

D = 2048
T = 2048
TB = 512
NBLK = 4
KT = 16
SEQ = 8192
EPS_LN = 1e-5
EPS_S2 = 1e-6
EPS_MOE = 1e-6
ISQL = float(1.0 / np.sqrt(128.0))

FUSED = True



class Prog:
    def __init__(self):
        self.nc = bass.Bass("TRN2", target_bir_lowering=False,
                            disable_frame_to_traceback=True)
        self.ops = {e: [] for e in ENGINES}
        self.sems = {}
        self.sem_order = []
        self._h = {}

    def _bump(self, name, n):
        if name not in self.sems:
            self.sems[name] = 0
            self.sem_order.append(name)
        self.sems[name] += n
        return (name, self.sems[name])

    def tok(self, name):
        return (name, self.sems.get(name, 0))

    def wait(self, eng, tok):
        if tok is None:
            return
        name, val = tok
        if val <= 0:
            return
        assert val <= self.sems.get(name, 0), f"wait beyond incs: {name} {val}"
        self.ops[eng].append(lambda e, n=name, v=val: e.wait_ge(self._h[n], v))

    def dma(self, q, out, in_, transpose=False, sem=None):
        s = sem or (q + "dma")
        tok = self._bump(s, 16)
        def fn(e, out=out, in_=in_, t=transpose, n=s):
            e.dma_start(out=out, in_=in_, transpose=t).then_inc(self._h[n], 16)
        self.ops[q].append(fn)
        return tok

    def op(self, eng, fn, sig=False):
        if not sig:
            self.ops[eng].append(fn)
            return None
        s = eng + "sig"
        tok = self._bump(s, 1)
        def wrapped(e, n=s):
            fn(e).then_inc(self._h[n], 1)
        self.ops[eng].append(wrapped)
        return tok

    def cc(self, kind, ins, outs, name, op=None):
        tok = self._bump(name, 1)
        aop = op if op is not None else ALU.bypass
        def fn(e, n=name):
            e.collective_compute(kind, aop, replica_groups=RG,
                                 ins=ins, outs=outs).then_inc(self._h[n])
        self.ops["gp"].append(fn)
        return tok

    def finalize(self):
        nc = self.nc
        stack = ExitStack()
        for n in self.sem_order:
            self._h[n] = stack.enter_context(nc.semaphore(n))
        block = stack.enter_context(nc.Block())
        decos = {"te": block.tensor, "ve": block.vector, "sc": block.scalar,
                 "gp": block.gpsimd, "sy": block.sync}
        for name in ENGINES:
            def mk(oplist):
                def body(e):
                    for fn in oplist:
                        fn(e)
                return body
            decos[name](mk(self.ops[name]))
        stack.close()
        return nc


class Rot:
    def __init__(self, p, banks):
        self.p, self.banks = p, banks
        self.free = [None] * len(banks)
        self.i = 0
    def get(self):
        j = self.i % len(self.banks)
        self.i += 1
        return j, self.banks[j], self.free[j]
    def done(self, j, tok):
        self.free[j] = tok


def stream_slabs(p, n, make_load, body, q="sy"):
    loads = [None] * n
    free = {}
    def issue(i):
        if i - 2 >= 0:
            p.wait(q, free[i - 2])
        loads[i] = make_load(i, i % 2)
    p.wait(q, p.tok("tesig"))
    if n > 0:
        issue(0)
    if n > 1:
        issue(1)
    for i in range(n):
        free[i] = body(i, i % 2, loads[i])
        assert free[i] is not None
        if i + 2 < n:
            issue(i + 2)


class Arenas:
    def __init__(self, nc, stack):
        E = stack.enter_context
        self.A1 = E(nc.sbuf_tensor("A1", [128, NBLK, D], F32))
        self.A2 = E(nc.sbuf_tensor("A2", [128, NBLK, D], BF16))
        self.A3 = E(nc.sbuf_tensor("A3", [128, KT, TB], BF16))
        self.A4 = E(nc.sbuf_tensor("A4", [128, 32768], BF16))
        self.A5 = E(nc.sbuf_tensor("A5", [128, 2, KT, TB], BF16))
        self.A6 = E(nc.sbuf_tensor("A6", [128, 6144], F32))
        self.A7 = E(nc.sbuf_tensor("A7", [128, 128], F32))
        self.BG = E(nc.sbuf_tensor("BG", [128, 64], F32))
        self.RMSW = E(nc.sbuf_tensor("RMSW", [128, 4], F32))
        self.BD = E(nc.sbuf_tensor("BD", [128, 4], F32))
        self.BL = E(nc.sbuf_tensor("BL", [128, 1], F32))
        self.ID = E(nc.sbuf_tensor("ID", [128, 128], BF16))
        self.ONES = E(nc.sbuf_tensor("ONES", [128, 128], BF16))
        self.MSK = E(nc.sbuf_tensor("MSK", [128, 1024], BF16))
        self.WB = E(nc.sbuf_tensor("WB", [128, 2304], BF16))
        self.PS = [E(nc.psum_tensor(f"PS{i}", [128, 512], F32)) for i in range(6)]
        self.PB = [E(nc.psum_tensor(f"PB{i}", [128, 1024], BF16)) for i in range(2)]

    def v4(self, off, a_, b_):
        return self.A4[:, off:off + a_ * b_].rearrange("p (a b) -> p a b", a=a_)

    def f4(self, off, n):
        return self.A4[:, off:off + n]

    def r4(self, off, n):
        return self.A4[0:1, off:off + n]

    def v6(self, off, n):
        return self.A6[:, off:off + n]

    def v6b(self, off, a_, b_):
        return self.A6[:, off:off + a_ * b_].rearrange("p (a b) -> p a b", a=a_)


def rr(ap, p=128):
    return ap.rearrange("(a p) c -> p a c", p=p)



def emit_wt_construct(p, a, w_hbm, wb_unused, dst, O, K, perm=None):
    nob, nkb = O // 128, K // 128
    perm = perm or list(range(nob))
    p.wait("gp", p.tok("tesig"))
    p.wait("te", p.tok("wts0"))
    p.wait("te", p.tok("wts1"))
    p.wait("te", p.tok("vesig"))
    buf = [a.A5[:, 0].rearrange("p a b -> p (a b)"),
           a.A5[:, 1].rearrange("p a b -> p (a b)")]
    stage = [a.A4[:, 0:8192], a.A4[:, 8192:16384]]
    ngrp = (nkb + 7) // 8
    load_free = [None, None]
    stage_free = [None, None]
    bank_free = [None, None]
    for i in range(nob):
        lp = i % 2
        sp = i % 2
        p.wait("gp", load_free[lp])
        ld = p.dma("gp", buf[lp][:, 0:K], w_hbm[i * 128:(i + 1) * 128, :],
                   sem=f"wcl{lp}")
        p.wait("te", ld)
        p.wait("te", stage_free[sp])
        ev = None
        for g in range(ngrp):
            nk = min(8, nkb - g * 8)
            bp = g % 2
            p.wait("te", bank_free[bp])
            tp = None
            for j in range(nk):
                k = g * 8 + j
                tp = p.op("te", lambda e, k=k, j=j, lp=lp, bp=bp: e.transpose(
                    out=a.PB[bp][:, j * 128:(j + 1) * 128],
                    in_=buf[lp][:, k * 128:(k + 1) * 128],
                    identity=a.ID[:]), sig=(j == nk - 1))
            p.wait("ve", tp)
            ev = p.op("ve", lambda e, g=g, nk=nk, bp=bp, sp=sp: e.tensor_copy(
                out=stage[sp][:, g * 1024:g * 1024 + nk * 128],
                in_=a.PB[bp][:, 0:nk * 128]), sig=True)
            bank_free[bp] = ev
        load_free[lp] = p.tok("tesig")
        p.wait("sy", ev)
        ob = perm.index(i) if perm else i
        stage_free[sp] = p.dma(
            "sy",
            rr(dst)[:, :, ob * 128:(ob + 1) * 128],
            stage[sp][:, 0:nkb * 128].rearrange("p (a c) -> p a c", c=128),
            sem=f"wts{sp}")
    p.wait("sy", stage_free[0])
    p.wait("sy", stage_free[1])



def emit_ln_block(p, a, src_x, blk, lnt=None, to_fm=True, ln_rows=None,
                  prior=None):
    p.wait("sy", prior)
    xt = p.dma("sy", a.A1[:], rr(src_x[blk * TB:(blk + 1) * TB, :]), sem="xld")
    SB = a.v6b(5632, 16, 8)
    dummy = a.v6(0, 2048)
    p.wait("sc", prior)
    p.wait("ve", prior)
    atoks = []
    for tt in range(4):
        p.wait("ve", xt)
        p.wait("sc", xt)
        t0 = p.op("ve", lambda e, tt=tt: e.reduce_sum(
            out=SB[:, tt, 0:1], in_=a.A1[:, tt, :], axis=AX.X), sig=True)
        s_ssq = p.op("sc", lambda e, tt=tt: e.activation(
            out=dummy, in_=a.A1[:, tt, :], func=AF.Square,
            accum_out=SB[:, tt, 1:2]), sig=True)
        p.wait("ve", s_ssq)
        p.wait("ve", t0)
        t1 = p.op("ve", lambda e, tt=tt: e.tensor_scalar(
            out=SB[:, tt, 2:3], in0=SB[:, tt, 0:1], scalar1=1.0 / D,
            scalar2=None, op0=ALU.mult), sig=True)
        p.wait("ve", t1)
        t2 = p.op("ve", lambda e, tt=tt: e.tensor_tensor(
            out=SB[:, tt, 3:4], in0=SB[:, tt, 2:3], in1=SB[:, tt, 2:3],
            op=ALU.mult), sig=True)
        t3 = p.op("ve", lambda e, tt=tt: e.tensor_scalar(
            out=SB[:, tt, 4:5], in0=SB[:, tt, 1:2], scalar1=1.0 / D,
            scalar2=EPS_LN, op0=ALU.mult, op1=ALU.add), sig=True)
        p.wait("ve", t2)
        p.wait("ve", t3)
        vtok = p.op("ve", lambda e, tt=tt: e.tensor_tensor(
            out=SB[:, tt, 5:6], in0=SB[:, tt, 4:5], in1=SB[:, tt, 3:4],
            op=ALU.subtract), sig=True)
        p.wait("sc", vtok)
        stok = p.op("sc", lambda e, tt=tt: e.activation(
            out=SB[:, tt, 6:7], in_=SB[:, tt, 5:6], func=AF.Sqrt), sig=True)
        p.wait("ve", stok)
        t4 = p.op("ve", lambda e, tt=tt: e.reciprocal(
            out=SB[:, tt, 7:8], in_=SB[:, tt, 6:7]), sig=True)
        p.wait("ve", t4)
        atok = p.op("ve", lambda e, tt=tt: e.tensor_scalar(
            out=a.A2[:, tt, :], in0=a.A1[:, tt, :], scalar1=SB[:, tt, 2:3],
            scalar2=SB[:, tt, 7:8], op0=ALU.subtract, op1=ALU.mult), sig=True)
        atoks.append(atok)
    p.wait("sy", atoks[-1])
    if ln_rows is not None:
        dst, off = ln_rows
        lntok = p.dma("sy", rr(dst[off + blk * TB: off + (blk + 1) * TB, :]),
                      a.A2[:], sem="lns")
        return xt, lntok
    lntok = p.dma("sy", rr(lnt[blk * TB:(blk + 1) * TB, :]), a.A2[:], sem="lns")
    fm_tok = None
    if to_fm:
        p.wait("sy", lntok)
        p.wait("sy", p.tok("tesig"))
        for kt in range(KT):
            fm_tok = p.dma("sy", a.A3[:, kt, :],
                           lnt[blk * TB:(blk + 1) * TB, kt * 128:(kt + 1) * 128],
                           transpose=True, sem="fmt")
    return xt, fm_tok



def emit_ffn0(p, a, src_x, dst_x, lnt, gup_T, wdn_T, prior0=None):
    prior = prior0
    for blk in range(NBLK):
        xt, fm = emit_ln_block(p, a, src_x, blk, lnt=lnt, prior=prior)
        HM = a.v4(0, 64, 512)
        SG = a.v6(4096, 512)
        ve_all = p.tok("vesig")
        sc_all = p.tok("scsig")
        rot = Rot(p, [(0, 1), (2, 3)])
        hm_last = [None]

        def load_gu(i, par):
            return p.dma("sy", a.A5[:, par],
                         rr(gup_T)[:, :, i * 512:(i + 1) * 512],
                         sem=f"sl{par}")

        def body_gu(i, par, ltok):
            last_te = None
            for half in range(2):
                op = 2 * i + half
                j, pair, ftok = rot.get()
                pg, pu = a.PS[pair[0]], a.PS[pair[1]]
                p.wait("te", ftok)
                p.wait("te", ltok)
                if op == 0:
                    p.wait("te", fm)
                    p.wait("te", ve_all)
                    p.wait("te", sc_all)
                for kt in range(KT):
                    p.op("te", lambda e, kt=kt, par=par, half=half, pg=pg: e.matmul(
                        out=pg[:], lhsT=a.A5[:, par, kt, half * 256:half * 256 + 128],
                        rhs=a.A3[:, kt, :], start=(kt == 0), stop=(kt == KT - 1)))
                for kt in range(KT):
                    tok = p.op("te", lambda e, kt=kt, par=par, half=half, pu=pu: e.matmul(
                        out=pu[:], lhsT=a.A5[:, par, kt, half * 256 + 128:half * 256 + 256],
                        rhs=a.A3[:, kt, :], start=(kt == 0), stop=(kt == KT - 1)),
                        sig=(kt == KT - 1))
                    if kt == KT - 1:
                        last_te = tok
                p.wait("sc", last_te)
                sg = p.op("sc", lambda e, pg=pg: e.activation(
                    out=SG, in_=pg[:], func=AF.Silu), sig=True)
                p.wait("ve", sg)
                ev = p.op("ve", lambda e, op=op, pu=pu: e.tensor_tensor(
                    out=HM[:, op, :], in0=SG, in1=pu[:], op=ALU.mult), sig=True)
                rot.done(j, ev)
                hm_last[0] = ev
            return last_te

        stream_slabs(p, 32, load_gu, body_gu)
        hm_done = hm_last[0]

        STG = a.v6b(0, 4, 512)
        for fh in range(4):
            prev_ev = p.tok("vesig")

            def load_dn(g, par, fh=fh):
                return p.dma("sy", a.A5[:, par, 0:8, :],
                             rr(wdn_T)[:, g * 8:(g + 1) * 8,
                                       fh * 512:(fh + 1) * 512],
                             sem=f"sl{par}")

            def body_dn(g, par, ltok, fh=fh):
                p.wait("te", ltok)
                if g == 0:
                    p.wait("te", hm_done)
                    p.wait("te", prev_ev)
                tok = None
                for kk in range(8):
                    for tt in range(4):
                        sig = (kk == 7 and tt == 3)
                        tok2 = p.op("te", lambda e, g=g, kk=kk, tt=tt, par=par: e.matmul(
                            out=a.PS[tt][:],
                            lhsT=HM[:, g * 8 + kk, tt * 128:(tt + 1) * 128],
                            rhs=a.A5[:, par, kk, :],
                            start=(g == 0 and kk == 0),
                            stop=(g == 7 and kk == 7)), sig=sig)
                        if sig:
                            tok = tok2
                return tok

            mm_done = [None]

            def body_dn2(g, par, ltok, fh=fh):
                t = body_dn(g, par, ltok, fh)
                mm_done[0] = t
                return t

            stream_slabs(p, 8, load_dn, body_dn2)
            p.wait("ve", mm_done[0])
            p.wait("ve", prior)
            ev = None
            for tt in range(4):
                ev = p.op("ve", lambda e, tt=tt, fh=fh: e.scalar_tensor_tensor(
                    out=STG[:, tt, :], in0=a.PS[tt][:], scalar=0.5,
                    in1=a.A1[:, tt, fh * 512:(fh + 1) * 512],
                    op0=ALU.mult, op1=ALU.add), sig=(tt == 3))
            p.wait("gp", ev)
            prior = p.dma("gp",
                          rr(dst_x[blk * TB:(blk + 1) * TB,
                                   fh * 512:(fh + 1) * 512]), STG[:],
                          sem="xst")
    return prior



def emit_l0attn(p, a, src_x, dst_x, lnt, win_T, wgt_T, wout_T,
                cp_scr, cl_scr, o_scr, agin, agout, prior0=None):
    ZER = a.v6(3584, 512)
    p.op("ve", lambda e: e.memset(ZER, 0.0))
    CPC = a.A7[:, 64:80]
    CLC = a.A7[:, 80:96]
    CIN = a.A7[:, 96:112]
    SCM = a.A7[:, 112:115]
    p.op("ve", lambda e: e.memset(CPC, 1.0))
    p.op("ve", lambda e: e.memset(CLC, 0.0))

    XPF = a.v4(0, 16, 512)
    CPS = a.v4(8192, 16, 512)
    CLS = a.v4(16384, 16, 512)
    OS = a.v4(24576, 16, 512)
    T1, T2, T3, T4 = (a.v6(512, 512), a.v6(1024, 512), a.v6(1536, 512),
                      a.v6(2048, 512))

    prior = prior0
    for blk in range(NBLK):
        xt, fm = emit_ln_block(p, a, src_x, blk, lnt=lnt, prior=prior)
        ve_all = p.tok("vesig")
        sc_all = p.tok("scsig")
        rot = Rot(p, [0, 1, 2, 3])

        def load_xp(i, par):
            return p.dma("sy", a.A5[:, par],
                         rr(win_T)[:, :, i * 512:(i + 1) * 512], sem=f"sl{par}")

        def body_xp(i, par, ltok):
            last = None
            for q in range(4):
                ht = i * 4 + q
                j, bank, ftok = rot.get()
                p.wait("te", ftok)
                p.wait("te", ltok)
                if ht == 0:
                    p.wait("te", fm)
                    p.wait("te", ve_all)
                    p.wait("te", sc_all)
                for kt in range(KT):
                    last = p.op("te", lambda e, kt=kt, par=par, q=q, bank=bank: e.matmul(
                        out=a.PS[bank][:],
                        lhsT=a.A5[:, par, kt, q * 128:(q + 1) * 128],
                        rhs=a.A3[:, kt, :], start=(kt == 0), stop=(kt == KT - 1)),
                        sig=(kt == KT - 1))
                p.wait("sc", last)
                ev = p.op("sc", lambda e, ht=ht, bank=bank: e.copy(
                    out=XPF[:, ht, :], in_=a.PS[bank][:]), sig=True)
                rot.done(j, ev)
            return last

        stream_slabs(p, 4, load_xp, body_xp)
        xp_done = p.tok("scsig")

        def load_g(i, par):
            return p.dma("sy", a.A5[:, par],
                         rr(wgt_T)[:, :, i * 512:(i + 1) * 512], sem=f"sl{par}")

        def body_g(i, par, ltok):
            head, jj = i // 4, i % 4
            ct = i
            gt = head * 16
            p.wait("te", ltok)
            if i == 0:
                p.wait("te", xp_done)
            else:
                p.wait("te", p.tok("scsig"))
            tok = None
            for ty in range(4):
                for kt in range(KT):
                    sig = (ty == 3 and kt == KT - 1)
                    tok2 = p.op("te", lambda e, ty=ty, kt=kt, par=par: e.matmul(
                        out=a.PS[ty][:],
                        lhsT=a.A5[:, par, kt, ty * 128:(ty + 1) * 128],
                        rhs=XPF[:, kt, :], start=(kt == 0), stop=(kt == KT - 1)),
                        sig=sig)
                    if sig:
                        tok = tok2
            p.wait("sc", tok)
            c_i, c_f, c_o, c_c = gt + jj, gt + 4 + jj, gt + 8 + jj, gt + 12 + jj
            p.wait("sc", p.tok("vesig"))
            p.op("sc", lambda e, c=c_i: e.activation(
                out=T1, in_=a.PS[0][:], func=AF.Exp, bias=a.BG[:, c:c + 1]))
            p.op("sc", lambda e, c=c_c: e.activation(
                out=T2, in_=a.PS[3][:], func=AF.Tanh, bias=a.BG[:, c:c + 1]))
            p.op("sc", lambda e, c=c_f: e.activation(
                out=T4, in_=a.PS[1][:], func=AF.Sigmoid, bias=a.BG[:, c:c + 1]))
            sce = p.op("sc", lambda e, c=c_o, ct=ct: e.activation(
                out=OS[:, ct, :], in_=a.PS[2][:], func=AF.Sigmoid,
                bias=a.BG[:, c:c + 1]), sig=True)
            p.wait("ve", sce)
            v1 = p.op("ve", lambda e: e.tensor_tensor(out=T3, in0=T1, in1=T2,
                                                 op=ALU.mult), sig=True)
            v2 = p.op("ve", lambda e, ct=ct: e.tensor_tensor_scan(
                out=CPS[:, ct, :], data0=T4, data1=ZER,
                initial=CPC[:, ct:ct + 1], op0=ALU.mult, op1=ALU.add), sig=True)
            p.wait("ve", v2)
            v3 = p.op("ve", lambda e, ct=ct: e.tensor_copy(
                out=CPC[:, ct:ct + 1], in_=CPS[:, ct, 511:512]), sig=True)
            p.wait("ve", v1)
            v4 = p.op("ve", lambda e, ct=ct: e.tensor_tensor_scan(
                out=CLS[:, ct, :], data0=T4, data1=T3,
                initial=CLC[:, ct:ct + 1], op0=ALU.mult, op1=ALU.add), sig=True)
            p.wait("ve", v4)
            p.op("ve", lambda e, ct=ct: e.tensor_copy(
                out=CLC[:, ct:ct + 1], in_=CLS[:, ct, 511:512]), sig=True)
            return tok

        stream_slabs(p, 16, load_g, body_g)
        scans_done = p.tok("vesig")
        p.wait("gp", scans_done)
        p.dma("gp", rr(cp_scr)[:, :, blk * TB:(blk + 1) * TB], CPS[:])
        p.dma("gp", rr(cl_scr)[:, :, blk * TB:(blk + 1) * TB], CLS[:])
        prior = p.dma("gp", rr(o_scr)[:, :, blk * TB:(blk + 1) * TB], OS[:])

    agt = p.dma("gp", agin[:], a.A7[:, 64:96])
    p.wait("gp", agt)
    cct = p.cc("AllGather", [agin[:]], [agout[:]], "ccscan")
    p.wait("gp", cct)
    G = []
    gt_tok = None
    for r in range(4):
        Gr = a.A6[:, 4096 + r * 32: 4096 + (r + 1) * 32]
        G.append(Gr)
        gt_tok = p.dma("gp", Gr, agout[r * 128:(r + 1) * 128, :])
    p.wait("ve", gt_tok)
    TA = a.A6[:, 4224:4240]
    TBv = a.A6[:, 4240:4256]
    tk = p.op("ve", lambda e: e.memset(CIN, 0.0), sig=True)
    for r in range(3):
        p.wait("ve", tk)
        tk = p.op("ve", lambda e, r=r: e.tensor_scalar(
            out=TA, in0=G[r][:, 0:16], scalar1=1.0, scalar2=SCM[:, r:r + 1],
            op0=ALU.subtract, op1=ALU.mult), sig=True)
        p.wait("ve", tk)
        tk = p.op("ve", lambda e: e.tensor_scalar(
            out=TA, in0=TA, scalar1=1.0, scalar2=None, op0=ALU.add), sig=True)
        p.wait("ve", tk)
        tk = p.op("ve", lambda e: e.tensor_tensor(out=CIN, in0=CIN, in1=TA,
                                             op=ALU.mult), sig=True)
        p.wait("ve", tk)
        tk = p.op("ve", lambda e, r=r: e.tensor_scalar(
            out=TBv, in0=G[r][:, 16:32], scalar1=SCM[:, r:r + 1], scalar2=None,
            op0=ALU.mult), sig=True)
        p.wait("ve", tk)
        tk = p.op("ve", lambda e: e.tensor_tensor(out=CIN, in0=CIN, in1=TBv,
                                             op=ALU.add), sig=True)
    p.wait("ve", tk)
    cin_done = p.op("ve", lambda e: e.tensor_copy(out=TA, in_=CIN), sig=True)

    HFM = a.v4(0, 16, 512)
    Cr = a.v6b(0, 4, 512)
    CN = a.v6(2048, 512)
    RrowF = a.A6[0:1, 2560:3072]
    Rrow16 = a.r4(30720, 512)
    prior2 = p.tok("gpdma")
    prev_ps4 = None
    prev_ps5 = None
    for blk in range(NBLK):
        p.wait("gp", prior2)
        xt = p.dma("gp", a.A1[:], rr(src_x[blk * TB:(blk + 1) * TB, :]))
        p.dma("gp", CPS[:], rr(cp_scr)[:, :, blk * TB:(blk + 1) * TB])
        p.dma("gp", CLS[:], rr(cl_scr)[:, :, blk * TB:(blk + 1) * TB])
        l3 = p.dma("gp", OS[:], rr(o_scr)[:, :, blk * TB:(blk + 1) * TB])
        p.wait("ve", prior2)
        p.wait("ve", l3)
        p.wait("ve", cin_done)
        for head in range(4):
            ctok = None
            for jj in range(4):
                ct = head * 4 + jj
                ctok = p.op("ve", lambda e, ct=ct, jj=jj: e.scalar_tensor_tensor(
                    out=Cr[:, jj, :], in0=CPS[:, ct, :],
                    scalar=CIN[:, ct:ct + 1], in1=CLS[:, ct, :],
                    op0=ALU.mult, op1=ALU.add), sig=(jj == 3))
            p.wait("sc", ctok)
            sq_last = None
            for jj in range(4):
                ct = head * 4 + jj
                sq_last = p.op("sc", lambda e, ct=ct, jj=jj: e.activation(
                    out=HFM[:, ct, :], in_=Cr[:, jj, :], func=AF.Square),
                    sig=(jj == 3))
            p.wait("te", sq_last)
            p.wait("te", prev_ps4)
            mt = None
            for jj in range(4):
                ct = head * 4 + jj
                mt = p.op("te", lambda e, ct=ct, jj=jj: e.matmul(
                    out=a.PS[4][0:1, :], lhsT=a.ONES[:, 0:1],
                    rhs=HFM[:, ct, :], start=(jj == 0), stop=(jj == 3)),
                    sig=(jj == 3))
            p.wait("ve", mt)
            vt = p.op("ve", lambda e: e.tensor_scalar(
                out=RrowF, in0=a.PS[4][0:1, :], scalar1=1.0 / 512.0,
                scalar2=EPS_S2, op0=ALU.mult, op1=ALU.add), sig=True)
            prev_ps4 = vt
            p.wait("sc", vt)
            st2 = p.op("sc", lambda e: e.activation(
                out=RrowF, in_=RrowF, func=AF.Sqrt), sig=True)
            p.wait("ve", st2)
            t5 = p.op("ve", lambda e: e.reciprocal(out=RrowF, in_=RrowF),
                      sig=True)
            p.wait("ve", t5)
            rv = p.op("ve", lambda e: e.tensor_copy(out=Rrow16, in_=RrowF),
                      sig=True)
            p.wait("te", rv)
            p.wait("te", prev_ps5)
            bt = p.op("te", lambda e: e.matmul(
                out=a.PS[5][:], lhsT=a.ONES[0:1, :], rhs=Rrow16,
                start=True, stop=True), sig=True)
            p.wait("ve", bt)
            for jj in range(4):
                ct = head * 4 + jj
                cn = p.op("ve", lambda e, jj=jj: e.tensor_tensor(
                    out=CN, in0=Cr[:, jj, :], in1=a.PS[5][:], op=ALU.mult),
                    sig=True)
                if jj == 3:
                    prev_ps5 = cn
                p.wait("sc", cn)
                th = p.op("sc", lambda e, ct=ct, jj=jj: e.activation(
                    out=HFM[:, ct, :], in_=CN, func=AF.Tanh,
                    scale=a.RMSW[:, jj:jj + 1]), sig=True)
                p.wait("ve", th)
                p.op("ve", lambda e, ct=ct: e.tensor_tensor(
                    out=HFM[:, ct, :], in0=HFM[:, ct, :], in1=OS[:, ct, :],
                    op=ALU.mult), sig=True)
        h_done = p.tok("vesig")

        rot = Rot(p, [0, 1, 2, 3])
        STG = a.v6b(0, 4, 512)
        stores = [None]

        def load_wo(i, par):
            return p.dma("sy", a.A5[:, par],
                         rr(wout_T)[:, :, i * 512:(i + 1) * 512], sem=f"sl{par}")

        def body_wo(cc, par, ltok):
            p.wait("te", ltok)
            if cc == 0:
                p.wait("te", h_done)
            p.wait("ve", stores[0])
            last = None
            ev = None
            for tt in range(4):
                j, bank, ftok = rot.get()
                p.wait("te", ftok)
                for ctc in range(KT):
                    last = p.op("te", lambda e, ctc=ctc, tt=tt, par=par, bank=bank: e.matmul(
                        out=a.PS[bank][:],
                        lhsT=HFM[:, ctc, tt * 128:(tt + 1) * 128],
                        rhs=a.A5[:, par, ctc, :],
                        start=(ctc == 0), stop=(ctc == KT - 1)),
                        sig=(ctc == KT - 1))
                p.wait("ve", last)
                p.wait("ve", xt)
                ev = p.op("ve", lambda e, tt=tt, cc=cc, bank=bank: e.tensor_tensor(
                    out=STG[:, tt, :], in0=a.PS[bank][:],
                    in1=a.A1[:, tt, cc * 512:(cc + 1) * 512], op=ALU.add),
                    sig=True)
                rot.done(j, ev)
            p.wait("gp", ev)
            stores[0] = p.dma(
                "gp", rr(dst_x[blk * TB:(blk + 1) * TB,
                               cc * 512:(cc + 1) * 512]), STG[:], sem="xst")
            return last

        stream_slabs(p, 4, load_wo, body_wo)
        prior2 = stores[0]
    return prior2



def emit_moe(p, a, src_x, dst_x, lnt, acc_scr, w, tag, prior0=None):
    p.wait("gp", p.tok("tesig"))
    p.wait("gp", p.tok("vesig"))
    p.wait("gp", p.tok("scsig"))
    NW = a.f4(8192, 2048)
    PRJB = a.r4(10240, 2048)
    NWROW = a.r4(12288, 2048)
    EB = a.r4(14336, 16384)
    RTRB = a.r4(30720, 8)
    SH = [a.f4(30728, 512), a.f4(31240, 512)]
    ACC = a.v4(0, 4, 2048)
    RW = a.WB[:, 2048:2176].rearrange("p (a b) -> p a b", a=16)

    p.dma("gp", PRJB, w["proj_b"])
    p.dma("gp", RTRB, w["router_b"])
    p.dma("gp", NWROW, w["norm_w"])
    p.dma("gp", EB, w["expert_b"])
    cw = p.dma("gp", RW, rr(w["router_w"]))
    p.wait("te", cw)
    nwt = None
    for cc in range(4):
        p.wait("te", nwt)
        bt = p.op("te", lambda e, cc=cc: e.matmul(
            out=a.PS[4][:], lhsT=a.ONES[0:1, 0:128],
            rhs=NWROW[0:1, cc * 512:(cc + 1) * 512], start=True, stop=True),
            sig=True)
        p.wait("sc", bt)
        nwt = p.op("sc", lambda e, cc=cc: e.copy(
            out=NW[:, cc * 512:(cc + 1) * 512], in_=a.PS[4][:]), sig=True)
    p.wait("te", nwt)

    LG = a.v6b(4096, 4, 8)
    W8 = a.v6b(4224, 4, 8)
    TOPW = a.v6b(4352, 4, 16)
    Y = a.v6b(0, 2, 2048)
    XN = a.v6(4096, 2048)
    q2 = a.A7[:, 0:1]
    rq = a.A7[:, 1:2]

    prior = prior0
    for blk in range(NBLK):
        xt, fm = emit_ln_block(p, a, src_x, blk, lnt=lnt, prior=prior)
        ve_all = p.tok("vesig")
        sc_all = p.tok("scsig")
        p.wait("te", fm)
        p.wait("te", ve_all)
        p.wait("te", sc_all)
        rt = None
        for tt in range(4):
            p.op("te", lambda e, tt=tt: e.matmul(
                out=a.PS[4][:, tt * 8:(tt + 1) * 8], lhsT=a.ONES[0:1, 0:128],
                rhs=RTRB, start=True, stop=False))
            for kt in range(KT):
                rt = p.op("te", lambda e, tt=tt, kt=kt: e.matmul(
                    out=a.PS[4][:, tt * 8:(tt + 1) * 8],
                    lhsT=a.A3[:, kt, tt * 128:(tt + 1) * 128], rhs=RW[:, kt, :],
                    start=False, stop=(kt == KT - 1)), sig=(kt == KT - 1 and tt == 3))
        p.wait("ve", rt)
        tk = p.op("ve", lambda e: e.tensor_copy(out=LG.rearrange("p a b -> p (a b)"),
                                                in_=a.PS[4][:, 0:32]), sig=True)
        for tt in range(4):
            m1 = TOPW[:, tt, 0:1]
            s1 = TOPW[:, tt, 1:2]
            m2 = TOPW[:, tt, 2:3]
            s2 = TOPW[:, tt, 3:4]
            dd = TOPW[:, tt, 4:5]
            e2 = TOPW[:, tt, 5:6]
            w1 = TOPW[:, tt, 6:7]
            w2 = TOPW[:, tt, 7:8]
            EQ1 = TOPW[:, tt, 8:16]
            LM = W8[:, tt, :]
            def ch(fn, tk):
                p.wait("ve", tk)
                return p.op("ve", fn, sig=True)
            tk = ch(lambda e, tt=tt, m1=m1: e.reduce_max(
                out=m1, in_=LG[:, tt, :], axis=AX.X), tk)
            tk = ch(lambda e, tt=tt, m1=m1, EQ1=EQ1: e.tensor_scalar(
                out=EQ1, in0=LG[:, tt, :], scalar1=m1, scalar2=None,
                op0=ALU.is_equal), tk)
            tk = ch(lambda e, s1=s1, EQ1=EQ1: e.reduce_sum(
                out=s1, in_=EQ1, axis=AX.X), tk)
            tk = ch(lambda e, s1=s1: e.reciprocal(out=s1, in_=s1), tk)
            tk = ch(lambda e, tt=tt, EQ1=EQ1, LM=LM: e.scalar_tensor_tensor(
                out=LM, in0=EQ1, scalar=-1e9, in1=LG[:, tt, :],
                op0=ALU.mult, op1=ALU.add), tk)
            tk = ch(lambda e, m2=m2, LM=LM: e.reduce_max(
                out=m2, in_=LM, axis=AX.X), tk)
            tk = ch(lambda e, dd=dd, m2=m2, m1=m1: e.tensor_tensor(
                out=dd, in0=m2, in1=m1, op=ALU.subtract), tk)
            p.wait("sc", tk)
            etok = p.op("sc", lambda e, e2=e2, dd=dd: e.activation(
                out=e2, in_=dd, func=AF.Exp), sig=True)
            p.wait("ve", etok)
            tk = p.op("ve", lambda e, w1=w1, e2=e2: e.tensor_scalar(
                out=w1, in0=e2, scalar1=1.0, scalar2=None, op0=ALU.add), sig=True)
            tk = ch(lambda e, w1=w1: e.reciprocal(out=w1, in_=w1), tk)
            tk = ch(lambda e, w2=w2, e2=e2, w1=w1: e.tensor_tensor(
                out=w2, in0=e2, in1=w1, op=ALU.mult), tk)
            tk = ch(lambda e, m2=m2, LM=LM: e.tensor_scalar(
                out=LM, in0=LM, scalar1=m2, scalar2=None, op0=ALU.is_equal), tk)
            tk = ch(lambda e, s2=s2, LM=LM: e.reduce_sum(
                out=s2, in_=LM, axis=AX.X), tk)
            tk = ch(lambda e, s2=s2: e.reciprocal(out=s2, in_=s2), tk)
            tk = ch(lambda e, w2=w2, s2=s2: e.tensor_tensor(
                out=w2, in0=w2, in1=s2, op=ALU.mult), tk)
            tk = ch(lambda e, w1=w1, s1=s1: e.tensor_tensor(
                out=w1, in0=w1, in1=s1, op=ALU.mult), tk)
            tk = ch(lambda e, tt=tt, EQ1=EQ1, w1=w1: e.tensor_scalar(
                out=EQ1, in0=EQ1, scalar1=w1, scalar2=None, op0=ALU.mult), tk)
            tk = ch(lambda e, tt=tt, LM=LM, w2=w2, EQ1=EQ1: e.scalar_tensor_tensor(
                out=W8[:, tt, :], in0=LM, scalar=w2, in1=EQ1,
                op0=ALU.mult, op1=ALU.add), tk)
        w8_done = p.tok("vesig")

        rot = Rot(p, [0, 1, 2, 3])
        shp = [0]

        def load_ex(i, par):
            e_, fc = i // 4, i % 4
            return p.dma("gp", a.A5[:, par],
                         rr(w["expert_w"][e_][:, fc * 512:(fc + 1) * 512]),
                         sem=f"exl{par}")

        def body_ex(i, par, ltok):
            e_, fc = i // 4, i % 4
            p.wait("te", ltok)
            if i == 0:
                p.wait("te", w8_done)
            last = None
            for tt in range(4):
                j, bank, ftok = rot.get()
                p.wait("te", ftok)
                p.op("te", lambda e, e_=e_, fc=fc, tt=tt, bank=bank: e.matmul(
                    out=a.PS[bank][:], lhsT=a.ONES[0:1, 0:128],
                    rhs=EB[0:1, e_ * 2048 + fc * 512: e_ * 2048 + (fc + 1) * 512],
                    start=True, stop=False))
                for kt in range(KT):
                    last = p.op("te", lambda e, kt=kt, tt=tt, par=par, bank=bank: e.matmul(
                        out=a.PS[bank][:], lhsT=a.A3[:, kt, tt * 128:(tt + 1) * 128],
                        rhs=a.A5[:, par, kt, :], start=False,
                        stop=(kt == KT - 1)), sig=(kt == KT - 1))
                p.wait("sc", last)
                sp = shp[0] % 2
                shp[0] += 1
                p.wait("sc", p.tok("vesig") if shp[0] > 2 else None)
                sg = p.op("sc", lambda e, bank=bank, sp=sp: e.activation(
                    out=SH[sp], in_=a.PS[bank][:], func=AF.Silu), sig=True)
                p.wait("ve", sg)
                if e_ == 0:
                    ev = p.op("ve", lambda e, tt=tt, fc=fc, sp=sp: e.tensor_scalar(
                        out=ACC[:, tt, fc * 512:(fc + 1) * 512], in0=SH[sp],
                        scalar1=W8[:, tt, 0:1], scalar2=None, op0=ALU.mult),
                        sig=True)
                else:
                    ev = p.op("ve", lambda e, tt=tt, fc=fc, sp=sp, e_=e_: e.scalar_tensor_tensor(
                        out=ACC[:, tt, fc * 512:(fc + 1) * 512], in0=SH[sp],
                        scalar=W8[:, tt, e_:e_ + 1],
                        in1=ACC[:, tt, fc * 512:(fc + 1) * 512],
                        op0=ALU.mult, op1=ALU.add), sig=True)
                rot.done(j, ev)
            return last

        stream_slabs(p, 32, load_ex, body_ex, q="gp")
        acc_done = p.tok("vesig")

        p.wait("sy", acc_done)
        accst = p.dma("sy", rr(acc_scr[blk * TB:(blk + 1) * TB, :]), ACC[:],
                      sem="lns")
        p.wait("sy", accst)
        p.wait("sy", p.tok("tesig"))
        fmt = None
        for kt in range(KT):
            fmt = p.dma("sy", a.A3[:, kt, :],
                        acc_scr[blk * TB:(blk + 1) * TB, kt * 128:(kt + 1) * 128],
                        transpose=True, sem="fmt")
        last_store = None
        for tth in range(2):
            rot2 = Rot(p, [0, 1, 2, 3])
            yev = None

            def load_pj(cc, par, tth=tth):
                return p.dma("gp", a.A5[:, par],
                             rr(w["proj_w"][:, cc * 512:(cc + 1) * 512]),
                             sem=f"exl{par}")

            def body_pj(cc, par, ltok, tth=tth):
                nonlocal yev
                p.wait("te", ltok)
                if cc == 0:
                    p.wait("te", fmt)
                    p.wait("te", p.tok("vesig"))
                last = None
                for tt2 in range(2):
                    tt = tth * 2 + tt2
                    j, bank, ftok = rot2.get()
                    p.wait("te", ftok)
                    p.op("te", lambda e, cc=cc, bank=bank: e.matmul(
                        out=a.PS[bank][:], lhsT=a.ONES[0:1, 0:128],
                        rhs=PRJB[0:1, cc * 512:(cc + 1) * 512],
                        start=True, stop=False))
                    for ft in range(KT):
                        last = p.op("te", lambda e, ft=ft, tt=tt, par=par, bank=bank: e.matmul(
                            out=a.PS[bank][:],
                            lhsT=a.A3[:, ft, tt * 128:(tt + 1) * 128],
                            rhs=a.A5[:, par, ft, :], start=False,
                            stop=(ft == KT - 1)), sig=(ft == KT - 1))
                    p.wait("ve", last)
                    p.wait("ve", prior)
                    yev = p.op("ve", lambda e, tt2=tt2, tt=tt, cc=cc, bank=bank: e.tensor_tensor(
                        out=Y[:, tt2, cc * 512:(cc + 1) * 512], in0=a.PS[bank][:],
                        in1=a.A2[:, tt, cc * 512:(cc + 1) * 512], op=ALU.add),
                        sig=True)
                    rot2.done(j, yev)
                return last

            stream_slabs(p, 4, load_pj, body_pj, q="gp")
            for tt2 in range(2):
                tt = tth * 2 + tt2
                Yt = Y[:, tt2, :]
                p.wait("ve", last_store)
                p.wait("ve", prior)
                p.wait("ve", yev)
                tk2 = p.op("ve", lambda e, Yt=Yt: e.scalar_tensor_tensor(
                    out=XN, in0=Yt, scalar=1.0, in1=Yt,
                    op0=ALU.mult, op1=ALU.mult, accum_out=q2), sig=True)
                p.wait("ve", tk2)
                tk2 = p.op("ve", lambda e: e.tensor_scalar(
                    out=rq, in0=q2, scalar1=1.0 / D, scalar2=EPS_MOE,
                    op0=ALU.mult, op1=ALU.add), sig=True)
                p.wait("sc", tk2)
                sq = p.op("sc", lambda e: e.activation(
                    out=rq, in_=rq, func=AF.Sqrt), sig=True)
                p.wait("ve", sq)
                tk2 = p.op("ve", lambda e: e.reciprocal(out=rq, in_=rq), sig=True)
                p.wait("ve", tk2)
                tk2 = p.op("ve", lambda e: e.tensor_scalar(
                    out=rq, in0=rq, scalar1=0.5, scalar2=None, op0=ALU.mult),
                    sig=True)
                p.wait("ve", tk2)
                tk2 = p.op("ve", lambda e, Yt=Yt: e.scalar_tensor_tensor(
                    out=XN, in0=Yt, scalar=rq, in1=NW,
                    op0=ALU.mult, op1=ALU.mult), sig=True)
                p.wait("ve", tk2)
                xntok = p.op("ve", lambda e, tt=tt: e.tensor_tensor(
                    out=XN, in0=XN, in1=a.A1[:, tt, :], op=ALU.add), sig=True)
                p.wait("gp", xntok)
                last_store = p.dma(
                    "gp", dst_x[blk * TB + tt * 128: blk * TB + (tt + 1) * 128, :],
                    XN, sem="xst")
        prior = last_store
    return prior



def emit_l1attn(p, a, src_x, dst_x, agin_ln, agout_ln, proj_part, rs_out, w, prior0=None):
    KTf = a.f4(0, 8192)
    KTM = a.v4(8192, 64, 128)
    QH = a.v4(16384, 4, 512)
    E2 = [a.f4(18432, 512), a.f4(18944, 512)]
    RBF = a.f4(19456, 512)
    YS4 = a.v4(19968, 4, 512)
    RB16 = a.r4(22016, 512)
    WPJ = a.A4[:, 23040:23040 + 8192].rearrange(
        "p (a c d) -> p a c d", a=4, c=4)
    RrowF = a.A6[0:1, 2560:3072]
    WL = a.WB[:, 0:2048].rearrange("p (a b) -> p a b", a=16)

    prior = prior0
    for blk in range(NBLK):
        xt, lntok = emit_ln_block(p, a, src_x, blk, ln_rows=(agin_ln, 0),
                                  prior=prior)
        prior = lntok
    p.wait("gp", prior)
    cct = p.cc("AllGather", [agin_ln[:]], [agout_ln[:]], "ccln")

    p.wait("gp", p.tok("tesig"))
    p.wait("gp", p.tok("vesig"))
    p.wait("gp", p.tok("scsig"))
    cw1 = p.dma("gp", WL, rr(w["W_lat"]))
    cw2 = p.dma("gp", a.A5[:, 0], rr(w["W_d_slice"]))
    cw3 = p.dma("gp", WPJ,
                rr(w["W_proj_slice"]).rearrange("p a (c d) -> p a c d", d=512))
    p.wait("sy", cct)

    STG = a.v6b(0, 4, 512)
    prev_q_evac = None
    prev_e2g = [None, None]
    prev_ab = None
    prev_ys = None
    prev_rc = None
    prev_head_exps = [None, None]
    prev_stg_store = None
    ktm_ev = [None, None]
    for sc in range(16):
        p.wait("sy", p.tok("tesig"))
        fmt = None
        for kt in range(KT):
            fmt = p.dma("sy", a.A3[:, kt, :],
                        agout_ln[sc * 512:(sc + 1) * 512,
                                 kt * 128:(kt + 1) * 128],
                        transpose=True, sem="fmt")
        p.wait("te", fmt)
        p.wait("te", prev_q_evac)
        p.wait("te", cw1)
        if sc == 0:
            p.wait("te", cw2)
            p.wait("te", cw3)
        km = None
        for kt in range(KT):
            km = p.op("te", lambda e, kt=kt, sc=sc: e.matmul(
                out=a.PS[0][:], lhsT=WL[:, kt, :], rhs=a.A3[:, kt, :],
                start=(kt == 0), stop=(kt == KT - 1)), sig=(kt == KT - 1))
        p.wait("sc", km)
        kcopy = p.op("sc", lambda e, sc=sc: e.activation(
            out=KTf[:, sc * 512:(sc + 1) * 512], in_=a.PS[0][:], func=AF.Identity,
            bias=a.BL[:, 0:1]), sig=True)
        par = sc % 2
        p.wait("te", kcopy)
        p.wait("te", ktm_ev[par])
        tpt = None
        for ss in range(4):
            tpt = p.op("te", lambda e, sc=sc, ss=ss, par=par: e.transpose(
                out=a.PB[par][:, ss * 128:(ss + 1) * 128],
                in_=KTf[:, sc * 512 + ss * 128: sc * 512 + (ss + 1) * 128],
                identity=a.ID[:]), sig=(ss == 3))
        p.wait("ve", tpt)
        ktm_ev[par] = p.op("ve", lambda e, sc=sc, par=par: e.tensor_copy(
            out=KTM[:, 4 * sc:4 * sc + 4, :].rearrange("p a b -> p (a b)"),
            in_=a.PB[par][:, 0:512]), sig=True)
        qe = None
        for h in range(4):
            p.wait("te", prev_q_evac)
            qm = None
            for kt in range(KT):
                qm = p.op("te", lambda e, kt=kt, h=h: e.matmul(
                    out=a.PS[1][:], lhsT=a.A5[:, 0, kt, h * 128:(h + 1) * 128],
                    rhs=a.A3[:, kt, :], start=(kt == 0), stop=(kt == KT - 1)),
                    sig=(kt == KT - 1))
            p.wait("sc", qm)
            qe = p.op("sc", lambda e, h=h: e.activation(
                out=QH[:, h, :], in_=a.PS[1][:], func=AF.Identity,
                bias=a.BD[:, h:h + 1]), sig=True)
            prev_q_evac = qe
        nsb = 4 * (sc + 1)
        p.wait("te", ktm_ev[par])
        ACCE = a.v6(3072, 512)
        for h in range(4):
            if h == 0:
                p.wait("te", qe)
            eready = {}
            ex_tok = {}
            av_tok = {}
            acc_tok = {}
            st_tok = {}
            for step in range(nsb + 1):
                if step < nsb:
                    sb = step
                    bpar = sb % 2
                    bank = a.PS[2 + bpar]
                    p.wait("te", ex_tok.get(sb - 2))
                    if sb < 2:
                        p.wait("te", prev_head_exps[sb])
                    st_tok[sb] = p.op("te", lambda e, sb=sb, h=h, bank=bank: e.matmul(
                        out=bank[:], lhsT=KTf[:, sb * 128:(sb + 1) * 128],
                        rhs=QH[:, h, :], start=True, stop=True), sig=True)
                    p.wait("sc", st_tok[sb])
                    p.wait("sc", av_tok.get(sb - 2))
                    p.wait("sc", acc_tok.get(sb - 2))
                    if sb < 2:
                        p.wait("sc", prev_e2g[sb])
                    ex = p.op("sc", lambda e, bpar=bpar, bank=bank: e.activation(
                        out=E2[bpar], in_=bank[:], func=AF.Exp, scale=ISQL),
                        sig=True)
                    ex_tok[sb] = ex
                    v = sb - 4 * sc
                    if v >= 0:
                        p.wait("ve", ex)
                        mk = p.op("ve", lambda e, bpar=bpar, v=v: e.tensor_tensor(
                            out=E2[bpar], in0=E2[bpar],
                            in1=a.MSK[:, 512 - v * 128:1024 - v * 128],
                            op=ALU.mult), sig=True)
                        eready[sb] = mk
                    else:
                        eready[sb] = ex
                    p.wait("ve", eready[sb])
                    p.wait("ve", acc_tok.get(sb - 1))
                    if sb == 0:
                        p.wait("ve", prev_ab)
                        acc_tok[sb] = p.op("ve", lambda e, bpar=bpar: e.tensor_scalar(
                            out=ACCE, in0=E2[bpar], scalar1=1.0, scalar2=None,
                            op0=ALU.mult), sig=True)
                    else:
                        acc_tok[sb] = p.op("ve", lambda e, bpar=bpar: e.tensor_tensor(
                            out=ACCE, in0=ACCE, in1=E2[bpar], op=ALU.add),
                            sig=True)
                if step >= 1:
                    sb2 = step - 1
                    bpar2 = sb2 % 2
                    p.wait("te", eready[sb2])
                    if sb2 == 0:
                        p.wait("te", prev_ys)
                    av_tok[sb2] = p.op("te", lambda e, sb2=sb2, bpar2=bpar2, nsb=nsb: e.matmul(
                        out=a.PS[4][:], lhsT=KTM[:, sb2, :], rhs=E2[bpar2],
                        start=(sb2 == 0), stop=(sb2 == nsb - 1)), sig=True)
            prev_e2g = [av_tok[nsb - 2], av_tok[nsb - 1]]
            p.wait("ve", acc_tok[nsb - 1])
            ab = p.op("ve", lambda e: e.tensor_copy(out=RB16_acc(a), in_=ACCE),
                      sig=True)
            prev_ab = ab
            p.wait("te", ab)
            p.wait("te", prev_rc)
            dnm = p.op("te", lambda e: e.matmul(
                out=a.PS[5][0:1, :], lhsT=a.ONES[:, 0:1], rhs=RB16_acc(a),
                start=True, stop=True), sig=True)
            prev_head_exps = [eready.get(nsb - 2), eready.get(nsb - 1)]
            prev_head_exps = [eready.get(nsb - 2), eready.get(nsb - 1)]
            p.wait("ve", dnm)
            t6 = p.op("ve", lambda e: e.reciprocal(
                out=RrowF, in_=a.PS[5][0:1, :]), sig=True)
            prev_rc = t6
            p.wait("ve", t6)
            rv = p.op("ve", lambda e: e.tensor_copy(out=RB16, in_=RrowF),
                      sig=True)
            p.wait("te", rv)
            p.wait("te", prev_q_evac)
            bc = p.op("te", lambda e: e.matmul(
                out=a.PS[1][:], lhsT=a.ONES[0:1, :], rhs=RB16,
                start=True, stop=True), sig=True)
            p.wait("sc", bc)
            rbf = p.op("sc", lambda e: e.copy(out=RBF, in_=a.PS[1][:]), sig=True)
            prev_q_evac = rbf
            p.wait("ve", rbf)
            p.wait("ve", av_tok[nsb - 1])
            ys = p.op("ve", lambda e, h=h: e.tensor_tensor(
                out=YS4[:, h, :], in0=a.PS[4][:], in1=RBF, op=ALU.mult),
                sig=True)
            prev_ys = ys
        ys_done = prev_ys
        p.wait("te", ys_done)
        for cc_ in range(4):
            p.wait("ve", prev_stg_store)
            last = None
            evs = None
            for tt in range(4):
                bank = cc_ % 2
                p.wait("te", evs)
                for h in range(4):
                    last = p.op("te", lambda e, h=h, tt=tt, cc_=cc_, bank=bank: e.matmul(
                        out=a.PS[bank][:],
                        lhsT=YS4[:, h, tt * 128:(tt + 1) * 128],
                        rhs=WPJ[:, h, cc_, :],
                        start=(h == 0), stop=(h == 3)), sig=(h == 3))
                p.wait("ve", last)
                evs = p.op("ve", lambda e, tt=tt, bank=bank: e.tensor_copy(
                    out=STG[:, tt, :], in_=a.PS[bank][:]), sig=True)
            p.wait("gp", evs)
            prev_stg_store = p.dma(
                "gp", rr(proj_part[sc * 512:(sc + 1) * 512,
                                   cc_ * 512:(cc_ + 1) * 512]), STG[:],
                sem="ppst")
        prev_q_evac = p.tok("vesig")

    p.wait("gp", prev_stg_store)
    ccr = p.cc("ReduceScatter", [proj_part[:]], [rs_out[:]], "ccrs", op=ALU.add)
    p.wait("gp", ccr)

    bp2 = p.dma("gp", a.r4(0, 2048), w["b_proj"], sem="bpl")
    p.wait("te", bp2)
    p.wait("te", p.tok("vesig"))
    p.wait("te", p.tok("scsig"))
    bft = None
    for cc_ in range(4):
        p.wait("te", bft)
        bt = p.op("te", lambda e, cc_=cc_: e.matmul(
            out=a.PS[1][:], lhsT=a.ONES[0:1, :],
            rhs=a.r4(cc_ * 512, 512), start=True, stop=True), sig=True)
        p.wait("sc", bt)
        bft = p.op("sc", lambda e, cc_=cc_: e.copy(
            out=a.A2[:, 0, cc_ * 512:(cc_ + 1) * 512], in_=a.PS[1][:]),
            sig=True)

    RS6 = a.v6(0, 2048)
    XN6 = a.v6(2048, 2048)
    prior = None
    for blk in range(NBLK):
        p.wait("gp", prior)
        xt = p.dma("gp", a.A1[:], rr(src_x[blk * TB:(blk + 1) * TB, :]))
        for tt in range(4):
            r0 = blk * TB + tt * 128
            lt = p.dma("gp", RS6, rs_out[r0:r0 + 128, :], sem="rsl")
            p.wait("ve", lt)
            p.wait("ve", xt)
            p.wait("ve", bft)
            p.wait("ve", prior)
            t7 = p.op("ve", lambda e: e.tensor_tensor(
                out=XN6, in0=RS6, in1=a.A2[:, 0, :], op=ALU.add), sig=True)
            p.wait("ve", t7)
            t8 = p.op("ve", lambda e, tt=tt: e.tensor_tensor(
                out=XN6, in0=XN6, in1=a.A1[:, tt, :], op=ALU.add), sig=True)
            p.wait("gp", t8)
            prior = p.dma("gp", dst_x[r0:r0 + 128, :], XN6, sem="xst")
    return prior


def RB16_acc(a):
    return a.f4(22528, 512)



WT_SPECS = {
    "gup": ("L0_ffn_w_gate_up", 16384, 2048),
    "wdn": ("L0_ffn_w_down", 2048, 8192),
    "win": ("L0_attn_W_in", 2048, 2048),
    "wgt": ("L0_attn_W_gate", 8192, 2048),
    "wout": ("L0_attn_W_out", 2048, 2048),
}

def wt_perm(name):
    if name == "gup":
        return [(i // 2) + 64 * (i % 2) for i in range(128)]
    if name == "wgt":
        perm = []
        for i in range(64):
            head, jj, ty = i // 16, (i // 4) % 4, i % 4
            perm.append(head * 16 + ty * 4 + jj)
        return perm
    return None


def build_program(secs, inputs_present):
    p = Prog()
    nc = p.nc
    stack = ExitStack()
    a = Arenas(nc, stack)

    def din(name, shape):
        return nc.dram_tensor(name, shape, F32, kind="ExternalInput").ap()

    xin = din("xin", [T, D])
    xout = nc.dram_tensor("xout", [T, D], F32, kind="ExternalOutput").ap()

    idt = nc.dram_tensor("c_ident", [128, 128], F32, kind="ExternalInput").ap()
    onest = nc.dram_tensor("c_ones", [128, 128], F32, kind="ExternalInput").ap()
    mskt = nc.dram_tensor("c_mask", [128, 1024], F32, kind="ExternalInput").ap()
    ld1 = p.dma("gp", a.ID[:], idt[:])
    ld2 = p.dma("gp", a.ONES[:], onest[:])
    ld3 = p.dma("gp", a.MSK[:], mskt[:])

    need = set()
    for s in secs:
        if s in (1, 3):
            need |= {"gup", "wdn"}
        if s == 2:
            need |= {"win", "wgt", "wout"}

    lnt = nc.dram_tensor("lnt", [T, D], BF16).ap()
    wt = {}
    for nm in sorted(need):
        src_name, O, K = WT_SPECS[nm]
        w_hbm = din(src_name, [O, K])
        dst = nc.dram_tensor(f"wt_{nm}", [K, O], BF16).ap()
        emit_wt_construct(p, a, w_hbm, None, dst, O, K, perm=wt_perm(nm))
        wt[nm] = dst
    for eng in ENGINES:
        p.wait(eng, p.tok("wts0"))
        p.wait(eng, p.tok("wts1"))

    n = len(secs)
    if n == 1:
        hops = [(xin, xout)]
    else:
        xs = [nc.dram_tensor(f"xs{i}", [T, D], F32).ap() for i in range(n - 1)]
        hops = []
        cur = xin
        for i in range(n - 1):
            hops.append((cur, xs[i]))
            cur = xs[i]
        hops.append((cur, xout))

    prior_tok = None
    for (s, (src, dst)) in zip(secs, hops):
        if s in (1, 3):
            prior_tok = emit_ffn0(p, a, src, dst, lnt, wt["gup"], wt["wdn"],
                                  prior0=prior_tok)
        elif s == 2:
            for nm2 in ("b_gate", "rms_w"):
                pass
            bgt = din("L0_attn_b_gate_r", [128, 64])
            rwt = din("L0_attn_rms_w_r", [128, 4])
            b1 = p.dma("gp", a.BG[:], bgt[:])
            b2 = p.dma("gp", a.RMSW[:], rwt[:])
            p.wait("sc", b2)
            cp_scr = nc.dram_tensor("cp_scr", [T, D], BF16).ap()
            cl_scr = nc.dram_tensor("cl_scr", [T, D], BF16).ap()
            o_scr = nc.dram_tensor("o_scr", [T, D], BF16).ap()
            agin = nc.dram_tensor("ag_scan_in", [128, 32], F32).ap()
            agout = nc.dram_tensor("ag_scan_out", [512, 32], F32).ap()
            scm = din("scanmask", [128, 3])
            b3 = p.dma("gp", a.A7[:, 112:115], scm[:])
            p.wait("ve", b3)
            prior_tok = emit_l0attn(p, a, src, dst, lnt, wt["win"], wt["wgt"],
                                    wt["wout"], cp_scr, cl_scr, o_scr, agin,
                                    agout, prior0=prior_tok)
        elif s in (4, 6):
            w = {
                "router_w": din("L1_ffn_router_w", [D, 8]),
                "router_b": din("L1_ffn_router_b_r", [1, 8]),
                "expert_w": din("L1_ffn_expert_w", [8, D, D]),
                "expert_b": din("L1_ffn_expert_b_r", [1, 16384]),
                "proj_w": din("L1_ffn_proj_w", [D, D]),
                "proj_b": din("L1_ffn_proj_b_r", [1, D]),
                "norm_w": din("L1_ffn_norm_w_r", [1, D]),
            } if s == 4 or 4 not in secs else build_program._moe_w
            if s == 4 or 4 not in secs:
                build_program._moe_w = w
            acc_scr = nc.dram_tensor(f"acc_scr{s}", [T, D], BF16).ap()
            prior_tok = emit_moe(p, a, src, dst, lnt, acc_scr, w, tag=f"m{s}",
                                 prior0=prior_tok)
        elif s == 5:
            w5 = {
                "W_lat": din("L1_attn_W_lat", [D, 128]),
                "W_d_slice": din("W_d_slice", [D, 512]),
                "W_proj_slice": din("W_proj_slice", [512, D]),
                "b_proj": din("L1_attn_b_proj_r", [1, D]),
            }
            blt = din("b_lat_r", [128, 1])
            bdt = din("b_d_slice_r", [128, 4])
            p.dma("gp", a.BL[:], blt[:])
            b4 = p.dma("gp", a.BD[:], bdt[:])
            p.wait("sc", b4)
            agin_ln = nc.dram_tensor("agin_ln", [T, D], BF16).ap()
            agout_ln = nc.dram_tensor("agout_ln", [SEQ, D], BF16).ap()
            proj_part = nc.dram_tensor("proj_part", [SEQ, D], F32).ap()
            rs_out = nc.dram_tensor("rs_out", [T, D], F32).ap()
            prior_tok = emit_l1attn(p, a, src, dst, agin_ln, agout_ln,
                                    proj_part, rs_out, w5, prior0=prior_tok)

    p.wait("gp", p.tok("xst"))
    p.wait("gp", p.tok("gpdma"))
    nc_final = p.finalize()
    stack.close()
    return nc_final



def _consts():
    ident = np.eye(128, dtype=np.float32)
    ones = np.ones((128, 128), dtype=np.float32)
    msk = np.zeros((128, 1024), dtype=np.float32)
    pp, qq = np.meshgrid(np.arange(128), np.arange(1024), indexing="ij")
    msk[pp <= qq - 512] = 1.0
    return ident, ones, msk


def _prep_core_inputs(inputs, core):
    b, me = core // 4, core % 4
    ident, ones, msk = _consts()
    d = {
        "c_ident": ident, "c_ones": ones, "c_mask": msk,
        "L0_ffn_w_gate_up": inputs["L0_ffn_w_gate_up"],
        "L0_ffn_w_down": inputs["L0_ffn_w_down"],
        "L0_attn_W_in": inputs["L0_attn_W_in"],
        "L0_attn_W_gate": inputs["L0_attn_W_gate"],
        "L0_attn_W_out": inputs["L0_attn_W_out"],
        "L0_attn_b_gate_r": np.ascontiguousarray(
            inputs["L0_attn_b_gate"].reshape(64, 128).T),
        "L0_attn_rms_w_r": np.ascontiguousarray(
            inputs["L0_attn_rms_w"].reshape(4, 128).T),
        "scanmask": np.broadcast_to(
            (np.arange(3) < me).astype(np.float32), (128, 3)).copy(),
        "L1_ffn_router_w": inputs["L1_ffn_router_w"],
        "L1_ffn_router_b_r": inputs["L1_ffn_router_b"].reshape(1, 8),
        "L1_ffn_expert_w": inputs["L1_ffn_expert_w"],
        "L1_ffn_expert_b_r": inputs["L1_ffn_expert_b"].reshape(1, 16384),
        "L1_ffn_proj_w": inputs["L1_ffn_proj_w"],
        "L1_ffn_proj_b_r": inputs["L1_ffn_proj_b"].reshape(1, 2048),
        "L1_ffn_norm_w_r": inputs["L1_ffn_norm_w"].reshape(1, 2048),
        "L1_attn_W_lat": inputs["L1_attn_W_lat"],
        "W_d_slice": np.ascontiguousarray(
            inputs["L1_attn_W_d"][:, me * 512:(me + 1) * 512]),
        "W_proj_slice": np.ascontiguousarray(
            inputs["L1_attn_W_proj"][me * 512:(me + 1) * 512, :]),
        "L1_attn_b_proj_r": inputs["L1_attn_b_proj"].reshape(1, 2048),
        "b_lat_r": inputs["L1_attn_b_lat"].reshape(128, 1),
        "b_d_slice_r": np.ascontiguousarray(
            inputs["L1_attn_b_d"][me * 512:(me + 1) * 512].reshape(4, 128).T),
    }
    return {k: np.ascontiguousarray(np.asarray(v, dtype=np.float32))
            for k, v in d.items()}


_IN_BY_SEC = {
    1: ["L0_ffn_w_gate_up", "L0_ffn_w_down"],
    2: ["L0_attn_W_in", "L0_attn_W_gate", "L0_attn_W_out",
        "L0_attn_b_gate_r", "L0_attn_rms_w_r", "scanmask"],
    3: ["L0_ffn_w_gate_up", "L0_ffn_w_down"],
    4: ["L1_ffn_router_w", "L1_ffn_router_b_r", "L1_ffn_expert_w",
        "L1_ffn_expert_b_r", "L1_ffn_proj_w", "L1_ffn_proj_b_r",
        "L1_ffn_norm_w_r"],
    5: ["L1_attn_W_lat", "W_d_slice", "W_proj_slice", "L1_attn_b_proj_r",
        "b_lat_r", "b_d_slice_r"],
    6: ["L1_ffn_router_w", "L1_ffn_router_b_r", "L1_ffn_expert_w",
        "L1_ffn_expert_b_r", "L1_ffn_proj_w", "L1_ffn_proj_b_r",
        "L1_ffn_norm_w_r"],
}


def run_chain(inputs, groups):
    x = np.asarray(inputs["L0_attn_x"], dtype=np.float32)
    shards = [np.ascontiguousarray(x[c // 4, (c % 4) * T:(c % 4 + 1) * T, :])
              for c in range(8)]
    per_core = [_prep_core_inputs(inputs, c) for c in range(8)]
    for gi, secs in enumerate(groups):
        if hasattr(build_program, "_moe_w"):
            del build_program._moe_w
        nc = build_program(secs, None)
        names = set()
        for s in secs:
            names.update(_IN_BY_SEC[s])
        names |= {"c_ident", "c_ones", "c_mask"}
        in_maps = []
        for c in range(8):
            m = {k: per_core[c][k] for k in names if k in per_core[c]}
            m["xin"] = shards[c]
            in_maps.append(m)
        res = run_bass_kernel_spmd(nc, in_maps, core_ids=list(range(8)))
        shards = [np.asarray(res.results[c]["xout"], dtype=np.float32)
                  for c in range(8)]
    out = np.empty((2, SEQ, D), dtype=np.float32)
    for c in range(8):
        out[c // 4, (c % 4) * T:(c % 4 + 1) * T, :] = shards[c]
    return out


def kernel(**inputs):
    if FUSED:
        return run_chain(inputs, [[1, 2, 3, 4, 5, 6]])
    return run_chain(inputs, [[1], [2], [3], [4], [5], [6]])


# version 04
# speedup vs baseline: 1.3352x; 1.0062x over previous
import numpy as np
import ml_dtypes
from contextlib import ExitStack
import concourse.bass as bass
import concourse.mybir as mybir
from concourse.bass_utils import run_bass_kernel_spmd

F32 = mybir.dt.float32
BF16 = mybir.dt.bfloat16
AF = mybir.ActivationFunctionType
ALU = mybir.AluOpType
AX = mybir.AxisListType

ENGINES = ("te", "ve", "sc", "gp", "sy")
RG = [[0, 1, 2, 3], [4, 5, 6, 7]]

D = 2048
T = 2048
TB = 512
NBLK = 4
KT = 16
SEQ = 8192
EPS_LN = 1e-5
EPS_S2 = 1e-6
EPS_MOE = 1e-6
ISQL = float(1.0 / np.sqrt(128.0))

FUSED = True



class Prog:
    def __init__(self):
        self.nc = bass.Bass("TRN2", target_bir_lowering=False,
                            disable_frame_to_traceback=True)
        self.ops = {e: [] for e in ENGINES}
        self.sems = {}
        self.sem_order = []
        self._h = {}

    def _bump(self, name, n):
        if name not in self.sems:
            self.sems[name] = 0
            self.sem_order.append(name)
        self.sems[name] += n
        return (name, self.sems[name])

    def tok(self, name):
        return (name, self.sems.get(name, 0))

    def wait(self, eng, tok):
        if tok is None:
            return
        name, val = tok
        if val <= 0:
            return
        assert val <= self.sems.get(name, 0), f"wait beyond incs: {name} {val}"
        self.ops[eng].append(lambda e, n=name, v=val: e.wait_ge(self._h[n], v))

    def dma(self, q, out, in_, transpose=False, sem=None):
        s = sem or (q + "dma")
        tok = self._bump(s, 16)
        def fn(e, out=out, in_=in_, t=transpose, n=s):
            e.dma_start(out=out, in_=in_, transpose=t).then_inc(self._h[n], 16)
        self.ops[q].append(fn)
        return tok

    def op(self, eng, fn, sig=False):
        if not sig:
            self.ops[eng].append(fn)
            return None
        s = eng + "sig"
        tok = self._bump(s, 1)
        def wrapped(e, n=s):
            fn(e).then_inc(self._h[n], 1)
        self.ops[eng].append(wrapped)
        return tok

    def cc(self, kind, ins, outs, name, op=None):
        tok = self._bump(name, 1)
        aop = op if op is not None else ALU.bypass
        def fn(e, n=name):
            e.collective_compute(kind, aop, replica_groups=RG,
                                 ins=ins, outs=outs).then_inc(self._h[n])
        self.ops["gp"].append(fn)
        return tok

    def finalize(self):
        nc = self.nc
        stack = ExitStack()
        for n in self.sem_order:
            self._h[n] = stack.enter_context(nc.semaphore(n))
        block = stack.enter_context(nc.Block())
        decos = {"te": block.tensor, "ve": block.vector, "sc": block.scalar,
                 "gp": block.gpsimd, "sy": block.sync}
        for name in ENGINES:
            def mk(oplist):
                def body(e):
                    for fn in oplist:
                        fn(e)
                return body
            decos[name](mk(self.ops[name]))
        stack.close()
        return nc


class Rot:
    def __init__(self, p, banks):
        self.p, self.banks = p, banks
        self.free = [None] * len(banks)
        self.i = 0
    def get(self):
        j = self.i % len(self.banks)
        self.i += 1
        return j, self.banks[j], self.free[j]
    def done(self, j, tok):
        self.free[j] = tok


def stream_slabs(p, n, make_load, body, q="sy"):
    loads = [None] * n
    free = {}
    def issue(i):
        if i - 2 >= 0:
            p.wait(q, free[i - 2])
        loads[i] = make_load(i, i % 2)
    p.wait(q, p.tok("tesig"))
    if n > 0:
        issue(0)
    if n > 1:
        issue(1)
    for i in range(n):
        free[i] = body(i, i % 2, loads[i])
        assert free[i] is not None
        if i + 2 < n:
            issue(i + 2)


class Arenas:
    def __init__(self, nc, stack):
        E = stack.enter_context
        self.A1 = E(nc.sbuf_tensor("A1", [128, NBLK, D], F32))
        self.A2 = E(nc.sbuf_tensor("A2", [128, NBLK, D], BF16))
        self.A3 = E(nc.sbuf_tensor("A3", [128, KT, TB], BF16))
        self.A4 = E(nc.sbuf_tensor("A4", [128, 32768], BF16))
        self.A5 = E(nc.sbuf_tensor("A5", [128, 2, KT, TB], BF16))
        self.A6 = E(nc.sbuf_tensor("A6", [128, 6144], F32))
        self.A7 = E(nc.sbuf_tensor("A7", [128, 128], F32))
        self.BG = E(nc.sbuf_tensor("BG", [128, 64], F32))
        self.RMSW = E(nc.sbuf_tensor("RMSW", [128, 4], F32))
        self.BD = E(nc.sbuf_tensor("BD", [128, 4], F32))
        self.BL = E(nc.sbuf_tensor("BL", [128, 1], F32))
        self.ID = E(nc.sbuf_tensor("ID", [128, 128], BF16))
        self.ONES = E(nc.sbuf_tensor("ONES", [128, 128], BF16))
        self.MSK = E(nc.sbuf_tensor("MSK", [128, 1024], BF16))
        self.WB = E(nc.sbuf_tensor("WB", [128, 2304], BF16))
        self.PS = [E(nc.psum_tensor(f"PS{i}", [128, 512], F32)) for i in range(6)]
        self.PB = [E(nc.psum_tensor(f"PB{i}", [128, 1024], BF16)) for i in range(2)]

    def v4(self, off, a_, b_):
        return self.A4[:, off:off + a_ * b_].rearrange("p (a b) -> p a b", a=a_)

    def f4(self, off, n):
        return self.A4[:, off:off + n]

    def r4(self, off, n):
        return self.A4[0:1, off:off + n]

    def v6(self, off, n):
        return self.A6[:, off:off + n]

    def v6b(self, off, a_, b_):
        return self.A6[:, off:off + a_ * b_].rearrange("p (a b) -> p a b", a=a_)


def rr(ap, p=128):
    return ap.rearrange("(a p) c -> p a c", p=p)



def emit_wt_construct(p, a, w_hbm, wb_unused, dst, O, K, perm=None):
    nob, nkb = O // 128, K // 128
    perm = perm or list(range(nob))
    inv = [0] * nob
    for i, pv in enumerate(perm):
        inv[pv] = i
    nb = 4 if K == 2048 else 1
    p.wait("gp", p.tok("tesig"))
    p.wait("te", p.tok("wts0"))
    p.wait("te", p.tok("wts1"))
    p.wait("te", p.tok("vesig"))
    buf = [a.A5[:, 0].rearrange("p a b -> p (a b)"),
           a.A5[:, 1].rearrange("p a b -> p (a b)")]
    stage = [a.A4[:, 0:8192], a.A4[:, 8192:16384]]
    ngrp = (nkb + 7) // 8
    load_free = [None, None]
    stage_free = [None, None]
    bank_free = [None, None]
    for i0 in range(0, nob, nb):
        lp = (i0 // nb) % 2
        sp = lp
        p.wait("gp", load_free[lp])
        ld = p.dma("gp", buf[lp][:, 0:nb * K].rearrange("p (g c) -> p g c", g=nb),
                   w_hbm[i0 * 128:(i0 + nb) * 128, :].rearrange(
                       "(g p) c -> p g c", p=128),
                   sem=f"wcl{lp}")
        p.wait("te", ld)
        p.wait("te", stage_free[sp])
        ev = None
        for g in range(nb):
            for gr in range(ngrp):
                nk = min(8, nkb - gr * 8)
                bp = (g * ngrp + gr) % 2
                p.wait("te", bank_free[bp])
                tp = None
                for j in range(nk):
                    k = gr * 8 + j
                    tp = p.op("te", lambda e, k=k, j=j, g=g, lp=lp, bp=bp: e.transpose(
                        out=a.PB[bp][:, j * 128:(j + 1) * 128],
                        in_=buf[lp][:, g * K + k * 128: g * K + (k + 1) * 128],
                        identity=a.ID[:]), sig=(j == nk - 1))
                p.wait("ve", tp)
                ev = p.op("ve", lambda e, g=g, gr=gr, nk=nk, bp=bp, sp=sp: e.tensor_copy(
                    out=stage[sp][:, g * (nkb * 128) + gr * 1024:
                                  g * (nkb * 128) + gr * 1024 + nk * 128],
                    in_=a.PB[bp][:, 0:nk * 128]), sig=True)
                bank_free[bp] = ev
        load_free[lp] = p.tok("tesig")
        p.wait("sy", ev)
        if nb == 1:
            ob = inv[i0]
            stage_free[sp] = p.dma(
                "sy", rr(dst)[:, :, ob * 128:(ob + 1) * 128],
                stage[sp][:, 0:nkb * 128].rearrange("p (a c) -> p a c", c=128),
                sem=f"wts{sp}")
        else:
            b0 = inv[i0]
            s = inv[i0 + 1] - b0 if nb > 1 else 1
            for j in range(nb):
                assert inv[i0 + j] == b0 + j * s, (i0, j, inv[i0 + j], b0, s)
            stv = stage[sp][:, 0:nb * nkb * 128].rearrange(
                "p (g a c) -> p g a c", g=nb, c=128)
            for j in range(nb):
                ob = inv[i0 + j]
                stage_free[sp] = p.dma(
                    "sy", rr(dst)[:, :, ob * 128:(ob + 1) * 128],
                    stv[:, j], sem=f"wts{sp}")
    p.wait("sy", stage_free[0])
    p.wait("sy", stage_free[1])



def emit_ln_block(p, a, src_x, blk, lnt=None, to_fm=True, ln_rows=None,
                  prior=None):
    p.wait("sy", prior)
    xt = p.dma("sy", a.A1[:], rr(src_x[blk * TB:(blk + 1) * TB, :]), sem="xld")
    SB = a.v6b(5632, 16, 8)
    dummy = a.v6(0, 2048)
    p.wait("sc", prior)
    p.wait("ve", prior)
    atoks = []
    for tt in range(4):
        p.wait("ve", xt)
        p.wait("sc", xt)
        t0 = p.op("ve", lambda e, tt=tt: e.reduce_sum(
            out=SB[:, tt, 0:1], in_=a.A1[:, tt, :], axis=AX.X), sig=True)
        s_ssq = p.op("sc", lambda e, tt=tt: e.activation(
            out=dummy, in_=a.A1[:, tt, :], func=AF.Square,
            accum_out=SB[:, tt, 1:2]), sig=True)
        p.wait("ve", s_ssq)
        p.wait("ve", t0)
        t1 = p.op("ve", lambda e, tt=tt: e.tensor_scalar(
            out=SB[:, tt, 2:3], in0=SB[:, tt, 0:1], scalar1=1.0 / D,
            scalar2=None, op0=ALU.mult), sig=True)
        p.wait("ve", t1)
        t2 = p.op("ve", lambda e, tt=tt: e.tensor_tensor(
            out=SB[:, tt, 3:4], in0=SB[:, tt, 2:3], in1=SB[:, tt, 2:3],
            op=ALU.mult), sig=True)
        t3 = p.op("ve", lambda e, tt=tt: e.tensor_scalar(
            out=SB[:, tt, 4:5], in0=SB[:, tt, 1:2], scalar1=1.0 / D,
            scalar2=EPS_LN, op0=ALU.mult, op1=ALU.add), sig=True)
        p.wait("ve", t2)
        p.wait("ve", t3)
        vtok = p.op("ve", lambda e, tt=tt: e.tensor_tensor(
            out=SB[:, tt, 5:6], in0=SB[:, tt, 4:5], in1=SB[:, tt, 3:4],
            op=ALU.subtract), sig=True)
        p.wait("sc", vtok)
        stok = p.op("sc", lambda e, tt=tt: e.activation(
            out=SB[:, tt, 6:7], in_=SB[:, tt, 5:6], func=AF.Sqrt), sig=True)
        p.wait("ve", stok)
        t4 = p.op("ve", lambda e, tt=tt: e.reciprocal(
            out=SB[:, tt, 7:8], in_=SB[:, tt, 6:7]), sig=True)
        p.wait("ve", t4)
        atok = p.op("ve", lambda e, tt=tt: e.tensor_scalar(
            out=a.A2[:, tt, :], in0=a.A1[:, tt, :], scalar1=SB[:, tt, 2:3],
            scalar2=SB[:, tt, 7:8], op0=ALU.subtract, op1=ALU.mult), sig=True)
        atoks.append(atok)
    p.wait("sy", atoks[-1])
    if ln_rows is not None:
        dst, off = ln_rows
        lntok = p.dma("sy", rr(dst[off + blk * TB: off + (blk + 1) * TB, :]),
                      a.A2[:], sem="lns")
        return xt, lntok
    lntok = p.dma("sy", rr(lnt[blk * TB:(blk + 1) * TB, :]), a.A2[:], sem="lns")
    fm_tok = None
    if to_fm:
        p.wait("sy", lntok)
        p.wait("sy", p.tok("tesig"))
        for kt in range(KT):
            fm_tok = p.dma("sy", a.A3[:, kt, :],
                           lnt[blk * TB:(blk + 1) * TB, kt * 128:(kt + 1) * 128],
                           transpose=True, sem="fmt")
    return xt, fm_tok



def emit_ffn0(p, a, src_x, dst_x, lnt, gup_T, wdn_T, prior0=None):
    prior = prior0
    for blk in range(NBLK):
        xt, fm = emit_ln_block(p, a, src_x, blk, lnt=lnt, prior=prior)
        HM = a.v4(0, 64, 512)
        SG = a.v6(4096, 512)
        ve_all = p.tok("vesig")
        sc_all = p.tok("scsig")
        rot = Rot(p, [(0, 1), (2, 3)])
        hm_last = [None]

        def load_gu(i, par):
            return p.dma("sy", a.A5[:, par],
                         rr(gup_T)[:, :, i * 512:(i + 1) * 512],
                         sem=f"sl{par}")

        def body_gu(i, par, ltok):
            last_te = None
            for half in range(2):
                op = 2 * i + half
                j, pair, ftok = rot.get()
                pg, pu = a.PS[pair[0]], a.PS[pair[1]]
                p.wait("te", ftok)
                p.wait("te", ltok)
                if op == 0:
                    p.wait("te", fm)
                    p.wait("te", ve_all)
                    p.wait("te", sc_all)
                for kt in range(KT):
                    p.op("te", lambda e, kt=kt, par=par, half=half, pg=pg: e.matmul(
                        out=pg[:], lhsT=a.A5[:, par, kt, half * 256:half * 256 + 128],
                        rhs=a.A3[:, kt, :], start=(kt == 0), stop=(kt == KT - 1)))
                for kt in range(KT):
                    tok = p.op("te", lambda e, kt=kt, par=par, half=half, pu=pu: e.matmul(
                        out=pu[:], lhsT=a.A5[:, par, kt, half * 256 + 128:half * 256 + 256],
                        rhs=a.A3[:, kt, :], start=(kt == 0), stop=(kt == KT - 1)),
                        sig=(kt == KT - 1))
                    if kt == KT - 1:
                        last_te = tok
                p.wait("sc", last_te)
                sg = p.op("sc", lambda e, pg=pg: e.activation(
                    out=SG, in_=pg[:], func=AF.Silu), sig=True)
                p.wait("ve", sg)
                ev = p.op("ve", lambda e, op=op, pu=pu: e.tensor_tensor(
                    out=HM[:, op, :], in0=SG, in1=pu[:], op=ALU.mult), sig=True)
                rot.done(j, ev)
                hm_last[0] = ev
            return last_te

        stream_slabs(p, 32, load_gu, body_gu)
        hm_done = hm_last[0]

        STG = a.v6b(0, 4, 512)
        for fh in range(4):
            prev_ev = p.tok("vesig")

            def load_dn(g, par, fh=fh):
                return p.dma("sy", a.A5[:, par, 0:8, :],
                             rr(wdn_T)[:, g * 8:(g + 1) * 8,
                                       fh * 512:(fh + 1) * 512],
                             sem=f"sl{par}")

            def body_dn(g, par, ltok, fh=fh):
                p.wait("te", ltok)
                if g == 0:
                    p.wait("te", hm_done)
                    p.wait("te", prev_ev)
                tok = None
                for kk in range(8):
                    for tt in range(4):
                        sig = (kk == 7 and tt == 3)
                        tok2 = p.op("te", lambda e, g=g, kk=kk, tt=tt, par=par: e.matmul(
                            out=a.PS[tt][:],
                            lhsT=HM[:, g * 8 + kk, tt * 128:(tt + 1) * 128],
                            rhs=a.A5[:, par, kk, :],
                            start=(g == 0 and kk == 0),
                            stop=(g == 7 and kk == 7)), sig=sig)
                        if sig:
                            tok = tok2
                return tok

            mm_done = [None]

            def body_dn2(g, par, ltok, fh=fh):
                t = body_dn(g, par, ltok, fh)
                mm_done[0] = t
                return t

            stream_slabs(p, 8, load_dn, body_dn2)
            p.wait("ve", mm_done[0])
            p.wait("ve", prior)
            ev = None
            for tt in range(4):
                ev = p.op("ve", lambda e, tt=tt, fh=fh: e.scalar_tensor_tensor(
                    out=STG[:, tt, :], in0=a.PS[tt][:], scalar=0.5,
                    in1=a.A1[:, tt, fh * 512:(fh + 1) * 512],
                    op0=ALU.mult, op1=ALU.add), sig=(tt == 3))
            p.wait("gp", ev)
            prior = p.dma("gp",
                          rr(dst_x[blk * TB:(blk + 1) * TB,
                                   fh * 512:(fh + 1) * 512]), STG[:],
                          sem="xst")
    return prior



def emit_l0attn(p, a, src_x, dst_x, lnt, win_T, wgt_T, wout_T,
                cp_scr, cl_scr, o_scr, agin, agout, prior0=None):
    ZER = a.v6(3584, 512)
    p.op("ve", lambda e: e.memset(ZER, 0.0))
    CPC = a.A7[:, 64:80]
    CLC = a.A7[:, 80:96]
    CIN = a.A7[:, 96:112]
    SCM = a.A7[:, 112:115]
    p.op("ve", lambda e: e.memset(CPC, 1.0))
    p.op("ve", lambda e: e.memset(CLC, 0.0))

    XPF = a.v4(0, 16, 512)
    CPS = a.v4(8192, 16, 512)
    CLS = a.v4(16384, 16, 512)
    OS = a.v4(24576, 16, 512)
    T1, T2, T3, T4 = (a.v6(512, 512), a.v6(1024, 512), a.v6(1536, 512),
                      a.v6(2048, 512))

    prior = prior0
    for blk in range(NBLK):
        xt, fm = emit_ln_block(p, a, src_x, blk, lnt=lnt, prior=prior)
        ve_all = p.tok("vesig")
        sc_all = p.tok("scsig")
        rot = Rot(p, [0, 1, 2, 3])

        def load_xp(i, par):
            return p.dma("sy", a.A5[:, par],
                         rr(win_T)[:, :, i * 512:(i + 1) * 512], sem=f"sl{par}")

        def body_xp(i, par, ltok):
            last = None
            for q in range(4):
                ht = i * 4 + q
                j, bank, ftok = rot.get()
                p.wait("te", ftok)
                p.wait("te", ltok)
                if ht == 0:
                    p.wait("te", fm)
                    p.wait("te", ve_all)
                    p.wait("te", sc_all)
                for kt in range(KT):
                    last = p.op("te", lambda e, kt=kt, par=par, q=q, bank=bank: e.matmul(
                        out=a.PS[bank][:],
                        lhsT=a.A5[:, par, kt, q * 128:(q + 1) * 128],
                        rhs=a.A3[:, kt, :], start=(kt == 0), stop=(kt == KT - 1)),
                        sig=(kt == KT - 1))
                p.wait("sc", last)
                ev = p.op("sc", lambda e, ht=ht, bank=bank: e.copy(
                    out=XPF[:, ht, :], in_=a.PS[bank][:]), sig=True)
                rot.done(j, ev)
            return last

        stream_slabs(p, 4, load_xp, body_xp)
        xp_done = p.tok("scsig")

        def load_g(i, par):
            return p.dma("sy", a.A5[:, par],
                         rr(wgt_T)[:, :, i * 512:(i + 1) * 512], sem=f"sl{par}")

        def body_g(i, par, ltok):
            head, jj = i // 4, i % 4
            ct = i
            gt = head * 16
            p.wait("te", ltok)
            if i == 0:
                p.wait("te", xp_done)
            else:
                p.wait("te", p.tok("scsig"))
            tok = None
            for ty in range(4):
                for kt in range(KT):
                    sig = (ty == 3 and kt == KT - 1)
                    tok2 = p.op("te", lambda e, ty=ty, kt=kt, par=par: e.matmul(
                        out=a.PS[ty][:],
                        lhsT=a.A5[:, par, kt, ty * 128:(ty + 1) * 128],
                        rhs=XPF[:, kt, :], start=(kt == 0), stop=(kt == KT - 1)),
                        sig=sig)
                    if sig:
                        tok = tok2
            p.wait("sc", tok)
            c_i, c_f, c_o, c_c = gt + jj, gt + 4 + jj, gt + 8 + jj, gt + 12 + jj
            p.wait("sc", p.tok("vesig"))
            p.op("sc", lambda e, c=c_i: e.activation(
                out=T1, in_=a.PS[0][:], func=AF.Exp, bias=a.BG[:, c:c + 1]))
            p.op("sc", lambda e, c=c_c: e.activation(
                out=T2, in_=a.PS[3][:], func=AF.Tanh, bias=a.BG[:, c:c + 1]))
            p.op("sc", lambda e, c=c_f: e.activation(
                out=T4, in_=a.PS[1][:], func=AF.Sigmoid, bias=a.BG[:, c:c + 1]))
            sce = p.op("sc", lambda e, c=c_o, ct=ct: e.activation(
                out=OS[:, ct, :], in_=a.PS[2][:], func=AF.Sigmoid,
                bias=a.BG[:, c:c + 1]), sig=True)
            p.wait("ve", sce)
            v1 = p.op("ve", lambda e: e.tensor_tensor(out=T3, in0=T1, in1=T2,
                                                 op=ALU.mult), sig=True)
            v2 = p.op("ve", lambda e, ct=ct: e.tensor_tensor_scan(
                out=CPS[:, ct, :], data0=T4, data1=ZER,
                initial=CPC[:, ct:ct + 1], op0=ALU.mult, op1=ALU.add), sig=True)
            p.wait("ve", v2)
            v3 = p.op("ve", lambda e, ct=ct: e.tensor_copy(
                out=CPC[:, ct:ct + 1], in_=CPS[:, ct, 511:512]), sig=True)
            p.wait("ve", v1)
            v4 = p.op("ve", lambda e, ct=ct: e.tensor_tensor_scan(
                out=CLS[:, ct, :], data0=T4, data1=T3,
                initial=CLC[:, ct:ct + 1], op0=ALU.mult, op1=ALU.add), sig=True)
            p.wait("ve", v4)
            p.op("ve", lambda e, ct=ct: e.tensor_copy(
                out=CLC[:, ct:ct + 1], in_=CLS[:, ct, 511:512]), sig=True)
            return tok

        stream_slabs(p, 16, load_g, body_g)
        scans_done = p.tok("vesig")
        p.wait("gp", scans_done)
        p.dma("gp", rr(cp_scr)[:, :, blk * TB:(blk + 1) * TB], CPS[:])
        p.dma("gp", rr(cl_scr)[:, :, blk * TB:(blk + 1) * TB], CLS[:])
        prior = p.dma("gp", rr(o_scr)[:, :, blk * TB:(blk + 1) * TB], OS[:])

    agt = p.dma("gp", agin[:], a.A7[:, 64:96])
    p.wait("gp", agt)
    cct = p.cc("AllGather", [agin[:]], [agout[:]], "ccscan")
    p.wait("gp", cct)
    G = []
    gt_tok = None
    for r in range(4):
        Gr = a.A6[:, 4096 + r * 32: 4096 + (r + 1) * 32]
        G.append(Gr)
        gt_tok = p.dma("gp", Gr, agout[r * 128:(r + 1) * 128, :])
    p.wait("ve", gt_tok)
    TA = a.A6[:, 4224:4240]
    TBv = a.A6[:, 4240:4256]
    tk = p.op("ve", lambda e: e.memset(CIN, 0.0), sig=True)
    for r in range(3):
        p.wait("ve", tk)
        tk = p.op("ve", lambda e, r=r: e.tensor_scalar(
            out=TA, in0=G[r][:, 0:16], scalar1=1.0, scalar2=SCM[:, r:r + 1],
            op0=ALU.subtract, op1=ALU.mult), sig=True)
        p.wait("ve", tk)
        tk = p.op("ve", lambda e: e.tensor_scalar(
            out=TA, in0=TA, scalar1=1.0, scalar2=None, op0=ALU.add), sig=True)
        p.wait("ve", tk)
        tk = p.op("ve", lambda e: e.tensor_tensor(out=CIN, in0=CIN, in1=TA,
                                             op=ALU.mult), sig=True)
        p.wait("ve", tk)
        tk = p.op("ve", lambda e, r=r: e.tensor_scalar(
            out=TBv, in0=G[r][:, 16:32], scalar1=SCM[:, r:r + 1], scalar2=None,
            op0=ALU.mult), sig=True)
        p.wait("ve", tk)
        tk = p.op("ve", lambda e: e.tensor_tensor(out=CIN, in0=CIN, in1=TBv,
                                             op=ALU.add), sig=True)
    p.wait("ve", tk)
    cin_done = p.op("ve", lambda e: e.tensor_copy(out=TA, in_=CIN), sig=True)

    HFM = a.v4(0, 16, 512)
    Cr = a.v6b(0, 4, 512)
    CN = a.v6(2048, 512)
    RrowF = a.A6[0:1, 2560:3072]
    Rrow16 = a.r4(30720, 512)
    prior2 = p.tok("gpdma")
    prev_ps4 = None
    prev_ps5 = None
    for blk in range(NBLK):
        p.wait("gp", prior2)
        xt = p.dma("gp", a.A1[:], rr(src_x[blk * TB:(blk + 1) * TB, :]))
        p.dma("gp", CPS[:], rr(cp_scr)[:, :, blk * TB:(blk + 1) * TB])
        p.dma("gp", CLS[:], rr(cl_scr)[:, :, blk * TB:(blk + 1) * TB])
        l3 = p.dma("gp", OS[:], rr(o_scr)[:, :, blk * TB:(blk + 1) * TB])
        p.wait("ve", prior2)
        p.wait("ve", l3)
        p.wait("ve", cin_done)
        for head in range(4):
            ctok = None
            for jj in range(4):
                ct = head * 4 + jj
                ctok = p.op("ve", lambda e, ct=ct, jj=jj: e.scalar_tensor_tensor(
                    out=Cr[:, jj, :], in0=CPS[:, ct, :],
                    scalar=CIN[:, ct:ct + 1], in1=CLS[:, ct, :],
                    op0=ALU.mult, op1=ALU.add), sig=(jj == 3))
            p.wait("sc", ctok)
            sq_last = None
            for jj in range(4):
                ct = head * 4 + jj
                sq_last = p.op("sc", lambda e, ct=ct, jj=jj: e.activation(
                    out=HFM[:, ct, :], in_=Cr[:, jj, :], func=AF.Square),
                    sig=(jj == 3))
            p.wait("te", sq_last)
            p.wait("te", prev_ps4)
            mt = None
            for jj in range(4):
                ct = head * 4 + jj
                mt = p.op("te", lambda e, ct=ct, jj=jj: e.matmul(
                    out=a.PS[4][0:1, :], lhsT=a.ONES[:, 0:1],
                    rhs=HFM[:, ct, :], start=(jj == 0), stop=(jj == 3)),
                    sig=(jj == 3))
            p.wait("ve", mt)
            vt = p.op("ve", lambda e: e.tensor_scalar(
                out=RrowF, in0=a.PS[4][0:1, :], scalar1=1.0 / 512.0,
                scalar2=EPS_S2, op0=ALU.mult, op1=ALU.add), sig=True)
            prev_ps4 = vt
            p.wait("sc", vt)
            st2 = p.op("sc", lambda e: e.activation(
                out=RrowF, in_=RrowF, func=AF.Sqrt), sig=True)
            p.wait("ve", st2)
            t5 = p.op("ve", lambda e: e.reciprocal(out=RrowF, in_=RrowF),
                      sig=True)
            p.wait("ve", t5)
            rv = p.op("ve", lambda e: e.tensor_copy(out=Rrow16, in_=RrowF),
                      sig=True)
            p.wait("te", rv)
            p.wait("te", prev_ps5)
            bt = p.op("te", lambda e: e.matmul(
                out=a.PS[5][:], lhsT=a.ONES[0:1, :], rhs=Rrow16,
                start=True, stop=True), sig=True)
            p.wait("ve", bt)
            for jj in range(4):
                ct = head * 4 + jj
                cn = p.op("ve", lambda e, jj=jj: e.tensor_tensor(
                    out=CN, in0=Cr[:, jj, :], in1=a.PS[5][:], op=ALU.mult),
                    sig=True)
                if jj == 3:
                    prev_ps5 = cn
                p.wait("sc", cn)
                th = p.op("sc", lambda e, ct=ct, jj=jj: e.activation(
                    out=HFM[:, ct, :], in_=CN, func=AF.Tanh,
                    scale=a.RMSW[:, jj:jj + 1]), sig=True)
                p.wait("ve", th)
                p.op("ve", lambda e, ct=ct: e.tensor_tensor(
                    out=HFM[:, ct, :], in0=HFM[:, ct, :], in1=OS[:, ct, :],
                    op=ALU.mult), sig=True)
        h_done = p.tok("vesig")

        rot = Rot(p, [0, 1, 2, 3])
        STG = a.v6b(0, 4, 512)
        stores = [None]

        def load_wo(i, par):
            return p.dma("sy", a.A5[:, par],
                         rr(wout_T)[:, :, i * 512:(i + 1) * 512], sem=f"sl{par}")

        def body_wo(cc, par, ltok):
            p.wait("te", ltok)
            if cc == 0:
                p.wait("te", h_done)
            p.wait("ve", stores[0])
            last = None
            ev = None
            for tt in range(4):
                j, bank, ftok = rot.get()
                p.wait("te", ftok)
                for ctc in range(KT):
                    last = p.op("te", lambda e, ctc=ctc, tt=tt, par=par, bank=bank: e.matmul(
                        out=a.PS[bank][:],
                        lhsT=HFM[:, ctc, tt * 128:(tt + 1) * 128],
                        rhs=a.A5[:, par, ctc, :],
                        start=(ctc == 0), stop=(ctc == KT - 1)),
                        sig=(ctc == KT - 1))
                p.wait("ve", last)
                p.wait("ve", xt)
                ev = p.op("ve", lambda e, tt=tt, cc=cc, bank=bank: e.tensor_tensor(
                    out=STG[:, tt, :], in0=a.PS[bank][:],
                    in1=a.A1[:, tt, cc * 512:(cc + 1) * 512], op=ALU.add),
                    sig=True)
                rot.done(j, ev)
            p.wait("gp", ev)
            stores[0] = p.dma(
                "gp", rr(dst_x[blk * TB:(blk + 1) * TB,
                               cc * 512:(cc + 1) * 512]), STG[:], sem="xst")
            return last

        stream_slabs(p, 4, load_wo, body_wo)
        prior2 = stores[0]
    return prior2



def emit_moe(p, a, src_x, dst_x, lnt, acc_scr, w, tag, prior0=None):
    toks = getattr(p, "_a4_free", None) or (
        p.tok("tesig"), p.tok("vesig"), p.tok("scsig"))
    p._a4_free = None
    for tk_ in toks:
        p.wait("gp", tk_)
    NW = a.f4(8192, 2048)
    PRJB = a.r4(10240, 2048)
    NWROW = a.r4(12288, 2048)
    EB = a.r4(14336, 16384)
    RTRB = a.r4(30720, 8)
    SH = [a.f4(30728, 512), a.f4(31240, 512)]
    ACC = a.v4(0, 4, 2048)
    RW = a.WB[:, 2048:2176].rearrange("p (a b) -> p a b", a=16)

    p.dma("gp", PRJB, w["proj_b"])
    p.dma("gp", RTRB, w["router_b"])
    p.dma("gp", NWROW, w["norm_w"])
    p.dma("gp", EB, w["expert_b"])
    cw = p.dma("gp", RW, rr(w["router_w"]))
    p.wait("te", cw)
    nwt = None
    for cc in range(4):
        p.wait("te", nwt)
        bt = p.op("te", lambda e, cc=cc: e.matmul(
            out=a.PS[4][:], lhsT=a.ONES[0:1, 0:128],
            rhs=NWROW[0:1, cc * 512:(cc + 1) * 512], start=True, stop=True),
            sig=True)
        p.wait("sc", bt)
        nwt = p.op("sc", lambda e, cc=cc: e.copy(
            out=NW[:, cc * 512:(cc + 1) * 512], in_=a.PS[4][:]), sig=True)
    p.wait("te", nwt)

    LG = a.v6b(4096, 4, 8)
    W8 = a.v6b(4224, 4, 8)
    TOPW = a.v6b(4352, 4, 16)
    Y = a.v6b(0, 2, 2048)
    XN = a.v6(4096, 2048)
    q2 = a.A7[:, 0:1]
    rq = a.A7[:, 1:2]

    prior = prior0
    for blk in range(NBLK):
        xt, fm = emit_ln_block(p, a, src_x, blk, lnt=lnt, prior=prior)
        ve_all = p.tok("vesig")
        sc_all = p.tok("scsig")
        p.wait("te", fm)
        p.wait("te", ve_all)
        p.wait("te", sc_all)
        rt = None
        for tt in range(4):
            p.op("te", lambda e, tt=tt: e.matmul(
                out=a.PS[4][:, tt * 8:(tt + 1) * 8], lhsT=a.ONES[0:1, 0:128],
                rhs=RTRB, start=True, stop=False))
            for kt in range(KT):
                rt = p.op("te", lambda e, tt=tt, kt=kt: e.matmul(
                    out=a.PS[4][:, tt * 8:(tt + 1) * 8],
                    lhsT=a.A3[:, kt, tt * 128:(tt + 1) * 128], rhs=RW[:, kt, :],
                    start=False, stop=(kt == KT - 1)), sig=(kt == KT - 1 and tt == 3))
        p.wait("ve", rt)
        tk = p.op("ve", lambda e: e.tensor_copy(out=LG.rearrange("p a b -> p (a b)"),
                                                in_=a.PS[4][:, 0:32]), sig=True)
        for tt in range(4):
            m1 = TOPW[:, tt, 0:1]
            s1 = TOPW[:, tt, 1:2]
            m2 = TOPW[:, tt, 2:3]
            s2 = TOPW[:, tt, 3:4]
            dd = TOPW[:, tt, 4:5]
            e2 = TOPW[:, tt, 5:6]
            w1 = TOPW[:, tt, 6:7]
            w2 = TOPW[:, tt, 7:8]
            EQ1 = TOPW[:, tt, 8:16]
            LM = W8[:, tt, :]
            def ch(fn, tk):
                p.wait("ve", tk)
                return p.op("ve", fn, sig=True)
            tk = ch(lambda e, tt=tt, m1=m1: e.reduce_max(
                out=m1, in_=LG[:, tt, :], axis=AX.X), tk)
            tk = ch(lambda e, tt=tt, m1=m1, EQ1=EQ1: e.tensor_scalar(
                out=EQ1, in0=LG[:, tt, :], scalar1=m1, scalar2=None,
                op0=ALU.is_equal), tk)
            tk = ch(lambda e, s1=s1, EQ1=EQ1: e.reduce_sum(
                out=s1, in_=EQ1, axis=AX.X), tk)
            tk = ch(lambda e, s1=s1: e.reciprocal(out=s1, in_=s1), tk)
            tk = ch(lambda e, tt=tt, EQ1=EQ1, LM=LM: e.scalar_tensor_tensor(
                out=LM, in0=EQ1, scalar=-1e9, in1=LG[:, tt, :],
                op0=ALU.mult, op1=ALU.add), tk)
            tk = ch(lambda e, m2=m2, LM=LM: e.reduce_max(
                out=m2, in_=LM, axis=AX.X), tk)
            tk = ch(lambda e, dd=dd, m2=m2, m1=m1: e.tensor_tensor(
                out=dd, in0=m2, in1=m1, op=ALU.subtract), tk)
            p.wait("sc", tk)
            etok = p.op("sc", lambda e, e2=e2, dd=dd: e.activation(
                out=e2, in_=dd, func=AF.Exp), sig=True)
            p.wait("ve", etok)
            tk = p.op("ve", lambda e, w1=w1, e2=e2: e.tensor_scalar(
                out=w1, in0=e2, scalar1=1.0, scalar2=None, op0=ALU.add), sig=True)
            tk = ch(lambda e, w1=w1: e.reciprocal(out=w1, in_=w1), tk)
            tk = ch(lambda e, w2=w2, e2=e2, w1=w1: e.tensor_tensor(
                out=w2, in0=e2, in1=w1, op=ALU.mult), tk)
            tk = ch(lambda e, m2=m2, LM=LM: e.tensor_scalar(
                out=LM, in0=LM, scalar1=m2, scalar2=None, op0=ALU.is_equal), tk)
            tk = ch(lambda e, s2=s2, LM=LM: e.reduce_sum(
                out=s2, in_=LM, axis=AX.X), tk)
            tk = ch(lambda e, s2=s2: e.reciprocal(out=s2, in_=s2), tk)
            tk = ch(lambda e, w2=w2, s2=s2: e.tensor_tensor(
                out=w2, in0=w2, in1=s2, op=ALU.mult), tk)
            tk = ch(lambda e, w1=w1, s1=s1: e.tensor_tensor(
                out=w1, in0=w1, in1=s1, op=ALU.mult), tk)
            tk = ch(lambda e, tt=tt, EQ1=EQ1, w1=w1: e.tensor_scalar(
                out=EQ1, in0=EQ1, scalar1=w1, scalar2=None, op0=ALU.mult), tk)
            tk = ch(lambda e, tt=tt, LM=LM, w2=w2, EQ1=EQ1: e.scalar_tensor_tensor(
                out=W8[:, tt, :], in0=LM, scalar=w2, in1=EQ1,
                op0=ALU.mult, op1=ALU.add), tk)
        w8_done = p.tok("vesig")

        rot = Rot(p, [0, 1, 2, 3])
        shp = [0]

        def load_ex(i, par):
            e_, fc = i // 4, i % 4
            return p.dma("gp", a.A5[:, par],
                         rr(w["expert_w"][e_][:, fc * 512:(fc + 1) * 512]),
                         sem=f"exl{par}")

        def body_ex(i, par, ltok):
            e_, fc = i // 4, i % 4
            p.wait("te", ltok)
            if i == 0:
                p.wait("te", w8_done)
            last = None
            for tt in range(4):
                j, bank, ftok = rot.get()
                p.wait("te", ftok)
                p.op("te", lambda e, e_=e_, fc=fc, tt=tt, bank=bank: e.matmul(
                    out=a.PS[bank][:], lhsT=a.ONES[0:1, 0:128],
                    rhs=EB[0:1, e_ * 2048 + fc * 512: e_ * 2048 + (fc + 1) * 512],
                    start=True, stop=False))
                for kt in range(KT):
                    last = p.op("te", lambda e, kt=kt, tt=tt, par=par, bank=bank: e.matmul(
                        out=a.PS[bank][:], lhsT=a.A3[:, kt, tt * 128:(tt + 1) * 128],
                        rhs=a.A5[:, par, kt, :], start=False,
                        stop=(kt == KT - 1)), sig=(kt == KT - 1))
                p.wait("sc", last)
                sp = shp[0] % 2
                shp[0] += 1
                p.wait("sc", p.tok("vesig") if shp[0] > 2 else None)
                sg = p.op("sc", lambda e, bank=bank, sp=sp: e.activation(
                    out=SH[sp], in_=a.PS[bank][:], func=AF.Silu), sig=True)
                p.wait("ve", sg)
                if e_ == 0:
                    ev = p.op("ve", lambda e, tt=tt, fc=fc, sp=sp: e.tensor_scalar(
                        out=ACC[:, tt, fc * 512:(fc + 1) * 512], in0=SH[sp],
                        scalar1=W8[:, tt, 0:1], scalar2=None, op0=ALU.mult),
                        sig=True)
                else:
                    ev = p.op("ve", lambda e, tt=tt, fc=fc, sp=sp, e_=e_: e.scalar_tensor_tensor(
                        out=ACC[:, tt, fc * 512:(fc + 1) * 512], in0=SH[sp],
                        scalar=W8[:, tt, e_:e_ + 1],
                        in1=ACC[:, tt, fc * 512:(fc + 1) * 512],
                        op0=ALU.mult, op1=ALU.add), sig=True)
                rot.done(j, ev)
            return last

        stream_slabs(p, 32, load_ex, body_ex, q="gp")
        acc_done = p.tok("vesig")

        p.wait("sy", acc_done)
        accst = p.dma("sy", rr(acc_scr[blk * TB:(blk + 1) * TB, :]), ACC[:],
                      sem="lns")
        p.wait("sy", accst)
        p.wait("sy", p.tok("tesig"))
        fmt = None
        for kt in range(KT):
            fmt = p.dma("sy", a.A3[:, kt, :],
                        acc_scr[blk * TB:(blk + 1) * TB, kt * 128:(kt + 1) * 128],
                        transpose=True, sem="fmt")
        last_store = None
        for tth in range(2):
            rot2 = Rot(p, [0, 1, 2, 3])
            yev = None

            def load_pj(cc, par, tth=tth):
                return p.dma("gp", a.A5[:, par],
                             rr(w["proj_w"][:, cc * 512:(cc + 1) * 512]),
                             sem=f"exl{par}")

            def body_pj(cc, par, ltok, tth=tth):
                nonlocal yev
                p.wait("te", ltok)
                if cc == 0:
                    p.wait("te", fmt)
                    p.wait("te", p.tok("vesig"))
                last = None
                for tt2 in range(2):
                    tt = tth * 2 + tt2
                    j, bank, ftok = rot2.get()
                    p.wait("te", ftok)
                    p.op("te", lambda e, cc=cc, bank=bank: e.matmul(
                        out=a.PS[bank][:], lhsT=a.ONES[0:1, 0:128],
                        rhs=PRJB[0:1, cc * 512:(cc + 1) * 512],
                        start=True, stop=False))
                    for ft in range(KT):
                        last = p.op("te", lambda e, ft=ft, tt=tt, par=par, bank=bank: e.matmul(
                            out=a.PS[bank][:],
                            lhsT=a.A3[:, ft, tt * 128:(tt + 1) * 128],
                            rhs=a.A5[:, par, ft, :], start=False,
                            stop=(ft == KT - 1)), sig=(ft == KT - 1))
                    p.wait("ve", last)
                    p.wait("ve", prior)
                    yev = p.op("ve", lambda e, tt2=tt2, tt=tt, cc=cc, bank=bank: e.tensor_tensor(
                        out=Y[:, tt2, cc * 512:(cc + 1) * 512], in0=a.PS[bank][:],
                        in1=a.A2[:, tt, cc * 512:(cc + 1) * 512], op=ALU.add),
                        sig=True)
                    rot2.done(j, yev)
                return last

            stream_slabs(p, 4, load_pj, body_pj, q="gp")
            for tt2 in range(2):
                tt = tth * 2 + tt2
                Yt = Y[:, tt2, :]
                p.wait("ve", last_store)
                p.wait("ve", prior)
                p.wait("ve", yev)
                tk2 = p.op("ve", lambda e, Yt=Yt: e.scalar_tensor_tensor(
                    out=XN, in0=Yt, scalar=1.0, in1=Yt,
                    op0=ALU.mult, op1=ALU.mult, accum_out=q2), sig=True)
                p.wait("ve", tk2)
                tk2 = p.op("ve", lambda e: e.tensor_scalar(
                    out=rq, in0=q2, scalar1=1.0 / D, scalar2=EPS_MOE,
                    op0=ALU.mult, op1=ALU.add), sig=True)
                p.wait("sc", tk2)
                sq = p.op("sc", lambda e: e.activation(
                    out=rq, in_=rq, func=AF.Sqrt), sig=True)
                p.wait("ve", sq)
                tk2 = p.op("ve", lambda e: e.reciprocal(out=rq, in_=rq), sig=True)
                p.wait("ve", tk2)
                tk2 = p.op("ve", lambda e: e.tensor_scalar(
                    out=rq, in0=rq, scalar1=0.5, scalar2=None, op0=ALU.mult),
                    sig=True)
                p.wait("ve", tk2)
                tk2 = p.op("ve", lambda e, Yt=Yt: e.scalar_tensor_tensor(
                    out=XN, in0=Yt, scalar=rq, in1=NW,
                    op0=ALU.mult, op1=ALU.mult), sig=True)
                p.wait("ve", tk2)
                xntok = p.op("ve", lambda e, tt=tt: e.tensor_tensor(
                    out=XN, in0=XN, in1=a.A1[:, tt, :], op=ALU.add), sig=True)
                p.wait("gp", xntok)
                last_store = p.dma(
                    "gp", dst_x[blk * TB + tt * 128: blk * TB + (tt + 1) * 128, :],
                    XN, sem="xst")
        prior = last_store
    return prior



def emit_l1attn(p, a, src_x, dst_x, agin_ln, agout_ln, proj_part, rs_out, w, prior0=None):
    KTf = a.f4(0, 8192)
    KTM = a.v4(8192, 64, 128)
    QH = a.v4(16384, 4, 512)
    E2 = [a.f4(18432, 512), a.f4(18944, 512)]
    RBF = a.f4(19456, 512)
    YS4 = a.v4(19968, 4, 512)
    RB16 = a.r4(22016, 512)
    WPJ = a.A4[:, 23040:23040 + 8192].rearrange(
        "p (a c d) -> p a c d", a=4, c=4)
    RrowF = a.A6[0:1, 2560:3072]
    WL = a.WB[:, 0:2048].rearrange("p (a b) -> p a b", a=16)

    prior = prior0
    for blk in range(NBLK):
        xt, lntok = emit_ln_block(p, a, src_x, blk, ln_rows=(agin_ln, 0),
                                  prior=prior)
        prior = lntok
    p.wait("gp", prior)
    cct = p.cc("AllGather", [agin_ln[:]], [agout_ln[:]], "ccln")

    p.wait("gp", p.tok("tesig"))
    p.wait("gp", p.tok("vesig"))
    p.wait("gp", p.tok("scsig"))
    cw1 = p.dma("gp", WL, rr(w["W_lat"]))
    cw2 = p.dma("gp", a.A5[:, 0], rr(w["W_d_slice"]))
    cw3 = p.dma("gp", WPJ,
                rr(w["W_proj_slice"]).rearrange("p a (c d) -> p a c d", d=512))
    p.wait("sy", cct)

    STG = a.v6b(0, 4, 512)
    prev_q_evac = None
    prev_e2g = [None, None]
    prev_ab = None
    prev_ys = None
    prev_rc = None
    prev_head_exps = [None, None]
    prev_stg_store = None
    ktm_ev = [None, None]
    for sc in range(16):
        p.wait("sy", p.tok("tesig"))
        fmt = None
        for kt in range(KT):
            fmt = p.dma("sy", a.A3[:, kt, :],
                        agout_ln[sc * 512:(sc + 1) * 512,
                                 kt * 128:(kt + 1) * 128],
                        transpose=True, sem="fmt")
        p.wait("te", fmt)
        p.wait("te", prev_q_evac)
        p.wait("te", cw1)
        if sc == 0:
            p.wait("te", cw2)
            p.wait("te", cw3)
        km = None
        for kt in range(KT):
            km = p.op("te", lambda e, kt=kt, sc=sc: e.matmul(
                out=a.PS[0][:], lhsT=WL[:, kt, :], rhs=a.A3[:, kt, :],
                start=(kt == 0), stop=(kt == KT - 1)), sig=(kt == KT - 1))
        p.wait("sc", km)
        kcopy = p.op("sc", lambda e, sc=sc: e.activation(
            out=KTf[:, sc * 512:(sc + 1) * 512], in_=a.PS[0][:], func=AF.Identity,
            bias=a.BL[:, 0:1]), sig=True)
        par = sc % 2
        p.wait("te", kcopy)
        p.wait("te", ktm_ev[par])
        tpt = None
        for ss in range(4):
            tpt = p.op("te", lambda e, sc=sc, ss=ss, par=par: e.transpose(
                out=a.PB[par][:, ss * 128:(ss + 1) * 128],
                in_=KTf[:, sc * 512 + ss * 128: sc * 512 + (ss + 1) * 128],
                identity=a.ID[:]), sig=(ss == 3))
        p.wait("ve", tpt)
        ktm_ev[par] = p.op("ve", lambda e, sc=sc, par=par: e.tensor_copy(
            out=KTM[:, 4 * sc:4 * sc + 4, :].rearrange("p a b -> p (a b)"),
            in_=a.PB[par][:, 0:512]), sig=True)
        qe = None
        for h in range(4):
            p.wait("te", prev_q_evac)
            qm = None
            for kt in range(KT):
                qm = p.op("te", lambda e, kt=kt, h=h: e.matmul(
                    out=a.PS[1][:], lhsT=a.A5[:, 0, kt, h * 128:(h + 1) * 128],
                    rhs=a.A3[:, kt, :], start=(kt == 0), stop=(kt == KT - 1)),
                    sig=(kt == KT - 1))
            p.wait("sc", qm)
            qe = p.op("sc", lambda e, h=h: e.activation(
                out=QH[:, h, :], in_=a.PS[1][:], func=AF.Identity,
                bias=a.BD[:, h:h + 1]), sig=True)
            prev_q_evac = qe
        nsb = 4 * (sc + 1)
        p.wait("te", ktm_ev[par])
        ACCE = a.v6(3072, 512)
        for h in range(4):
            if h == 0:
                p.wait("te", qe)
            eready = {}
            ex_tok = {}
            av_tok = {}
            acc_tok = {}
            st_tok = {}
            for step in range(nsb + 1):
                if step < nsb:
                    sb = step
                    bpar = sb % 2
                    bank = a.PS[2 + bpar]
                    p.wait("te", ex_tok.get(sb - 2))
                    if sb < 2:
                        p.wait("te", prev_head_exps[sb])
                    st_tok[sb] = p.op("te", lambda e, sb=sb, h=h, bank=bank: e.matmul(
                        out=bank[:], lhsT=KTf[:, sb * 128:(sb + 1) * 128],
                        rhs=QH[:, h, :], start=True, stop=True), sig=True)
                    p.wait("sc", st_tok[sb])
                    p.wait("sc", av_tok.get(sb - 2))
                    p.wait("sc", acc_tok.get(sb - 2))
                    if sb < 2:
                        p.wait("sc", prev_e2g[sb])
                    ex = p.op("sc", lambda e, bpar=bpar, bank=bank: e.activation(
                        out=E2[bpar], in_=bank[:], func=AF.Exp, scale=ISQL),
                        sig=True)
                    ex_tok[sb] = ex
                    v = sb - 4 * sc
                    if v >= 0:
                        p.wait("ve", ex)
                        mk = p.op("ve", lambda e, bpar=bpar, v=v: e.tensor_tensor(
                            out=E2[bpar], in0=E2[bpar],
                            in1=a.MSK[:, 512 - v * 128:1024 - v * 128],
                            op=ALU.mult), sig=True)
                        eready[sb] = mk
                    else:
                        eready[sb] = ex
                    p.wait("ve", eready[sb])
                    p.wait("ve", acc_tok.get(sb - 1))
                    if sb == 0:
                        p.wait("ve", prev_ab)
                        acc_tok[sb] = p.op("ve", lambda e, bpar=bpar: e.tensor_scalar(
                            out=ACCE, in0=E2[bpar], scalar1=1.0, scalar2=None,
                            op0=ALU.mult), sig=True)
                    else:
                        acc_tok[sb] = p.op("ve", lambda e, bpar=bpar: e.tensor_tensor(
                            out=ACCE, in0=ACCE, in1=E2[bpar], op=ALU.add),
                            sig=True)
                if step >= 1:
                    sb2 = step - 1
                    bpar2 = sb2 % 2
                    p.wait("te", eready[sb2])
                    if sb2 == 0:
                        p.wait("te", prev_ys)
                    av_tok[sb2] = p.op("te", lambda e, sb2=sb2, bpar2=bpar2, nsb=nsb: e.matmul(
                        out=a.PS[4][:], lhsT=KTM[:, sb2, :], rhs=E2[bpar2],
                        start=(sb2 == 0), stop=(sb2 == nsb - 1)), sig=True)
            prev_e2g = [av_tok[nsb - 2], av_tok[nsb - 1]]
            p.wait("ve", acc_tok[nsb - 1])
            ab = p.op("ve", lambda e: e.tensor_copy(out=RB16_acc(a), in_=ACCE),
                      sig=True)
            prev_ab = ab
            p.wait("te", ab)
            p.wait("te", prev_rc)
            dnm = p.op("te", lambda e: e.matmul(
                out=a.PS[5][0:1, :], lhsT=a.ONES[:, 0:1], rhs=RB16_acc(a),
                start=True, stop=True), sig=True)
            prev_head_exps = [eready.get(nsb - 2), eready.get(nsb - 1)]
            prev_head_exps = [eready.get(nsb - 2), eready.get(nsb - 1)]
            p.wait("ve", dnm)
            t6 = p.op("ve", lambda e: e.reciprocal(
                out=RrowF, in_=a.PS[5][0:1, :]), sig=True)
            prev_rc = t6
            p.wait("ve", t6)
            rv = p.op("ve", lambda e: e.tensor_copy(out=RB16, in_=RrowF),
                      sig=True)
            p.wait("te", rv)
            p.wait("te", prev_q_evac)
            bc = p.op("te", lambda e: e.matmul(
                out=a.PS[1][:], lhsT=a.ONES[0:1, :], rhs=RB16,
                start=True, stop=True), sig=True)
            p.wait("sc", bc)
            rbf = p.op("sc", lambda e: e.copy(out=RBF, in_=a.PS[1][:]), sig=True)
            prev_q_evac = rbf
            p.wait("ve", rbf)
            p.wait("ve", av_tok[nsb - 1])
            ys = p.op("ve", lambda e, h=h: e.tensor_tensor(
                out=YS4[:, h, :], in0=a.PS[4][:], in1=RBF, op=ALU.mult),
                sig=True)
            prev_ys = ys
        ys_done = prev_ys
        p.wait("te", ys_done)
        for cc_ in range(4):
            p.wait("ve", prev_stg_store)
            last = None
            evs = None
            for tt in range(4):
                bank = cc_ % 2
                p.wait("te", evs)
                for h in range(4):
                    last = p.op("te", lambda e, h=h, tt=tt, cc_=cc_, bank=bank: e.matmul(
                        out=a.PS[bank][:],
                        lhsT=YS4[:, h, tt * 128:(tt + 1) * 128],
                        rhs=WPJ[:, h, cc_, :],
                        start=(h == 0), stop=(h == 3)), sig=(h == 3))
                p.wait("ve", last)
                evs = p.op("ve", lambda e, tt=tt, bank=bank: e.tensor_copy(
                    out=STG[:, tt, :], in_=a.PS[bank][:]), sig=True)
            p.wait("gp", evs)
            prev_stg_store = p.dma(
                "gp", rr(proj_part[sc * 512:(sc + 1) * 512,
                                   cc_ * 512:(cc_ + 1) * 512]), STG[:],
                sem="ppst")
        prev_q_evac = p.tok("vesig")

    p.wait("gp", prev_stg_store)
    ccr = p.cc("ReduceScatter", [proj_part[:]], [rs_out[:]], "ccrs", op=ALU.add)
    p.wait("gp", ccr)

    p._a4_free = (p.tok("tesig"), p.tok("vesig"), p.tok("scsig"))
    bp2 = p.dma("gp", a.r4(0, 2048), w["b_proj"], sem="bpl")
    p.wait("te", bp2)
    p.wait("te", p.tok("vesig"))
    p.wait("te", p.tok("scsig"))
    bft = None
    for cc_ in range(4):
        p.wait("te", bft)
        bt = p.op("te", lambda e, cc_=cc_: e.matmul(
            out=a.PS[1][:], lhsT=a.ONES[0:1, :],
            rhs=a.r4(cc_ * 512, 512), start=True, stop=True), sig=True)
        p.wait("sc", bt)
        bft = p.op("sc", lambda e, cc_=cc_: e.copy(
            out=a.A2[:, 0, cc_ * 512:(cc_ + 1) * 512], in_=a.PS[1][:]),
            sig=True)

    RS6 = a.v6(0, 2048)
    XN6 = a.v6(2048, 2048)
    prior = None
    for blk in range(NBLK):
        p.wait("gp", prior)
        xt = p.dma("gp", a.A1[:], rr(src_x[blk * TB:(blk + 1) * TB, :]))
        for tt in range(4):
            r0 = blk * TB + tt * 128
            lt = p.dma("gp", RS6, rs_out[r0:r0 + 128, :], sem="rsl")
            p.wait("ve", lt)
            p.wait("ve", xt)
            p.wait("ve", bft)
            p.wait("ve", prior)
            t7 = p.op("ve", lambda e: e.tensor_tensor(
                out=XN6, in0=RS6, in1=a.A2[:, 0, :], op=ALU.add), sig=True)
            p.wait("ve", t7)
            t8 = p.op("ve", lambda e, tt=tt: e.tensor_tensor(
                out=XN6, in0=XN6, in1=a.A1[:, tt, :], op=ALU.add), sig=True)
            p.wait("gp", t8)
            prior = p.dma("gp", dst_x[r0:r0 + 128, :], XN6, sem="xst")
    return prior


def RB16_acc(a):
    return a.f4(22528, 512)



WT_SPECS = {
    "gup": ("L0_ffn_w_gate_up", 16384, 2048),
    "wdn": ("L0_ffn_w_down", 2048, 8192),
    "win": ("L0_attn_W_in", 2048, 2048),
    "wgt": ("L0_attn_W_gate", 8192, 2048),
    "wout": ("L0_attn_W_out", 2048, 2048),
}

def wt_perm(name):
    if name == "gup":
        return [(i // 2) + 64 * (i % 2) for i in range(128)]
    if name == "wgt":
        perm = []
        for i in range(64):
            head, jj, ty = i // 16, (i // 4) % 4, i % 4
            perm.append(head * 16 + ty * 4 + jj)
        return perm
    return None


def build_program(secs, inputs_present):
    p = Prog()
    nc = p.nc
    stack = ExitStack()
    a = Arenas(nc, stack)

    def din(name, shape):
        return nc.dram_tensor(name, shape, F32, kind="ExternalInput").ap()

    xin = din("xin", [T, D])
    xout = nc.dram_tensor("xout", [T, D], F32, kind="ExternalOutput").ap()

    idt = nc.dram_tensor("c_ident", [128, 128], F32, kind="ExternalInput").ap()
    onest = nc.dram_tensor("c_ones", [128, 128], F32, kind="ExternalInput").ap()
    mskt = nc.dram_tensor("c_mask", [128, 1024], F32, kind="ExternalInput").ap()
    ld1 = p.dma("gp", a.ID[:], idt[:])
    ld2 = p.dma("gp", a.ONES[:], onest[:])
    ld3 = p.dma("gp", a.MSK[:], mskt[:])

    need = set()
    for s in secs:
        if s in (1, 3):
            need |= {"gup", "wdn"}
        if s == 2:
            need |= {"win", "wgt", "wout"}

    lnt = nc.dram_tensor("lnt", [T, D], BF16).ap()
    wt = {}
    for nm in sorted(need):
        src_name, O, K = WT_SPECS[nm]
        w_hbm = din(src_name, [O, K])
        dst = nc.dram_tensor(f"wt_{nm}", [K, O], BF16).ap()
        emit_wt_construct(p, a, w_hbm, None, dst, O, K, perm=wt_perm(nm))
        wt[nm] = dst
    for eng in ENGINES:
        p.wait(eng, p.tok("wts0"))
        p.wait(eng, p.tok("wts1"))

    n = len(secs)
    if n == 1:
        hops = [(xin, xout)]
    else:
        xs = [nc.dram_tensor(f"xs{i}", [T, D], F32).ap() for i in range(n - 1)]
        hops = []
        cur = xin
        for i in range(n - 1):
            hops.append((cur, xs[i]))
            cur = xs[i]
        hops.append((cur, xout))

    prior_tok = None
    for (s, (src, dst)) in zip(secs, hops):
        if s in (1, 3):
            prior_tok = emit_ffn0(p, a, src, dst, lnt, wt["gup"], wt["wdn"],
                                  prior0=prior_tok)
        elif s == 2:
            for nm2 in ("b_gate", "rms_w"):
                pass
            bgt = din("L0_attn_b_gate_r", [128, 64])
            rwt = din("L0_attn_rms_w_r", [128, 4])
            b1 = p.dma("gp", a.BG[:], bgt[:])
            b2 = p.dma("gp", a.RMSW[:], rwt[:])
            p.wait("sc", b2)
            cp_scr = nc.dram_tensor("cp_scr", [T, D], BF16).ap()
            cl_scr = nc.dram_tensor("cl_scr", [T, D], BF16).ap()
            o_scr = nc.dram_tensor("o_scr", [T, D], BF16).ap()
            agin = nc.dram_tensor("ag_scan_in", [128, 32], F32).ap()
            agout = nc.dram_tensor("ag_scan_out", [512, 32], F32).ap()
            scm = din("scanmask", [128, 3])
            b3 = p.dma("gp", a.A7[:, 112:115], scm[:])
            p.wait("ve", b3)
            prior_tok = emit_l0attn(p, a, src, dst, lnt, wt["win"], wt["wgt"],
                                    wt["wout"], cp_scr, cl_scr, o_scr, agin,
                                    agout, prior0=prior_tok)
        elif s in (4, 6):
            w = {
                "router_w": din("L1_ffn_router_w", [D, 8]),
                "router_b": din("L1_ffn_router_b_r", [1, 8]),
                "expert_w": din("L1_ffn_expert_w", [8, D, D]),
                "expert_b": din("L1_ffn_expert_b_r", [1, 16384]),
                "proj_w": din("L1_ffn_proj_w", [D, D]),
                "proj_b": din("L1_ffn_proj_b_r", [1, D]),
                "norm_w": din("L1_ffn_norm_w_r", [1, D]),
            } if s == 4 or 4 not in secs else build_program._moe_w
            if s == 4 or 4 not in secs:
                build_program._moe_w = w
            acc_scr = nc.dram_tensor(f"acc_scr{s}", [T, D], BF16).ap()
            prior_tok = emit_moe(p, a, src, dst, lnt, acc_scr, w, tag=f"m{s}",
                                 prior0=prior_tok)
        elif s == 5:
            w5 = {
                "W_lat": din("L1_attn_W_lat", [D, 128]),
                "W_d_slice": din("W_d_slice", [D, 512]),
                "W_proj_slice": din("W_proj_slice", [512, D]),
                "b_proj": din("L1_attn_b_proj_r", [1, D]),
            }
            blt = din("b_lat_r", [128, 1])
            bdt = din("b_d_slice_r", [128, 4])
            p.dma("gp", a.BL[:], blt[:])
            b4 = p.dma("gp", a.BD[:], bdt[:])
            p.wait("sc", b4)
            agin_ln = nc.dram_tensor("agin_ln", [T, D], BF16).ap()
            agout_ln = nc.dram_tensor("agout_ln", [SEQ, D], BF16).ap()
            proj_part = nc.dram_tensor("proj_part", [SEQ, D], F32).ap()
            rs_out = nc.dram_tensor("rs_out", [T, D], F32).ap()
            prior_tok = emit_l1attn(p, a, src, dst, agin_ln, agout_ln,
                                    proj_part, rs_out, w5, prior0=prior_tok)

    p.wait("gp", p.tok("xst"))
    p.wait("gp", p.tok("gpdma"))
    nc_final = p.finalize()
    stack.close()
    return nc_final



def _consts():
    ident = np.eye(128, dtype=np.float32)
    ones = np.ones((128, 128), dtype=np.float32)
    msk = np.zeros((128, 1024), dtype=np.float32)
    pp, qq = np.meshgrid(np.arange(128), np.arange(1024), indexing="ij")
    msk[pp <= qq - 512] = 1.0
    return ident, ones, msk


def _prep_core_inputs(inputs, core):
    b, me = core // 4, core % 4
    ident, ones, msk = _consts()
    d = {
        "c_ident": ident, "c_ones": ones, "c_mask": msk,
        "L0_ffn_w_gate_up": inputs["L0_ffn_w_gate_up"],
        "L0_ffn_w_down": inputs["L0_ffn_w_down"],
        "L0_attn_W_in": inputs["L0_attn_W_in"],
        "L0_attn_W_gate": inputs["L0_attn_W_gate"],
        "L0_attn_W_out": inputs["L0_attn_W_out"],
        "L0_attn_b_gate_r": np.ascontiguousarray(
            inputs["L0_attn_b_gate"].reshape(64, 128).T),
        "L0_attn_rms_w_r": np.ascontiguousarray(
            inputs["L0_attn_rms_w"].reshape(4, 128).T),
        "scanmask": np.broadcast_to(
            (np.arange(3) < me).astype(np.float32), (128, 3)).copy(),
        "L1_ffn_router_w": inputs["L1_ffn_router_w"],
        "L1_ffn_router_b_r": inputs["L1_ffn_router_b"].reshape(1, 8),
        "L1_ffn_expert_w": inputs["L1_ffn_expert_w"],
        "L1_ffn_expert_b_r": inputs["L1_ffn_expert_b"].reshape(1, 16384),
        "L1_ffn_proj_w": inputs["L1_ffn_proj_w"],
        "L1_ffn_proj_b_r": inputs["L1_ffn_proj_b"].reshape(1, 2048),
        "L1_ffn_norm_w_r": inputs["L1_ffn_norm_w"].reshape(1, 2048),
        "L1_attn_W_lat": inputs["L1_attn_W_lat"],
        "W_d_slice": np.ascontiguousarray(
            inputs["L1_attn_W_d"][:, me * 512:(me + 1) * 512]),
        "W_proj_slice": np.ascontiguousarray(
            inputs["L1_attn_W_proj"][me * 512:(me + 1) * 512, :]),
        "L1_attn_b_proj_r": inputs["L1_attn_b_proj"].reshape(1, 2048),
        "b_lat_r": inputs["L1_attn_b_lat"].reshape(128, 1),
        "b_d_slice_r": np.ascontiguousarray(
            inputs["L1_attn_b_d"][me * 512:(me + 1) * 512].reshape(4, 128).T),
    }
    return {k: np.ascontiguousarray(np.asarray(v, dtype=np.float32))
            for k, v in d.items()}


_IN_BY_SEC = {
    1: ["L0_ffn_w_gate_up", "L0_ffn_w_down"],
    2: ["L0_attn_W_in", "L0_attn_W_gate", "L0_attn_W_out",
        "L0_attn_b_gate_r", "L0_attn_rms_w_r", "scanmask"],
    3: ["L0_ffn_w_gate_up", "L0_ffn_w_down"],
    4: ["L1_ffn_router_w", "L1_ffn_router_b_r", "L1_ffn_expert_w",
        "L1_ffn_expert_b_r", "L1_ffn_proj_w", "L1_ffn_proj_b_r",
        "L1_ffn_norm_w_r"],
    5: ["L1_attn_W_lat", "W_d_slice", "W_proj_slice", "L1_attn_b_proj_r",
        "b_lat_r", "b_d_slice_r"],
    6: ["L1_ffn_router_w", "L1_ffn_router_b_r", "L1_ffn_expert_w",
        "L1_ffn_expert_b_r", "L1_ffn_proj_w", "L1_ffn_proj_b_r",
        "L1_ffn_norm_w_r"],
}


def run_chain(inputs, groups):
    x = np.asarray(inputs["L0_attn_x"], dtype=np.float32)
    shards = [np.ascontiguousarray(x[c // 4, (c % 4) * T:(c % 4 + 1) * T, :])
              for c in range(8)]
    per_core = [_prep_core_inputs(inputs, c) for c in range(8)]
    for gi, secs in enumerate(groups):
        if hasattr(build_program, "_moe_w"):
            del build_program._moe_w
        nc = build_program(secs, None)
        names = set()
        for s in secs:
            names.update(_IN_BY_SEC[s])
        names |= {"c_ident", "c_ones", "c_mask"}
        in_maps = []
        for c in range(8):
            m = {k: per_core[c][k] for k in names if k in per_core[c]}
            m["xin"] = shards[c]
            in_maps.append(m)
        res = run_bass_kernel_spmd(nc, in_maps, core_ids=list(range(8)))
        shards = [np.asarray(res.results[c]["xout"], dtype=np.float32)
                  for c in range(8)]
    out = np.empty((2, SEQ, D), dtype=np.float32)
    for c in range(8):
        out[c // 4, (c % 4) * T:(c % 4 + 1) * T, :] = shards[c]
    return out


def kernel(**inputs):
    if FUSED:
        return run_chain(inputs, [[1, 2, 3, 4, 5, 6]])
    return run_chain(inputs, [[1], [2], [3], [4], [5], [6]])


# version 05
# speedup vs baseline: 1.3662x; 1.0233x over previous
import numpy as np
import ml_dtypes
from contextlib import ExitStack
import concourse.bass as bass
import concourse.mybir as mybir
from concourse.bass_utils import run_bass_kernel_spmd

F32 = mybir.dt.float32
BF16 = mybir.dt.bfloat16
AF = mybir.ActivationFunctionType
ALU = mybir.AluOpType
AX = mybir.AxisListType

ENGINES = ("te", "ve", "sc", "gp", "sy")
RG = [[0, 1, 2, 3], [4, 5, 6, 7]]

D = 2048
T = 2048
TB = 512
NBLK = 4
KT = 16
SEQ = 8192
EPS_LN = 1e-5
EPS_S2 = 1e-6
EPS_MOE = 1e-6
ISQL = float(1.0 / np.sqrt(128.0))

FUSED = True



class Prog:
    def __init__(self):
        self.nc = bass.Bass("TRN2", target_bir_lowering=False,
                            disable_frame_to_traceback=True)
        self.ops = {e: [] for e in ENGINES}
        self.sems = {}
        self.sem_order = []
        self._h = {}

    def _bump(self, name, n):
        if name not in self.sems:
            self.sems[name] = 0
            self.sem_order.append(name)
        self.sems[name] += n
        return (name, self.sems[name])

    def tok(self, name):
        return (name, self.sems.get(name, 0))

    def wait(self, eng, tok):
        if tok is None:
            return
        name, val = tok
        if val <= 0:
            return
        assert val <= self.sems.get(name, 0), f"wait beyond incs: {name} {val}"
        self.ops[eng].append(lambda e, n=name, v=val: e.wait_ge(self._h[n], v))

    def dma(self, q, out, in_, transpose=False, sem=None):
        s = sem or (q + "dma")
        tok = self._bump(s, 16)
        def fn(e, out=out, in_=in_, t=transpose, n=s):
            e.dma_start(out=out, in_=in_, transpose=t).then_inc(self._h[n], 16)
        self.ops[q].append(fn)
        return tok

    def op(self, eng, fn, sig=False):
        if not sig:
            self.ops[eng].append(fn)
            return None
        s = eng + "sig"
        tok = self._bump(s, 1)
        def wrapped(e, n=s):
            fn(e).then_inc(self._h[n], 1)
        self.ops[eng].append(wrapped)
        return tok

    def cc(self, kind, ins, outs, name, op=None):
        tok = self._bump(name, 1)
        aop = op if op is not None else ALU.bypass
        def fn(e, n=name):
            e.collective_compute(kind, aop, replica_groups=RG,
                                 ins=ins, outs=outs).then_inc(self._h[n])
        self.ops["gp"].append(fn)
        return tok

    def finalize(self):
        nc = self.nc
        stack = ExitStack()
        for n in self.sem_order:
            self._h[n] = stack.enter_context(nc.semaphore(n))
        block = stack.enter_context(nc.Block())
        decos = {"te": block.tensor, "ve": block.vector, "sc": block.scalar,
                 "gp": block.gpsimd, "sy": block.sync}
        for name in ENGINES:
            def mk(oplist):
                def body(e):
                    for fn in oplist:
                        fn(e)
                return body
            decos[name](mk(self.ops[name]))
        stack.close()
        return nc


class Rot:
    def __init__(self, p, banks):
        self.p, self.banks = p, banks
        self.free = [None] * len(banks)
        self.i = 0
    def get(self):
        j = self.i % len(self.banks)
        self.i += 1
        return j, self.banks[j], self.free[j]
    def done(self, j, tok):
        self.free[j] = tok


def stream_slabs(p, n, make_load, body, q="sy"):
    loads = [None] * n
    free = {}
    def issue(i):
        if i - 2 >= 0:
            p.wait(q, free[i - 2])
        loads[i] = make_load(i, i % 2)
    p.wait(q, p.tok("tesig"))
    if n > 0:
        issue(0)
    if n > 1:
        issue(1)
    for i in range(n):
        free[i] = body(i, i % 2, loads[i])
        assert free[i] is not None
        if i + 2 < n:
            issue(i + 2)


class Arenas:
    def __init__(self, nc, stack):
        E = stack.enter_context
        self.A1 = E(nc.sbuf_tensor("A1", [128, NBLK, D], F32))
        self.A2 = E(nc.sbuf_tensor("A2", [128, NBLK, D], BF16))
        self.A3 = E(nc.sbuf_tensor("A3", [128, KT, TB], BF16))
        self.A4 = E(nc.sbuf_tensor("A4", [128, 32768], BF16))
        self.A5 = E(nc.sbuf_tensor("A5", [128, 2, KT, TB], BF16))
        self.A6 = E(nc.sbuf_tensor("A6", [128, 6144], F32))
        self.A7 = E(nc.sbuf_tensor("A7", [128, 128], F32))
        self.BG = E(nc.sbuf_tensor("BG", [128, 64], F32))
        self.RMSW = E(nc.sbuf_tensor("RMSW", [128, 4], F32))
        self.BD = E(nc.sbuf_tensor("BD", [128, 4], F32))
        self.BL = E(nc.sbuf_tensor("BL", [128, 1], F32))
        self.ID = E(nc.sbuf_tensor("ID", [128, 128], BF16))
        self.ONES = E(nc.sbuf_tensor("ONES", [128, 128], BF16))
        self.MSK = E(nc.sbuf_tensor("MSK", [128, 1024], BF16))
        self.WB = E(nc.sbuf_tensor("WB", [128, 2304], BF16))
        self.PS = [E(nc.psum_tensor(f"PS{i}", [128, 512], F32)) for i in range(6)]
        self.PB = [E(nc.psum_tensor(f"PB{i}", [128, 1024], BF16)) for i in range(2)]

    def v4(self, off, a_, b_):
        return self.A4[:, off:off + a_ * b_].rearrange("p (a b) -> p a b", a=a_)

    def f4(self, off, n):
        return self.A4[:, off:off + n]

    def r4(self, off, n):
        return self.A4[0:1, off:off + n]

    def v6(self, off, n):
        return self.A6[:, off:off + n]

    def v6b(self, off, a_, b_):
        return self.A6[:, off:off + a_ * b_].rearrange("p (a b) -> p a b", a=a_)


def rr(ap, p=128):
    return ap.rearrange("(a p) c -> p a c", p=p)



def emit_wt_construct(p, a, w_hbm, wb_unused, dst, O, K, perm=None):
    nob, nkb = O // 128, K // 128
    perm = perm or list(range(nob))
    inv = [0] * nob
    for i, pv in enumerate(perm):
        inv[pv] = i
    nb = 4 if K == 2048 else 1
    p.wait("gp", p.tok("tesig"))
    p.wait("te", p.tok("wts0"))
    p.wait("te", p.tok("wts1"))
    p.wait("te", p.tok("vesig"))
    buf = [a.A5[:, 0].rearrange("p a b -> p (a b)"),
           a.A5[:, 1].rearrange("p a b -> p (a b)")]
    stage = [a.A4[:, 0:8192], a.A4[:, 8192:16384]]
    ngrp = (nkb + 7) // 8
    load_free = [None, None]
    stage_free = [None, None]
    bank_free = [None, None]
    for i0 in range(0, nob, nb):
        lp = (i0 // nb) % 2
        sp = lp
        p.wait("gp", load_free[lp])
        ld = p.dma("gp", buf[lp][:, 0:nb * K].rearrange("p (g c) -> p g c", g=nb),
                   w_hbm[i0 * 128:(i0 + nb) * 128, :].rearrange(
                       "(g p) c -> p g c", p=128),
                   sem=f"wcl{lp}")
        p.wait("te", ld)
        p.wait("te", stage_free[sp])
        ev = None
        for g in range(nb):
            for gr in range(ngrp):
                nk = min(8, nkb - gr * 8)
                bp = (g * ngrp + gr) % 2
                p.wait("te", bank_free[bp])
                tp = None
                for j in range(nk):
                    k = gr * 8 + j
                    tp = p.op("te", lambda e, k=k, j=j, g=g, lp=lp, bp=bp: e.transpose(
                        out=a.PB[bp][:, j * 128:(j + 1) * 128],
                        in_=buf[lp][:, g * K + k * 128: g * K + (k + 1) * 128],
                        identity=a.ID[:]), sig=(j == nk - 1))
                p.wait("ve", tp)
                ev = p.op("ve", lambda e, g=g, gr=gr, nk=nk, bp=bp, sp=sp: e.tensor_copy(
                    out=stage[sp][:, g * (nkb * 128) + gr * 1024:
                                  g * (nkb * 128) + gr * 1024 + nk * 128],
                    in_=a.PB[bp][:, 0:nk * 128]), sig=True)
                bank_free[bp] = ev
        load_free[lp] = p.tok("tesig")
        if nb == 1:
            ob = inv[i0]
            p.wait("sc", ev)
            stage_free[sp] = p.dma(
                "sc", rr(dst)[:, :, ob * 128:(ob + 1) * 128],
                stage[sp][:, 0:nkb * 128].rearrange("p (a c) -> p a c", c=128),
                sem=f"wts{sp}")
        else:
            b0 = inv[i0]
            s = inv[i0 + 1] - b0 if nb > 1 else 1
            for j in range(nb):
                assert inv[i0 + j] == b0 + j * s, (i0, j, inv[i0 + j], b0, s)
            stv = stage[sp][:, 0:nb * nkb * 128].rearrange(
                "p (g a c) -> p g a c", g=nb, c=128)
            p.wait("sc", ev)
            for j in range(nb):
                ob = inv[i0 + j]
                stage_free[sp] = p.dma(
                    "sc", rr(dst)[:, :, ob * 128:(ob + 1) * 128],
                    stv[:, j], sem=f"wts{sp}")
    p.wait("sc", stage_free[0])
    p.wait("sc", stage_free[1])



def emit_ln_block(p, a, src_x, blk, lnt=None, to_fm=True, ln_rows=None,
                  prior=None):
    p.wait("sy", prior)
    xt = p.dma("sy", a.A1[:], rr(src_x[blk * TB:(blk + 1) * TB, :]), sem="xld")
    SB = a.v6b(5632, 16, 8)
    dummy = a.v6(0, 2048)
    p.wait("sc", prior)
    p.wait("ve", prior)
    atoks = []
    for tt in range(4):
        p.wait("ve", xt)
        p.wait("sc", xt)
        t0 = p.op("ve", lambda e, tt=tt: e.reduce_sum(
            out=SB[:, tt, 0:1], in_=a.A1[:, tt, :], axis=AX.X), sig=True)
        s_ssq = p.op("sc", lambda e, tt=tt: e.activation(
            out=dummy, in_=a.A1[:, tt, :], func=AF.Square,
            accum_out=SB[:, tt, 1:2]), sig=True)
        p.wait("ve", s_ssq)
        p.wait("ve", t0)
        t1 = p.op("ve", lambda e, tt=tt: e.tensor_scalar(
            out=SB[:, tt, 2:3], in0=SB[:, tt, 0:1], scalar1=1.0 / D,
            scalar2=None, op0=ALU.mult), sig=True)
        p.wait("ve", t1)
        t2 = p.op("ve", lambda e, tt=tt: e.tensor_tensor(
            out=SB[:, tt, 3:4], in0=SB[:, tt, 2:3], in1=SB[:, tt, 2:3],
            op=ALU.mult), sig=True)
        t3 = p.op("ve", lambda e, tt=tt: e.tensor_scalar(
            out=SB[:, tt, 4:5], in0=SB[:, tt, 1:2], scalar1=1.0 / D,
            scalar2=EPS_LN, op0=ALU.mult, op1=ALU.add), sig=True)
        p.wait("ve", t2)
        p.wait("ve", t3)
        vtok = p.op("ve", lambda e, tt=tt: e.tensor_tensor(
            out=SB[:, tt, 5:6], in0=SB[:, tt, 4:5], in1=SB[:, tt, 3:4],
            op=ALU.subtract), sig=True)
        p.wait("sc", vtok)
        stok = p.op("sc", lambda e, tt=tt: e.activation(
            out=SB[:, tt, 6:7], in_=SB[:, tt, 5:6], func=AF.Sqrt), sig=True)
        p.wait("ve", stok)
        t4 = p.op("ve", lambda e, tt=tt: e.reciprocal(
            out=SB[:, tt, 7:8], in_=SB[:, tt, 6:7]), sig=True)
        p.wait("ve", t4)
        atok = p.op("ve", lambda e, tt=tt: e.tensor_scalar(
            out=a.A2[:, tt, :], in0=a.A1[:, tt, :], scalar1=SB[:, tt, 2:3],
            scalar2=SB[:, tt, 7:8], op0=ALU.subtract, op1=ALU.mult), sig=True)
        atoks.append(atok)
    p.wait("sy", atoks[-1])
    if ln_rows is not None:
        dst, off = ln_rows
        lntok = p.dma("sy", rr(dst[off + blk * TB: off + (blk + 1) * TB, :]),
                      a.A2[:], sem="lns")
        return xt, lntok
    lntok = p.dma("sy", rr(lnt[blk * TB:(blk + 1) * TB, :]), a.A2[:], sem="lns")
    fm_tok = None
    if to_fm:
        p.wait("sy", lntok)
        p.wait("sy", p.tok("tesig"))
        for kt in range(KT):
            fm_tok = p.dma("sy", a.A3[:, kt, :],
                           lnt[blk * TB:(blk + 1) * TB, kt * 128:(kt + 1) * 128],
                           transpose=True, sem="fmt")
    return xt, fm_tok



def emit_ffn0(p, a, src_x, dst_x, lnt, gup_T, wdn_T, prior0=None):
    prior = prior0
    for blk in range(NBLK):
        xt, fm = emit_ln_block(p, a, src_x, blk, lnt=lnt, prior=prior)
        HM = a.v4(0, 64, 512)
        SG = a.v6(4096, 512)
        ve_all = p.tok("vesig")
        sc_all = p.tok("scsig")
        rot = Rot(p, [(0, 1), (2, 3)])
        hm_last = [None]

        def load_gu(i, par):
            return p.dma("sy", a.A5[:, par],
                         rr(gup_T)[:, :, i * 512:(i + 1) * 512],
                         sem=f"sl{par}")

        def body_gu(i, par, ltok):
            last_te = None
            for half in range(2):
                op = 2 * i + half
                j, pair, ftok = rot.get()
                pg, pu = a.PS[pair[0]], a.PS[pair[1]]
                p.wait("te", ftok)
                p.wait("te", ltok)
                if op == 0:
                    p.wait("te", fm)
                    p.wait("te", ve_all)
                    p.wait("te", sc_all)
                for kt in range(KT):
                    p.op("te", lambda e, kt=kt, par=par, half=half, pg=pg: e.matmul(
                        out=pg[:], lhsT=a.A5[:, par, kt, half * 256:half * 256 + 128],
                        rhs=a.A3[:, kt, :], start=(kt == 0), stop=(kt == KT - 1)))
                for kt in range(KT):
                    tok = p.op("te", lambda e, kt=kt, par=par, half=half, pu=pu: e.matmul(
                        out=pu[:], lhsT=a.A5[:, par, kt, half * 256 + 128:half * 256 + 256],
                        rhs=a.A3[:, kt, :], start=(kt == 0), stop=(kt == KT - 1)),
                        sig=(kt == KT - 1))
                    if kt == KT - 1:
                        last_te = tok
                p.wait("sc", last_te)
                sg = p.op("sc", lambda e, pg=pg: e.activation(
                    out=SG, in_=pg[:], func=AF.Silu), sig=True)
                p.wait("ve", sg)
                ev = p.op("ve", lambda e, op=op, pu=pu: e.tensor_tensor(
                    out=HM[:, op, :], in0=SG, in1=pu[:], op=ALU.mult), sig=True)
                rot.done(j, ev)
                hm_last[0] = ev
            return last_te

        stream_slabs(p, 32, load_gu, body_gu)
        hm_done = hm_last[0]

        STG = a.v6b(0, 4, 512)
        for fh in range(4):
            prev_ev = p.tok("vesig")

            def load_dn(g, par, fh=fh):
                return p.dma("sy", a.A5[:, par, 0:8, :],
                             rr(wdn_T)[:, g * 8:(g + 1) * 8,
                                       fh * 512:(fh + 1) * 512],
                             sem=f"sl{par}")

            def body_dn(g, par, ltok, fh=fh):
                p.wait("te", ltok)
                if g == 0:
                    p.wait("te", hm_done)
                    p.wait("te", prev_ev)
                tok = None
                for kk in range(8):
                    for tt in range(4):
                        sig = (kk == 7 and tt == 3)
                        tok2 = p.op("te", lambda e, g=g, kk=kk, tt=tt, par=par: e.matmul(
                            out=a.PS[tt][:],
                            lhsT=HM[:, g * 8 + kk, tt * 128:(tt + 1) * 128],
                            rhs=a.A5[:, par, kk, :],
                            start=(g == 0 and kk == 0),
                            stop=(g == 7 and kk == 7)), sig=sig)
                        if sig:
                            tok = tok2
                return tok

            mm_done = [None]

            def body_dn2(g, par, ltok, fh=fh):
                t = body_dn(g, par, ltok, fh)
                mm_done[0] = t
                return t

            stream_slabs(p, 8, load_dn, body_dn2)
            p.wait("ve", mm_done[0])
            p.wait("ve", prior)
            ev = None
            for tt in range(4):
                ev = p.op("ve", lambda e, tt=tt, fh=fh: e.scalar_tensor_tensor(
                    out=STG[:, tt, :], in0=a.PS[tt][:], scalar=0.5,
                    in1=a.A1[:, tt, fh * 512:(fh + 1) * 512],
                    op0=ALU.mult, op1=ALU.add), sig=(tt == 3))
            p.wait("gp", ev)
            prior = p.dma("gp",
                          rr(dst_x[blk * TB:(blk + 1) * TB,
                                   fh * 512:(fh + 1) * 512]), STG[:],
                          sem="xst")
    return prior



def emit_l0attn(p, a, src_x, dst_x, lnt, win_T, wgt_T, wout_T,
                cp_scr, cl_scr, o_scr, agin, agout, prior0=None):
    ZER = a.v6(3584, 512)
    p.op("ve", lambda e: e.memset(ZER, 0.0))
    CPC = a.A7[:, 64:80]
    CLC = a.A7[:, 80:96]
    CIN = a.A7[:, 96:112]
    SCM = a.A7[:, 112:115]
    p.op("ve", lambda e: e.memset(CPC, 1.0))
    p.op("ve", lambda e: e.memset(CLC, 0.0))

    XPF = a.v4(0, 16, 512)
    CPS = a.v4(8192, 16, 512)
    CLS = a.v4(16384, 16, 512)
    OS = a.v4(24576, 16, 512)
    T1, T2, T3, T4 = (a.v6(512, 512), a.v6(1024, 512), a.v6(1536, 512),
                      a.v6(2048, 512))

    prior = prior0
    for blk in range(NBLK):
        xt, fm = emit_ln_block(p, a, src_x, blk, lnt=lnt, prior=prior)
        ve_all = p.tok("vesig")
        sc_all = p.tok("scsig")
        rot = Rot(p, [0, 1, 2, 3])

        def load_xp(i, par):
            return p.dma("sy", a.A5[:, par],
                         rr(win_T)[:, :, i * 512:(i + 1) * 512], sem=f"sl{par}")

        def body_xp(i, par, ltok):
            last = None
            for q in range(4):
                ht = i * 4 + q
                j, bank, ftok = rot.get()
                p.wait("te", ftok)
                p.wait("te", ltok)
                if ht == 0:
                    p.wait("te", fm)
                    p.wait("te", ve_all)
                    p.wait("te", sc_all)
                for kt in range(KT):
                    last = p.op("te", lambda e, kt=kt, par=par, q=q, bank=bank: e.matmul(
                        out=a.PS[bank][:],
                        lhsT=a.A5[:, par, kt, q * 128:(q + 1) * 128],
                        rhs=a.A3[:, kt, :], start=(kt == 0), stop=(kt == KT - 1)),
                        sig=(kt == KT - 1))
                p.wait("sc", last)
                ev = p.op("sc", lambda e, ht=ht, bank=bank: e.copy(
                    out=XPF[:, ht, :], in_=a.PS[bank][:]), sig=True)
                rot.done(j, ev)
            return last

        stream_slabs(p, 4, load_xp, body_xp)
        xp_done = p.tok("scsig")

        def load_g(i, par):
            return p.dma("sy", a.A5[:, par],
                         rr(wgt_T)[:, :, i * 512:(i + 1) * 512], sem=f"sl{par}")

        def body_g(i, par, ltok):
            head, jj = i // 4, i % 4
            ct = i
            gt = head * 16
            p.wait("te", ltok)
            if i == 0:
                p.wait("te", xp_done)
            else:
                p.wait("te", p.tok("scsig"))
            tok = None
            for ty in range(4):
                for kt in range(KT):
                    sig = (ty == 3 and kt == KT - 1)
                    tok2 = p.op("te", lambda e, ty=ty, kt=kt, par=par: e.matmul(
                        out=a.PS[ty][:],
                        lhsT=a.A5[:, par, kt, ty * 128:(ty + 1) * 128],
                        rhs=XPF[:, kt, :], start=(kt == 0), stop=(kt == KT - 1)),
                        sig=sig)
                    if sig:
                        tok = tok2
            p.wait("sc", tok)
            c_i, c_f, c_o, c_c = gt + jj, gt + 4 + jj, gt + 8 + jj, gt + 12 + jj
            p.wait("sc", p.tok("vesig"))
            p.op("sc", lambda e, c=c_i: e.activation(
                out=T1, in_=a.PS[0][:], func=AF.Exp, bias=a.BG[:, c:c + 1]))
            p.op("sc", lambda e, c=c_c: e.activation(
                out=T2, in_=a.PS[3][:], func=AF.Tanh, bias=a.BG[:, c:c + 1]))
            p.op("sc", lambda e, c=c_f: e.activation(
                out=T4, in_=a.PS[1][:], func=AF.Sigmoid, bias=a.BG[:, c:c + 1]))
            sce = p.op("sc", lambda e, c=c_o, ct=ct: e.activation(
                out=OS[:, ct, :], in_=a.PS[2][:], func=AF.Sigmoid,
                bias=a.BG[:, c:c + 1]), sig=True)
            p.wait("ve", sce)
            v1 = p.op("ve", lambda e: e.tensor_tensor(out=T3, in0=T1, in1=T2,
                                                 op=ALU.mult), sig=True)
            v2 = p.op("ve", lambda e, ct=ct: e.tensor_tensor_scan(
                out=CPS[:, ct, :], data0=T4, data1=ZER,
                initial=CPC[:, ct:ct + 1], op0=ALU.mult, op1=ALU.add), sig=True)
            p.wait("ve", v2)
            v3 = p.op("ve", lambda e, ct=ct: e.tensor_copy(
                out=CPC[:, ct:ct + 1], in_=CPS[:, ct, 511:512]), sig=True)
            p.wait("ve", v1)
            v4 = p.op("ve", lambda e, ct=ct: e.tensor_tensor_scan(
                out=CLS[:, ct, :], data0=T4, data1=T3,
                initial=CLC[:, ct:ct + 1], op0=ALU.mult, op1=ALU.add), sig=True)
            p.wait("ve", v4)
            p.op("ve", lambda e, ct=ct: e.tensor_copy(
                out=CLC[:, ct:ct + 1], in_=CLS[:, ct, 511:512]), sig=True)
            return tok

        stream_slabs(p, 16, load_g, body_g)
        scans_done = p.tok("vesig")
        p.wait("gp", scans_done)
        p.dma("gp", rr(cp_scr)[:, :, blk * TB:(blk + 1) * TB], CPS[:])
        p.dma("gp", rr(cl_scr)[:, :, blk * TB:(blk + 1) * TB], CLS[:])
        prior = p.dma("gp", rr(o_scr)[:, :, blk * TB:(blk + 1) * TB], OS[:])

    agt = p.dma("gp", agin[:], a.A7[:, 64:96])
    p.wait("gp", agt)
    cct = p.cc("AllGather", [agin[:]], [agout[:]], "ccscan")
    p.wait("gp", cct)
    G = []
    gt_tok = None
    for r in range(4):
        Gr = a.A6[:, 4096 + r * 32: 4096 + (r + 1) * 32]
        G.append(Gr)
        gt_tok = p.dma("gp", Gr, agout[r * 128:(r + 1) * 128, :])
    p.wait("ve", gt_tok)
    TA = a.A6[:, 4224:4240]
    TBv = a.A6[:, 4240:4256]
    tk = p.op("ve", lambda e: e.memset(CIN, 0.0), sig=True)
    for r in range(3):
        p.wait("ve", tk)
        tk = p.op("ve", lambda e, r=r: e.tensor_scalar(
            out=TA, in0=G[r][:, 0:16], scalar1=1.0, scalar2=SCM[:, r:r + 1],
            op0=ALU.subtract, op1=ALU.mult), sig=True)
        p.wait("ve", tk)
        tk = p.op("ve", lambda e: e.tensor_scalar(
            out=TA, in0=TA, scalar1=1.0, scalar2=None, op0=ALU.add), sig=True)
        p.wait("ve", tk)
        tk = p.op("ve", lambda e: e.tensor_tensor(out=CIN, in0=CIN, in1=TA,
                                             op=ALU.mult), sig=True)
        p.wait("ve", tk)
        tk = p.op("ve", lambda e, r=r: e.tensor_scalar(
            out=TBv, in0=G[r][:, 16:32], scalar1=SCM[:, r:r + 1], scalar2=None,
            op0=ALU.mult), sig=True)
        p.wait("ve", tk)
        tk = p.op("ve", lambda e: e.tensor_tensor(out=CIN, in0=CIN, in1=TBv,
                                             op=ALU.add), sig=True)
    p.wait("ve", tk)
    cin_done = p.op("ve", lambda e: e.tensor_copy(out=TA, in_=CIN), sig=True)

    HFM = a.v4(0, 16, 512)
    Cr = a.v6b(0, 4, 512)
    CN = a.v6(2048, 512)
    RrowF = a.A6[0:1, 2560:3072]
    Rrow16 = a.r4(30720, 512)
    prior2 = p.tok("gpdma")
    prev_ps4 = None
    prev_ps5 = None
    for blk in range(NBLK):
        p.wait("gp", prior2)
        xt = p.dma("gp", a.A1[:], rr(src_x[blk * TB:(blk + 1) * TB, :]))
        p.dma("gp", CPS[:], rr(cp_scr)[:, :, blk * TB:(blk + 1) * TB])
        p.dma("gp", CLS[:], rr(cl_scr)[:, :, blk * TB:(blk + 1) * TB])
        l3 = p.dma("gp", OS[:], rr(o_scr)[:, :, blk * TB:(blk + 1) * TB])
        p.wait("ve", prior2)
        p.wait("ve", l3)
        p.wait("ve", cin_done)
        for head in range(4):
            ctok = None
            for jj in range(4):
                ct = head * 4 + jj
                ctok = p.op("ve", lambda e, ct=ct, jj=jj: e.scalar_tensor_tensor(
                    out=Cr[:, jj, :], in0=CPS[:, ct, :],
                    scalar=CIN[:, ct:ct + 1], in1=CLS[:, ct, :],
                    op0=ALU.mult, op1=ALU.add), sig=(jj == 3))
            p.wait("sc", ctok)
            sq_last = None
            for jj in range(4):
                ct = head * 4 + jj
                sq_last = p.op("sc", lambda e, ct=ct, jj=jj: e.activation(
                    out=HFM[:, ct, :], in_=Cr[:, jj, :], func=AF.Square),
                    sig=(jj == 3))
            p.wait("te", sq_last)
            p.wait("te", prev_ps4)
            mt = None
            for jj in range(4):
                ct = head * 4 + jj
                mt = p.op("te", lambda e, ct=ct, jj=jj: e.matmul(
                    out=a.PS[4][0:1, :], lhsT=a.ONES[:, 0:1],
                    rhs=HFM[:, ct, :], start=(jj == 0), stop=(jj == 3)),
                    sig=(jj == 3))
            p.wait("ve", mt)
            vt = p.op("ve", lambda e: e.tensor_scalar(
                out=RrowF, in0=a.PS[4][0:1, :], scalar1=1.0 / 512.0,
                scalar2=EPS_S2, op0=ALU.mult, op1=ALU.add), sig=True)
            prev_ps4 = vt
            p.wait("sc", vt)
            st2 = p.op("sc", lambda e: e.activation(
                out=RrowF, in_=RrowF, func=AF.Sqrt), sig=True)
            p.wait("ve", st2)
            t5 = p.op("ve", lambda e: e.reciprocal(out=RrowF, in_=RrowF),
                      sig=True)
            p.wait("ve", t5)
            rv = p.op("ve", lambda e: e.tensor_copy(out=Rrow16, in_=RrowF),
                      sig=True)
            p.wait("te", rv)
            p.wait("te", prev_ps5)
            bt = p.op("te", lambda e: e.matmul(
                out=a.PS[5][:], lhsT=a.ONES[0:1, :], rhs=Rrow16,
                start=True, stop=True), sig=True)
            p.wait("ve", bt)
            for jj in range(4):
                ct = head * 4 + jj
                cn = p.op("ve", lambda e, jj=jj: e.tensor_tensor(
                    out=CN, in0=Cr[:, jj, :], in1=a.PS[5][:], op=ALU.mult),
                    sig=True)
                if jj == 3:
                    prev_ps5 = cn
                p.wait("sc", cn)
                th = p.op("sc", lambda e, ct=ct, jj=jj: e.activation(
                    out=HFM[:, ct, :], in_=CN, func=AF.Tanh,
                    scale=a.RMSW[:, jj:jj + 1]), sig=True)
                p.wait("ve", th)
                p.op("ve", lambda e, ct=ct: e.tensor_tensor(
                    out=HFM[:, ct, :], in0=HFM[:, ct, :], in1=OS[:, ct, :],
                    op=ALU.mult), sig=True)
        h_done = p.tok("vesig")

        rot = Rot(p, [0, 1, 2, 3])
        STG = a.v6b(0, 4, 512)
        stores = [None]

        def load_wo(i, par):
            return p.dma("sy", a.A5[:, par],
                         rr(wout_T)[:, :, i * 512:(i + 1) * 512], sem=f"sl{par}")

        def body_wo(cc, par, ltok):
            p.wait("te", ltok)
            if cc == 0:
                p.wait("te", h_done)
            p.wait("ve", stores[0])
            last = None
            ev = None
            for tt in range(4):
                j, bank, ftok = rot.get()
                p.wait("te", ftok)
                for ctc in range(KT):
                    last = p.op("te", lambda e, ctc=ctc, tt=tt, par=par, bank=bank: e.matmul(
                        out=a.PS[bank][:],
                        lhsT=HFM[:, ctc, tt * 128:(tt + 1) * 128],
                        rhs=a.A5[:, par, ctc, :],
                        start=(ctc == 0), stop=(ctc == KT - 1)),
                        sig=(ctc == KT - 1))
                p.wait("ve", last)
                p.wait("ve", xt)
                ev = p.op("ve", lambda e, tt=tt, cc=cc, bank=bank: e.tensor_tensor(
                    out=STG[:, tt, :], in0=a.PS[bank][:],
                    in1=a.A1[:, tt, cc * 512:(cc + 1) * 512], op=ALU.add),
                    sig=True)
                rot.done(j, ev)
            p.wait("gp", ev)
            stores[0] = p.dma(
                "gp", rr(dst_x[blk * TB:(blk + 1) * TB,
                               cc * 512:(cc + 1) * 512]), STG[:], sem="xst")
            return last

        stream_slabs(p, 4, load_wo, body_wo)
        prior2 = stores[0]
    return prior2



def emit_moe(p, a, src_x, dst_x, lnt, acc_scr, w, tag, prior0=None):
    toks = getattr(p, "_a4_free", None) or (
        p.tok("tesig"), p.tok("vesig"), p.tok("scsig"))
    p._a4_free = None
    for tk_ in toks:
        p.wait("gp", tk_)
    NW = a.f4(8192, 2048)
    PRJB = a.r4(10240, 2048)
    NWROW = a.r4(12288, 2048)
    EB = a.r4(14336, 16384)
    RTRB = a.r4(30720, 8)
    SH = [a.f4(30728, 512), a.f4(31240, 512)]
    ACC = a.v4(0, 4, 2048)
    RW = a.WB[:, 2048:2176].rearrange("p (a b) -> p a b", a=16)

    p.dma("gp", PRJB, w["proj_b"])
    p.dma("gp", RTRB, w["router_b"])
    p.dma("gp", NWROW, w["norm_w"])
    p.dma("gp", EB, w["expert_b"])
    cw = p.dma("gp", RW, rr(w["router_w"]))
    p.wait("te", cw)
    nwt = None
    for cc in range(4):
        p.wait("te", nwt)
        bt = p.op("te", lambda e, cc=cc: e.matmul(
            out=a.PS[4][:], lhsT=a.ONES[0:1, 0:128],
            rhs=NWROW[0:1, cc * 512:(cc + 1) * 512], start=True, stop=True),
            sig=True)
        p.wait("sc", bt)
        nwt = p.op("sc", lambda e, cc=cc: e.copy(
            out=NW[:, cc * 512:(cc + 1) * 512], in_=a.PS[4][:]), sig=True)
    p.wait("te", nwt)

    LG = a.v6b(4096, 4, 8)
    W8 = a.v6b(4224, 4, 8)
    TOPW = a.v6b(4352, 4, 16)
    Y = a.v6b(0, 2, 2048)
    XN = a.v6(4096, 2048)
    q2 = a.A7[:, 0:1]
    rq = a.A7[:, 1:2]

    prior = prior0
    for blk in range(NBLK):
        xt, fm = emit_ln_block(p, a, src_x, blk, lnt=lnt, prior=prior)
        ve_all = p.tok("vesig")
        sc_all = p.tok("scsig")
        p.wait("te", fm)
        p.wait("te", ve_all)
        p.wait("te", sc_all)
        rt = None
        for tt in range(4):
            p.op("te", lambda e, tt=tt: e.matmul(
                out=a.PS[4][:, tt * 8:(tt + 1) * 8], lhsT=a.ONES[0:1, 0:128],
                rhs=RTRB, start=True, stop=False))
            for kt in range(KT):
                rt = p.op("te", lambda e, tt=tt, kt=kt: e.matmul(
                    out=a.PS[4][:, tt * 8:(tt + 1) * 8],
                    lhsT=a.A3[:, kt, tt * 128:(tt + 1) * 128], rhs=RW[:, kt, :],
                    start=False, stop=(kt == KT - 1)), sig=(kt == KT - 1 and tt == 3))
        p.wait("ve", rt)
        tk = p.op("ve", lambda e: e.tensor_copy(out=LG.rearrange("p a b -> p (a b)"),
                                                in_=a.PS[4][:, 0:32]), sig=True)
        for tt in range(4):
            m1 = TOPW[:, tt, 0:1]
            s1 = TOPW[:, tt, 1:2]
            m2 = TOPW[:, tt, 2:3]
            s2 = TOPW[:, tt, 3:4]
            dd = TOPW[:, tt, 4:5]
            e2 = TOPW[:, tt, 5:6]
            w1 = TOPW[:, tt, 6:7]
            w2 = TOPW[:, tt, 7:8]
            EQ1 = TOPW[:, tt, 8:16]
            LM = W8[:, tt, :]
            def ch(fn, tk):
                p.wait("ve", tk)
                return p.op("ve", fn, sig=True)
            tk = ch(lambda e, tt=tt, m1=m1: e.reduce_max(
                out=m1, in_=LG[:, tt, :], axis=AX.X), tk)
            tk = ch(lambda e, tt=tt, m1=m1, EQ1=EQ1: e.tensor_scalar(
                out=EQ1, in0=LG[:, tt, :], scalar1=m1, scalar2=None,
                op0=ALU.is_equal), tk)
            tk = ch(lambda e, s1=s1, EQ1=EQ1: e.reduce_sum(
                out=s1, in_=EQ1, axis=AX.X), tk)
            tk = ch(lambda e, s1=s1: e.reciprocal(out=s1, in_=s1), tk)
            tk = ch(lambda e, tt=tt, EQ1=EQ1, LM=LM: e.scalar_tensor_tensor(
                out=LM, in0=EQ1, scalar=-1e9, in1=LG[:, tt, :],
                op0=ALU.mult, op1=ALU.add), tk)
            tk = ch(lambda e, m2=m2, LM=LM: e.reduce_max(
                out=m2, in_=LM, axis=AX.X), tk)
            tk = ch(lambda e, dd=dd, m2=m2, m1=m1: e.tensor_tensor(
                out=dd, in0=m2, in1=m1, op=ALU.subtract), tk)
            p.wait("sc", tk)
            etok = p.op("sc", lambda e, e2=e2, dd=dd: e.activation(
                out=e2, in_=dd, func=AF.Exp), sig=True)
            p.wait("ve", etok)
            tk = p.op("ve", lambda e, w1=w1, e2=e2: e.tensor_scalar(
                out=w1, in0=e2, scalar1=1.0, scalar2=None, op0=ALU.add), sig=True)
            tk = ch(lambda e, w1=w1: e.reciprocal(out=w1, in_=w1), tk)
            tk = ch(lambda e, w2=w2, e2=e2, w1=w1: e.tensor_tensor(
                out=w2, in0=e2, in1=w1, op=ALU.mult), tk)
            tk = ch(lambda e, m2=m2, LM=LM: e.tensor_scalar(
                out=LM, in0=LM, scalar1=m2, scalar2=None, op0=ALU.is_equal), tk)
            tk = ch(lambda e, s2=s2, LM=LM: e.reduce_sum(
                out=s2, in_=LM, axis=AX.X), tk)
            tk = ch(lambda e, s2=s2: e.reciprocal(out=s2, in_=s2), tk)
            tk = ch(lambda e, w2=w2, s2=s2: e.tensor_tensor(
                out=w2, in0=w2, in1=s2, op=ALU.mult), tk)
            tk = ch(lambda e, w1=w1, s1=s1: e.tensor_tensor(
                out=w1, in0=w1, in1=s1, op=ALU.mult), tk)
            tk = ch(lambda e, tt=tt, EQ1=EQ1, w1=w1: e.tensor_scalar(
                out=EQ1, in0=EQ1, scalar1=w1, scalar2=None, op0=ALU.mult), tk)
            tk = ch(lambda e, tt=tt, LM=LM, w2=w2, EQ1=EQ1: e.scalar_tensor_tensor(
                out=W8[:, tt, :], in0=LM, scalar=w2, in1=EQ1,
                op0=ALU.mult, op1=ALU.add), tk)
        w8_done = p.tok("vesig")

        rot = Rot(p, [0, 1, 2, 3])
        shp = [0]

        def load_ex(i, par):
            e_, fc = i // 4, i % 4
            return p.dma("gp", a.A5[:, par],
                         rr(w["expert_w"][e_][:, fc * 512:(fc + 1) * 512]),
                         sem=f"exl{par}")

        def body_ex(i, par, ltok):
            e_, fc = i // 4, i % 4
            p.wait("te", ltok)
            if i == 0:
                p.wait("te", w8_done)
            last = None
            for tt in range(4):
                j, bank, ftok = rot.get()
                p.wait("te", ftok)
                p.op("te", lambda e, e_=e_, fc=fc, tt=tt, bank=bank: e.matmul(
                    out=a.PS[bank][:], lhsT=a.ONES[0:1, 0:128],
                    rhs=EB[0:1, e_ * 2048 + fc * 512: e_ * 2048 + (fc + 1) * 512],
                    start=True, stop=False))
                for kt in range(KT):
                    last = p.op("te", lambda e, kt=kt, tt=tt, par=par, bank=bank: e.matmul(
                        out=a.PS[bank][:], lhsT=a.A3[:, kt, tt * 128:(tt + 1) * 128],
                        rhs=a.A5[:, par, kt, :], start=False,
                        stop=(kt == KT - 1)), sig=(kt == KT - 1))
                p.wait("sc", last)
                sp = shp[0] % 2
                shp[0] += 1
                p.wait("sc", p.tok("vesig") if shp[0] > 2 else None)
                sg = p.op("sc", lambda e, bank=bank, sp=sp: e.activation(
                    out=SH[sp], in_=a.PS[bank][:], func=AF.Silu), sig=True)
                p.wait("ve", sg)
                if e_ == 0:
                    ev = p.op("ve", lambda e, tt=tt, fc=fc, sp=sp: e.tensor_scalar(
                        out=ACC[:, tt, fc * 512:(fc + 1) * 512], in0=SH[sp],
                        scalar1=W8[:, tt, 0:1], scalar2=None, op0=ALU.mult),
                        sig=True)
                else:
                    ev = p.op("ve", lambda e, tt=tt, fc=fc, sp=sp, e_=e_: e.scalar_tensor_tensor(
                        out=ACC[:, tt, fc * 512:(fc + 1) * 512], in0=SH[sp],
                        scalar=W8[:, tt, e_:e_ + 1],
                        in1=ACC[:, tt, fc * 512:(fc + 1) * 512],
                        op0=ALU.mult, op1=ALU.add), sig=True)
                rot.done(j, ev)
            return last

        stream_slabs(p, 32, load_ex, body_ex, q="gp")
        acc_done = p.tok("vesig")

        p.wait("sy", acc_done)
        accst = p.dma("sy", rr(acc_scr[blk * TB:(blk + 1) * TB, :]), ACC[:],
                      sem="lns")
        p.wait("sy", accst)
        p.wait("sy", p.tok("tesig"))
        fmt = None
        for kt in range(KT):
            fmt = p.dma("sy", a.A3[:, kt, :],
                        acc_scr[blk * TB:(blk + 1) * TB, kt * 128:(kt + 1) * 128],
                        transpose=True, sem="fmt")
        last_store = None
        for tth in range(2):
            rot2 = Rot(p, [0, 1, 2, 3])
            yev = None

            def load_pj(cc, par, tth=tth):
                return p.dma("gp", a.A5[:, par],
                             rr(w["proj_w"][:, cc * 512:(cc + 1) * 512]),
                             sem=f"exl{par}")

            def body_pj(cc, par, ltok, tth=tth):
                nonlocal yev
                p.wait("te", ltok)
                if cc == 0:
                    p.wait("te", fmt)
                    p.wait("te", p.tok("vesig"))
                last = None
                for tt2 in range(2):
                    tt = tth * 2 + tt2
                    j, bank, ftok = rot2.get()
                    p.wait("te", ftok)
                    p.op("te", lambda e, cc=cc, bank=bank: e.matmul(
                        out=a.PS[bank][:], lhsT=a.ONES[0:1, 0:128],
                        rhs=PRJB[0:1, cc * 512:(cc + 1) * 512],
                        start=True, stop=False))
                    for ft in range(KT):
                        last = p.op("te", lambda e, ft=ft, tt=tt, par=par, bank=bank: e.matmul(
                            out=a.PS[bank][:],
                            lhsT=a.A3[:, ft, tt * 128:(tt + 1) * 128],
                            rhs=a.A5[:, par, ft, :], start=False,
                            stop=(ft == KT - 1)), sig=(ft == KT - 1))
                    p.wait("ve", last)
                    p.wait("ve", prior)
                    yev = p.op("ve", lambda e, tt2=tt2, tt=tt, cc=cc, bank=bank: e.tensor_tensor(
                        out=Y[:, tt2, cc * 512:(cc + 1) * 512], in0=a.PS[bank][:],
                        in1=a.A2[:, tt, cc * 512:(cc + 1) * 512], op=ALU.add),
                        sig=True)
                    rot2.done(j, yev)
                return last

            stream_slabs(p, 4, load_pj, body_pj, q="gp")
            for tt2 in range(2):
                tt = tth * 2 + tt2
                Yt = Y[:, tt2, :]
                p.wait("ve", last_store)
                p.wait("ve", prior)
                p.wait("ve", yev)
                tk2 = p.op("ve", lambda e, Yt=Yt: e.scalar_tensor_tensor(
                    out=XN, in0=Yt, scalar=1.0, in1=Yt,
                    op0=ALU.mult, op1=ALU.mult, accum_out=q2), sig=True)
                p.wait("ve", tk2)
                tk2 = p.op("ve", lambda e: e.tensor_scalar(
                    out=rq, in0=q2, scalar1=1.0 / D, scalar2=EPS_MOE,
                    op0=ALU.mult, op1=ALU.add), sig=True)
                p.wait("sc", tk2)
                sq = p.op("sc", lambda e: e.activation(
                    out=rq, in_=rq, func=AF.Sqrt), sig=True)
                p.wait("ve", sq)
                tk2 = p.op("ve", lambda e: e.reciprocal(out=rq, in_=rq), sig=True)
                p.wait("ve", tk2)
                tk2 = p.op("ve", lambda e: e.tensor_scalar(
                    out=rq, in0=rq, scalar1=0.5, scalar2=None, op0=ALU.mult),
                    sig=True)
                p.wait("ve", tk2)
                tk2 = p.op("ve", lambda e, Yt=Yt: e.scalar_tensor_tensor(
                    out=XN, in0=Yt, scalar=rq, in1=NW,
                    op0=ALU.mult, op1=ALU.mult), sig=True)
                p.wait("ve", tk2)
                xntok = p.op("ve", lambda e, tt=tt: e.tensor_tensor(
                    out=XN, in0=XN, in1=a.A1[:, tt, :], op=ALU.add), sig=True)
                p.wait("gp", xntok)
                last_store = p.dma(
                    "gp", dst_x[blk * TB + tt * 128: blk * TB + (tt + 1) * 128, :],
                    XN, sem="xst")
        prior = last_store
    return prior



def emit_l1attn(p, a, src_x, dst_x, agin_ln, agout_ln, proj_part, rs_out, w, prior0=None):
    KTf = a.f4(0, 8192)
    KTM = a.v4(8192, 64, 128)
    QH = a.v4(16384, 4, 512)
    E2 = [a.f4(18432, 512), a.f4(18944, 512)]
    RBF = a.f4(19456, 512)
    YS4 = a.v4(19968, 4, 512)
    RB16 = a.r4(22016, 512)
    WPJ = a.A4[:, 23040:23040 + 8192].rearrange(
        "p (a c d) -> p a c d", a=4, c=4)
    RrowF = a.A6[0:1, 2560:3072]
    WL = a.WB[:, 0:2048].rearrange("p (a b) -> p a b", a=16)

    prior = prior0
    for blk in range(NBLK):
        xt, lntok = emit_ln_block(p, a, src_x, blk, ln_rows=(agin_ln, 0),
                                  prior=prior)
        prior = lntok
    p.wait("gp", prior)
    cct = p.cc("AllGather", [agin_ln[:]], [agout_ln[:]], "ccln")

    p.wait("gp", p.tok("tesig"))
    p.wait("gp", p.tok("vesig"))
    p.wait("gp", p.tok("scsig"))
    cw1 = p.dma("gp", WL, rr(w["W_lat"]))
    cw2 = p.dma("gp", a.A5[:, 0], rr(w["W_d_slice"]))
    cw3 = p.dma("gp", WPJ,
                rr(w["W_proj_slice"]).rearrange("p a (c d) -> p a c d", d=512))
    p.wait("sy", cct)

    STG = a.v6b(0, 4, 512)
    prev_q_evac = None
    prev_e2g = [None, None]
    prev_ab = None
    prev_ys = None
    prev_rc = None
    prev_head_exps = [None, None]
    prev_stg_store = None
    ktm_ev = [None, None]
    for sc in range(16):
        p.wait("sy", p.tok("tesig"))
        fmt = None
        for kt in range(KT):
            fmt = p.dma("sy", a.A3[:, kt, :],
                        agout_ln[sc * 512:(sc + 1) * 512,
                                 kt * 128:(kt + 1) * 128],
                        transpose=True, sem="fmt")
        p.wait("te", fmt)
        p.wait("te", prev_q_evac)
        p.wait("te", cw1)
        if sc == 0:
            p.wait("te", cw2)
            p.wait("te", cw3)
        km = None
        for kt in range(KT):
            km = p.op("te", lambda e, kt=kt, sc=sc: e.matmul(
                out=a.PS[0][:], lhsT=WL[:, kt, :], rhs=a.A3[:, kt, :],
                start=(kt == 0), stop=(kt == KT - 1)), sig=(kt == KT - 1))
        p.wait("sc", km)
        kcopy = p.op("sc", lambda e, sc=sc: e.activation(
            out=KTf[:, sc * 512:(sc + 1) * 512], in_=a.PS[0][:], func=AF.Identity,
            bias=a.BL[:, 0:1]), sig=True)
        par = sc % 2
        p.wait("te", kcopy)
        p.wait("te", ktm_ev[par])
        tpt = None
        for ss in range(4):
            tpt = p.op("te", lambda e, sc=sc, ss=ss, par=par: e.transpose(
                out=a.PB[par][:, ss * 128:(ss + 1) * 128],
                in_=KTf[:, sc * 512 + ss * 128: sc * 512 + (ss + 1) * 128],
                identity=a.ID[:]), sig=(ss == 3))
        p.wait("ve", tpt)
        ktm_ev[par] = p.op("ve", lambda e, sc=sc, par=par: e.tensor_copy(
            out=KTM[:, 4 * sc:4 * sc + 4, :].rearrange("p a b -> p (a b)"),
            in_=a.PB[par][:, 0:512]), sig=True)
        qe = None
        for h in range(4):
            p.wait("te", prev_q_evac)
            qm = None
            for kt in range(KT):
                qm = p.op("te", lambda e, kt=kt, h=h: e.matmul(
                    out=a.PS[1][:], lhsT=a.A5[:, 0, kt, h * 128:(h + 1) * 128],
                    rhs=a.A3[:, kt, :], start=(kt == 0), stop=(kt == KT - 1)),
                    sig=(kt == KT - 1))
            p.wait("sc", qm)
            qe = p.op("sc", lambda e, h=h: e.activation(
                out=QH[:, h, :], in_=a.PS[1][:], func=AF.Identity,
                bias=a.BD[:, h:h + 1]), sig=True)
            prev_q_evac = qe
        nsb = 4 * (sc + 1)
        p.wait("te", ktm_ev[par])
        ACCE = a.v6(3072, 512)
        for h in range(4):
            if h == 0:
                p.wait("te", qe)
            eready = {}
            ex_tok = {}
            av_tok = {}
            acc_tok = {}
            st_tok = {}
            for step in range(nsb + 1):
                if step < nsb:
                    sb = step
                    bpar = sb % 2
                    bank = a.PS[2 + bpar]
                    p.wait("te", ex_tok.get(sb - 2))
                    if sb < 2:
                        p.wait("te", prev_head_exps[sb])
                    st_tok[sb] = p.op("te", lambda e, sb=sb, h=h, bank=bank: e.matmul(
                        out=bank[:], lhsT=KTf[:, sb * 128:(sb + 1) * 128],
                        rhs=QH[:, h, :], start=True, stop=True), sig=True)
                    p.wait("sc", st_tok[sb])
                    p.wait("sc", av_tok.get(sb - 2))
                    p.wait("sc", acc_tok.get(sb - 2))
                    if sb < 2:
                        p.wait("sc", prev_e2g[sb])
                    ex = p.op("sc", lambda e, bpar=bpar, bank=bank: e.activation(
                        out=E2[bpar], in_=bank[:], func=AF.Exp, scale=ISQL),
                        sig=True)
                    ex_tok[sb] = ex
                    v = sb - 4 * sc
                    if v >= 0:
                        p.wait("ve", ex)
                        mk = p.op("ve", lambda e, bpar=bpar, v=v: e.tensor_tensor(
                            out=E2[bpar], in0=E2[bpar],
                            in1=a.MSK[:, 512 - v * 128:1024 - v * 128],
                            op=ALU.mult), sig=True)
                        eready[sb] = mk
                    else:
                        eready[sb] = ex
                    p.wait("ve", eready[sb])
                    p.wait("ve", acc_tok.get(sb - 1))
                    if sb == 0:
                        p.wait("ve", prev_ab)
                        acc_tok[sb] = p.op("ve", lambda e, bpar=bpar: e.tensor_scalar(
                            out=ACCE, in0=E2[bpar], scalar1=1.0, scalar2=None,
                            op0=ALU.mult), sig=True)
                    else:
                        acc_tok[sb] = p.op("ve", lambda e, bpar=bpar: e.tensor_tensor(
                            out=ACCE, in0=ACCE, in1=E2[bpar], op=ALU.add),
                            sig=True)
                if step >= 1:
                    sb2 = step - 1
                    bpar2 = sb2 % 2
                    p.wait("te", eready[sb2])
                    if sb2 == 0:
                        p.wait("te", prev_ys)
                    av_tok[sb2] = p.op("te", lambda e, sb2=sb2, bpar2=bpar2, nsb=nsb: e.matmul(
                        out=a.PS[4][:], lhsT=KTM[:, sb2, :], rhs=E2[bpar2],
                        start=(sb2 == 0), stop=(sb2 == nsb - 1)), sig=True)
            prev_e2g = [av_tok[nsb - 2], av_tok[nsb - 1]]
            p.wait("ve", acc_tok[nsb - 1])
            p.wait("ve", prev_stg_store)
            ab = p.op("ve", lambda e: e.tensor_copy(out=RB16_acc(a), in_=ACCE),
                      sig=True)
            prev_ab = ab
            p.wait("te", ab)
            p.wait("te", prev_rc)
            dnm = p.op("te", lambda e: e.matmul(
                out=a.PS[5][0:1, :], lhsT=a.ONES[:, 0:1], rhs=RB16_acc(a),
                start=True, stop=True), sig=True)
            prev_head_exps = [eready.get(nsb - 2), eready.get(nsb - 1)]
            prev_head_exps = [eready.get(nsb - 2), eready.get(nsb - 1)]
            p.wait("ve", dnm)
            t6 = p.op("ve", lambda e: e.reciprocal(
                out=RrowF, in_=a.PS[5][0:1, :]), sig=True)
            prev_rc = t6
            p.wait("ve", t6)
            rv = p.op("ve", lambda e: e.tensor_copy(out=RB16, in_=RrowF),
                      sig=True)
            p.wait("te", rv)
            p.wait("te", prev_q_evac)
            bc = p.op("te", lambda e: e.matmul(
                out=a.PS[1][:], lhsT=a.ONES[0:1, :], rhs=RB16,
                start=True, stop=True), sig=True)
            p.wait("sc", bc)
            rbf = p.op("sc", lambda e: e.copy(out=RBF, in_=a.PS[1][:]), sig=True)
            prev_q_evac = rbf
            p.wait("ve", rbf)
            p.wait("ve", av_tok[nsb - 1])
            ys = p.op("ve", lambda e, h=h: e.tensor_tensor(
                out=YS4[:, h, :], in0=a.PS[4][:], in1=RBF, op=ALU.mult),
                sig=True)
            prev_ys = ys
        ys_done = prev_ys
        p.wait("te", ys_done)
        STGB = [a.f4(31232, 512), a.f4(31744, 512), a.f4(32256, 512),
                a.f4(22528, 512)]
        for cc_ in range(4):
            p.wait("ve", prev_stg_store)
            last = None
            evs = None
            for tt in range(4):
                bank = cc_ % 2
                p.wait("te", evs)
                for h in range(4):
                    last = p.op("te", lambda e, h=h, tt=tt, cc_=cc_, bank=bank: e.matmul(
                        out=a.PS[bank][:],
                        lhsT=YS4[:, h, tt * 128:(tt + 1) * 128],
                        rhs=WPJ[:, h, cc_, :],
                        start=(h == 0), stop=(h == 3)), sig=(h == 3))
                p.wait("ve", last)
                evs = p.op("ve", lambda e, tt=tt, bank=bank: e.tensor_copy(
                    out=STGB[tt], in_=a.PS[bank][:]), sig=True)
            p.wait("gp", evs)
            pp = rr(proj_part[sc * 512:(sc + 1) * 512,
                              cc_ * 512:(cc_ + 1) * 512])
            for tt in range(4):
                prev_stg_store = p.dma("gp", pp[:, tt:tt + 1, :], STGB[tt],
                                       sem="ppst")
        prev_q_evac = p.tok("vesig")

    p.wait("gp", prev_stg_store)
    ccr = p.cc("ReduceScatter", [proj_part[:]], [rs_out[:]], "ccrs", op=ALU.add)
    p.wait("gp", ccr)

    p._a4_free = (p.tok("tesig"), p.tok("vesig"), p.tok("scsig"))
    bp2 = p.dma("gp", a.r4(0, 2048), w["b_proj"], sem="bpl")
    p.wait("te", bp2)
    p.wait("te", p.tok("vesig"))
    p.wait("te", p.tok("scsig"))
    bft = None
    for cc_ in range(4):
        p.wait("te", bft)
        bt = p.op("te", lambda e, cc_=cc_: e.matmul(
            out=a.PS[1][:], lhsT=a.ONES[0:1, :],
            rhs=a.r4(cc_ * 512, 512), start=True, stop=True), sig=True)
        p.wait("sc", bt)
        bft = p.op("sc", lambda e, cc_=cc_: e.copy(
            out=a.A2[:, 0, cc_ * 512:(cc_ + 1) * 512], in_=a.PS[1][:]),
            sig=True)

    RS6 = a.v6(0, 2048)
    XN6 = a.v6(2048, 2048)
    prior = None
    for blk in range(NBLK):
        p.wait("gp", prior)
        xt = p.dma("gp", a.A1[:], rr(src_x[blk * TB:(blk + 1) * TB, :]))
        for tt in range(4):
            r0 = blk * TB + tt * 128
            lt = p.dma("gp", RS6, rs_out[r0:r0 + 128, :], sem="rsl")
            p.wait("ve", lt)
            p.wait("ve", xt)
            p.wait("ve", bft)
            p.wait("ve", prior)
            t7 = p.op("ve", lambda e: e.tensor_tensor(
                out=XN6, in0=RS6, in1=a.A2[:, 0, :], op=ALU.add), sig=True)
            p.wait("ve", t7)
            t8 = p.op("ve", lambda e, tt=tt: e.tensor_tensor(
                out=XN6, in0=XN6, in1=a.A1[:, tt, :], op=ALU.add), sig=True)
            p.wait("gp", t8)
            prior = p.dma("gp", dst_x[r0:r0 + 128, :], XN6, sem="xst")
    return prior


def RB16_acc(a):
    return a.f4(22528, 512)



WT_SPECS = {
    "gup": ("L0_ffn_w_gate_up", 16384, 2048),
    "wdn": ("L0_ffn_w_down", 2048, 8192),
    "win": ("L0_attn_W_in", 2048, 2048),
    "wgt": ("L0_attn_W_gate", 8192, 2048),
    "wout": ("L0_attn_W_out", 2048, 2048),
}

def wt_perm(name):
    if name == "gup":
        return [(i // 2) + 64 * (i % 2) for i in range(128)]
    if name == "wgt":
        perm = []
        for i in range(64):
            head, jj, ty = i // 16, (i // 4) % 4, i % 4
            perm.append(head * 16 + ty * 4 + jj)
        return perm
    return None


def build_program(secs, inputs_present):
    p = Prog()
    nc = p.nc
    stack = ExitStack()
    a = Arenas(nc, stack)

    def din(name, shape):
        return nc.dram_tensor(name, shape, F32, kind="ExternalInput").ap()

    xin = din("xin", [T, D])
    xout = nc.dram_tensor("xout", [T, D], F32, kind="ExternalOutput").ap()

    idt = nc.dram_tensor("c_ident", [128, 128], F32, kind="ExternalInput").ap()
    onest = nc.dram_tensor("c_ones", [128, 128], F32, kind="ExternalInput").ap()
    mskt = nc.dram_tensor("c_mask", [128, 1024], F32, kind="ExternalInput").ap()
    ld1 = p.dma("gp", a.ID[:], idt[:])
    ld2 = p.dma("gp", a.ONES[:], onest[:])
    ld3 = p.dma("gp", a.MSK[:], mskt[:])

    need = set()
    for s in secs:
        if s in (1, 3):
            need |= {"gup", "wdn"}
        if s == 2:
            need |= {"win", "wgt", "wout"}

    lnt = nc.dram_tensor("lnt", [T, D], BF16).ap()
    wt = {}
    for nm in sorted(need):
        src_name, O, K = WT_SPECS[nm]
        w_hbm = din(src_name, [O, K])
        dst = nc.dram_tensor(f"wt_{nm}", [K, O], BF16).ap()
        emit_wt_construct(p, a, w_hbm, None, dst, O, K, perm=wt_perm(nm))
        wt[nm] = dst
    for eng in ENGINES:
        p.wait(eng, p.tok("wts0"))
        p.wait(eng, p.tok("wts1"))

    n = len(secs)
    if n == 1:
        hops = [(xin, xout)]
    else:
        xs = [nc.dram_tensor(f"xs{i}", [T, D], F32).ap() for i in range(n - 1)]
        hops = []
        cur = xin
        for i in range(n - 1):
            hops.append((cur, xs[i]))
            cur = xs[i]
        hops.append((cur, xout))

    prior_tok = None
    for (s, (src, dst)) in zip(secs, hops):
        if s in (1, 3):
            prior_tok = emit_ffn0(p, a, src, dst, lnt, wt["gup"], wt["wdn"],
                                  prior0=prior_tok)
        elif s == 2:
            for nm2 in ("b_gate", "rms_w"):
                pass
            bgt = din("L0_attn_b_gate_r", [128, 64])
            rwt = din("L0_attn_rms_w_r", [128, 4])
            b1 = p.dma("gp", a.BG[:], bgt[:])
            b2 = p.dma("gp", a.RMSW[:], rwt[:])
            p.wait("sc", b2)
            cp_scr = nc.dram_tensor("cp_scr", [T, D], BF16).ap()
            cl_scr = nc.dram_tensor("cl_scr", [T, D], BF16).ap()
            o_scr = nc.dram_tensor("o_scr", [T, D], BF16).ap()
            agin = nc.dram_tensor("ag_scan_in", [128, 32], F32).ap()
            agout = nc.dram_tensor("ag_scan_out", [512, 32], F32).ap()
            scm = din("scanmask", [128, 3])
            b3 = p.dma("gp", a.A7[:, 112:115], scm[:])
            p.wait("ve", b3)
            prior_tok = emit_l0attn(p, a, src, dst, lnt, wt["win"], wt["wgt"],
                                    wt["wout"], cp_scr, cl_scr, o_scr, agin,
                                    agout, prior0=prior_tok)
        elif s in (4, 6):
            w = {
                "router_w": din("L1_ffn_router_w", [D, 8]),
                "router_b": din("L1_ffn_router_b_r", [1, 8]),
                "expert_w": din("L1_ffn_expert_w", [8, D, D]),
                "expert_b": din("L1_ffn_expert_b_r", [1, 16384]),
                "proj_w": din("L1_ffn_proj_w", [D, D]),
                "proj_b": din("L1_ffn_proj_b_r", [1, D]),
                "norm_w": din("L1_ffn_norm_w_r", [1, D]),
            } if s == 4 or 4 not in secs else build_program._moe_w
            if s == 4 or 4 not in secs:
                build_program._moe_w = w
            acc_scr = nc.dram_tensor(f"acc_scr{s}", [T, D], BF16).ap()
            prior_tok = emit_moe(p, a, src, dst, lnt, acc_scr, w, tag=f"m{s}",
                                 prior0=prior_tok)
        elif s == 5:
            w5 = {
                "W_lat": din("L1_attn_W_lat", [D, 128]),
                "W_d_slice": din("W_d_slice", [D, 512]),
                "W_proj_slice": din("W_proj_slice", [512, D]),
                "b_proj": din("L1_attn_b_proj_r", [1, D]),
            }
            blt = din("b_lat_r", [128, 1])
            bdt = din("b_d_slice_r", [128, 4])
            p.dma("gp", a.BL[:], blt[:])
            b4 = p.dma("gp", a.BD[:], bdt[:])
            p.wait("sc", b4)
            agin_ln = nc.dram_tensor("agin_ln", [T, D], BF16).ap()
            agout_ln = nc.dram_tensor("agout_ln", [SEQ, D], BF16).ap()
            proj_part = nc.dram_tensor("proj_part", [SEQ, D], BF16).ap()
            rs_out = nc.dram_tensor("rs_out", [T, D], BF16).ap()
            prior_tok = emit_l1attn(p, a, src, dst, agin_ln, agout_ln,
                                    proj_part, rs_out, w5, prior0=prior_tok)

    p.wait("gp", p.tok("xst"))
    p.wait("gp", p.tok("gpdma"))
    nc_final = p.finalize()
    stack.close()
    return nc_final



def _consts():
    ident = np.eye(128, dtype=np.float32)
    ones = np.ones((128, 128), dtype=np.float32)
    msk = np.zeros((128, 1024), dtype=np.float32)
    pp, qq = np.meshgrid(np.arange(128), np.arange(1024), indexing="ij")
    msk[pp <= qq - 512] = 1.0
    return ident, ones, msk


def _prep_core_inputs(inputs, core):
    b, me = core // 4, core % 4
    ident, ones, msk = _consts()
    d = {
        "c_ident": ident, "c_ones": ones, "c_mask": msk,
        "L0_ffn_w_gate_up": inputs["L0_ffn_w_gate_up"],
        "L0_ffn_w_down": inputs["L0_ffn_w_down"],
        "L0_attn_W_in": inputs["L0_attn_W_in"],
        "L0_attn_W_gate": inputs["L0_attn_W_gate"],
        "L0_attn_W_out": inputs["L0_attn_W_out"],
        "L0_attn_b_gate_r": np.ascontiguousarray(
            inputs["L0_attn_b_gate"].reshape(64, 128).T),
        "L0_attn_rms_w_r": np.ascontiguousarray(
            inputs["L0_attn_rms_w"].reshape(4, 128).T),
        "scanmask": np.broadcast_to(
            (np.arange(3) < me).astype(np.float32), (128, 3)).copy(),
        "L1_ffn_router_w": inputs["L1_ffn_router_w"],
        "L1_ffn_router_b_r": inputs["L1_ffn_router_b"].reshape(1, 8),
        "L1_ffn_expert_w": inputs["L1_ffn_expert_w"],
        "L1_ffn_expert_b_r": inputs["L1_ffn_expert_b"].reshape(1, 16384),
        "L1_ffn_proj_w": inputs["L1_ffn_proj_w"],
        "L1_ffn_proj_b_r": inputs["L1_ffn_proj_b"].reshape(1, 2048),
        "L1_ffn_norm_w_r": inputs["L1_ffn_norm_w"].reshape(1, 2048),
        "L1_attn_W_lat": inputs["L1_attn_W_lat"],
        "W_d_slice": np.ascontiguousarray(
            inputs["L1_attn_W_d"][:, me * 512:(me + 1) * 512]),
        "W_proj_slice": np.ascontiguousarray(
            inputs["L1_attn_W_proj"][me * 512:(me + 1) * 512, :]),
        "L1_attn_b_proj_r": inputs["L1_attn_b_proj"].reshape(1, 2048),
        "b_lat_r": inputs["L1_attn_b_lat"].reshape(128, 1),
        "b_d_slice_r": np.ascontiguousarray(
            inputs["L1_attn_b_d"][me * 512:(me + 1) * 512].reshape(4, 128).T),
    }
    return {k: np.ascontiguousarray(np.asarray(v, dtype=np.float32))
            for k, v in d.items()}


_IN_BY_SEC = {
    1: ["L0_ffn_w_gate_up", "L0_ffn_w_down"],
    2: ["L0_attn_W_in", "L0_attn_W_gate", "L0_attn_W_out",
        "L0_attn_b_gate_r", "L0_attn_rms_w_r", "scanmask"],
    3: ["L0_ffn_w_gate_up", "L0_ffn_w_down"],
    4: ["L1_ffn_router_w", "L1_ffn_router_b_r", "L1_ffn_expert_w",
        "L1_ffn_expert_b_r", "L1_ffn_proj_w", "L1_ffn_proj_b_r",
        "L1_ffn_norm_w_r"],
    5: ["L1_attn_W_lat", "W_d_slice", "W_proj_slice", "L1_attn_b_proj_r",
        "b_lat_r", "b_d_slice_r"],
    6: ["L1_ffn_router_w", "L1_ffn_router_b_r", "L1_ffn_expert_w",
        "L1_ffn_expert_b_r", "L1_ffn_proj_w", "L1_ffn_proj_b_r",
        "L1_ffn_norm_w_r"],
}


def run_chain(inputs, groups):
    x = np.asarray(inputs["L0_attn_x"], dtype=np.float32)
    shards = [np.ascontiguousarray(x[c // 4, (c % 4) * T:(c % 4 + 1) * T, :])
              for c in range(8)]
    per_core = [_prep_core_inputs(inputs, c) for c in range(8)]
    for gi, secs in enumerate(groups):
        if hasattr(build_program, "_moe_w"):
            del build_program._moe_w
        nc = build_program(secs, None)
        names = set()
        for s in secs:
            names.update(_IN_BY_SEC[s])
        names |= {"c_ident", "c_ones", "c_mask"}
        in_maps = []
        for c in range(8):
            m = {k: per_core[c][k] for k in names if k in per_core[c]}
            m["xin"] = shards[c]
            in_maps.append(m)
        res = run_bass_kernel_spmd(nc, in_maps, core_ids=list(range(8)))
        shards = [np.asarray(res.results[c]["xout"], dtype=np.float32)
                  for c in range(8)]
    out = np.empty((2, SEQ, D), dtype=np.float32)
    for c in range(8):
        out[c // 4, (c % 4) * T:(c % 4 + 1) * T, :] = shards[c]
    return out


def kernel(**inputs):
    if FUSED:
        return run_chain(inputs, [[1, 2, 3, 4, 5, 6]])
    return run_chain(inputs, [[1], [2], [3], [4], [5], [6]])


# version 06 (final)
# speedup vs baseline: 1.3954x; 1.0213x over previous
import numpy as np
import ml_dtypes
from contextlib import ExitStack
import concourse.bass as bass
import concourse.mybir as mybir
from concourse.bass_utils import run_bass_kernel_spmd

F32 = mybir.dt.float32
BF16 = mybir.dt.bfloat16
AF = mybir.ActivationFunctionType
ALU = mybir.AluOpType
AX = mybir.AxisListType

ENGINES = ("te", "ve", "sc", "gp", "sy")
RG = [[0, 1, 2, 3], [4, 5, 6, 7]]

D = 2048
T = 2048
TB = 512
NBLK = 4
KT = 16
SEQ = 8192
EPS_LN = 1e-5
EPS_S2 = 1e-6
EPS_MOE = 1e-6
ISQL = float(1.0 / np.sqrt(128.0))

FUSED = True



class Prog:
    def __init__(self):
        self.nc = bass.Bass("TRN2", target_bir_lowering=False,
                            disable_frame_to_traceback=True)
        self.ops = {e: [] for e in ENGINES}
        self.sems = {}
        self.sem_order = []
        self._h = {}

    def _bump(self, name, n):
        if name not in self.sems:
            self.sems[name] = 0
            self.sem_order.append(name)
        self.sems[name] += n
        return (name, self.sems[name])

    def tok(self, name):
        return (name, self.sems.get(name, 0))

    def wait(self, eng, tok):
        if tok is None:
            return
        name, val = tok
        if val <= 0:
            return
        assert val <= self.sems.get(name, 0), f"wait beyond incs: {name} {val}"
        self.ops[eng].append(lambda e, n=name, v=val: e.wait_ge(self._h[n], v))

    def dma(self, q, out, in_, transpose=False, sem=None):
        s = sem or (q + "dma")
        tok = self._bump(s, 16)
        def fn(e, out=out, in_=in_, t=transpose, n=s):
            e.dma_start(out=out, in_=in_, transpose=t).then_inc(self._h[n], 16)
        self.ops[q].append(fn)
        return tok

    def op(self, eng, fn, sig=False):
        if not sig:
            self.ops[eng].append(fn)
            return None
        s = eng + "sig"
        tok = self._bump(s, 1)
        def wrapped(e, n=s):
            fn(e).then_inc(self._h[n], 1)
        self.ops[eng].append(wrapped)
        return tok

    def cc(self, kind, ins, outs, name, op=None):
        tok = self._bump(name, 1)
        aop = op if op is not None else ALU.bypass
        def fn(e, n=name):
            e.collective_compute(kind, aop, replica_groups=RG,
                                 ins=ins, outs=outs).then_inc(self._h[n])
        self.ops["gp"].append(fn)
        return tok

    def finalize(self):
        nc = self.nc
        stack = ExitStack()
        for n in self.sem_order:
            self._h[n] = stack.enter_context(nc.semaphore(n))
        block = stack.enter_context(nc.Block())
        decos = {"te": block.tensor, "ve": block.vector, "sc": block.scalar,
                 "gp": block.gpsimd, "sy": block.sync}
        for name in ENGINES:
            def mk(oplist):
                def body(e):
                    for fn in oplist:
                        fn(e)
                return body
            decos[name](mk(self.ops[name]))
        stack.close()
        return nc


class Rot:
    def __init__(self, p, banks):
        self.p, self.banks = p, banks
        self.free = [None] * len(banks)
        self.i = 0
    def get(self):
        j = self.i % len(self.banks)
        self.i += 1
        return j, self.banks[j], self.free[j]
    def done(self, j, tok):
        self.free[j] = tok


def stream_slabs(p, n, make_load, body, q="sy"):
    loads = [None] * n
    free = {}
    def issue(i):
        if i - 2 >= 0:
            p.wait(q, free[i - 2])
        loads[i] = make_load(i, i % 2)
    p.wait(q, p.tok("tesig"))
    if n > 0:
        issue(0)
    if n > 1:
        issue(1)
    for i in range(n):
        free[i] = body(i, i % 2, loads[i])
        assert free[i] is not None
        if i + 2 < n:
            issue(i + 2)


class Arenas:
    def __init__(self, nc, stack):
        E = stack.enter_context
        self.A1 = E(nc.sbuf_tensor("A1", [128, NBLK, D], F32))
        self.A2 = E(nc.sbuf_tensor("A2", [128, NBLK, D], BF16))
        self.A3 = E(nc.sbuf_tensor("A3", [128, KT, TB], BF16))
        self.A4 = E(nc.sbuf_tensor("A4", [128, 32768], BF16))
        self.A5 = E(nc.sbuf_tensor("A5", [128, 2, KT, TB], BF16))
        self.A6 = E(nc.sbuf_tensor("A6", [128, 6144], F32))
        self.A7 = E(nc.sbuf_tensor("A7", [128, 128], F32))
        self.BG = E(nc.sbuf_tensor("BG", [128, 64], F32))
        self.RMSW = E(nc.sbuf_tensor("RMSW", [128, 4], F32))
        self.BD = E(nc.sbuf_tensor("BD", [128, 4], F32))
        self.BL = E(nc.sbuf_tensor("BL", [128, 1], F32))
        self.ID = E(nc.sbuf_tensor("ID", [128, 128], BF16))
        self.ONES = E(nc.sbuf_tensor("ONES", [128, 128], BF16))
        self.MSK = E(nc.sbuf_tensor("MSK", [128, 1024], BF16))
        self.WB = E(nc.sbuf_tensor("WB", [128, 2304], BF16))
        self.PS = [E(nc.psum_tensor(f"PS{i}", [128, 512], F32)) for i in range(6)]
        self.PB = [E(nc.psum_tensor(f"PB{i}", [128, 1024], BF16)) for i in range(2)]

    def v4(self, off, a_, b_):
        return self.A4[:, off:off + a_ * b_].rearrange("p (a b) -> p a b", a=a_)

    def f4(self, off, n):
        return self.A4[:, off:off + n]

    def r4(self, off, n):
        return self.A4[0:1, off:off + n]

    def v6(self, off, n):
        return self.A6[:, off:off + n]

    def v6b(self, off, a_, b_):
        return self.A6[:, off:off + a_ * b_].rearrange("p (a b) -> p a b", a=a_)


def rr(ap, p=128):
    return ap.rearrange("(a p) c -> p a c", p=p)



def emit_wt_construct(p, a, w_hbm, wb_unused, dst, O, K, perm=None):
    nob, nkb = O // 128, K // 128
    perm = perm or list(range(nob))
    inv = [0] * nob
    for i, pv in enumerate(perm):
        inv[pv] = i
    nb = 4 if K == 2048 else 1
    p.wait("gp", p.tok("tesig"))
    p.wait("te", p.tok("wts0"))
    p.wait("te", p.tok("wts1"))
    p.wait("te", p.tok("vesig"))
    buf = [a.A5[:, 0].rearrange("p a b -> p (a b)"),
           a.A5[:, 1].rearrange("p a b -> p (a b)")]
    stage = [a.A4[:, 0:8192], a.A4[:, 8192:16384]]
    ngrp = (nkb + 7) // 8
    load_free = [None, None]
    stage_free = [None, None]
    bank_free = [None, None]
    for i0 in range(0, nob, nb):
        lp = (i0 // nb) % 2
        sp = lp
        p.wait("gp", load_free[lp])
        ld = p.dma("gp", buf[lp][:, 0:nb * K].rearrange("p (g c) -> p g c", g=nb),
                   w_hbm[i0 * 128:(i0 + nb) * 128, :].rearrange(
                       "(g p) c -> p g c", p=128),
                   sem=f"wcl{lp}")
        p.wait("te", ld)
        p.wait("te", stage_free[sp])
        ev = None
        for g in range(nb):
            for gr in range(ngrp):
                nk = min(8, nkb - gr * 8)
                bp = (g * ngrp + gr) % 2
                p.wait("te", bank_free[bp])
                tp = None
                for j in range(nk):
                    k = gr * 8 + j
                    tp = p.op("te", lambda e, k=k, j=j, g=g, lp=lp, bp=bp: e.transpose(
                        out=a.PB[bp][:, j * 128:(j + 1) * 128],
                        in_=buf[lp][:, g * K + k * 128: g * K + (k + 1) * 128],
                        identity=a.ID[:]), sig=(j == nk - 1))
                p.wait("ve", tp)
                ev = p.op("ve", lambda e, g=g, gr=gr, nk=nk, bp=bp, sp=sp: e.tensor_copy(
                    out=stage[sp][:, g * (nkb * 128) + gr * 1024:
                                  g * (nkb * 128) + gr * 1024 + nk * 128],
                    in_=a.PB[bp][:, 0:nk * 128]), sig=True)
                bank_free[bp] = ev
        load_free[lp] = p.tok("tesig")
        if nb == 1:
            ob = inv[i0]
            p.wait("sc", ev)
            stage_free[sp] = p.dma(
                "sc", rr(dst)[:, :, ob * 128:(ob + 1) * 128],
                stage[sp][:, 0:nkb * 128].rearrange("p (a c) -> p a c", c=128),
                sem=f"wts{sp}")
        else:
            b0 = inv[i0]
            s = inv[i0 + 1] - b0 if nb > 1 else 1
            for j in range(nb):
                assert inv[i0 + j] == b0 + j * s, (i0, j, inv[i0 + j], b0, s)
            stv = stage[sp][:, 0:nb * nkb * 128].rearrange(
                "p (g a c) -> p g a c", g=nb, c=128)
            p.wait("sc", ev)
            for j in range(nb):
                ob = inv[i0 + j]
                stage_free[sp] = p.dma(
                    "sc", rr(dst)[:, :, ob * 128:(ob + 1) * 128],
                    stv[:, j], sem=f"wts{sp}")
    p.wait("sc", stage_free[0])
    p.wait("sc", stage_free[1])



def emit_ln_block(p, a, src_x, blk, lnt=None, to_fm=True, ln_rows=None,
                  prior=None):
    p.wait("sy", prior)
    xt = p.dma("sy", a.A1[:], rr(src_x[blk * TB:(blk + 1) * TB, :]), sem="xld")
    SB = a.v6b(5632, 16, 8)
    dummy = a.v6(0, 2048)
    p.wait("sc", prior)
    p.wait("ve", prior)
    atoks = []
    for tt in range(4):
        p.wait("ve", xt)
        p.wait("sc", xt)
        t0 = p.op("ve", lambda e, tt=tt: e.reduce_sum(
            out=SB[:, tt, 0:1], in_=a.A1[:, tt, :], axis=AX.X), sig=True)
        s_ssq = p.op("sc", lambda e, tt=tt: e.activation(
            out=dummy, in_=a.A1[:, tt, :], func=AF.Square,
            accum_out=SB[:, tt, 1:2]), sig=True)
        p.wait("ve", s_ssq)
        p.wait("ve", t0)
        t1 = p.op("ve", lambda e, tt=tt: e.tensor_scalar(
            out=SB[:, tt, 2:3], in0=SB[:, tt, 0:1], scalar1=1.0 / D,
            scalar2=None, op0=ALU.mult), sig=True)
        p.wait("ve", t1)
        t2 = p.op("ve", lambda e, tt=tt: e.tensor_tensor(
            out=SB[:, tt, 3:4], in0=SB[:, tt, 2:3], in1=SB[:, tt, 2:3],
            op=ALU.mult), sig=True)
        t3 = p.op("ve", lambda e, tt=tt: e.tensor_scalar(
            out=SB[:, tt, 4:5], in0=SB[:, tt, 1:2], scalar1=1.0 / D,
            scalar2=EPS_LN, op0=ALU.mult, op1=ALU.add), sig=True)
        p.wait("ve", t2)
        p.wait("ve", t3)
        vtok = p.op("ve", lambda e, tt=tt: e.tensor_tensor(
            out=SB[:, tt, 5:6], in0=SB[:, tt, 4:5], in1=SB[:, tt, 3:4],
            op=ALU.subtract), sig=True)
        p.wait("sc", vtok)
        stok = p.op("sc", lambda e, tt=tt: e.activation(
            out=SB[:, tt, 6:7], in_=SB[:, tt, 5:6], func=AF.Sqrt), sig=True)
        p.wait("ve", stok)
        t4 = p.op("ve", lambda e, tt=tt: e.reciprocal(
            out=SB[:, tt, 7:8], in_=SB[:, tt, 6:7]), sig=True)
        p.wait("ve", t4)
        atok = p.op("ve", lambda e, tt=tt: e.tensor_scalar(
            out=a.A2[:, tt, :], in0=a.A1[:, tt, :], scalar1=SB[:, tt, 2:3],
            scalar2=SB[:, tt, 7:8], op0=ALU.subtract, op1=ALU.mult), sig=True)
        atoks.append(atok)
    p.wait("sy", atoks[-1])
    if ln_rows is not None:
        dst, off = ln_rows
        lntok = p.dma("sy", rr(dst[off + blk * TB: off + (blk + 1) * TB, :]),
                      a.A2[:], sem="lns")
        return xt, lntok
    lntok = p.dma("sy", rr(lnt[blk * TB:(blk + 1) * TB, :]), a.A2[:], sem="lns")
    fm_tok = None
    if to_fm:
        p.wait("sy", lntok)
        p.wait("sy", p.tok("tesig"))
        for kt in range(KT):
            fm_tok = p.dma("sy", a.A3[:, kt, :],
                           lnt[blk * TB:(blk + 1) * TB, kt * 128:(kt + 1) * 128],
                           transpose=True, sem="fmt")
    return xt, fm_tok



def emit_ffn0(p, a, src_x, dst_x, lnt, gup_T, wdn_T, prior0=None):
    prior = prior0
    for blk in range(NBLK):
        xt, fm = emit_ln_block(p, a, src_x, blk, lnt=lnt, prior=prior)
        HM = a.v4(0, 64, 512)
        SG = a.v6(4096, 512)
        ve_all = p.tok("vesig")
        sc_all = p.tok("scsig")
        rot = Rot(p, [(0, 1), (2, 3)])
        hm_last = [None]

        def load_gu(i, par):
            return p.dma("sy", a.A5[:, par],
                         rr(gup_T)[:, :, i * 512:(i + 1) * 512],
                         sem=f"sl{par}")

        def body_gu(i, par, ltok):
            last_te = None
            for half in range(2):
                op = 2 * i + half
                j, pair, ftok = rot.get()
                pg, pu = a.PS[pair[0]], a.PS[pair[1]]
                p.wait("te", ftok)
                p.wait("te", ltok)
                if op == 0:
                    p.wait("te", fm)
                    p.wait("te", ve_all)
                    p.wait("te", sc_all)
                for kt in range(KT):
                    p.op("te", lambda e, kt=kt, par=par, half=half, pg=pg: e.matmul(
                        out=pg[:], lhsT=a.A5[:, par, kt, half * 256:half * 256 + 128],
                        rhs=a.A3[:, kt, :], start=(kt == 0), stop=(kt == KT - 1)))
                for kt in range(KT):
                    tok = p.op("te", lambda e, kt=kt, par=par, half=half, pu=pu: e.matmul(
                        out=pu[:], lhsT=a.A5[:, par, kt, half * 256 + 128:half * 256 + 256],
                        rhs=a.A3[:, kt, :], start=(kt == 0), stop=(kt == KT - 1)),
                        sig=(kt == KT - 1))
                    if kt == KT - 1:
                        last_te = tok
                p.wait("sc", last_te)
                sg = p.op("sc", lambda e, pg=pg: e.activation(
                    out=SG, in_=pg[:], func=AF.Silu), sig=True)
                p.wait("ve", sg)
                ev = p.op("ve", lambda e, op=op, pu=pu: e.tensor_tensor(
                    out=HM[:, op, :], in0=SG, in1=pu[:], op=ALU.mult), sig=True)
                rot.done(j, ev)
                hm_last[0] = ev
            return last_te

        stream_slabs(p, 32, load_gu, body_gu)
        hm_done = hm_last[0]

        STG = a.v6b(0, 4, 512)
        for fh in range(4):
            prev_ev = p.tok("vesig")

            def load_dn(g, par, fh=fh):
                return p.dma("sy", a.A5[:, par, 0:8, :],
                             rr(wdn_T)[:, g * 8:(g + 1) * 8,
                                       fh * 512:(fh + 1) * 512],
                             sem=f"sl{par}")

            def body_dn(g, par, ltok, fh=fh):
                p.wait("te", ltok)
                if g == 0:
                    p.wait("te", hm_done)
                    p.wait("te", prev_ev)
                tok = None
                for kk in range(8):
                    for tt in range(4):
                        sig = (kk == 7 and tt == 3)
                        tok2 = p.op("te", lambda e, g=g, kk=kk, tt=tt, par=par: e.matmul(
                            out=a.PS[tt][:],
                            lhsT=HM[:, g * 8 + kk, tt * 128:(tt + 1) * 128],
                            rhs=a.A5[:, par, kk, :],
                            start=(g == 0 and kk == 0),
                            stop=(g == 7 and kk == 7)), sig=sig)
                        if sig:
                            tok = tok2
                return tok

            mm_done = [None]

            def body_dn2(g, par, ltok, fh=fh):
                t = body_dn(g, par, ltok, fh)
                mm_done[0] = t
                return t

            stream_slabs(p, 8, load_dn, body_dn2)
            p.wait("ve", mm_done[0])
            p.wait("ve", prior)
            ev = None
            for tt in range(4):
                ev = p.op("ve", lambda e, tt=tt, fh=fh: e.scalar_tensor_tensor(
                    out=STG[:, tt, :], in0=a.PS[tt][:], scalar=0.5,
                    in1=a.A1[:, tt, fh * 512:(fh + 1) * 512],
                    op0=ALU.mult, op1=ALU.add), sig=(tt == 3))
            p.wait("gp", ev)
            prior = p.dma("gp",
                          rr(dst_x[blk * TB:(blk + 1) * TB,
                                   fh * 512:(fh + 1) * 512]), STG[:],
                          sem="xst")
    return prior



def emit_l0attn(p, a, src_x, dst_x, lnt, win_T, wgt_T, wout_T,
                cp_scr, cl_scr, o_scr, agin, agout, prior0=None):
    ZER = a.v6(3584, 512)
    p.op("ve", lambda e: e.memset(ZER, 0.0))
    CPC = a.A7[:, 64:80]
    CLC = a.A7[:, 80:96]
    CIN = a.A7[:, 96:112]
    SCM = a.A7[:, 112:115]
    p.op("ve", lambda e: e.memset(CPC, 1.0))
    p.op("ve", lambda e: e.memset(CLC, 0.0))

    XPF = a.v4(0, 16, 512)
    CPS = a.v4(8192, 16, 512)
    CLS = a.v4(16384, 16, 512)
    OS = a.v4(24576, 16, 512)
    T1, T2, T3, T4 = (a.v6(512, 512), a.v6(1024, 512), a.v6(1536, 512),
                      a.v6(2048, 512))

    prior = prior0
    for blk in range(NBLK):
        xt, fm = emit_ln_block(p, a, src_x, blk, lnt=lnt, prior=prior)
        ve_all = p.tok("vesig")
        sc_all = p.tok("scsig")
        rot = Rot(p, [0, 1, 2, 3])

        def load_xp(i, par):
            return p.dma("sy", a.A5[:, par],
                         rr(win_T)[:, :, i * 512:(i + 1) * 512], sem=f"sl{par}")

        def body_xp(i, par, ltok):
            last = None
            for q in range(4):
                ht = i * 4 + q
                j, bank, ftok = rot.get()
                p.wait("te", ftok)
                p.wait("te", ltok)
                if ht == 0:
                    p.wait("te", fm)
                    p.wait("te", ve_all)
                    p.wait("te", sc_all)
                for kt in range(KT):
                    last = p.op("te", lambda e, kt=kt, par=par, q=q, bank=bank: e.matmul(
                        out=a.PS[bank][:],
                        lhsT=a.A5[:, par, kt, q * 128:(q + 1) * 128],
                        rhs=a.A3[:, kt, :], start=(kt == 0), stop=(kt == KT - 1)),
                        sig=(kt == KT - 1))
                p.wait("sc", last)
                ev = p.op("sc", lambda e, ht=ht, bank=bank: e.copy(
                    out=XPF[:, ht, :], in_=a.PS[bank][:]), sig=True)
                rot.done(j, ev)
            return last

        stream_slabs(p, 4, load_xp, body_xp)
        xp_done = p.tok("scsig")

        def load_g(i, par):
            return p.dma("sy", a.A5[:, par],
                         rr(wgt_T)[:, :, i * 512:(i + 1) * 512], sem=f"sl{par}")

        def body_g(i, par, ltok):
            head, jj = i // 4, i % 4
            ct = i
            gt = head * 16
            p.wait("te", ltok)
            if i == 0:
                p.wait("te", xp_done)
            else:
                p.wait("te", p.tok("scsig"))
            tok = None
            for ty in range(4):
                for kt in range(KT):
                    sig = (ty == 3 and kt == KT - 1)
                    tok2 = p.op("te", lambda e, ty=ty, kt=kt, par=par: e.matmul(
                        out=a.PS[ty][:],
                        lhsT=a.A5[:, par, kt, ty * 128:(ty + 1) * 128],
                        rhs=XPF[:, kt, :], start=(kt == 0), stop=(kt == KT - 1)),
                        sig=sig)
                    if sig:
                        tok = tok2
            p.wait("sc", tok)
            c_i, c_f, c_o, c_c = gt + jj, gt + 4 + jj, gt + 8 + jj, gt + 12 + jj
            p.wait("sc", p.tok("vesig"))
            p.op("sc", lambda e, c=c_i: e.activation(
                out=T1, in_=a.PS[0][:], func=AF.Exp, bias=a.BG[:, c:c + 1]))
            p.op("sc", lambda e, c=c_c: e.activation(
                out=T2, in_=a.PS[3][:], func=AF.Tanh, bias=a.BG[:, c:c + 1]))
            p.op("sc", lambda e, c=c_f: e.activation(
                out=T4, in_=a.PS[1][:], func=AF.Sigmoid, bias=a.BG[:, c:c + 1]))
            sce = p.op("sc", lambda e, c=c_o, ct=ct: e.activation(
                out=OS[:, ct, :], in_=a.PS[2][:], func=AF.Sigmoid,
                bias=a.BG[:, c:c + 1]), sig=True)
            p.wait("ve", sce)
            v1 = p.op("ve", lambda e: e.tensor_tensor(out=T3, in0=T1, in1=T2,
                                                 op=ALU.mult), sig=True)
            v2 = p.op("ve", lambda e, ct=ct: e.tensor_tensor_scan(
                out=CPS[:, ct, :], data0=T4, data1=ZER,
                initial=CPC[:, ct:ct + 1], op0=ALU.mult, op1=ALU.add), sig=True)
            p.wait("ve", v2)
            v3 = p.op("ve", lambda e, ct=ct: e.tensor_copy(
                out=CPC[:, ct:ct + 1], in_=CPS[:, ct, 511:512]), sig=True)
            p.wait("ve", v1)
            v4 = p.op("ve", lambda e, ct=ct: e.tensor_tensor_scan(
                out=CLS[:, ct, :], data0=T4, data1=T3,
                initial=CLC[:, ct:ct + 1], op0=ALU.mult, op1=ALU.add), sig=True)
            p.wait("ve", v4)
            p.op("ve", lambda e, ct=ct: e.tensor_copy(
                out=CLC[:, ct:ct + 1], in_=CLS[:, ct, 511:512]), sig=True)
            return tok

        stream_slabs(p, 16, load_g, body_g)
        scans_done = p.tok("vesig")
        p.wait("gp", scans_done)
        p.dma("gp", rr(cp_scr)[:, :, blk * TB:(blk + 1) * TB], CPS[:])
        p.dma("gp", rr(cl_scr)[:, :, blk * TB:(blk + 1) * TB], CLS[:])
        prior = p.dma("gp", rr(o_scr)[:, :, blk * TB:(blk + 1) * TB], OS[:])

    agt = p.dma("gp", agin[:], a.A7[:, 64:96])
    p.wait("gp", agt)
    cct = p.cc("AllGather", [agin[:]], [agout[:]], "ccscan")
    p.wait("gp", cct)
    G = []
    gt_tok = None
    for r in range(4):
        Gr = a.A6[:, 4096 + r * 32: 4096 + (r + 1) * 32]
        G.append(Gr)
        gt_tok = p.dma("gp", Gr, agout[r * 128:(r + 1) * 128, :])
    p.wait("ve", gt_tok)
    TA = a.A6[:, 4224:4240]
    TBv = a.A6[:, 4240:4256]
    tk = p.op("ve", lambda e: e.memset(CIN, 0.0), sig=True)
    for r in range(3):
        p.wait("ve", tk)
        tk = p.op("ve", lambda e, r=r: e.tensor_scalar(
            out=TA, in0=G[r][:, 0:16], scalar1=1.0, scalar2=SCM[:, r:r + 1],
            op0=ALU.subtract, op1=ALU.mult), sig=True)
        p.wait("ve", tk)
        tk = p.op("ve", lambda e: e.tensor_scalar(
            out=TA, in0=TA, scalar1=1.0, scalar2=None, op0=ALU.add), sig=True)
        p.wait("ve", tk)
        tk = p.op("ve", lambda e: e.tensor_tensor(out=CIN, in0=CIN, in1=TA,
                                             op=ALU.mult), sig=True)
        p.wait("ve", tk)
        tk = p.op("ve", lambda e, r=r: e.tensor_scalar(
            out=TBv, in0=G[r][:, 16:32], scalar1=SCM[:, r:r + 1], scalar2=None,
            op0=ALU.mult), sig=True)
        p.wait("ve", tk)
        tk = p.op("ve", lambda e: e.tensor_tensor(out=CIN, in0=CIN, in1=TBv,
                                             op=ALU.add), sig=True)
    p.wait("ve", tk)
    cin_done = p.op("ve", lambda e: e.tensor_copy(out=TA, in_=CIN), sig=True)

    HFM = a.v4(0, 16, 512)
    Cr = a.v6b(0, 4, 512)
    CN = a.v6(2048, 512)
    RrowF = a.A6[0:1, 2560:3072]
    Rrow16 = a.r4(30720, 512)
    prior2 = p.tok("gpdma")
    prev_ps4 = None
    prev_ps5 = None
    for blk in range(NBLK):
        p.wait("gp", prior2)
        xt = p.dma("gp", a.A1[:], rr(src_x[blk * TB:(blk + 1) * TB, :]))
        p.dma("gp", CPS[:], rr(cp_scr)[:, :, blk * TB:(blk + 1) * TB])
        p.dma("gp", CLS[:], rr(cl_scr)[:, :, blk * TB:(blk + 1) * TB])
        l3 = p.dma("gp", OS[:], rr(o_scr)[:, :, blk * TB:(blk + 1) * TB])
        p.wait("ve", prior2)
        p.wait("ve", l3)
        p.wait("ve", cin_done)
        for head in range(4):
            ctok = None
            for jj in range(4):
                ct = head * 4 + jj
                ctok = p.op("ve", lambda e, ct=ct, jj=jj: e.scalar_tensor_tensor(
                    out=Cr[:, jj, :], in0=CPS[:, ct, :],
                    scalar=CIN[:, ct:ct + 1], in1=CLS[:, ct, :],
                    op0=ALU.mult, op1=ALU.add), sig=(jj == 3))
            p.wait("sc", ctok)
            sq_last = None
            for jj in range(4):
                ct = head * 4 + jj
                sq_last = p.op("sc", lambda e, ct=ct, jj=jj: e.activation(
                    out=HFM[:, ct, :], in_=Cr[:, jj, :], func=AF.Square),
                    sig=(jj == 3))
            p.wait("te", sq_last)
            p.wait("te", prev_ps4)
            mt = None
            for jj in range(4):
                ct = head * 4 + jj
                mt = p.op("te", lambda e, ct=ct, jj=jj: e.matmul(
                    out=a.PS[4][0:1, :], lhsT=a.ONES[:, 0:1],
                    rhs=HFM[:, ct, :], start=(jj == 0), stop=(jj == 3)),
                    sig=(jj == 3))
            p.wait("ve", mt)
            vt = p.op("ve", lambda e: e.tensor_scalar(
                out=RrowF, in0=a.PS[4][0:1, :], scalar1=1.0 / 512.0,
                scalar2=EPS_S2, op0=ALU.mult, op1=ALU.add), sig=True)
            prev_ps4 = vt
            p.wait("sc", vt)
            st2 = p.op("sc", lambda e: e.activation(
                out=RrowF, in_=RrowF, func=AF.Sqrt), sig=True)
            p.wait("ve", st2)
            t5 = p.op("ve", lambda e: e.reciprocal(out=RrowF, in_=RrowF),
                      sig=True)
            p.wait("ve", t5)
            rv = p.op("ve", lambda e: e.tensor_copy(out=Rrow16, in_=RrowF),
                      sig=True)
            p.wait("te", rv)
            p.wait("te", prev_ps5)
            bt = p.op("te", lambda e: e.matmul(
                out=a.PS[5][:], lhsT=a.ONES[0:1, :], rhs=Rrow16,
                start=True, stop=True), sig=True)
            p.wait("ve", bt)
            for jj in range(4):
                ct = head * 4 + jj
                cn = p.op("ve", lambda e, jj=jj: e.tensor_tensor(
                    out=CN, in0=Cr[:, jj, :], in1=a.PS[5][:], op=ALU.mult),
                    sig=True)
                if jj == 3:
                    prev_ps5 = cn
                p.wait("sc", cn)
                th = p.op("sc", lambda e, ct=ct, jj=jj: e.activation(
                    out=HFM[:, ct, :], in_=CN, func=AF.Tanh,
                    scale=a.RMSW[:, jj:jj + 1]), sig=True)
                p.wait("ve", th)
                p.op("ve", lambda e, ct=ct: e.tensor_tensor(
                    out=HFM[:, ct, :], in0=HFM[:, ct, :], in1=OS[:, ct, :],
                    op=ALU.mult), sig=True)
        h_done = p.tok("vesig")

        rot = Rot(p, [0, 1, 2, 3])
        STG = a.v6b(0, 4, 512)
        stores = [None]

        def load_wo(i, par):
            return p.dma("sy", a.A5[:, par],
                         rr(wout_T)[:, :, i * 512:(i + 1) * 512], sem=f"sl{par}")

        def body_wo(cc, par, ltok):
            p.wait("te", ltok)
            if cc == 0:
                p.wait("te", h_done)
            p.wait("ve", stores[0])
            last = None
            ev = None
            for tt in range(4):
                j, bank, ftok = rot.get()
                p.wait("te", ftok)
                for ctc in range(KT):
                    last = p.op("te", lambda e, ctc=ctc, tt=tt, par=par, bank=bank: e.matmul(
                        out=a.PS[bank][:],
                        lhsT=HFM[:, ctc, tt * 128:(tt + 1) * 128],
                        rhs=a.A5[:, par, ctc, :],
                        start=(ctc == 0), stop=(ctc == KT - 1)),
                        sig=(ctc == KT - 1))
                p.wait("ve", last)
                p.wait("ve", xt)
                ev = p.op("ve", lambda e, tt=tt, cc=cc, bank=bank: e.tensor_tensor(
                    out=STG[:, tt, :], in0=a.PS[bank][:],
                    in1=a.A1[:, tt, cc * 512:(cc + 1) * 512], op=ALU.add),
                    sig=True)
                rot.done(j, ev)
            p.wait("gp", ev)
            stores[0] = p.dma(
                "gp", rr(dst_x[blk * TB:(blk + 1) * TB,
                               cc * 512:(cc + 1) * 512]), STG[:], sem="xst")
            return last

        stream_slabs(p, 4, load_wo, body_wo)
        prior2 = stores[0]
    return prior2



def emit_moe(p, a, src_x, dst_x, lnt, acc_scr, w, tag, prior0=None):
    toks = getattr(p, "_a4_free", None) or (
        p.tok("tesig"), p.tok("vesig"), p.tok("scsig"))
    p._a4_free = None
    for tk_ in toks:
        p.wait("gp", tk_)
    NW = a.f4(8192, 2048)
    PRJB = a.r4(10240, 2048)
    NWROW = a.r4(12288, 2048)
    EB = a.r4(14336, 16384)
    RTRB = a.r4(30720, 8)
    SH = [a.f4(30728, 512), a.f4(31240, 512)]
    ACC = a.v4(0, 4, 2048)
    RW = a.WB[:, 2048:2176].rearrange("p (a b) -> p a b", a=16)

    p.dma("gp", PRJB, w["proj_b"])
    p.dma("gp", RTRB, w["router_b"])
    p.dma("gp", NWROW, w["norm_w"])
    p.dma("gp", EB, w["expert_b"])
    cw = p.dma("gp", RW, rr(w["router_w"]))
    p.wait("te", cw)
    nwt = None
    for cc in range(4):
        p.wait("te", nwt)
        bt = p.op("te", lambda e, cc=cc: e.matmul(
            out=a.PS[4][:], lhsT=a.ONES[0:1, 0:128],
            rhs=NWROW[0:1, cc * 512:(cc + 1) * 512], start=True, stop=True),
            sig=True)
        p.wait("sc", bt)
        nwt = p.op("sc", lambda e, cc=cc: e.copy(
            out=NW[:, cc * 512:(cc + 1) * 512], in_=a.PS[4][:]), sig=True)
    p.wait("te", nwt)

    LG = a.v6b(4096, 4, 8)
    W8 = a.v6b(4224, 4, 8)
    TOPW = a.v6b(4352, 4, 16)
    Y = a.v6b(0, 2, 2048)
    XN = a.v6(4096, 2048)
    q2 = a.A7[:, 0:1]
    rq = a.A7[:, 1:2]

    prior = prior0
    for blk in range(NBLK):
        xt, fm = emit_ln_block(p, a, src_x, blk, lnt=lnt, prior=prior)
        ve_all = p.tok("vesig")
        sc_all = p.tok("scsig")
        p.wait("te", fm)
        p.wait("te", ve_all)
        p.wait("te", sc_all)
        rt = None
        for tt in range(4):
            p.op("te", lambda e, tt=tt: e.matmul(
                out=a.PS[4][:, tt * 8:(tt + 1) * 8], lhsT=a.ONES[0:1, 0:128],
                rhs=RTRB, start=True, stop=False))
            for kt in range(KT):
                rt = p.op("te", lambda e, tt=tt, kt=kt: e.matmul(
                    out=a.PS[4][:, tt * 8:(tt + 1) * 8],
                    lhsT=a.A3[:, kt, tt * 128:(tt + 1) * 128], rhs=RW[:, kt, :],
                    start=False, stop=(kt == KT - 1)), sig=(kt == KT - 1 and tt == 3))
        p.wait("ve", rt)
        tk = p.op("ve", lambda e: e.tensor_copy(out=LG.rearrange("p a b -> p (a b)"),
                                                in_=a.PS[4][:, 0:32]), sig=True)
        for tt in range(4):
            m1 = TOPW[:, tt, 0:1]
            s1 = TOPW[:, tt, 1:2]
            m2 = TOPW[:, tt, 2:3]
            s2 = TOPW[:, tt, 3:4]
            dd = TOPW[:, tt, 4:5]
            e2 = TOPW[:, tt, 5:6]
            w1 = TOPW[:, tt, 6:7]
            w2 = TOPW[:, tt, 7:8]
            EQ1 = TOPW[:, tt, 8:16]
            LM = W8[:, tt, :]
            def ch(fn, tk):
                p.wait("ve", tk)
                return p.op("ve", fn, sig=True)
            tk = ch(lambda e, tt=tt, m1=m1: e.reduce_max(
                out=m1, in_=LG[:, tt, :], axis=AX.X), tk)
            tk = ch(lambda e, tt=tt, m1=m1, EQ1=EQ1: e.tensor_scalar(
                out=EQ1, in0=LG[:, tt, :], scalar1=m1, scalar2=None,
                op0=ALU.is_equal), tk)
            tk = ch(lambda e, s1=s1, EQ1=EQ1: e.reduce_sum(
                out=s1, in_=EQ1, axis=AX.X), tk)
            tk = ch(lambda e, s1=s1: e.reciprocal(out=s1, in_=s1), tk)
            tk = ch(lambda e, tt=tt, EQ1=EQ1, LM=LM: e.scalar_tensor_tensor(
                out=LM, in0=EQ1, scalar=-1e9, in1=LG[:, tt, :],
                op0=ALU.mult, op1=ALU.add), tk)
            tk = ch(lambda e, m2=m2, LM=LM: e.reduce_max(
                out=m2, in_=LM, axis=AX.X), tk)
            tk = ch(lambda e, dd=dd, m2=m2, m1=m1: e.tensor_tensor(
                out=dd, in0=m2, in1=m1, op=ALU.subtract), tk)
            p.wait("sc", tk)
            etok = p.op("sc", lambda e, e2=e2, dd=dd: e.activation(
                out=e2, in_=dd, func=AF.Exp), sig=True)
            p.wait("ve", etok)
            tk = p.op("ve", lambda e, w1=w1, e2=e2: e.tensor_scalar(
                out=w1, in0=e2, scalar1=1.0, scalar2=None, op0=ALU.add), sig=True)
            tk = ch(lambda e, w1=w1: e.reciprocal(out=w1, in_=w1), tk)
            tk = ch(lambda e, w2=w2, e2=e2, w1=w1: e.tensor_tensor(
                out=w2, in0=e2, in1=w1, op=ALU.mult), tk)
            tk = ch(lambda e, m2=m2, LM=LM: e.tensor_scalar(
                out=LM, in0=LM, scalar1=m2, scalar2=None, op0=ALU.is_equal), tk)
            tk = ch(lambda e, s2=s2, LM=LM: e.reduce_sum(
                out=s2, in_=LM, axis=AX.X), tk)
            tk = ch(lambda e, s2=s2: e.reciprocal(out=s2, in_=s2), tk)
            tk = ch(lambda e, w2=w2, s2=s2: e.tensor_tensor(
                out=w2, in0=w2, in1=s2, op=ALU.mult), tk)
            tk = ch(lambda e, w1=w1, s1=s1: e.tensor_tensor(
                out=w1, in0=w1, in1=s1, op=ALU.mult), tk)
            tk = ch(lambda e, tt=tt, EQ1=EQ1, w1=w1: e.tensor_scalar(
                out=EQ1, in0=EQ1, scalar1=w1, scalar2=None, op0=ALU.mult), tk)
            tk = ch(lambda e, tt=tt, LM=LM, w2=w2, EQ1=EQ1: e.scalar_tensor_tensor(
                out=W8[:, tt, :], in0=LM, scalar=w2, in1=EQ1,
                op0=ALU.mult, op1=ALU.add), tk)
        w8_done = p.tok("vesig")

        rot = Rot(p, [0, 1, 2, 3])
        shp = [0]

        def load_ex(i, par):
            e_, fc = i // 4, i % 4
            return p.dma("gp", a.A5[:, par],
                         rr(w["expert_w"][e_][:, fc * 512:(fc + 1) * 512]),
                         sem=f"exl{par}")

        def body_ex(i, par, ltok):
            e_, fc = i // 4, i % 4
            p.wait("te", ltok)
            if i == 0:
                p.wait("te", w8_done)
            last = None
            for tt in range(4):
                j, bank, ftok = rot.get()
                p.wait("te", ftok)
                p.op("te", lambda e, e_=e_, fc=fc, tt=tt, bank=bank: e.matmul(
                    out=a.PS[bank][:], lhsT=a.ONES[0:1, 0:128],
                    rhs=EB[0:1, e_ * 2048 + fc * 512: e_ * 2048 + (fc + 1) * 512],
                    start=True, stop=False))
                for kt in range(KT):
                    last = p.op("te", lambda e, kt=kt, tt=tt, par=par, bank=bank: e.matmul(
                        out=a.PS[bank][:], lhsT=a.A3[:, kt, tt * 128:(tt + 1) * 128],
                        rhs=a.A5[:, par, kt, :], start=False,
                        stop=(kt == KT - 1)), sig=(kt == KT - 1))
                p.wait("sc", last)
                sp = shp[0] % 2
                shp[0] += 1
                p.wait("sc", p.tok("vesig") if shp[0] > 2 else None)
                sg = p.op("sc", lambda e, bank=bank, sp=sp: e.activation(
                    out=SH[sp], in_=a.PS[bank][:], func=AF.Silu), sig=True)
                p.wait("ve", sg)
                if e_ == 0:
                    ev = p.op("ve", lambda e, tt=tt, fc=fc, sp=sp: e.tensor_scalar(
                        out=ACC[:, tt, fc * 512:(fc + 1) * 512], in0=SH[sp],
                        scalar1=W8[:, tt, 0:1], scalar2=None, op0=ALU.mult),
                        sig=True)
                else:
                    ev = p.op("ve", lambda e, tt=tt, fc=fc, sp=sp, e_=e_: e.scalar_tensor_tensor(
                        out=ACC[:, tt, fc * 512:(fc + 1) * 512], in0=SH[sp],
                        scalar=W8[:, tt, e_:e_ + 1],
                        in1=ACC[:, tt, fc * 512:(fc + 1) * 512],
                        op0=ALU.mult, op1=ALU.add), sig=True)
                rot.done(j, ev)
            return last

        stream_slabs(p, 32, load_ex, body_ex, q="gp")
        acc_done = p.tok("vesig")

        p.wait("sy", acc_done)
        accst = p.dma("sy", rr(acc_scr[blk * TB:(blk + 1) * TB, :]), ACC[:],
                      sem="lns")
        p.wait("sy", accst)
        p.wait("sy", p.tok("tesig"))
        fmt = None
        for kt in range(KT):
            fmt = p.dma("sy", a.A3[:, kt, :],
                        acc_scr[blk * TB:(blk + 1) * TB, kt * 128:(kt + 1) * 128],
                        transpose=True, sem="fmt")
        last_store = None
        for tth in range(2):
            rot2 = Rot(p, [0, 1, 2, 3])
            yev = None

            def load_pj(cc, par, tth=tth):
                return p.dma("gp", a.A5[:, par],
                             rr(w["proj_w"][:, cc * 512:(cc + 1) * 512]),
                             sem=f"exl{par}")

            def body_pj(cc, par, ltok, tth=tth):
                nonlocal yev
                p.wait("te", ltok)
                if cc == 0:
                    p.wait("te", fmt)
                    p.wait("te", p.tok("vesig"))
                last = None
                for tt2 in range(2):
                    tt = tth * 2 + tt2
                    j, bank, ftok = rot2.get()
                    p.wait("te", ftok)
                    p.op("te", lambda e, cc=cc, bank=bank: e.matmul(
                        out=a.PS[bank][:], lhsT=a.ONES[0:1, 0:128],
                        rhs=PRJB[0:1, cc * 512:(cc + 1) * 512],
                        start=True, stop=False))
                    for ft in range(KT):
                        last = p.op("te", lambda e, ft=ft, tt=tt, par=par, bank=bank: e.matmul(
                            out=a.PS[bank][:],
                            lhsT=a.A3[:, ft, tt * 128:(tt + 1) * 128],
                            rhs=a.A5[:, par, ft, :], start=False,
                            stop=(ft == KT - 1)), sig=(ft == KT - 1))
                    p.wait("ve", last)
                    p.wait("ve", prior)
                    yev = p.op("ve", lambda e, tt2=tt2, tt=tt, cc=cc, bank=bank: e.tensor_tensor(
                        out=Y[:, tt2, cc * 512:(cc + 1) * 512], in0=a.PS[bank][:],
                        in1=a.A2[:, tt, cc * 512:(cc + 1) * 512], op=ALU.add),
                        sig=True)
                    rot2.done(j, yev)
                return last

            stream_slabs(p, 4, load_pj, body_pj, q="gp")
            for tt2 in range(2):
                tt = tth * 2 + tt2
                Yt = Y[:, tt2, :]
                p.wait("ve", last_store)
                p.wait("ve", prior)
                p.wait("ve", yev)
                tk2 = p.op("ve", lambda e, Yt=Yt: e.scalar_tensor_tensor(
                    out=XN, in0=Yt, scalar=1.0, in1=Yt,
                    op0=ALU.mult, op1=ALU.mult, accum_out=q2), sig=True)
                p.wait("ve", tk2)
                tk2 = p.op("ve", lambda e: e.tensor_scalar(
                    out=rq, in0=q2, scalar1=1.0 / D, scalar2=EPS_MOE,
                    op0=ALU.mult, op1=ALU.add), sig=True)
                p.wait("sc", tk2)
                sq = p.op("sc", lambda e: e.activation(
                    out=rq, in_=rq, func=AF.Sqrt), sig=True)
                p.wait("ve", sq)
                tk2 = p.op("ve", lambda e: e.reciprocal(out=rq, in_=rq), sig=True)
                p.wait("ve", tk2)
                tk2 = p.op("ve", lambda e: e.tensor_scalar(
                    out=rq, in0=rq, scalar1=0.5, scalar2=None, op0=ALU.mult),
                    sig=True)
                p.wait("ve", tk2)
                tk2 = p.op("ve", lambda e, Yt=Yt: e.scalar_tensor_tensor(
                    out=XN, in0=Yt, scalar=rq, in1=NW,
                    op0=ALU.mult, op1=ALU.mult), sig=True)
                p.wait("ve", tk2)
                xntok = p.op("ve", lambda e, tt=tt: e.tensor_tensor(
                    out=XN, in0=XN, in1=a.A1[:, tt, :], op=ALU.add), sig=True)
                p.wait("gp", xntok)
                last_store = p.dma(
                    "gp", dst_x[blk * TB + tt * 128: blk * TB + (tt + 1) * 128, :],
                    XN, sem="xst")
        prior = last_store
    return prior



def emit_l1attn(p, a, src_x, dst_x, agin_ln, agout_ln, proj_part, rs_out, w, prior0=None):
    KTf = a.f4(0, 8192)
    KTM = a.v4(8192, 64, 128)
    QH = a.v4(16384, 4, 512)
    E2 = [a.f4(18432, 512), a.f4(18944, 512)]
    E2B = [a.f4(25088, 512), a.f4(25600, 512)]
    ACCE2 = None
    RBF = a.f4(19456, 512)
    YS4 = a.v4(19968, 4, 512)
    RB16 = a.r4(22016, 512)
    WPJ = a.A5[:, 1].rearrange("p a b -> p (a b)")[:, 0:8192].rearrange(
        "p (a c d) -> p a c d", a=4, c=4)
    RrowF = a.A6[0:1, 2560:3072]
    WL = a.WB[:, 0:2048].rearrange("p (a b) -> p a b", a=16)

    prior = prior0
    for blk in range(NBLK):
        xt, lntok = emit_ln_block(p, a, src_x, blk, ln_rows=(agin_ln, 0),
                                  prior=prior)
        prior = lntok
    p.wait("gp", prior)
    cct = p.cc("AllGather", [agin_ln[:]], [agout_ln[:]], "ccln")

    p.wait("gp", p.tok("tesig"))
    p.wait("gp", p.tok("vesig"))
    p.wait("gp", p.tok("scsig"))
    cw1 = p.dma("gp", WL, rr(w["W_lat"]))
    cw2 = p.dma("gp", a.A5[:, 0], rr(w["W_d_slice"]))
    cw3 = p.dma("gp", WPJ,
                rr(w["W_proj_slice"]).rearrange("p a (c d) -> p a c d", d=512))
    p.wait("sy", cct)

    STG = a.v6b(0, 4, 512)
    prev_q_evac = None
    prev_e2g = [None, None, None, None]
    prev_e2r = [None, None, None, None]
    prev_ab = None
    prev_ys = None
    prev_rc = None
    prev_head_exps = [None, None]
    prev_stg_store = None
    ktm_ev = [None, None]
    for sc in range(16):
        p.wait("sy", p.tok("tesig"))
        fmt = None
        for kt in range(KT):
            fmt = p.dma("sy", a.A3[:, kt, :],
                        agout_ln[sc * 512:(sc + 1) * 512,
                                 kt * 128:(kt + 1) * 128],
                        transpose=True, sem="fmt")
        p.wait("te", fmt)
        p.wait("te", prev_q_evac)
        p.wait("te", cw1)
        if sc == 0:
            p.wait("te", cw2)
            p.wait("te", cw3)
        km = None
        for kt in range(KT):
            km = p.op("te", lambda e, kt=kt, sc=sc: e.matmul(
                out=a.PS[0][:], lhsT=WL[:, kt, :], rhs=a.A3[:, kt, :],
                start=(kt == 0), stop=(kt == KT - 1)), sig=(kt == KT - 1))
        p.wait("sc", km)
        kcopy = p.op("sc", lambda e, sc=sc: e.activation(
            out=KTf[:, sc * 512:(sc + 1) * 512], in_=a.PS[0][:], func=AF.Identity,
            bias=a.BL[:, 0:1]), sig=True)
        par = sc % 2
        p.wait("te", kcopy)
        p.wait("te", ktm_ev[par])
        tpt = None
        for ss in range(4):
            tpt = p.op("te", lambda e, sc=sc, ss=ss, par=par: e.transpose(
                out=a.PB[par][:, ss * 128:(ss + 1) * 128],
                in_=KTf[:, sc * 512 + ss * 128: sc * 512 + (ss + 1) * 128],
                identity=a.ID[:]), sig=(ss == 3))
        p.wait("ve", tpt)
        ktm_ev[par] = p.op("ve", lambda e, sc=sc, par=par: e.tensor_copy(
            out=KTM[:, 4 * sc:4 * sc + 4, :].rearrange("p a b -> p (a b)"),
            in_=a.PB[par][:, 0:512]), sig=True)
        qe = None
        for h in range(4):
            p.wait("te", prev_q_evac)
            qm = None
            for kt in range(KT):
                qm = p.op("te", lambda e, kt=kt, h=h: e.matmul(
                    out=a.PS[1][:], lhsT=a.A5[:, 0, kt, h * 128:(h + 1) * 128],
                    rhs=a.A3[:, kt, :], start=(kt == 0), stop=(kt == KT - 1)),
                    sig=(kt == KT - 1))
            p.wait("sc", qm)
            qe = p.op("sc", lambda e, h=h: e.activation(
                out=QH[:, h, :], in_=a.PS[1][:], func=AF.Identity,
                bias=a.BD[:, h:h + 1]), sig=True)
            prev_q_evac = qe
        nsb = 4 * (sc + 1)
        p.wait("te", ktm_ev[par])
        ACCE_A = a.v6(3072, 512)
        ACCE_B = a.v6(3584, 512)
        for hp in range(2):
            hA, hB = 2 * hp, 2 * hp + 1
            if hp == 0:
                p.wait("te", qe)
            sides = [
                dict(h=hA, Sb=(2, 3), AV=4, E=E2, ACC=ACCE_A,
                     eready={}, ex={}, av={}, acc={}, dn_bank=2),
                dict(h=hB, Sb=(0, 1), AV=5, E=E2B, ACC=ACCE_B,
                     eready={}, ex={}, av={}, acc={}, dn_bank=3),
            ]
            for step in range(nsb + 1):
                for si, S in enumerate(sides):
                    if step < nsb:
                        sb = step
                        bpar = sb % 2
                        bank = a.PS[S["Sb"][bpar]]
                        p.wait("te", S["ex"].get(sb - 2))
                        if sb < 2:
                            p.wait("te", prev_e2g[2 * si + bpar])
                            if si == 1:
                                p.wait("te", prev_q_evac)
                        st = p.op("te", lambda e, sb=sb, S=S, bank=bank: e.matmul(
                            out=bank[:], lhsT=KTf[:, sb * 128:(sb + 1) * 128],
                            rhs=QH[:, S["h"], :], start=True, stop=True),
                            sig=True)
                        p.wait("sc", st)
                        p.wait("sc", S["av"].get(sb - 2))
                        p.wait("sc", S["acc"].get(sb - 2))
                        if sb < 2:
                            p.wait("sc", prev_e2r[2 * si + bpar])
                        ex = p.op("sc", lambda e, S=S, bpar=bpar, bank=bank: e.activation(
                            out=S["E"][bpar], in_=bank[:], func=AF.Exp,
                            scale=ISQL), sig=True)
                        S["ex"][sb] = ex
                        v = sb - 4 * sc
                        if v >= 0:
                            p.wait("ve", ex)
                            mk = p.op("ve", lambda e, S=S, bpar=bpar, v=v: e.tensor_tensor(
                                out=S["E"][bpar], in0=S["E"][bpar],
                                in1=a.MSK[:, 512 - v * 128:1024 - v * 128],
                                op=ALU.mult), sig=True)
                            S["eready"][sb] = mk
                        else:
                            S["eready"][sb] = ex
                        p.wait("ve", S["eready"][sb])
                        p.wait("ve", S["acc"].get(sb - 1))
                        if sb == 0:
                            p.wait("ve", prev_ab)
                            S["acc"][sb] = p.op("ve", lambda e, S=S, bpar=bpar: e.tensor_scalar(
                                out=S["ACC"], in0=S["E"][bpar], scalar1=1.0,
                                scalar2=None, op0=ALU.mult), sig=True)
                        else:
                            S["acc"][sb] = p.op("ve", lambda e, S=S, bpar=bpar: e.tensor_tensor(
                                out=S["ACC"], in0=S["ACC"], in1=S["E"][bpar],
                                op=ALU.add), sig=True)
                    if step >= 1:
                        sb2 = step - 1
                        bpar2 = sb2 % 2
                        p.wait("te", S["eready"][sb2])
                        if sb2 == 0:
                            p.wait("te", prev_ys)
                            p.wait("te", prev_rc)
                        S["av"][sb2] = p.op("te", lambda e, S=S, sb2=sb2, bpar2=bpar2, nsb=nsb: e.matmul(
                            out=a.PS[S["AV"]][:], lhsT=KTM[:, sb2, :],
                            rhs=S["E"][bpar2], start=(sb2 == 0),
                            stop=(sb2 == nsb - 1)), sig=True)
            prev_e2g = [sides[0]["av"][nsb - 2], sides[0]["av"][nsb - 1],
                        sides[1]["av"][nsb - 2], sides[1]["av"][nsb - 1]]
            prev_e2r = [sides[0]["acc"][nsb - 2], sides[0]["acc"][nsb - 1],
                        sides[1]["acc"][nsb - 2], sides[1]["acc"][nsb - 1]]
            for si, S in enumerate(sides):
                p.wait("ve", S["acc"][nsb - 1])
                p.wait("ve", prev_stg_store)
                p.wait("ve", prev_ab)
                ab = p.op("ve", lambda e, S=S: e.tensor_copy(
                    out=RB16_acc(a), in_=S["ACC"]), sig=True)
                prev_ab = ab
                p.wait("te", ab)
                p.wait("te", prev_rc)
                dnm = p.op("te", lambda e, S=S: e.matmul(
                    out=a.PS[S["dn_bank"]][0:1, :], lhsT=a.ONES[:, 0:1],
                    rhs=RB16_acc(a), start=True, stop=True), sig=True)
                p.wait("ve", dnm)
                t6 = p.op("ve", lambda e, S=S: e.reciprocal(
                    out=RrowF, in_=a.PS[S["dn_bank"]][0:1, :]), sig=True)
                prev_rc = t6
                p.wait("ve", t6)
                rv = p.op("ve", lambda e: e.tensor_copy(out=RB16, in_=RrowF),
                          sig=True)
                p.wait("te", rv)
                p.wait("te", prev_q_evac)
                bc = p.op("te", lambda e: e.matmul(
                    out=a.PS[1][:], lhsT=a.ONES[0:1, :], rhs=RB16,
                    start=True, stop=True), sig=True)
                p.wait("sc", bc)
                rbf = p.op("sc", lambda e: e.copy(out=RBF, in_=a.PS[1][:]),
                           sig=True)
                prev_q_evac = rbf
                p.wait("ve", rbf)
                p.wait("ve", S["av"][nsb - 1])
                ys = p.op("ve", lambda e, S=S: e.tensor_tensor(
                    out=YS4[:, S["h"], :], in0=a.PS[S["AV"]][:], in1=RBF,
                    op=ALU.mult), sig=True)
                prev_ys = ys
        ys_done = prev_ys
        p.wait("te", ys_done)
        STGB = [a.f4(23040, 512), a.f4(23552, 512), a.f4(24064, 512),
                a.f4(24576, 512)]
        for cc_ in range(4):
            p.wait("ve", prev_stg_store)
            last = None
            evs = None
            for tt in range(4):
                bank = cc_ % 2
                p.wait("te", evs)
                for h in range(4):
                    last = p.op("te", lambda e, h=h, tt=tt, cc_=cc_, bank=bank: e.matmul(
                        out=a.PS[bank][:],
                        lhsT=YS4[:, h, tt * 128:(tt + 1) * 128],
                        rhs=WPJ[:, h, cc_, :],
                        start=(h == 0), stop=(h == 3)), sig=(h == 3))
                p.wait("ve", last)
                evs = p.op("ve", lambda e, tt=tt, bank=bank: e.tensor_copy(
                    out=STGB[tt], in_=a.PS[bank][:]), sig=True)
            p.wait("gp", evs)
            pp = rr(proj_part[sc * 512:(sc + 1) * 512,
                              cc_ * 512:(cc_ + 1) * 512])
            for tt in range(4):
                prev_stg_store = p.dma("gp", pp[:, tt:tt + 1, :], STGB[tt],
                                       sem="ppst")
        prev_q_evac = p.tok("vesig")

    p.wait("gp", prev_stg_store)
    ccr = p.cc("ReduceScatter", [proj_part[:]], [rs_out[:]], "ccrs", op=ALU.add)
    p.wait("gp", ccr)

    p._a4_free = (p.tok("tesig"), p.tok("vesig"), p.tok("scsig"))
    bp2 = p.dma("gp", a.r4(0, 2048), w["b_proj"], sem="bpl")
    p.wait("te", bp2)
    p.wait("te", p.tok("vesig"))
    p.wait("te", p.tok("scsig"))
    bft = None
    for cc_ in range(4):
        p.wait("te", bft)
        bt = p.op("te", lambda e, cc_=cc_: e.matmul(
            out=a.PS[1][:], lhsT=a.ONES[0:1, :],
            rhs=a.r4(cc_ * 512, 512), start=True, stop=True), sig=True)
        p.wait("sc", bt)
        bft = p.op("sc", lambda e, cc_=cc_: e.copy(
            out=a.A2[:, 0, cc_ * 512:(cc_ + 1) * 512], in_=a.PS[1][:]),
            sig=True)

    RS6 = a.v6(0, 2048)
    XN6 = a.v6(2048, 2048)
    prior = None
    for blk in range(NBLK):
        p.wait("gp", prior)
        xt = p.dma("gp", a.A1[:], rr(src_x[blk * TB:(blk + 1) * TB, :]))
        for tt in range(4):
            r0 = blk * TB + tt * 128
            lt = p.dma("gp", RS6, rs_out[r0:r0 + 128, :], sem="rsl")
            p.wait("ve", lt)
            p.wait("ve", xt)
            p.wait("ve", bft)
            p.wait("ve", prior)
            t7 = p.op("ve", lambda e: e.tensor_tensor(
                out=XN6, in0=RS6, in1=a.A2[:, 0, :], op=ALU.add), sig=True)
            p.wait("ve", t7)
            t8 = p.op("ve", lambda e, tt=tt: e.tensor_tensor(
                out=XN6, in0=XN6, in1=a.A1[:, tt, :], op=ALU.add), sig=True)
            p.wait("gp", t8)
            prior = p.dma("gp", dst_x[r0:r0 + 128, :], XN6, sem="xst")
    return prior


def RB16_acc(a):
    return a.f4(22528, 512)



WT_SPECS = {
    "gup": ("L0_ffn_w_gate_up", 16384, 2048),
    "wdn": ("L0_ffn_w_down", 2048, 8192),
    "win": ("L0_attn_W_in", 2048, 2048),
    "wgt": ("L0_attn_W_gate", 8192, 2048),
    "wout": ("L0_attn_W_out", 2048, 2048),
}

def wt_perm(name):
    if name == "gup":
        return [(i // 2) + 64 * (i % 2) for i in range(128)]
    if name == "wgt":
        perm = []
        for i in range(64):
            head, jj, ty = i // 16, (i // 4) % 4, i % 4
            perm.append(head * 16 + ty * 4 + jj)
        return perm
    return None


def build_program(secs, inputs_present):
    p = Prog()
    nc = p.nc
    stack = ExitStack()
    a = Arenas(nc, stack)

    def din(name, shape):
        return nc.dram_tensor(name, shape, F32, kind="ExternalInput").ap()

    xin = din("xin", [T, D])
    xout = nc.dram_tensor("xout", [T, D], F32, kind="ExternalOutput").ap()

    idt = nc.dram_tensor("c_ident", [128, 128], F32, kind="ExternalInput").ap()
    onest = nc.dram_tensor("c_ones", [128, 128], F32, kind="ExternalInput").ap()
    mskt = nc.dram_tensor("c_mask", [128, 1024], F32, kind="ExternalInput").ap()
    ld1 = p.dma("gp", a.ID[:], idt[:])
    ld2 = p.dma("gp", a.ONES[:], onest[:])
    ld3 = p.dma("gp", a.MSK[:], mskt[:])

    need = set()
    for s in secs:
        if s in (1, 3):
            need |= {"gup", "wdn"}
        if s == 2:
            need |= {"win", "wgt", "wout"}

    lnt = nc.dram_tensor("lnt", [T, D], BF16).ap()
    wt = {}
    for nm in sorted(need):
        src_name, O, K = WT_SPECS[nm]
        w_hbm = din(src_name, [O, K])
        dst = nc.dram_tensor(f"wt_{nm}", [K, O], BF16).ap()
        emit_wt_construct(p, a, w_hbm, None, dst, O, K, perm=wt_perm(nm))
        wt[nm] = dst
    for eng in ENGINES:
        p.wait(eng, p.tok("wts0"))
        p.wait(eng, p.tok("wts1"))

    n = len(secs)
    if n == 1:
        hops = [(xin, xout)]
    else:
        xs = [nc.dram_tensor(f"xs{i}", [T, D], F32).ap() for i in range(n - 1)]
        hops = []
        cur = xin
        for i in range(n - 1):
            hops.append((cur, xs[i]))
            cur = xs[i]
        hops.append((cur, xout))

    prior_tok = None
    for (s, (src, dst)) in zip(secs, hops):
        if s in (1, 3):
            prior_tok = emit_ffn0(p, a, src, dst, lnt, wt["gup"], wt["wdn"],
                                  prior0=prior_tok)
        elif s == 2:
            for nm2 in ("b_gate", "rms_w"):
                pass
            bgt = din("L0_attn_b_gate_r", [128, 64])
            rwt = din("L0_attn_rms_w_r", [128, 4])
            b1 = p.dma("gp", a.BG[:], bgt[:])
            b2 = p.dma("gp", a.RMSW[:], rwt[:])
            p.wait("sc", b2)
            cp_scr = nc.dram_tensor("cp_scr", [T, D], BF16).ap()
            cl_scr = nc.dram_tensor("cl_scr", [T, D], BF16).ap()
            o_scr = nc.dram_tensor("o_scr", [T, D], BF16).ap()
            agin = nc.dram_tensor("ag_scan_in", [128, 32], F32).ap()
            agout = nc.dram_tensor("ag_scan_out", [512, 32], F32).ap()
            scm = din("scanmask", [128, 3])
            b3 = p.dma("gp", a.A7[:, 112:115], scm[:])
            p.wait("ve", b3)
            prior_tok = emit_l0attn(p, a, src, dst, lnt, wt["win"], wt["wgt"],
                                    wt["wout"], cp_scr, cl_scr, o_scr, agin,
                                    agout, prior0=prior_tok)
        elif s in (4, 6):
            w = {
                "router_w": din("L1_ffn_router_w", [D, 8]),
                "router_b": din("L1_ffn_router_b_r", [1, 8]),
                "expert_w": din("L1_ffn_expert_w", [8, D, D]),
                "expert_b": din("L1_ffn_expert_b_r", [1, 16384]),
                "proj_w": din("L1_ffn_proj_w", [D, D]),
                "proj_b": din("L1_ffn_proj_b_r", [1, D]),
                "norm_w": din("L1_ffn_norm_w_r", [1, D]),
            } if s == 4 or 4 not in secs else build_program._moe_w
            if s == 4 or 4 not in secs:
                build_program._moe_w = w
            acc_scr = nc.dram_tensor(f"acc_scr{s}", [T, D], BF16).ap()
            prior_tok = emit_moe(p, a, src, dst, lnt, acc_scr, w, tag=f"m{s}",
                                 prior0=prior_tok)
        elif s == 5:
            w5 = {
                "W_lat": din("L1_attn_W_lat", [D, 128]),
                "W_d_slice": din("W_d_slice", [D, 512]),
                "W_proj_slice": din("W_proj_slice", [512, D]),
                "b_proj": din("L1_attn_b_proj_r", [1, D]),
            }
            blt = din("b_lat_r", [128, 1])
            bdt = din("b_d_slice_r", [128, 4])
            p.dma("gp", a.BL[:], blt[:])
            b4 = p.dma("gp", a.BD[:], bdt[:])
            p.wait("sc", b4)
            agin_ln = nc.dram_tensor("agin_ln", [T, D], BF16).ap()
            agout_ln = nc.dram_tensor("agout_ln", [SEQ, D], BF16).ap()
            proj_part = nc.dram_tensor("proj_part", [SEQ, D], BF16).ap()
            rs_out = nc.dram_tensor("rs_out", [T, D], BF16).ap()
            prior_tok = emit_l1attn(p, a, src, dst, agin_ln, agout_ln,
                                    proj_part, rs_out, w5, prior0=prior_tok)

    p.wait("gp", p.tok("xst"))
    p.wait("gp", p.tok("gpdma"))
    nc_final = p.finalize()
    stack.close()
    return nc_final



def _consts():
    ident = np.eye(128, dtype=np.float32)
    ones = np.ones((128, 128), dtype=np.float32)
    msk = np.zeros((128, 1024), dtype=np.float32)
    pp, qq = np.meshgrid(np.arange(128), np.arange(1024), indexing="ij")
    msk[pp <= qq - 512] = 1.0
    return ident, ones, msk


def _prep_core_inputs(inputs, core):
    b, me = core // 4, core % 4
    ident, ones, msk = _consts()
    d = {
        "c_ident": ident, "c_ones": ones, "c_mask": msk,
        "L0_ffn_w_gate_up": inputs["L0_ffn_w_gate_up"],
        "L0_ffn_w_down": inputs["L0_ffn_w_down"],
        "L0_attn_W_in": inputs["L0_attn_W_in"],
        "L0_attn_W_gate": inputs["L0_attn_W_gate"],
        "L0_attn_W_out": inputs["L0_attn_W_out"],
        "L0_attn_b_gate_r": np.ascontiguousarray(
            inputs["L0_attn_b_gate"].reshape(64, 128).T),
        "L0_attn_rms_w_r": np.ascontiguousarray(
            inputs["L0_attn_rms_w"].reshape(4, 128).T),
        "scanmask": np.broadcast_to(
            (np.arange(3) < me).astype(np.float32), (128, 3)).copy(),
        "L1_ffn_router_w": inputs["L1_ffn_router_w"],
        "L1_ffn_router_b_r": inputs["L1_ffn_router_b"].reshape(1, 8),
        "L1_ffn_expert_w": inputs["L1_ffn_expert_w"],
        "L1_ffn_expert_b_r": inputs["L1_ffn_expert_b"].reshape(1, 16384),
        "L1_ffn_proj_w": inputs["L1_ffn_proj_w"],
        "L1_ffn_proj_b_r": inputs["L1_ffn_proj_b"].reshape(1, 2048),
        "L1_ffn_norm_w_r": inputs["L1_ffn_norm_w"].reshape(1, 2048),
        "L1_attn_W_lat": inputs["L1_attn_W_lat"],
        "W_d_slice": np.ascontiguousarray(
            inputs["L1_attn_W_d"][:, me * 512:(me + 1) * 512]),
        "W_proj_slice": np.ascontiguousarray(
            inputs["L1_attn_W_proj"][me * 512:(me + 1) * 512, :]),
        "L1_attn_b_proj_r": inputs["L1_attn_b_proj"].reshape(1, 2048),
        "b_lat_r": inputs["L1_attn_b_lat"].reshape(128, 1),
        "b_d_slice_r": np.ascontiguousarray(
            inputs["L1_attn_b_d"][me * 512:(me + 1) * 512].reshape(4, 128).T),
    }
    return {k: np.ascontiguousarray(np.asarray(v, dtype=np.float32))
            for k, v in d.items()}


_IN_BY_SEC = {
    1: ["L0_ffn_w_gate_up", "L0_ffn_w_down"],
    2: ["L0_attn_W_in", "L0_attn_W_gate", "L0_attn_W_out",
        "L0_attn_b_gate_r", "L0_attn_rms_w_r", "scanmask"],
    3: ["L0_ffn_w_gate_up", "L0_ffn_w_down"],
    4: ["L1_ffn_router_w", "L1_ffn_router_b_r", "L1_ffn_expert_w",
        "L1_ffn_expert_b_r", "L1_ffn_proj_w", "L1_ffn_proj_b_r",
        "L1_ffn_norm_w_r"],
    5: ["L1_attn_W_lat", "W_d_slice", "W_proj_slice", "L1_attn_b_proj_r",
        "b_lat_r", "b_d_slice_r"],
    6: ["L1_ffn_router_w", "L1_ffn_router_b_r", "L1_ffn_expert_w",
        "L1_ffn_expert_b_r", "L1_ffn_proj_w", "L1_ffn_proj_b_r",
        "L1_ffn_norm_w_r"],
}


def run_chain(inputs, groups):
    x = np.asarray(inputs["L0_attn_x"], dtype=np.float32)
    shards = [np.ascontiguousarray(x[c // 4, (c % 4) * T:(c % 4 + 1) * T, :])
              for c in range(8)]
    per_core = [_prep_core_inputs(inputs, c) for c in range(8)]
    for gi, secs in enumerate(groups):
        if hasattr(build_program, "_moe_w"):
            del build_program._moe_w
        nc = build_program(secs, None)
        names = set()
        for s in secs:
            names.update(_IN_BY_SEC[s])
        names |= {"c_ident", "c_ones", "c_mask"}
        in_maps = []
        for c in range(8):
            m = {k: per_core[c][k] for k in names if k in per_core[c]}
            m["xin"] = shards[c]
            in_maps.append(m)
        res = run_bass_kernel_spmd(nc, in_maps, core_ids=list(range(8)))
        shards = [np.asarray(res.results[c]["xout"], dtype=np.float32)
                  for c in range(8)]
    out = np.empty((2, SEQ, D), dtype=np.float32)
    for c in range(8):
        out[c // 4, (c % 4) * T:(c % 4 + 1) * T, :] = shards[c]
    return out


def kernel(**inputs):
    if FUSED:
        return run_chain(inputs, [[1, 2, 3, 4, 5, 6]])
    return run_chain(inputs, [[1], [2], [3], [4], [5], [6]])
